# version 12
# baseline (speedup 1.0000x reference)
import math
import contextlib
import numpy as np
import ml_dtypes

import concourse.bass as bass
import concourse.mybir as mybir
from concourse.bass_utils import run_bass_kernel_spmd

F32 = mybir.dt.float32
BF16 = mybir.dt.bfloat16
AF = mybir.ActivationFunctionType
ALU = mybir.AluOpType
AX = mybir.AxisListType
ts = bass.ts

D = 1024
NDC = 8
H = 128
NE = 8
DFF = 4096
FC = 256
NM = 144
BLK = 256
EPS_LN = 1e-5
EPS_U = 1e-6
SCALE_FFN = 1.0 / math.sqrt(1.0 + 2.0 / 8.0)
GC1 = 2.0 * math.sqrt(2.0 / math.pi)
GC2 = 2.0 * math.sqrt(2.0 / math.pi) * 0.044715
NEGBIG = -1e30
SM_SHIFT = -10.0


class Ctr:
    def __init__(self):
        self.c = {}

    @staticmethod
    def _k(sem):
        return getattr(sem, "name", None) or id(sem)

    def inc(self, sem, n=1):
        k = self._k(sem)
        self.c[k] = self.c.get(k, 0) + n
        return self.c[k]

    def val(self, sem):
        return self.c.get(self._k(sem), 0)




import bass_rust as _br

_KF_UID = [0]


@contextlib.contextmanager
def kreg(eng, base=0):
    _KF_UID[0] += 1
    r = eng.alloc_register(f"kr{_KF_UID[0]}")
    try:
        yield r
    finally:
        eng.free_register(r)


@contextlib.contextmanager
def kfori(nc, eng, etype, start, end, step=1):
    _KF_UID[0] += 1
    name = f"kf{_KF_UID[0]}"
    reg = eng.alloc_register(f"{name}_i")
    rh = _br.RegisterHandles((reg,))
    eng.reg_mov(reg, start)
    nc.br(f"{name}_L", engines=[etype])
    with nc.body(f"{name}_L", valid_engines=[etype]):
        yield reg
        eng.reg_add(reg, reg, step)
        nc.br_lt(rh, end, on_true=f"{name}_L", on_false=f"{name}_E", engines=[etype])
    nc.switch_bb(f"{name}_E")
    eng.free_register(reg)


_RV_CACHE = {}


def rowslice(reg, nrows, maxrow):
    key = (reg.name, nrows, maxrow)
    if key not in _RV_CACHE:
        rv = _br.RuntimeValue(_br.RegisterHandles((reg,)), 0, maxrow, nrows, 0)
        _RV_CACHE[key] = _br.DynSlice(rv, nrows)
    return _RV_CACHE[key]




class EW:

    def __init__(self, eng):
        object.__setattr__(self, "_e", eng)
        object.__setattr__(self, "last", None)

    def __getattr__(self, n):
        a = getattr(self._e, n)
        if callable(a):
            def w(*args, **kw):
                r = a(*args, **kw)
                if hasattr(r, "then_inc"):
                    object.__setattr__(self, "last", r)
                return r
            return w
        return a

    def inc_last(self, q, n=1):
        self.last.then_inc(q, n)


def build_nc(T=4096):
    NT = T
    NG = T // 512
    NBLK = T // BLK
    NTIL = T // 128
    NCH = T // 128

    nc = bass.Bass("TRN2", target_bir_lowering=False, debug=False)

    x_in = nc.dram_tensor("x", [D, NT], F32, kind="ExternalInput").ap()
    y_out = nc.dram_tensor("y", [D, NT], F32, kind="ExternalOutput").ap()

    aWq = nc.dram_tensor("aWq", [D, H], F32, kind="ExternalInput").ap()
    aWk = nc.dram_tensor("aWk", [D, H], F32, kind="ExternalInput").ap()
    aWv = nc.dram_tensor("aWv", [D, H], F32, kind="ExternalInput").ap()
    aWo = nc.dram_tensor("aWo", [H, D], F32, kind="ExternalInput").ap()
    abq = nc.dram_tensor("abq", [H, 1], F32, kind="ExternalInput").ap()
    abk = nc.dram_tensor("abk", [H, 1], F32, kind="ExternalInput").ap()
    abv = nc.dram_tensor("abv", [1, H], F32, kind="ExternalInput").ap()
    abo = nc.dram_tensor("abo", [128, NDC], F32, kind="ExternalInput").ap()

    bWq = nc.dram_tensor("bWq", [D, D], F32, kind="ExternalInput").ap()
    bWk = nc.dram_tensor("bWk", [D, D], F32, kind="ExternalInput").ap()
    bWv = nc.dram_tensor("bWv", [D, D], F32, kind="ExternalInput").ap()
    bWo = nc.dram_tensor("bWo", [D, D], F32, kind="ExternalInput").ap()

    rW = nc.dram_tensor("rW", [D, NE], F32, kind="ExternalInput").ap()
    W1T = nc.dram_tensor("W1T", [NE, D, DFF], F32, kind="ExternalInput").ap()
    W2T = nc.dram_tensor("W2T", [NE, DFF, D], F32, kind="ExternalInput").ap()
    Ws1T = nc.dram_tensor("Ws1T", [D, DFF], F32, kind="ExternalInput").ap()
    Ws2T = nc.dram_tensor("Ws2T", [DFF, D], F32, kind="ExternalInput").ap()
    b1fl = nc.dram_tensor("b1fl", [1, 9 * DFF], F32, kind="ExternalInput").ap()
    b2 = nc.dram_tensor("b2", [NE, D], F32, kind="ExternalInput").ap()
    bs2T = nc.dram_tensor("bs2T", [128, NDC], F32, kind="ExternalInput").ap()

    c_idf = nc.inline_tensor(np.eye(128, dtype=np.float32), "c_idf").ap()
    c_idb = nc.inline_tensor(np.eye(128).astype(ml_dtypes.bfloat16), "c_idb").ap()
    c_onescol_b = nc.inline_tensor(np.ones((128, 1), ml_dtypes.bfloat16), "c_onescol_b").ap()
    c_onescol_f = nc.inline_tensor(np.ones((128, 1), np.float32), "c_onescol_f").ap()
    c_onesrow_b = nc.inline_tensor(np.ones((128, 512), ml_dtypes.bfloat16), "c_onesrow_b").ap()
    c_onesrow_f = nc.inline_tensor(np.ones((128, 128), np.float32), "c_onesrow_f").ap()
    tri = (np.arange(128)[:, None] <= np.arange(128)[None, :]).astype(np.float32)
    c_triu = nc.inline_tensor(tri, "c_triu").ap()
    s_i = np.arange(128)[:, None]
    t_i = np.arange(128)[None, :]
    mB = np.where((s_i <= t_i) & (s_i >= t_i - 63), SM_SHIFT, NEGBIG).astype(np.float32)
    mA = np.where(s_i >= t_i + 65, SM_SHIFT, NEGBIG).astype(np.float32)
    c_maskB = nc.inline_tensor(mB, "c_maskB").ap()
    c_maskA = nc.inline_tensor(mA, "c_maskA").ap()
    sel = np.zeros((8, 8 * 128), np.float32)
    for e in range(8):
        sel[e, e * 128:(e + 1) * 128] = 1.0
    c_selE = nc.inline_tensor(sel, "c_selE").ap()

    W1bf = nc.dram_tensor("W1bf", [NM * 128, 2304], BF16, kind="Internal").ap()
    W2bf = nc.dram_tensor("W2bf", [NM * 128, 2048], BF16, kind="Internal").ap()

    es = contextlib.ExitStack()

    x_sb = nc.alloc_sbuf_tensor("x_sb", [128, NDC * NT], BF16)
    rW_s = nc.alloc_sbuf_tensor("rW_s", [128, NDC * NE], BF16)
    b2_s = nc.alloc_sbuf_tensor("b2_s", [8, D], F32)
    bs2T_s = nc.alloc_sbuf_tensor("bs2T_s", [128, NDC], F32)
    idf_s = nc.alloc_sbuf_tensor("idf_s", [128, 128], F32)
    idb_s = nc.alloc_sbuf_tensor("idb_s", [128, 128], BF16)
    onescol_b = nc.alloc_sbuf_tensor("onescol_b", [128, 1], BF16)
    onescol_f = nc.alloc_sbuf_tensor("onescol_f", [128, 1], F32)
    onesrow_b = nc.alloc_sbuf_tensor("onesrow_b", [128, 512], BF16)
    onesrow_f = nc.alloc_sbuf_tensor("onesrow_f", [128, 128], F32)
    triu_s = nc.alloc_sbuf_tensor("triu_s", [128, 128], F32)
    maskB_s = nc.alloc_sbuf_tensor("maskB_s", [128, 128], F32)
    maskA_s = nc.alloc_sbuf_tensor("maskA_s", [128, 128], F32)
    selE_s = nc.alloc_sbuf_tensor("selE_s", [8, 8 * 128], F32)
    statrows = nc.alloc_sbuf_tensor("statrows", [128, 1024], F32)
    meanrow = statrows[0:1, 0:512]
    varrow = statrows[0:1, 512:1024]
    rstdrow = statrows[32:33, 0:512]

    psH = nc.alloc_psum_tensor("psH", [128, 1024], F32)
    psC = nc.alloc_psum_tensor("psC", [128, 512], F32)
    psD = nc.alloc_psum_tensor("psD", [128, 512], F32)
    psY = nc.alloc_psum_tensor("psY", [128, 2048], F32)
    psA = psH[:, 0:512]
    psB = psH[:, 512:1024]

    qGP = es.enter_context(nc.semaphore("qGP"))
    qSTRM = {tg: tuple(es.enter_context(nc.semaphore(f"q{w}s{sl}{tg}"))
                       for w in ("W1", "W2") for sl in range(4))
             for tg in ("A", "B")}
    qSY = es.enter_context(nc.semaphore("qSY"))
    qT = es.enter_context(nc.semaphore("qT"))
    qV = es.enter_context(nc.semaphore("qV"))
    qS = es.enter_context(nc.semaphore("qS"))
    qP = es.enter_context(nc.semaphore("qP"))
    qTh = es.enter_context(nc.semaphore("qTh"))
    qTw2 = es.enter_context(nc.semaphore("qTw2"))
    qSq = es.enter_context(nc.semaphore("qSq"))
    qSg = es.enter_context(nc.semaphore("qSg"))
    qVp = es.enter_context(nc.semaphore("qVp"))
    qVu = es.enter_context(nc.semaphore("qVu"))
    qVg = es.enter_context(nc.semaphore("qVg"))
    qPg = es.enter_context(nc.semaphore("qPg"))
    qST = es.enter_context(nc.semaphore("qST"))
    qTs = es.enter_context(nc.semaphore("qTs"))
    qVs = es.enter_context(nc.semaphore("qVs"))
    qSs = es.enter_context(nc.semaphore("qSs"))

    ct = Ctr()
    cst = nc.alloc_sbuf_tensor("cst", [128, 2], F32)

    ov = contextlib.ExitStack()

    def ovt(name, shape, dt):
        return ov.enter_context(nc.sbuf_tensor(name, shape, dt))

    KT = ovt("KT", [128, NT], F32)
    Wbuf = KT
    UT = ovt("UT", [128, NT], F32)
    QTb = ovt("QTb", [128, NT], BF16)
    Vtm = ovt("Vtm", [128, NT], BF16)
    alphaT = ovt("alphaT", [128, NT], BF16)
    OT = UT.bitcast(BF16)[:, 0:NT]
    xn_grp = ovt("xn_grp", [128, NDC * 512], BF16)
    sq_scr = ovt("sq_scr", [128, NDC * 512], BF16)
    sqf = ovt("sqf", [128, 512], F32)
    rowsbig = ovt("rowsbig", [128, 2 * NT], F32)
    rdrow = rowsbig[0:1, 0:NT]
    srow = rowsbig[0:1, NT:2 * NT]
    n2row = rowsbig[64:65, 0:NT]
    arow = rowsbig[64:65, 0:NT]
    Asb = ovt("Asb", [128, 128], F32)
    ypair_sb = ovt("ypair_sb", [128, 2], F32)
    scs = ovt("scs", [1, 784], F32)
    scrow_b = ovt("scrow_b", [1, NT], BF16)
    Mst_sb = ovt("Mst_sb", [128, 128], BF16)
    atm = ovt("atm", [128, 128], BF16)
    psm_sb = ovt("psm_sb", [128, 128], BF16)
    aWq_s = ovt("aWq_s", [128, NDC * H], BF16)
    aWk_s = ovt("aWk_s", [128, NDC * H], BF16)
    aWv_s = ovt("aWv_s", [128, NDC * H], BF16)
    aWo_s = ovt("aWo_s", [128, D], BF16)
    abq_s = ovt("abq_s", [128, 1], F32)
    abk_s = ovt("abk_s", [128, 1], F32)
    abv_s = ovt("abv_s", [1, H], BF16)
    abo_s = ovt("abo_s", [128, NDC], F32)

    LN = dict(cst=cst, onescol_b=onescol_b, onesrow_f=onesrow_f, meanrow=meanrow,
              varrow=varrow, rstdrow=rstdrow, psA=psA, psB=psB, psC=psC, psD=psD,
              qT=qT, qV=qV, qS=qS)

    with nc.Block() as block:
        @block.vector
        def _(v):
            v = EW(v)
            v.memset(cst[:, 0:1], -1.0)
            v.memset(cst[:, 1:2], EPS_LN)
            v.inc_last(qV, 1)
        ct.inc(qV)

        @block.gpsimd
        def _(g):
            g = EW(g)
            g.dma_start(out=x_sb[:, :].rearrange("p (dc t) -> p dc t", dc=NDC),
                        in_=x_in.rearrange("(dc p) t -> p dc t", p=128)).then_inc(qGP, 16)
            g.dma_start(out=aWq_s[:, :].rearrange("p (dc h) -> p dc h", dc=NDC),
                        in_=aWq.rearrange("(dc p) h -> p dc h", p=128)).then_inc(qGP, 16)
            g.dma_start(out=aWk_s[:, :].rearrange("p (dc h) -> p dc h", dc=NDC),
                        in_=aWk.rearrange("(dc p) h -> p dc h", p=128)).then_inc(qGP, 16)
            g.dma_start(out=aWv_s[:, :].rearrange("p (dc h) -> p dc h", dc=NDC),
                        in_=aWv.rearrange("(dc p) h -> p dc h", p=128)).then_inc(qGP, 16)
            g.dma_start(out=aWo_s[:, :], in_=aWo[:, :]).then_inc(qGP, 16)
            g.dma_start(out=abv_s[:, :], in_=abv[:, :]).then_inc(qGP, 16)
            g.dma_start(out=rW_s[:, :].rearrange("p (dc e) -> p dc e", dc=NDC),
                        in_=rW.rearrange("(dc p) e -> p dc e", p=128)).then_inc(qGP, 16)
            ct.inc(qGP, 16 * 7)

        @block.sync
        def _(sy):
            sy = EW(sy)
            for dst, src in ((abq_s, abq), (abk_s, abk), (abo_s, abo), (b2_s, b2),
                             (bs2T_s, bs2T), (idf_s, c_idf), (idb_s, c_idb),
                             (onescol_b, c_onescol_b), (onescol_f, c_onescol_f),
                             (onesrow_b, c_onesrow_b), (onesrow_f, c_onesrow_f),
                             (triu_s, c_triu), (maskB_s, c_maskB), (maskA_s, c_maskA),
                             (selE_s, c_selE)):
                sy.dma_start(out=dst[:, :], in_=src[:, :]).then_inc(qSY, 16)
                ct.inc(qSY, 16)
        LOADS_SY = ct.val(qSY)
        LOADS_GP = ct.val(qGP)

        @block.gpsimd
        def _(g):
            g = EW(g)
            g.wait_ge(qGP, LOADS_GP)
            for m in range(NM):
                e, fc = m // 16, m % 16
                if e < 8:
                    src1 = bass.AP(W1T.tensor, e * D * DFF + fc * FC,
                                   [[DFF, 128], [128 * DFF, NDC], [1, FC]])
                    src2 = bass.AP(W2T.tensor, e * DFF * D + fc * FC * D,
                                   [[D, 128], [128 * D, 2], [1, D]])
                else:
                    src1 = bass.AP(Ws1T.tensor, fc * FC,
                                   [[DFF, 128], [128 * DFF, NDC], [1, FC]])
                    src2 = bass.AP(Ws2T.tensor, fc * FC * D,
                                   [[D, 128], [128 * D, 2], [1, D]])
                d1 = W1bf[m * 128:(m + 1) * 128, 0:2048].rearrange("p (a b) -> p a b", a=NDC)
                d2 = W2bf[m * 128:(m + 1) * 128, :].rearrange("p (a b) -> p a b", a=2)
                g.dma_start(out=d1, in_=src1).then_inc(qGP, 16)
                g.dma_start(out=d2, in_=src2).then_inc(qGP, 16)
                g.dma_start(out=W1bf[m * 128:(m + 1) * 128, 2048:2304],
                            in_=bass.AP(b1fl.tensor, m * FC, [[0, 128], [1, FC]])).then_inc(qGP, 16)
                ct.inc(qGP, 48)
        PRECAST_GP = ct.val(qGP)

        for gi in range(NG):
            t0 = gi * 512
            pre = [(qGP, LOADS_GP), (qSY, LOADS_SY)] if gi == 0 else None
            emit_ln(nc, block, ct, x_sb, xn_grp, sq_scr, t0, 512, NT, LN, pre_waits=pre)

            @block.tensor
            def _(t, t0=t0):
                t = EW(t)
                t.wait_ge(qV, ct.val(qV))
                if t0 == 0:
                    t.wait_ge(qGP, LOADS_GP)
                for dc in range(NDC):
                    t.matmul(psA[:, 0:512], aWq_s[:, dc * H:(dc + 1) * H],
                             xn_grp[:, dc * 512:(dc + 1) * 512],
                             start=(dc == 0), stop=(dc == NDC - 1))
                t.inc_last(qT, 1)
                for dc in range(NDC):
                    t.matmul(psB[:, 0:512], aWk_s[:, dc * H:(dc + 1) * H],
                             xn_grp[:, dc * 512:(dc + 1) * 512],
                             start=(dc == 0), stop=(dc == NDC - 1))
                t.inc_last(qT, 1)
                for ti in range(4):
                    for dc in range(NDC):
                        t.matmul(psC[:, ti * 128:(ti + 1) * 128],
                                 xn_grp[:, dc * 512 + ti * 128: dc * 512 + (ti + 1) * 128],
                                 aWv_s[:, dc * H:(dc + 1) * H],
                                 start=(dc == 0), stop=False)
                    t.matmul(psC[:, ti * 128:(ti + 1) * 128], onesrow_b[0:1, 0:128],
                             abv_s[:, :], start=False, stop=True)
                t.inc_last(qT, 1)
            ct.inc(qT, 3)
            tq = ct.val(qT)

            @block.vector
            def _(v, t0=t0, tq=tq):
                v = EW(v)
                v.wait_ge(qT, tq - 2)
                v.tensor_scalar(out=QTb[:, t0:t0 + 512], in0=psA[:, 0:512],
                                scalar1=abq_s[:, :], scalar2=None, op0=ALU.add)
                v.wait_ge(qT, tq - 1)
                v.tensor_scalar(out=KT[:, t0:t0 + 512], in0=psB[:, 0:512],
                                scalar1=abk_s[:, :], scalar2=None, op0=ALU.add)
                v.wait_ge(qT, tq)
                v.tensor_copy(out=Vtm[:, t0:t0 + 512], in_=psC[:, 0:512])
                v.inc_last(qV, 1)
            ct.inc(qV)
            vq = ct.val(qV)

            @block.scalar
            def _(s, t0=t0, vq=vq):
                s = EW(s)
                s.wait_ge(qV, vq)
                s.activation(out=sqf[:, :], in_=KT[:, t0:t0 + 512], func=AF.Square)
                s.inc_last(qS, 1)
            ct.inc(qS)
            sq1 = ct.val(qS)

            @block.tensor
            def _(t, sq1=sq1):
                t = EW(t)
                t.wait_ge(qS, sq1)
                t.matmul(psA[0:1, 0:512], onescol_f[:, :], sqf[:, :], start=True, stop=True)
                t.inc_last(qT, 1)
            ct.inc(qT)
            n2q = ct.val(qT)

            @block.vector
            def _(v, t0=t0, n2q=n2q):
                v = EW(v)
                v.wait_ge(qT, n2q)
                v.tensor_copy(out=n2row[0:1, t0:t0 + 512], in_=psA[0:1, 0:512])
                v.tensor_tensor(out=sqf[:, :], in0=KT[:, t0:t0 + 512],
                                in1=QTb[:, t0:t0 + 512], op=ALU.mult)
                v.inc_last(qV, 1)
            ct.inc(qV)
            qkv = ct.val(qV)

            @block.tensor
            def _(t, qkv=qkv):
                t = EW(t)
                t.wait_ge(qV, qkv)
                t.matmul(psB[0:1, 0:512], onescol_f[:, :], sqf[:, :], start=True, stop=True)
                t.inc_last(qT, 1)
            ct.inc(qT)
            sq2 = ct.val(qT)

            @block.vector
            def _(v, t0=t0, sq2=sq2):
                v = EW(v)
                v.wait_ge(qT, sq2)
                v.tensor_copy(out=srow[0:1, t0:t0 + 512], in_=psB[0:1, 0:512])
                v.inc_last(qV, 1)
            ct.inc(qV)

        @block.scalar
        def _(s):
            s = EW(s)
            s.wait_ge(qV, ct.val(qV))
            s.activation(out=n2row[0:1, 0:NT], in_=n2row[0:1, 0:NT], func=AF.Sqrt)
            s.inc_last(qS, 1)
        ct.inc(qS)
        nq = ct.val(qS)

        @block.vector
        def _(v, nq=nq):
            v = EW(v)
            v.wait_ge(qS, nq)
            v.tensor_scalar(out=n2row[0:1, 0:NT], in0=n2row[0:1, 0:NT],
                            scalar1=EPS_U, scalar2=None, op0=ALU.add)
            v.drain()
            v.reciprocal(n2row[0:1, 0:NT], n2row[0:1, 0:NT])
            v.inc_last(qV, 1)
        ct.inc(qV)
        rno = ct.val(qV)

        for gi in range(NG):
            t0 = gi * 512

            @block.tensor
            def _(t, t0=t0, gi=gi, rno=rno):
                t = EW(t)
                t.wait_ge(qV, rno + gi)
                t.matmul(psA[:, 0:512], onesrow_f[64:65, :], n2row[0:1, t0:t0 + 512],
                         start=True, stop=True)
                t.inc_last(qT, 1)
            ct.inc(qT)
            bq_ = ct.val(qT)

            @block.vector
            def _(v, t0=t0, bq_=bq_):
                v = EW(v)
                v.wait_ge(qT, bq_)
                v.tensor_tensor(out=UT[:, t0:t0 + 512], in0=KT[:, t0:t0 + 512],
                                in1=psA[:, 0:512], op=ALU.mult)
                v.inc_last(qV, 1)
            ct.inc(qV)
        UT_V = ct.val(qV)

        A_ps = psY[:, 0:128]
        w_ps2 = psY[:, 512:514]
        g11_ps = psY[0:1, 1024:1025]
        g22_ps = psY[0:1, 1025:1026]
        gc_ps = psY[0:1, 1026:1027]
        yr1_ps = psY[0:1, 1536:1664]
        yr2_ps = psY[0:1, 1664:1792]
        yr12_ps = psY[0:1, 1536:1792]
        y12r = scs[0:1, 0:256]
        q1 = scs[0:1, 256:384]
        q2 = scs[0:1, 384:512]
        q3 = scs[0:1, 512:640]
        q4 = scs[0:1, 640:768]
        p1r = q1
        p2r = q3
        zrow = q1
        bb2 = scs[0:1, 768:770]
        nrow = scs[0:1, 770:772]
        rpr = scs[0:1, 772:774]
        m1s = scs[0:1, 774:775]
        m2s = scs[0:1, 775:776]
        s22v = scs[0:1, 776:777]
        sc1 = scs[0:1, 777:778]
        csb = scs[0:1, 778:779]
        NP = NT // 2

        @block.vector
        def _(v):
            v = EW(v)
            v.wait_ge(qV, UT_V)
            v.memset(scrow_b[0:1, :], 0.0)
            v.memset(zrow, 0.0)
            v.tensor_copy(out=Asb[:, :], in_=idf_s[:, :])
            v.inc_last(qVs, 1)

        @block.tensor
        def _(t):
            t = EW(t)
            t.wait_ge(qVs, 1)
            t.matmul(A_ps, zrow, zrow, start=True, stop=False,
                     skip_group_check=True)
            with kreg(t) as rW, kreg(t) as rTi, kreg(t) as rTi1:
                t.reg_mov(rW, 1)
                t.reg_mov(rTi, 0)
                t.reg_mov(rTi1, 1)
                with kfori(nc, t, mybir.EngineType.PE, 0, NP, 1):
                    t.wait_ge(qVs, rW)
                    t.matmul(w_ps2, Asb[:, :], UT[:, rowslice(rTi, 2, NT - 2)],
                             start=True, stop=True).then_inc(qTs, 1)
                    t.reg_add(rW, rW, 1)
                    t.wait_ge(qVs, rW)
                    t.transpose(yr1_ps, ypair_sb[:, 0:1], idf_s[:, :])
                    t.transpose(yr2_ps, ypair_sb[:, 1:2], idf_s[:, :])
                    t.matmul(g11_ps, ypair_sb[:, 0:1],
                             UT[:, rowslice(rTi, 1, NT - 1)], start=True, stop=True)
                    t.matmul(g22_ps, ypair_sb[:, 1:2],
                             UT[:, rowslice(rTi1, 1, NT - 1)], start=True, stop=True)
                    t.matmul(gc_ps, ypair_sb[:, 0:1],
                             UT[:, rowslice(rTi1, 1, NT - 1)], start=True,
                             stop=True).then_inc(qTs, 1)
                    t.reg_add(rW, rW, 1)
                    t.wait_ge(qVs, rW)
                    t.matmul(A_ps, p1r, y12r[0:1, 0:128], start=False, stop=False,
                             skip_group_check=True)
                    t.matmul(A_ps, p2r, y12r[0:1, 128:256], start=False, stop=False,
                             skip_group_check=True).then_inc(qTs, 1)
                    t.reg_add(rW, rW, 1)
                    t.reg_add(rTi, rTi, 2)
                    t.reg_add(rTi1, rTi1, 2)

        @block.vector
        def _(v):
            v = EW(v)
            with kreg(v) as rT, kreg(v) as rVi, kreg(v) as rVi1:
                v.reg_mov(rT, 1)
                v.reg_mov(rVi, 0)
                v.reg_mov(rVi1, 1)
                with kfori(nc, v, mybir.EngineType.DVE, 0, NP, 1):
                    v.wait_ge(qTs, rT)
                    v.tensor_copy(out=ypair_sb[:, :], in_=w_ps2).then_inc(qVs, 1)
                    v.reg_add(rT, rT, 1)
                    v.wait_ge(qTs, rT)
                    v.tensor_copy(out=y12r, in_=yr12_ps)
                    v.tensor_scalar(out=bb2, in0=psY[0:1, 1024:1026], scalar1=1.0,
                                    scalar2=None, op0=ALU.add)
                    v.tensor_copy(out=csb, in_=gc_ps)
                    v.drain()
                    v.tensor_tensor(out=m2s, in0=csb, in1=csb, op=ALU.mult)
                    v.tensor_tensor(out=m1s, in0=bb2[0:1, 0:1], in1=bb2[0:1, 1:2],
                                    op=ALU.mult)
                    v.tensor_scalar(out=nrow[0:1, 0:1], in0=bb2[0:1, 0:1],
                                    scalar1=-1.0, scalar2=None, op0=ALU.mult)
                    v.drain()
                    v.scalar_tensor_tensor(out=nrow[0:1, 1:2], in0=m1s, scalar=-1.0,
                                           in1=m2s, op0=ALU.mult, op1=ALU.add)
                    v.drain()
                    v.reciprocal(rpr, nrow)
                    v.drain()
                    v.tensor_copy(out=rdrow[0:1, rowslice(rVi, 1, NT - 1)],
                                  in_=rpr[0:1, 0:1])
                    v.tensor_scalar(out=rdrow[0:1, rowslice(rVi1, 1, NT - 1)],
                                    in0=bb2[0:1, 0:1], scalar1=rpr[0:1, 1:2],
                                    scalar2=None, op0=ALU.mult)
                    v.tensor_scalar(out=sc1, in0=gc_ps, scalar1=rpr[0:1, 1:2],
                                    scalar2=None, op0=ALU.mult)
                    v.tensor_scalar(out=s22v, in0=bb2[0:1, 1:2],
                                    scalar1=rpr[0:1, 1:2], scalar2=None, op0=ALU.mult)
                    v.drain()
                    v.tensor_copy(out=scrow_b[0:1, rowslice(rVi1, 1, NT - 1)],
                                  in_=sc1)
                    v.tensor_scalar(out=q1, in0=y12r[0:1, 0:128],
                                    scalar1=s22v[0:1, 0:1], scalar2=None, op0=ALU.mult)
                    v.tensor_scalar(out=q2, in0=y12r[0:1, 128:256],
                                    scalar1=sc1[0:1, 0:1],
                                    scalar2=None, op0=ALU.mult)
                    v.tensor_scalar(out=q3, in0=y12r[0:1, 128:256],
                                    scalar1=rdrow[0:1, rowslice(rVi1, 1, NT - 1)],
                                    scalar2=None, op0=ALU.mult)
                    v.tensor_scalar(out=q4, in0=y12r[0:1, 0:128],
                                    scalar1=sc1[0:1, 0:1],
                                    scalar2=None, op0=ALU.mult)
                    v.drain()
                    v.tensor_tensor(out=p1r, in0=q1, in1=q2, op=ALU.subtract)
                    v.tensor_tensor(out=p2r, in0=q3, in1=q4,
                                    op=ALU.subtract).then_inc(qVs, 1)
                    v.reg_add(rT, rT, 1)
                    v.wait_ge(qTs, rT)
                    v.scalar_tensor_tensor(out=Asb[:, :], in0=A_ps, scalar=1.0,
                                           in1=idf_s[:, :], op0=ALU.mult,
                                           op1=ALU.add)
                    v.tensor_copy(out=Wbuf[:, rowslice(rVi, 2, NT - 2)],
                                  in_=ypair_sb[:, :]).then_inc(qVs, 1)
                    v.reg_add(rT, rT, 1)
                    v.reg_add(rVi, rVi, 2)
                    v.reg_add(rVi1, rVi1, 2)

        SCAN_TS = 3 * NP
        SCAN_VS = 1 + 3 * NP
        ct.inc(qTs, SCAN_TS)
        ct.inc(qVs, SCAN_VS)

        @block.vector
        def _(v):
            v = EW(v)
            v.wait_ge(qVs, SCAN_VS)
            v.tensor_tensor(out=arow[0:1, 0:NT], in0=srow[0:1, 0:NT],
                            in1=rdrow[0:1, 0:NT], op=ALU.mult)
            v.tensor_tensor(out=scrow_b[0:1, 0:NT], in0=scrow_b[0:1, 0:NT],
                            in1=srow[0:1, 0:NT], op=ALU.mult)
            v.drain()
            v.tensor_scalar(out=arow[0:1, 0:NT], in0=arow[0:1, 0:NT],
                            scalar1=-1.0, scalar2=None, op0=ALU.mult)
            v.inc_last(qV, 1)
        ct.inc(qV)
        alv = ct.val(qV)

        for gi in range(NG):
            t0 = gi * 512

            @block.tensor
            def _(t, t0=t0, alv=alv):
                t = EW(t)
                t.wait_ge(qV, alv + 2 * gi)
                t.matmul(psA[:, 0:512], onesrow_f[64:65, :], arow[0:1, t0:t0 + 512],
                         start=True, stop=True)
                t.matmul(psC[:, 0:512], onesrow_b[0:1, 0:128],
                         scrow_b[0:1, t0:t0 + 512], start=True, stop=True)
                t.inc_last(qT, 1)
            ct.inc(qT)
            ab_ = ct.val(qT)

            @block.vector
            def _(v, gi=gi, t0=t0, ab_=ab_):
                v = EW(v)
                v.wait_ge(qT, ab_)
                v.tensor_tensor(out=alphaT[:, t0:t0 + 512], in0=Wbuf[:, t0:t0 + 512],
                                in1=psA[:, 0:512], op=ALU.mult)
                lo = 1 if gi == 0 else 0
                v.tensor_tensor(out=sq_scr[:, lo:512],
                                in0=Wbuf[:, t0 + lo - 1:t0 + 511],
                                in1=psC[:, lo:512], op=ALU.mult)
                v.drain()
                v.tensor_tensor(out=alphaT[:, t0 + lo:t0 + 512],
                                in0=alphaT[:, t0 + lo:t0 + 512],
                                in1=sq_scr[:, lo:512], op=ALU.add)
                v.inc_last(qV, 2)
            ct.inc(qV, 2)

        psBb = psH.bitcast(BF16)
        for c in range(NCH):
            t0 = c * 128

            @block.tensor
            def _(t, c=c, t0=t0):
                t = EW(t)
                t.wait_ge(qV, ct.val(qV))
                t.matmul(psA[:, 0:128], alphaT[:, t0:t0 + 128], QTb[:, t0:t0 + 128],
                         start=True, stop=True)
                t.inc_last(qT, 1)
                t.transpose(psBb[:, 1024:1152], alphaT[:, t0:t0 + 128], idb_s[:, :])
                t.inc_last(qT, 1)
            ct.inc(qT, 2)
            pq = ct.val(qT)

            @block.vector
            def _(v, c=c, pq=pq):
                v = EW(v)
                v.wait_ge(qT, pq - 1)
                v.tensor_tensor(out=psm_sb[:, :], in0=psA[:, 0:128], in1=triu_s[:, :],
                                op=ALU.mult)
                v.wait_ge(qT, pq)
                v.tensor_copy(out=atm[:, :], in_=psBb[:, 1024:1152])
                v.inc_last(qV, 1)
            ct.inc(qV)
            pv = ct.val(qV)

            @block.tensor
            def _(t, c=c, t0=t0, pv=pv):
                t = EW(t)
                t.wait_ge(qV, pv)
                t.matmul(psC[:, 0:128], Vtm[:, t0:t0 + 128], psm_sb[:, :],
                         start=True, stop=(c == 0))
                if c > 0:
                    t.matmul(psC[:, 0:128], Mst_sb[:, :], QTb[:, t0:t0 + 128],
                             start=False, stop=True)
                t.inc_last(qT, 1)
                t.matmul(psD[:, 0:128], atm[:, :], Vtm[:, t0:t0 + 128],
                         start=(c == 0), stop=(c == NCH - 1), skip_group_check=True)
                t.inc_last(qT, 1)
            ct.inc(qT, 2)
            oq = ct.val(qT)

            @block.vector
            def _(v, c=c, t0=t0, oq=oq):
                v = EW(v)
                v.wait_ge(qT, oq - 1)
                v.tensor_copy(out=OT[:, t0:t0 + 128], in_=psC[:, 0:128])
                v.wait_ge(qT, oq)
                v.tensor_copy(out=Mst_sb[:, :], in_=psD[:, 0:128])
                v.inc_last(qV, 1)
            ct.inc(qV)

        for gi in range(NG):
            t0 = gi * 512
            for dc in range(NDC):
                @block.tensor
                def _(t, t0=t0, dc=dc):
                    t = EW(t)
                    t.wait_ge(qV, ct.val(qV))
                    t.matmul(psA[:, 0:512], aWo_s[:, dc * 128:(dc + 1) * 128],
                             OT[:, t0:t0 + 512], start=True, stop=True)
                    t.inc_last(qT, 1)
                ct.inc(qT)
                tq2 = ct.val(qT)

                @block.vector
                def _(v, t0=t0, dc=dc, tq2=tq2):
                    v = EW(v)
                    v.wait_ge(qT, tq2)
                    v.tensor_scalar(out=sqf[:, :], in0=psA[:, 0:512],
                                    scalar1=abo_s[:, dc:dc + 1], scalar2=None,
                                    op0=ALU.add)
                    v.drain()
                    v.tensor_tensor(out=x_sb[:, dc * NT + t0: dc * NT + t0 + 512],
                                    in0=x_sb[:, dc * NT + t0: dc * NT + t0 + 512],
                                    in1=sqf[:, :], op=ALU.add)
                    v.inc_last(qV, 1)
                ct.inc(qV)
        ATTNA_V = ct.val(qV)

        ov.close()

        ffn = dict(x_sb=x_sb, W1bf=W1bf, W2bf=W2bf, b1fl=b1fl, qP=qP, qSTRM=qSTRM,
                   rW_s=rW_s, b2_s=b2_s, bs2T_s=bs2T_s,
                   selE_s=selE_s, idf_s=idf_s, onesrow_b=onesrow_b,
                   psH=psH, psC=psC, psD=psD, psY=psY, psA=psA, psB=psB,
                   qTh=qTh, qTw2=qTw2, qSq=qSq, qSg=qSg, qVp=qVp, qVu=qVu,
                   qVg=qVg, qPg=qPg, qSY=qSY, qGP=qGP, qST=qST,
                   PRECAST_GP=PRECAST_GP, y_out=y_out, LN=LN, NT=NT)

        emit_ffn(nc, block, ct, ffn, last=False, tag="A")

        attb = dict(x_sb=x_sb, bWq=bWq, bWk=bWk, bWv=bWv, bWo=bWo,
                    maskA_s=maskA_s, maskB_s=maskB_s, onescol_b=onescol_b,
                    onesrow_f=onesrow_f,
                    psH=psH, psC=psC, psD=psD, psA=psA, psB=psB,
                    qGP=qGP, qSY=qSY, qTw2=qTw2, LN=LN, NT=NT)

        emit_attn_b(nc, block, ct, attb)

        emit_ffn(nc, block, ct, ffn, last=True, tag="B")

        @block.sync
        def _(sy):
            sy = EW(sy)
            sy.wait_ge(qST, ct.val(qST))

    es.close()
    return nc


def emit_ln(nc, block, ct, x_sb, xn_out, sq_scr, t0, W, NT, LN, pre_waits=None,
            xn_stride=None):
    if xn_stride is None:
        xn_stride = W
    qT, qV, qS = LN["qT"], LN["qV"], LN["qS"]
    psA, psB, psC, psD = LN["psA"], LN["psB"], LN["psC"], LN["psD"]

    @block.scalar
    def _(s):
        s = EW(s)
        s.wait_ge(qV, ct.val(qV))
        if pre_waits:
            for (q, vv) in pre_waits:
                s.wait_ge(q, vv)
        for dc in range(NDC):
            s.activation(out=sq_scr[:, dc * W:(dc + 1) * W],
                         in_=x_sb[:, dc * NT + t0: dc * NT + t0 + W], func=AF.Square)
        s.inc_last(qS, 1)
    ct.inc(qS)
    sqq = ct.val(qS)

    @block.tensor
    def _(t, sqq=sqq):
        t = EW(t)
        t.wait_ge(qS, sqq)
        for dc in range(NDC):
            t.matmul(psA[0:1, 0:W], LN["onescol_b"][:, :],
                     x_sb[:, dc * NT + t0: dc * NT + t0 + W],
                     start=(dc == 0), stop=(dc == NDC - 1))
        for dc in range(NDC):
            t.matmul(psB[0:1, 0:W], LN["onescol_b"][:, :],
                     sq_scr[:, dc * W:(dc + 1) * W],
                     start=(dc == 0), stop=(dc == NDC - 1))
        t.inc_last(qT, 1)
    ct.inc(qT)
    stq = ct.val(qT)

    @block.vector
    def _(v, stq=stq):
        v = EW(v)
        v.wait_ge(qT, stq)
        v.tensor_scalar(out=LN["meanrow"][0:1, 0:W], in0=psA[0:1, 0:W],
                        scalar1=1.0 / D, scalar2=None, op0=ALU.mult)
        v.drain()
        v.tensor_tensor(out=LN["varrow"][0:1, 0:W], in0=LN["meanrow"][0:1, 0:W],
                        in1=LN["meanrow"][0:1, 0:W], op=ALU.mult)
        v.drain()
        v.scalar_tensor_tensor(out=LN["varrow"][0:1, 0:W], in0=psB[0:1, 0:W],
                               scalar=1.0 / D, in1=LN["varrow"][0:1, 0:W],
                               op0=ALU.mult, op1=ALU.subtract)
        v.inc_last(qV, 1)
    ct.inc(qV)
    vq = ct.val(qV)

    @block.scalar
    def _(s, vq=vq):
        s = EW(s)
        s.wait_ge(qV, vq)
        s.activation(out=LN["rstdrow"][0:1, 0:W], in_=LN["varrow"][0:1, 0:W],
                     func=AF.Sqrt, bias=LN["cst"][0:1, 1:2])
        s.inc_last(qS, 1)
    ct.inc(qS)
    sq2 = ct.val(qS)

    @block.vector
    def _(v, sq2=sq2):
        v = EW(v)
        v.wait_ge(qS, sq2)
        v.reciprocal(LN["rstdrow"][0:1, 0:W], LN["rstdrow"][0:1, 0:W])
        v.inc_last(qV, 1)
    ct.inc(qV)
    vq2 = ct.val(qV)

    @block.tensor
    def _(t, vq2=vq2):
        t = EW(t)
        t.wait_ge(qV, vq2)
        t.matmul(psC[:, 0:W], LN["onesrow_f"][0:1, :], LN["meanrow"][0:1, 0:W],
                 start=True, stop=True)
        t.matmul(psD[:, 0:W], LN["onesrow_f"][32:33, :], LN["rstdrow"][0:1, 0:W],
                 start=True, stop=True)
        t.inc_last(qT, 1)
    ct.inc(qT)
    bcq = ct.val(qT)

    @block.vector
    def _(v, bcq=bcq):
        v = EW(v)
        v.wait_ge(qT, bcq)
        for dc in range(NDC):
            v.tensor_tensor(out=sq_scr[:, dc * W:(dc + 1) * W],
                            in0=x_sb[:, dc * NT + t0: dc * NT + t0 + W],
                            in1=psC[:, 0:W], op=ALU.subtract)
        v.drain()
        for dc in range(NDC):
            v.tensor_tensor(out=xn_out[:, dc * xn_stride:dc * xn_stride + W],
                            in0=sq_scr[:, dc * W:(dc + 1) * W],
                            in1=psD[:, 0:W], op=ALU.mult)
        v.inc_last(qV, 1)
    ct.inc(qV)


def emit_ffn(nc, block, ct, P, last, tag=""):
    NT = P["NT"]
    NBLK = NT // BLK
    x_sb = P["x_sb"]
    psH, psC, psD, psY = P["psH"], P["psC"], P["psD"], P["psY"]
    psA, psB = P["psA"], P["psB"]
    LN = P["LN"]
    qT, qV, qS = LN["qT"], LN["qV"], LN["qS"]
    qTh, qTw2 = P["qTh"], P["qTw2"]
    qSq, qSg, qVp, qVu, qVg, qPg = P["qSq"], P["qSg"], P["qVp"], P["qVu"], P["qVg"], P["qPg"]
    qSY, qGP, qST, qP = P["qSY"], P["qGP"], P["qST"], P["qP"]
    qsW1 = list(P["qSTRM"][tag][0:4])
    qsW2 = list(P["qSTRM"][tag][4:8])

    sc = contextlib.ExitStack()

    def sct(n, shp, dt):
        return sc.enter_context(nc.sbuf_tensor(n + tag, shp, dt))

    P = dict(P)
    P["w1buf"] = sct("w1buf", [128, 4 * 2304], BF16)
    P["w2buf"] = sct("w2buf", [128, 4 * 2048], BF16)
    P["xn_blk"] = sct("xn_blk", [128, NDC * BLK], BF16)
    P["lnsq"] = sct("lnsq", [128, NDC * BLK], BF16)
    P["gateB"] = sct("gateB", [128, 9 * BLK], BF16)
    P["gatesT"] = sct("gatesT", [8, BLK], F32)
    P["gsc"] = sct("gsc", [128, 48], F32)
    P["sqg"] = sct("sqg", [128, 4 * 512], BF16)
    P["pg"] = sct("pg", [128, 512], BF16)
    P["u2g"] = sct("u2g", [128, 512], F32)
    P["sgg"] = sct("sgg", [128, 512], BF16)
    P["hg0"] = sct("hg0", [128, 4 * 512], BF16)
    P["hgb"] = sct("hgb", [128, 4 * 512], BF16)
    P["outst"] = sct("outst", [128, 2048], F32)
    HS = [psH[:, 0:512], psH[:, 512:1024], psC[:, 0:512], psD[:, 0:512]]
    ENTRY_V = ct.val(qV)
    ENTRY_SY = ct.val(qSY)
    ENTRY_TW2 = ct.val(qTw2)

    @block.gpsimd
    def _(g):
        g = EW(g)
        g.wait_ge(qV, ENTRY_V)
        g.wait_ge(qSY, ENTRY_SY)
        g.wait_ge(qTw2, ENTRY_TW2)
        g.inc_last(qP, 1)
        ct.inc(qP)

    @block.vector
    def _(v):
        v = EW(v)
        v.wait_ge(qV, ENTRY_V)
        v.memset(P["gateB"][:, 8 * BLK:9 * BLK], 1.0)
        v.inc_last(qV, 1)
    ct.inc(qV)
    ENTRY_P = ct.val(qP)

    CB_Tw2 = ct.val(qTw2)
    CB_W1 = [ct.val(q) for q in qsW1]
    CB_W2 = [ct.val(q) for q in qsW2]

    @block.sync
    def _(sy):
        sy = EW(sy)
        sy.wait_ge(qP, ENTRY_P)
        if not last:
            sy.wait_ge(qGP, P["PRECAST_GP"])
        for k in range(NBLK * NM):
            s = k % 4
            m = k % NM
            if k >= 4:
                sy.wait_ge(qTw2, CB_Tw2 + k - 3)
            sy.dma_start(out=P["w1buf"][:, s * 2304:(s + 1) * 2304],
                         in_=P["W1bf"][m * 128:(m + 1) * 128, :]).then_inc(qsW1[s], 16)
            sy.dma_start(out=P["w2buf"][:, s * 2048:(s + 1) * 2048],
                         in_=P["W2bf"][m * 128:(m + 1) * 128, :]).then_inc(qsW2[s], 16)
    for _q in qsW1 + qsW2:
        ct.inc(_q, 16 * (NM // 4) * NBLK)

    for blk in range(NBLK):
        t0 = blk * BLK
        emit_ln(nc, block, ct, x_sb, P["xn_blk"], P["lnsq"], t0, BLK, NT, LN)

        for ti in range(2):
            tt0 = ti * 128

            @block.tensor
            def _(t, tt0=tt0):
                t = EW(t)
                t.wait_ge(qV, ct.val(qV))
                for dc in range(NDC):
                    t.matmul(psC[0:128, 0:8],
                             P["xn_blk"][:, dc * BLK + tt0: dc * BLK + tt0 + 128],
                             P["rW_s"][:, dc * NE:(dc + 1) * NE],
                             start=(dc == 0), stop=(dc == NDC - 1))
                t.inc_last(qT, 1)
            ct.inc(qT)
            lq = ct.val(qT)

            @block.vector
            def _(v, lq=lq):
                v = EW(v)
                v.wait_ge(qT, lq)
                v.tensor_copy(out=P["gsc"][:, 0:8], in_=psC[0:128, 0:8])
                v.drain()
                v.reduce_max(out=P["gsc"][:, 8:9], in_=P["gsc"][:, 0:8],
                             axis=AX.X, negate=True)
                v.inc_last(qV, 1)
            ct.inc(qV)
            lv = ct.val(qV)

            @block.scalar
            def _(s, lv=lv):
                s = EW(s)
                s.wait_ge(qV, lv)
                s.activation(out=P["gsc"][:, 16:24], in_=P["gsc"][:, 0:8], func=AF.Exp,
                             bias=P["gsc"][:, 8:9], accum_out=P["gsc"][:, 9:10])
                s.inc_last(qS, 1)
            ct.inc(qS)
            ls = ct.val(qS)

            @block.vector
            def _(v, ls=ls):
                v = EW(v)
                v.wait_ge(qS, ls)
                v.reciprocal(P["gsc"][:, 10:11], P["gsc"][:, 9:10])
                v.max(out=P["gsc"][:, 24:32], in_=P["gsc"][:, 16:24])
                v.drain()
                v.tensor_scalar(out=P["gsc"][:, 32:40], in0=P["gsc"][:, 16:24],
                                scalar1=P["gsc"][:, 25:26], scalar2=None,
                                op0=ALU.is_ge)
                v.drain()
                v.tensor_tensor(out=P["gsc"][:, 40:48], in0=P["gsc"][:, 16:24],
                                in1=P["gsc"][:, 32:40], op=ALU.mult)
                v.drain()
                v.tensor_scalar(out=P["gsc"][:, 0:8], in0=P["gsc"][:, 40:48],
                                scalar1=P["gsc"][:, 10:11], scalar2=None,
                                op0=ALU.mult)
                v.inc_last(qV, 1)
            ct.inc(qV)
            gv = ct.val(qV)

            @block.tensor
            def _(t, gv=gv):
                t = EW(t)
                t.wait_ge(qV, gv)
                t.transpose(psC[0:8, 256:384], P["gsc"][:, 0:8], P["idf_s"][:, :])
                t.inc_last(qT, 1)
            ct.inc(qT)
            gq = ct.val(qT)

            @block.vector
            def _(v, tt0=tt0, gq=gq):
                v = EW(v)
                v.wait_ge(qT, gq)
                v.tensor_copy(out=P["gatesT"][:, tt0:tt0 + 128], in_=psC[0:8, 256:384])
                v.inc_last(qV, 1)
            ct.inc(qV)
        GATES_V = ct.val(qV)

        for e in range(8):
            @block.tensor
            def _(t, e=e, w=ct.val(qV)):
                t = EW(t)
                t.wait_ge(qV, w)
                t.matmul(psC[:, 0:256], P["selE_s"][:, e * 128:(e + 1) * 128],
                         P["gatesT"][:, :], start=True, stop=True)
                t.inc_last(qT, 1)
            ct.inc(qT)
            eq = ct.val(qT)

            @block.vector
            def _(v, e=e, eq=eq):
                v = EW(v)
                v.wait_ge(qT, eq)
                v.tensor_copy(out=P["gateB"][:, e * BLK:(e + 1) * BLK],
                              in_=psC[:, 0:256])
                v.inc_last(qV, 1)
            ct.inc(qV)
        GB_V = ct.val(qV)

        @block.tensor
        def _(t):
            t = EW(t)
            t.wait_ge(qV, GB_V)
            for dcp in range(4):
                for side in range(2):
                    dc = 2 * dcp + side
                    t.matmul(psY[:, dcp * 512 + side * 256: dcp * 512 + side * 256 + 256],
                             P["b2_s"][:, dc * 128:(dc + 1) * 128], P["gatesT"][:, :],
                             start=(side == 0), stop=False, skip_group_check=True)
            t.inc_last(qT, 1)
        ct.inc(qT)

        bqW1 = [CB_W1[s] + blk * 16 * (NM // 4) for s in range(4)]
        bqW2 = [CB_W2[s] + blk * 16 * (NM // 4) for s in range(4)]
        base_Th = ct.val(qTh)
        base_Tw2 = ct.val(qTw2)
        base_Sq = ct.val(qSq)
        base_Sg = ct.val(qSg)
        base_Vp = ct.val(qVp)
        base_Vu = ct.val(qVu)
        base_Vg = ct.val(qVg)
        base_Pg = ct.val(qPg)
        T_ENTRY = ct.val(qT)
        V_ENTRY = ct.val(qV)


        bq = bqW1[0]
        assert all(b == bq for b in bqW1 + bqW2)

        def t_w1(t, s):
            h = HS[s]
            for fsub in range(2):
                for dc in range(NDC):
                    t.matmul(h[:, fsub * 256: fsub * 256 + 256],
                             P["w1buf"][:, s * 2304 + dc * 256 + fsub * 128:
                                        s * 2304 + dc * 256 + fsub * 128 + 128],
                             P["xn_blk"][:, dc * BLK:(dc + 1) * BLK],
                             start=(dc == 0), stop=False)
                t.matmul(h[:, fsub * 256: fsub * 256 + 256],
                         P["w1buf"][0:1, s * 2304 + 2048 + fsub * 128:
                                    s * 2304 + 2048 + fsub * 128 + 128],
                         P["onesrow_b"][0:1, 0:256], start=False, stop=True)
            t.inc_last(qTh, 1)

        def t_w2(t, s):
            for fr in range(2):
                for dc in range(NDC):
                    t.matmul(psY[:, (dc // 2) * 512 + (dc % 2) * 256:
                                  (dc // 2) * 512 + (dc % 2) * 256 + 256],
                             P["w2buf"][:, s * 2048 + fr * 1024 + dc * 128:
                                        s * 2048 + fr * 1024 + dc * 128 + 128],
                             P["hgb"][:, s * 512 + fr * 256: s * 512 + fr * 256 + 256],
                             start=False, stop=False, skip_group_check=True)
            t.inc_last(qTw2, 1)

        @block.tensor
        def _(t):
            t = EW(t)
            for j in range(4):
                t.wait_ge(qsW1[j], bq + 16)
                if base_Vg + j - 3 > 0:
                    t.wait_ge(qVg, base_Vg + j - 3)
                t_w1(t, j)
                if j >= 1:
                    t.wait_ge(qPg, base_Pg + j)
                    t.wait_ge(qsW2[j - 1], bq + 16)
                    t_w2(t, j - 1)
            with kreg(t) as rQ, kreg(t) as rQm, kreg(t) as rVg, kreg(t) as rPg:
                t.reg_mov(rQ, bq + 32)
                t.reg_mov(rQm, bq + 16)
                t.reg_mov(rVg, base_Vg + 1)
                t.reg_mov(rPg, base_Pg + 4)
                with kfori(nc, t, mybir.EngineType.PE, 1, NM // 4, 1):
                    for j in range(4):
                        t.wait_ge(qsW1[j], rQ)
                        t.wait_ge(qVg, rVg)
                        t_w1(t, j)
                        t.wait_ge(qPg, rPg)
                        t.wait_ge(qsW2[(j + 3) % 4], rQm if j == 0 else rQ)
                        t_w2(t, (j + 3) % 4)
                        t.reg_add(rVg, rVg, 1)
                        t.reg_add(rPg, rPg, 1)
                    t.reg_add(rQ, rQ, 16)
                    t.reg_add(rQm, rQm, 16)
            t.wait_ge(qPg, base_Pg + NM)
            t.wait_ge(qsW2[3], bq + 16 * (NM // 4))
            t_w2(t, 3)
        ct.inc(qTh, NM)
        ct.inc(qTw2, NM)

        @block.scalar
        def _(s):
            s = EW(s)
            with kreg(s) as rTh, kreg(s) as rVu, kreg(s) as rVp:
                s.reg_mov(rTh, base_Th + 1)
                s.reg_mov(rVu, base_Vu + 1)
                s.reg_mov(rVp, max(base_Vp - 3, 0))
                with kfori(nc, s, mybir.EngineType.Activation, 0, NM // 4, 1):
                    for j in range(4):
                        s.wait_ge(qTh, rTh)
                        s.wait_ge(qVp, rVp)
                        s.activation(out=P["sqg"][:, j * 512:(j + 1) * 512],
                                     in_=HS[j][:, 0:512],
                                     func=AF.Square).then_inc(qSq, 1)
                        s.wait_ge(qVu, rVu)
                        s.activation(out=P["sgg"][:, :], in_=P["u2g"][:, :],
                                     func=AF.Sigmoid).then_inc(qSg, 1)
                        s.reg_add(rTh, rTh, 1)
                        s.reg_add(rVu, rVu, 1)
                        s.reg_add(rVp, rVp, 1)
        ct.inc(qSq, NM)
        ct.inc(qSg, NM)

        @block.vector
        def _(v):
            v = EW(v)
            v.wait_ge(qV, GB_V)
            with kreg(v) as rSq, kreg(v) as rSg, kreg(v) as rT2:
                v.reg_mov(rSq, base_Sq + 1)
                v.reg_mov(rSg, base_Sg + 1)
                v.reg_mov(rT2, max(base_Tw2 - 3, 0))
                for e in range(9):
                    goff = e * BLK
                    with kfori(nc, v, mybir.EngineType.DVE, e * 4, e * 4 + 4, 1):
                        for j in range(4):
                            v.wait_ge(qSq, rSq)
                            v.tensor_scalar(out=P["pg"][:, :],
                                            in0=P["sqg"][:, j * 512:(j + 1) * 512],
                                            scalar1=GC2, scalar2=GC1, op0=ALU.mult,
                                            op1=ALU.add).then_inc(qVp, 1)
                            v.drain()
                            v.tensor_tensor(out=P["u2g"][:, :],
                                            in0=HS[j][:, 0:512],
                                            in1=P["pg"][:, :],
                                            op=ALU.mult).then_inc(qVu, 1)
                            v.wait_ge(qSg, rSg)
                            v.tensor_tensor(out=P["hg0"][:, j * 512:(j + 1) * 512],
                                            in0=HS[j][:, 0:512],
                                            in1=P["sgg"][:, :],
                                            op=ALU.mult).then_inc(qVg, 1)
                            v.wait_ge(qTw2, rT2)
                            v.drain()
                            v.tensor_tensor(out=P["hgb"][:, j * 512: j * 512 + 256],
                                            in0=P["hg0"][:, j * 512: j * 512 + 256],
                                            in1=P["gateB"][:, goff:goff + 256],
                                            op=ALU.mult)
                            v.tensor_tensor(out=P["hgb"][:, j * 512 + 256: j * 512 + 512],
                                            in0=P["hg0"][:, j * 512 + 256: j * 512 + 512],
                                            in1=P["gateB"][:, goff:goff + 256],
                                            op=ALU.mult).then_inc(qPg, 1)
                            v.reg_add(rSq, rSq, 1)
                            v.reg_add(rSg, rSg, 1)
                            v.reg_add(rT2, rT2, 1)
        ct.inc(qVp, NM)
        ct.inc(qVu, NM)
        ct.inc(qVg, NM)
        ct.inc(qPg, NM)

        @block.vector
        def _(v, blk=blk, t0=t0, wst=ct.val(qST)):
            v = EW(v)
            v.wait_ge(qTw2, base_Tw2 + NM)
            if last:
                v.wait_ge(qST, wst)
            for dc in range(NDC):
                ysl = psY[:, (dc // 2) * 512 + (dc % 2) * 256:
                          (dc // 2) * 512 + (dc % 2) * 256 + 256]
                v.tensor_scalar(out=P["outst"][:, dc * 256:(dc + 1) * 256],
                                in0=ysl, scalar1=P["bs2T_s"][:, dc:dc + 1],
                                scalar2=None, op0=ALU.add)
            v.inc_last(qV, 1)
        ct.inc(qV)
        ev1 = ct.val(qV)

        @block.vector
        def _(v, blk=blk, t0=t0, ev1=ev1):
            v = EW(v)
            v.wait_ge(qV, ev1)
            for dc in range(NDC):
                v.scalar_tensor_tensor(out=P["outst"][:, dc * 256:(dc + 1) * 256],
                                       in0=P["outst"][:, dc * 256:(dc + 1) * 256],
                                       scalar=SCALE_FFN,
                                       in1=x_sb[:, dc * NT + t0: dc * NT + t0 + 256],
                                       op0=ALU.mult, op1=ALU.add)
            v.inc_last(qV, 1)
        ct.inc(qV)
        ev2 = ct.val(qV)

        if last:
            @block.gpsimd
            def _(g, blk=blk, t0=t0, ev2=ev2):
                g = EW(g)
                g.wait_ge(qV, ev2)
                g.dma_start(out=bass.AP(P["y_out"].tensor, t0,
                                        [[NT, 128], [128 * NT, NDC], [1, 256]]),
                            in_=P["outst"][:, :].rearrange("p (dc t) -> p dc t", dc=NDC)).then_inc(qST, 16)
            ct.inc(qST, 16)
        else:
            @block.vector
            def _(v, blk=blk, t0=t0, ev2=ev2):
                v = EW(v)
                v.wait_ge(qV, ev2)
                for dc in range(NDC):
                    v.tensor_copy(out=x_sb[:, dc * NT + t0: dc * NT + t0 + 256],
                                  in_=P["outst"][:, dc * 256:(dc + 1) * 256])
                v.inc_last(qV, 1)
            ct.inc(qV)

    sc.close()


def emit_attn_b(nc, block, ct, P):
    NT = P["NT"]
    NTIL = NT // 128
    x_sb = P["x_sb"]
    psH, psC, psD = P["psH"], P["psC"], P["psD"]
    psA, psB = P["psA"], P["psB"]
    LN = P["LN"]
    qT, qV, qS = LN["qT"], LN["qV"], LN["qS"]
    qGP, qSY, qTw2 = P["qGP"], P["qSY"], P["qTw2"]
    SCL = 1.0 / 8.0

    sc = contextlib.ExitStack()

    def sct(n, shp, dt):
        return sc.enter_context(nc.sbuf_tensor(n, shp, dt))

    P = dict(P)
    P["lnsq"] = sct("ab_lnsq", [128, NDC * 128], BF16)
    P["xnb"] = sct("ab_xnb", [128, NDC * 128], BF16)
    P["bQT"] = sct("ab_bQT", [128, NDC * 128], BF16)
    P["bKTr"] = sct("ab_bKTr", [128, 2 * NDC * 128], BF16)
    P["bVtr"] = sct("ab_bVtr", [128, 2 * NDC * 128], BF16)
    P["bexp"] = sct("ab_bexp", [128, 256], BF16)
    P["bexn"] = sct("ab_bexn", [128, 256], BF16)
    P["bctx"] = sct("ab_bctx", [128, NDC * 128], BF16)
    P["brs"] = sct("ab_brs", [1, 128], F32)
    P["bmsk"] = sct("ab_bmsk", [128, 256], F32)
    P["outst"] = sct("ab_outst", [128, NDC * 128], F32)
    P["bWq_s"] = sct("ab_bWq", [128, NDC * D], BF16)
    P["bWk_s"] = sct("ab_bWk", [128, NDC * D], BF16)
    P["bWv_s"] = sct("ab_bWv", [128, NDC * D], BF16)
    P["bWo_s"] = sct("ab_bWo", [128, NDC * D], BF16)
    ENTRY_V = ct.val(qV)

    @block.gpsimd
    def _(g):
        g = EW(g)
        g.wait_ge(qV, ENTRY_V)
        g.wait_ge(qSY, ct.val(qSY))
        g.wait_ge(qTw2, ct.val(qTw2))
        g.wait_ge(qGP, ct.val(qGP))
        for dst, src in ((P["bWq_s"], P["bWq"]), (P["bWk_s"], P["bWk"]),
                         (P["bWv_s"], P["bWv"]), (P["bWo_s"], P["bWo"])):
            g.dma_start(out=dst[:, :].rearrange("p (c o) -> p c o", c=NDC),
                        in_=src.rearrange("(c p) o -> p c o", p=128)).then_inc(qGP, 16)
            ct.inc(qGP, 16)
    BW_GP = ct.val(qGP)

    for ti in range(NTIL):
        t0 = ti * 128
        slot = ti % 2
        pre = [(qGP, BW_GP)] if ti == 0 else None
        emit_ln(nc, block, ct, x_sb, P["xnb"], P["lnsq"], t0, 128, NT, LN,
                pre_waits=pre)
        XN_V = ct.val(qV)

        @block.tensor
        def _(t, ti=ti):
            t = EW(t)
            t.wait_ge(qV, XN_V)
            for oc in range(NDC):
                for dc in range(NDC):
                    t.matmul(psH[:, oc * 128:(oc + 1) * 128],
                             P["bWq_s"][:, dc * D + oc * 128: dc * D + oc * 128 + 128],
                             P["xnb"][:, dc * 128:(dc + 1) * 128],
                             start=(dc == 0), stop=(dc == NDC - 1))
            t.inc_last(qT, 1)
        ct.inc(qT)
        q1 = ct.val(qT)

        @block.vector
        def _(v, q1=q1):
            v = EW(v)
            v.wait_ge(qT, q1)
            for oc in range(NDC):
                v.tensor_copy(out=P["bQT"][:, oc * 128:(oc + 1) * 128],
                              in_=psH[:, oc * 128:(oc + 1) * 128])
            v.inc_last(qV, 1)
        ct.inc(qV)
        q1v = ct.val(qV)

        @block.tensor
        def _(t, q1v=q1v):
            t = EW(t)
            t.wait_ge(qV, q1v)
            for oc in range(NDC):
                for dc in range(NDC):
                    t.matmul(psH[:, oc * 128:(oc + 1) * 128],
                             P["bWk_s"][:, dc * D + oc * 128: dc * D + oc * 128 + 128],
                             P["xnb"][:, dc * 128:(dc + 1) * 128],
                             start=(dc == 0), stop=(dc == NDC - 1))
            t.inc_last(qT, 1)
        ct.inc(qT)
        q2 = ct.val(qT)

        @block.vector
        def _(v, slot=slot, q2=q2):
            v = EW(v)
            v.wait_ge(qT, q2)
            for oc in range(NDC):
                v.tensor_copy(out=P["bKTr"][:, slot * 1024 + oc * 128:
                                            slot * 1024 + (oc + 1) * 128],
                              in_=psH[:, oc * 128:(oc + 1) * 128])
            v.inc_last(qV, 1)
        ct.inc(qV)
        q2v = ct.val(qV)

        @block.tensor
        def _(t, q2v=q2v):
            t = EW(t)
            t.wait_ge(qV, q2v)
            for nh in range(2):
                for dc in range(NDC):
                    t.matmul(psH[:, nh * 512:(nh + 1) * 512],
                             P["xnb"][:, dc * 128:(dc + 1) * 128],
                             P["bWv_s"][:, dc * D + nh * 512: dc * D + nh * 512 + 512],
                             start=(dc == 0), stop=(dc == NDC - 1))
            t.inc_last(qT, 1)
        ct.inc(qT)
        q3 = ct.val(qT)

        @block.vector
        def _(v, slot=slot, q3=q3):
            v = EW(v)
            v.wait_ge(qT, q3)
            v.tensor_copy(out=P["bVtr"][:, slot * 1024: slot * 1024 + 512],
                          in_=psH[:, 0:512])
            v.tensor_copy(out=P["bVtr"][:, slot * 1024 + 512: slot * 1024 + 1024],
                          in_=psH[:, 512:1024])
            v.inc_last(qV, 1)
        ct.inc(qV)
        QKV_V = ct.val(qV)

        for h in range(16):
            oc = h // 2
            pb = (h % 2) * 64
            stiles = ([((ti - 1) % 2, P["maskA_s"])] if ti > 0 else []) + \
                     [(slot, P["maskB_s"])]

            @block.tensor
            def _(t, h=h, oc=oc, pb=pb, stiles=stiles, slot=slot):
                t = EW(t)
                t.wait_ge(qV, ct.val(qV))
                for si, (ss, mk) in enumerate(stiles):
                    t.matmul(psC[:, si * 128:(si + 1) * 128],
                             P["bKTr"][pb:pb + 64, ss * 1024 + oc * 128:
                                       ss * 1024 + (oc + 1) * 128],
                             P["bQT"][pb:pb + 64, oc * 128:(oc + 1) * 128],
                             start=True, stop=True)
                t.inc_last(qT, 1)
            ct.inc(qT)
            scq = ct.val(qT)

            @block.vector
            def _(v, h=h, stiles=stiles, scq=scq):
                v = EW(v)
                v.wait_ge(qT, scq)
                for si, (ss, mk) in enumerate(stiles):
                    v.scalar_tensor_tensor(out=P["bmsk"][:, si * 128:(si + 1) * 128],
                                           in0=psC[:, si * 128:(si + 1) * 128],
                                           scalar=SCL, in1=mk[:, :],
                                           op0=ALU.mult, op1=ALU.add)
                v.inc_last(qV, 1)
            ct.inc(qV)
            mkv = ct.val(qV)

            @block.scalar
            def _(s, h=h, stiles=stiles, mkv=mkv):
                s = EW(s)
                s.wait_ge(qV, mkv)
                for si in range(len(stiles)):
                    s.activation(out=P["bexp"][:, si * 128:(si + 1) * 128],
                                 in_=P["bmsk"][:, si * 128:(si + 1) * 128],
                                 func=AF.Exp)
                s.inc_last(qS, 1)
            ct.inc(qS)
            exq = ct.val(qS)

            @block.tensor
            def _(t, h=h, stiles=stiles, exq=exq):
                t = EW(t)
                t.wait_ge(qS, exq)
                for si in range(len(stiles)):
                    t.matmul(psD[0:1, 0:128], P["onescol_b"][:, :],
                             P["bexp"][:, si * 128:(si + 1) * 128],
                             start=(si == 0), stop=(si == len(stiles) - 1))
                t.inc_last(qT, 1)
            ct.inc(qT)
            smq = ct.val(qT)

            @block.vector
            def _(v, smq=smq):
                v = EW(v)
                v.wait_ge(qT, smq)
                v.tensor_copy(out=P["brs"][:, :], in_=psD[0:1, 0:128])
                v.drain()
                v.reciprocal(P["brs"][:, :], P["brs"][:, :])
                v.inc_last(qV, 1)
            ct.inc(qV)
            rsv = ct.val(qV)

            @block.tensor
            def _(t, rsv=rsv):
                t = EW(t)
                t.wait_ge(qV, rsv)
                t.matmul(psD[:, 128:256], P["onesrow_f"][0:1, :], P["brs"][:, :],
                         start=True, stop=True)
                t.inc_last(qT, 1)
            ct.inc(qT)
            rbq = ct.val(qT)

            @block.vector
            def _(v, stiles=stiles, rbq=rbq):
                v = EW(v)
                v.wait_ge(qT, rbq)
                for si in range(len(stiles)):
                    v.tensor_tensor(out=P["bexn"][:, si * 128:(si + 1) * 128],
                                    in0=P["bexp"][:, si * 128:(si + 1) * 128],
                                    in1=psD[:, 128:256], op=ALU.mult)
                v.inc_last(qV, 1)
            ct.inc(qV)
            env = ct.val(qV)

            @block.tensor
            def _(t, h=h, oc=oc, pb=pb, stiles=stiles, env=env):
                t = EW(t)
                t.wait_ge(qV, env)
                for si, (ss, mk) in enumerate(stiles):
                    t.matmul(psD[0:64, 256:384],
                             P["bVtr"][:, ss * 1024 + oc * 128 + pb:
                                       ss * 1024 + oc * 128 + pb + 64],
                             P["bexn"][:, si * 128:(si + 1) * 128],
                             start=(si == 0), stop=(si == len(stiles) - 1))
                t.inc_last(qT, 1)
            ct.inc(qT)
            cxq = ct.val(qT)

            @block.vector
            def _(v, h=h, oc=oc, pb=pb, cxq=cxq):
                v = EW(v)
                v.wait_ge(qT, cxq)
                v.tensor_copy(out=P["bctx"][pb:pb + 64, oc * 128:(oc + 1) * 128],
                              in_=psD[0:64, 256:384])
                v.inc_last(qV, 1)
            ct.inc(qV)

        CTX_V = ct.val(qV)
        for dc in range(NDC):
            @block.tensor
            def _(t, dc=dc, t0=t0, w=None):
                t = EW(t)
                t.wait_ge(qV, ct.val(qV))
                for oc in range(NDC):
                    t.matmul(psA[:, 0:128],
                             P["bWo_s"][:, oc * D + dc * 128: oc * D + dc * 128 + 128],
                             P["bctx"][:, oc * 128:(oc + 1) * 128],
                             start=(oc == 0), stop=(oc == NDC - 1))
                t.inc_last(qT, 1)
            ct.inc(qT)
            opq = ct.val(qT)

            @block.vector
            def _(v, dc=dc, t0=t0, opq=opq):
                v = EW(v)
                v.wait_ge(qT, opq)
                v.scalar_tensor_tensor(out=P["outst"][:, dc * 128:(dc + 1) * 128],
                                       in0=psA[:, 0:128], scalar=1.0,
                                       in1=x_sb[:, dc * NT + t0: dc * NT + t0 + 128],
                                       op0=ALU.mult, op1=ALU.add)
                v.inc_last(qV, 1)
            ct.inc(qV)
        OP_V = ct.val(qV)

        @block.vector
        def _(v, t0=t0, OP_V=OP_V):
            v = EW(v)
            v.wait_ge(qV, OP_V)
            for dc in range(NDC):
                v.tensor_copy(out=x_sb[:, dc * NT + t0: dc * NT + t0 + 128],
                              in_=P["outst"][:, dc * 128:(dc + 1) * 128])
            v.inc_last(qV, 1)
        ct.inc(qV)

    sc.close()


def _prep_inputs(inputs, b, T):
    f32 = np.float32
    x = np.ascontiguousarray(inputs["L0_attn_a_x"][b].T.astype(f32))
    d = {
        "x": x,
        "aWq": np.ascontiguousarray(inputs["L0_attn_a_Wq"].T),
        "aWk": np.ascontiguousarray(inputs["L0_attn_a_Wk"].T),
        "aWv": np.ascontiguousarray(inputs["L0_attn_a_Wv"].T),
        "aWo": np.ascontiguousarray(inputs["L0_attn_a_Wo"].T),
        "abq": inputs["L0_attn_a_bq"].reshape(H, 1),
        "abk": inputs["L0_attn_a_bk"].reshape(H, 1),
        "abv": inputs["L0_attn_a_bv"].reshape(1, H),
        "abo": np.ascontiguousarray(inputs["L0_attn_a_bo"].reshape(NDC, 128).T),
        "bWq": inputs["L0_attn_b_Wq"],
        "bWk": inputs["L0_attn_b_Wk"],
        "bWv": inputs["L0_attn_b_Wv"],
        "bWo": inputs["L0_attn_b_Wo"],
        "rW": np.ascontiguousarray(inputs["L0_ffn_w_router"].T),
        "W1T": np.ascontiguousarray(inputs["L0_ffn_W1"].transpose(0, 2, 1)),
        "W2T": np.ascontiguousarray(inputs["L0_ffn_W2"].transpose(0, 2, 1)),
        "Ws1T": np.ascontiguousarray(inputs["L0_ffn_Ws1"].T),
        "Ws2T": np.ascontiguousarray(inputs["L0_ffn_Ws2"].T),
        "b1fl": np.concatenate([np.asarray(inputs["L0_ffn_b1"]).reshape(-1),
                                np.asarray(inputs["L0_ffn_bs1"]).reshape(-1)]).reshape(1, 9 * DFF),
        "b2": inputs["L0_ffn_b2"],
        "bs2T": np.ascontiguousarray(inputs["L0_ffn_bs2"].reshape(NDC, 128).T),
    }
    return {k: np.ascontiguousarray(v).astype(f32) for k, v in d.items()}


_NC_CACHE = {}


def kernel(**inputs):
    T = inputs["L0_attn_a_x"].shape[1]
    B = inputs["L0_attn_a_x"].shape[0]
    if T not in _NC_CACHE:
        _NC_CACHE[T] = build_nc(T)
    nc = _NC_CACHE[T]
    in_maps = [_prep_inputs(inputs, b, T) for b in range(B)]
    res = run_bass_kernel_spmd(nc, in_maps, core_ids=list(range(B)))
    out = np.stack([np.ascontiguousarray(r["y"].T) for r in res.results], axis=0)
    return out.astype(np.float32)


if __name__ == "__main__":
    import reference as _r
    ins = _r.setup_inputs(0)
    ins = {k: np.asarray(v) for k, v in ins.items()}
    out = kernel(**ins)
    exp = np.asarray(_r.reference(**ins))
    err = np.linalg.norm(out - exp) / np.linalg.norm(exp)
    print("rel err:", err)


# version 14
# speedup vs baseline: 1.2438x; 1.2438x over previous
import math
import contextlib
import numpy as np
import ml_dtypes

import concourse.bass as bass
import concourse.mybir as mybir
from concourse.bass_utils import run_bass_kernel_spmd

F32 = mybir.dt.float32
BF16 = mybir.dt.bfloat16
AF = mybir.ActivationFunctionType
ALU = mybir.AluOpType
AX = mybir.AxisListType
ts = bass.ts

D = 1024
NDC = 8
H = 128
NE = 8
DFF = 4096
FC = 256
NM = 144
BLK = 256
EPS_LN = 1e-5
EPS_U = 1e-6
SCALE_FFN = 1.0 / math.sqrt(1.0 + 2.0 / 8.0)
GC1 = 2.0 * math.sqrt(2.0 / math.pi)
GC2 = 2.0 * math.sqrt(2.0 / math.pi) * 0.044715
NEGBIG = -1e30
SM_SHIFT = -10.0


class Ctr:
    def __init__(self):
        self.c = {}

    @staticmethod
    def _k(sem):
        return getattr(sem, "name", None) or id(sem)

    def inc(self, sem, n=1):
        k = self._k(sem)
        self.c[k] = self.c.get(k, 0) + n
        return self.c[k]

    def val(self, sem):
        return self.c.get(self._k(sem), 0)




import bass_rust as _br

_KF_UID = [0]


@contextlib.contextmanager
def kreg(eng, base=0):
    _KF_UID[0] += 1
    r = eng.alloc_register(f"kr{_KF_UID[0]}")
    try:
        yield r
    finally:
        eng.free_register(r)


@contextlib.contextmanager
def kfori(nc, eng, etype, start, end, step=1):
    _KF_UID[0] += 1
    name = f"kf{_KF_UID[0]}"
    reg = eng.alloc_register(f"{name}_i")
    rh = _br.RegisterHandles((reg,))
    eng.reg_mov(reg, start)
    nc.br(f"{name}_L", engines=[etype])
    with nc.body(f"{name}_L", valid_engines=[etype]):
        yield reg
        eng.reg_add(reg, reg, step)
        nc.br_lt(rh, end, on_true=f"{name}_L", on_false=f"{name}_E", engines=[etype])
    nc.switch_bb(f"{name}_E")
    eng.free_register(reg)


_RV_CACHE = {}


def rowslice(reg, nrows, maxrow):
    key = (reg.name, nrows, maxrow)
    if key not in _RV_CACHE:
        rv = _br.RuntimeValue(_br.RegisterHandles((reg,)), 0, maxrow, nrows, 0)
        _RV_CACHE[key] = _br.DynSlice(rv, nrows)
    return _RV_CACHE[key]




class EW:

    def __init__(self, eng):
        object.__setattr__(self, "_e", eng)
        object.__setattr__(self, "last", None)

    def __getattr__(self, n):
        a = getattr(self._e, n)
        if callable(a):
            def w(*args, **kw):
                r = a(*args, **kw)
                if hasattr(r, "then_inc"):
                    object.__setattr__(self, "last", r)
                return r
            return w
        return a

    def inc_last(self, q, n=1):
        self.last.then_inc(q, n)


def build_nc(T=4096):
    NT = T
    NG = T // 512
    NBLK = T // BLK
    NTIL = T // 128
    NCH = T // 128

    nc = bass.Bass("TRN2", target_bir_lowering=False, debug=False)

    x_in = nc.dram_tensor("x", [D, NT], F32, kind="ExternalInput").ap()
    y_out = nc.dram_tensor("y", [D, NT], F32, kind="ExternalOutput").ap()

    aWq = nc.dram_tensor("aWq", [D, H], F32, kind="ExternalInput").ap()
    aWk = nc.dram_tensor("aWk", [D, H], F32, kind="ExternalInput").ap()
    aWv = nc.dram_tensor("aWv", [D, H], F32, kind="ExternalInput").ap()
    aWo = nc.dram_tensor("aWo", [H, D], F32, kind="ExternalInput").ap()
    abq = nc.dram_tensor("abq", [H, 1], F32, kind="ExternalInput").ap()
    abk = nc.dram_tensor("abk", [H, 1], F32, kind="ExternalInput").ap()
    abv = nc.dram_tensor("abv", [1, H], F32, kind="ExternalInput").ap()
    abo = nc.dram_tensor("abo", [128, NDC], F32, kind="ExternalInput").ap()

    bWq = nc.dram_tensor("bWq", [D, D], F32, kind="ExternalInput").ap()
    bWk = nc.dram_tensor("bWk", [D, D], F32, kind="ExternalInput").ap()
    bWv = nc.dram_tensor("bWv", [D, D], F32, kind="ExternalInput").ap()
    bWo = nc.dram_tensor("bWo", [D, D], F32, kind="ExternalInput").ap()

    rW = nc.dram_tensor("rW", [D, NE], F32, kind="ExternalInput").ap()
    W1T = nc.dram_tensor("W1T", [NE, D, DFF], F32, kind="ExternalInput").ap()
    W2T = nc.dram_tensor("W2T", [NE, DFF, D], F32, kind="ExternalInput").ap()
    Ws1T = nc.dram_tensor("Ws1T", [D, DFF], F32, kind="ExternalInput").ap()
    Ws2T = nc.dram_tensor("Ws2T", [DFF, D], F32, kind="ExternalInput").ap()
    b1fl = nc.dram_tensor("b1fl", [1, 9 * DFF], F32, kind="ExternalInput").ap()
    b2 = nc.dram_tensor("b2", [NE, D], F32, kind="ExternalInput").ap()
    bs2T = nc.dram_tensor("bs2T", [128, NDC], F32, kind="ExternalInput").ap()

    c_idf = nc.inline_tensor(np.eye(128, dtype=np.float32), "c_idf").ap()
    c_idb = nc.inline_tensor(np.eye(128).astype(ml_dtypes.bfloat16), "c_idb").ap()
    c_onescol_b = nc.inline_tensor(np.ones((128, 1), ml_dtypes.bfloat16), "c_onescol_b").ap()
    c_onescol_f = nc.inline_tensor(np.ones((128, 1), np.float32), "c_onescol_f").ap()
    c_onesrow_b = nc.inline_tensor(np.ones((128, 512), ml_dtypes.bfloat16), "c_onesrow_b").ap()
    c_onesrow_f = nc.inline_tensor(np.ones((128, 128), np.float32), "c_onesrow_f").ap()
    tri = (np.arange(128)[:, None] <= np.arange(128)[None, :]).astype(np.float32)
    c_triu = nc.inline_tensor(tri, "c_triu").ap()
    s_i = np.arange(128)[:, None]
    t_i = np.arange(128)[None, :]
    mB = np.where((s_i <= t_i) & (s_i >= t_i - 63), SM_SHIFT, NEGBIG).astype(np.float32)
    mA = np.where(s_i >= t_i + 65, SM_SHIFT, NEGBIG).astype(np.float32)
    c_maskB = nc.inline_tensor(mB, "c_maskB").ap()
    c_maskA = nc.inline_tensor(mA, "c_maskA").ap()
    sel = np.zeros((8, 8 * 128), np.float32)
    for e in range(8):
        sel[e, e * 128:(e + 1) * 128] = 1.0
    c_selE = nc.inline_tensor(sel, "c_selE").ap()

    W1bf = nc.dram_tensor("W1bf", [NM * 128, 2304], BF16, kind="Internal").ap()
    W2bf = nc.dram_tensor("W2bf", [NM * 128, 2048], BF16, kind="Internal").ap()

    es = contextlib.ExitStack()

    x_sb = nc.alloc_sbuf_tensor("x_sb", [128, NDC * NT], BF16)
    rW_s = nc.alloc_sbuf_tensor("rW_s", [128, NDC * NE], BF16)
    b2_s = nc.alloc_sbuf_tensor("b2_s", [8, D], F32)
    bs2T_s = nc.alloc_sbuf_tensor("bs2T_s", [128, NDC], F32)
    idf_s = nc.alloc_sbuf_tensor("idf_s", [128, 128], F32)
    idb_s = nc.alloc_sbuf_tensor("idb_s", [128, 128], BF16)
    onescol_b = nc.alloc_sbuf_tensor("onescol_b", [128, 1], BF16)
    onescol_f = nc.alloc_sbuf_tensor("onescol_f", [128, 1], F32)
    onesrow_b = nc.alloc_sbuf_tensor("onesrow_b", [128, 512], BF16)
    onesrow_f = nc.alloc_sbuf_tensor("onesrow_f", [128, 128], F32)
    triu_s = nc.alloc_sbuf_tensor("triu_s", [128, 128], F32)
    maskB_s = nc.alloc_sbuf_tensor("maskB_s", [128, 128], F32)
    maskA_s = nc.alloc_sbuf_tensor("maskA_s", [128, 128], F32)
    selE_s = nc.alloc_sbuf_tensor("selE_s", [8, 8 * 128], F32)
    statrows = nc.alloc_sbuf_tensor("statrows", [128, 1024], F32)
    meanrow = statrows[0:1, 0:512]
    varrow = statrows[0:1, 512:1024]
    rstdrow = statrows[32:33, 0:512]

    psH = nc.alloc_psum_tensor("psH", [128, 1024], F32)
    psC = nc.alloc_psum_tensor("psC", [128, 512], F32)
    psD = nc.alloc_psum_tensor("psD", [128, 512], F32)
    psY = nc.alloc_psum_tensor("psY", [128, 2048], F32)
    psA = psH[:, 0:512]
    psB = psH[:, 512:1024]

    qGP = es.enter_context(nc.semaphore("qGP"))
    qSTRM = {tg: tuple(es.enter_context(nc.semaphore(f"q{w}s{sl}{tg}"))
                       for w in ("W1", "W2") for sl in range(4))
             for tg in ("A", "B")}
    qSY = es.enter_context(nc.semaphore("qSY"))
    qT = es.enter_context(nc.semaphore("qT"))
    qV = es.enter_context(nc.semaphore("qV"))
    qS = es.enter_context(nc.semaphore("qS"))
    qP = es.enter_context(nc.semaphore("qP"))
    qTh = es.enter_context(nc.semaphore("qTh"))
    qTw2 = es.enter_context(nc.semaphore("qTw2"))
    qSq = es.enter_context(nc.semaphore("qSq"))
    qSg = es.enter_context(nc.semaphore("qSg"))
    qVp = es.enter_context(nc.semaphore("qVp"))
    qVu = es.enter_context(nc.semaphore("qVu"))
    qVg = es.enter_context(nc.semaphore("qVg"))
    qPg = es.enter_context(nc.semaphore("qPg"))
    qST = es.enter_context(nc.semaphore("qST"))
    qTs = es.enter_context(nc.semaphore("qTs"))
    qVs = es.enter_context(nc.semaphore("qVs"))
    qSs = es.enter_context(nc.semaphore("qSs"))

    ct = Ctr()
    cst = nc.alloc_sbuf_tensor("cst", [128, 2], F32)

    ov = contextlib.ExitStack()

    def ovt(name, shape, dt):
        return ov.enter_context(nc.sbuf_tensor(name, shape, dt))

    KT = ovt("KT", [128, NT], F32)
    Wbuf = KT
    UT = ovt("UT", [128, NT], F32)
    QTb = ovt("QTb", [128, NT], BF16)
    Vtm = ovt("Vtm", [128, NT], BF16)
    alphaT = ovt("alphaT", [128, NT], BF16)
    OT = UT.bitcast(BF16)[:, 0:NT]
    xn_grp = ovt("xn_grp", [128, NDC * 512], BF16)
    sq_scr = ovt("sq_scr", [128, NDC * 512], BF16)
    sqf = ovt("sqf", [128, 512], F32)
    rowsbig = ovt("rowsbig", [128, 2 * NT], F32)
    rdrow = rowsbig[0:1, 0:NT]
    srow = rowsbig[0:1, NT:2 * NT]
    n2row = rowsbig[64:65, 0:NT]
    arow = rowsbig[64:65, 0:NT]
    Asb = ovt("Asb", [128, 128], F32)
    ypair_sb = ovt("ypair_sb", [128, 2], F32)
    scs = ovt("scs", [1, 784], F32)
    scrow_b = ovt("scrow_b", [1, NT], BF16)
    Mst_sb = ovt("Mst_sb", [128, 128], BF16)
    atm = ovt("atm", [128, 128], BF16)
    psm_sb = ovt("psm_sb", [128, 128], BF16)
    aWq_s = ovt("aWq_s", [128, NDC * H], BF16)
    aWk_s = ovt("aWk_s", [128, NDC * H], BF16)
    aWv_s = ovt("aWv_s", [128, NDC * H], BF16)
    aWo_s = ovt("aWo_s", [128, D], BF16)
    abq_s = ovt("abq_s", [128, 1], F32)
    abk_s = ovt("abk_s", [128, 1], F32)
    abv_s = ovt("abv_s", [1, H], BF16)
    abo_s = ovt("abo_s", [128, NDC], F32)

    LN = dict(cst=cst, onescol_b=onescol_b, onesrow_f=onesrow_f, meanrow=meanrow,
              varrow=varrow, rstdrow=rstdrow, psA=psA, psB=psB, psC=psC, psD=psD,
              qT=qT, qV=qV, qS=qS)

    with nc.Block() as block:
        @block.vector
        def _(v):
            v = EW(v)
            v.memset(cst[:, 0:1], -1.0)
            v.memset(cst[:, 1:2], EPS_LN)
            v.inc_last(qV, 1)
        ct.inc(qV)

        @block.gpsimd
        def _(g):
            g = EW(g)
            g.dma_start(out=x_sb[:, :].rearrange("p (dc t) -> p dc t", dc=NDC),
                        in_=x_in.rearrange("(dc p) t -> p dc t", p=128)).then_inc(qGP, 16)
            g.dma_start(out=aWq_s[:, :].rearrange("p (dc h) -> p dc h", dc=NDC),
                        in_=aWq.rearrange("(dc p) h -> p dc h", p=128)).then_inc(qGP, 16)
            g.dma_start(out=aWk_s[:, :].rearrange("p (dc h) -> p dc h", dc=NDC),
                        in_=aWk.rearrange("(dc p) h -> p dc h", p=128)).then_inc(qGP, 16)
            g.dma_start(out=aWv_s[:, :].rearrange("p (dc h) -> p dc h", dc=NDC),
                        in_=aWv.rearrange("(dc p) h -> p dc h", p=128)).then_inc(qGP, 16)
            g.dma_start(out=aWo_s[:, :], in_=aWo[:, :]).then_inc(qGP, 16)
            g.dma_start(out=abv_s[:, :], in_=abv[:, :]).then_inc(qGP, 16)
            g.dma_start(out=rW_s[:, :].rearrange("p (dc e) -> p dc e", dc=NDC),
                        in_=rW.rearrange("(dc p) e -> p dc e", p=128)).then_inc(qGP, 16)
            ct.inc(qGP, 16 * 7)

        @block.sync
        def _(sy):
            sy = EW(sy)
            for dst, src in ((abq_s, abq), (abk_s, abk), (abo_s, abo), (b2_s, b2),
                             (bs2T_s, bs2T), (idf_s, c_idf), (idb_s, c_idb),
                             (onescol_b, c_onescol_b), (onescol_f, c_onescol_f),
                             (onesrow_b, c_onesrow_b), (onesrow_f, c_onesrow_f),
                             (triu_s, c_triu), (maskB_s, c_maskB), (maskA_s, c_maskA),
                             (selE_s, c_selE)):
                sy.dma_start(out=dst[:, :], in_=src[:, :]).then_inc(qSY, 16)
                ct.inc(qSY, 16)
        LOADS_SY = ct.val(qSY)
        LOADS_GP = ct.val(qGP)

        @block.gpsimd
        def _(g):
            g = EW(g)
            g.wait_ge(qGP, LOADS_GP)
            for m in range(NM):
                e, fc = m // 16, m % 16
                if e < 8:
                    src1 = bass.AP(W1T.tensor, e * D * DFF + fc * FC,
                                   [[DFF, 128], [128 * DFF, NDC], [1, FC]])
                    src2 = bass.AP(W2T.tensor, e * DFF * D + fc * FC * D,
                                   [[D, 128], [128 * D, 2], [1, D]])
                else:
                    src1 = bass.AP(Ws1T.tensor, fc * FC,
                                   [[DFF, 128], [128 * DFF, NDC], [1, FC]])
                    src2 = bass.AP(Ws2T.tensor, fc * FC * D,
                                   [[D, 128], [128 * D, 2], [1, D]])
                d1 = W1bf[m * 128:(m + 1) * 128, 0:2048].rearrange("p (a b) -> p a b", a=NDC)
                d2 = W2bf[m * 128:(m + 1) * 128, :].rearrange("p (a b) -> p a b", a=2)
                g.dma_start(out=d1, in_=src1).then_inc(qGP, 16)
                g.dma_start(out=d2, in_=src2).then_inc(qGP, 16)
                g.dma_start(out=W1bf[m * 128:(m + 1) * 128, 2048:2304],
                            in_=bass.AP(b1fl.tensor, m * FC, [[0, 128], [1, FC]])).then_inc(qGP, 16)
                ct.inc(qGP, 48)
        PRECAST_GP = ct.val(qGP)

        for gi in range(NG):
            t0 = gi * 512
            pre = [(qGP, LOADS_GP), (qSY, LOADS_SY)] if gi == 0 else None
            emit_ln(nc, block, ct, x_sb, xn_grp, sq_scr, t0, 512, NT, LN, pre_waits=pre)

            @block.tensor
            def _(t, t0=t0):
                t = EW(t)
                t.wait_ge(qV, ct.val(qV))
                if t0 == 0:
                    t.wait_ge(qGP, LOADS_GP)
                for dc in range(NDC):
                    t.matmul(psA[:, 0:512], aWq_s[:, dc * H:(dc + 1) * H],
                             xn_grp[:, dc * 512:(dc + 1) * 512],
                             start=(dc == 0), stop=(dc == NDC - 1))
                t.inc_last(qT, 1)
                for dc in range(NDC):
                    t.matmul(psB[:, 0:512], aWk_s[:, dc * H:(dc + 1) * H],
                             xn_grp[:, dc * 512:(dc + 1) * 512],
                             start=(dc == 0), stop=(dc == NDC - 1))
                t.inc_last(qT, 1)
                for ti in range(4):
                    for dc in range(NDC):
                        t.matmul(psC[:, ti * 128:(ti + 1) * 128],
                                 xn_grp[:, dc * 512 + ti * 128: dc * 512 + (ti + 1) * 128],
                                 aWv_s[:, dc * H:(dc + 1) * H],
                                 start=(dc == 0), stop=False)
                    t.matmul(psC[:, ti * 128:(ti + 1) * 128], onesrow_b[0:1, 0:128],
                             abv_s[:, :], start=False, stop=True)
                t.inc_last(qT, 1)
            ct.inc(qT, 3)
            tq = ct.val(qT)

            @block.vector
            def _(v, t0=t0, tq=tq):
                v = EW(v)
                v.wait_ge(qT, tq - 2)
                v.tensor_scalar(out=QTb[:, t0:t0 + 512], in0=psA[:, 0:512],
                                scalar1=abq_s[:, :], scalar2=None, op0=ALU.add)
                v.wait_ge(qT, tq - 1)
                v.tensor_scalar(out=KT[:, t0:t0 + 512], in0=psB[:, 0:512],
                                scalar1=abk_s[:, :], scalar2=None, op0=ALU.add)
                v.wait_ge(qT, tq)
                v.tensor_copy(out=Vtm[:, t0:t0 + 512], in_=psC[:, 0:512])
                v.inc_last(qV, 1)
            ct.inc(qV)
            vq = ct.val(qV)

            @block.scalar
            def _(s, t0=t0, vq=vq):
                s = EW(s)
                s.wait_ge(qV, vq)
                s.activation(out=sqf[:, :], in_=KT[:, t0:t0 + 512], func=AF.Square)
                s.inc_last(qS, 1)
            ct.inc(qS)
            sq1 = ct.val(qS)

            @block.tensor
            def _(t, sq1=sq1):
                t = EW(t)
                t.wait_ge(qS, sq1)
                t.matmul(psA[0:1, 0:512], onescol_f[:, :], sqf[:, :], start=True, stop=True)
                t.inc_last(qT, 1)
            ct.inc(qT)
            n2q = ct.val(qT)

            @block.vector
            def _(v, t0=t0, n2q=n2q):
                v = EW(v)
                v.wait_ge(qT, n2q)
                v.tensor_copy(out=n2row[0:1, t0:t0 + 512], in_=psA[0:1, 0:512])
                v.tensor_tensor(out=sqf[:, :], in0=KT[:, t0:t0 + 512],
                                in1=QTb[:, t0:t0 + 512], op=ALU.mult)
                v.inc_last(qV, 1)
            ct.inc(qV)
            qkv = ct.val(qV)

            @block.tensor
            def _(t, qkv=qkv):
                t = EW(t)
                t.wait_ge(qV, qkv)
                t.matmul(psB[0:1, 0:512], onescol_f[:, :], sqf[:, :], start=True, stop=True)
                t.inc_last(qT, 1)
            ct.inc(qT)
            sq2 = ct.val(qT)

            @block.vector
            def _(v, t0=t0, sq2=sq2):
                v = EW(v)
                v.wait_ge(qT, sq2)
                v.tensor_copy(out=srow[0:1, t0:t0 + 512], in_=psB[0:1, 0:512])
                v.inc_last(qV, 1)
            ct.inc(qV)

        @block.scalar
        def _(s):
            s = EW(s)
            s.wait_ge(qV, ct.val(qV))
            s.activation(out=n2row[0:1, 0:NT], in_=n2row[0:1, 0:NT], func=AF.Sqrt)
            s.inc_last(qS, 1)
        ct.inc(qS)
        nq = ct.val(qS)

        @block.vector
        def _(v, nq=nq):
            v = EW(v)
            v.wait_ge(qS, nq)
            v.tensor_scalar(out=n2row[0:1, 0:NT], in0=n2row[0:1, 0:NT],
                            scalar1=EPS_U, scalar2=None, op0=ALU.add)
            v.drain()
            v.reciprocal(n2row[0:1, 0:NT], n2row[0:1, 0:NT])
            v.inc_last(qV, 1)
        ct.inc(qV)
        rno = ct.val(qV)

        for gi in range(NG):
            t0 = gi * 512

            @block.tensor
            def _(t, t0=t0, gi=gi, rno=rno):
                t = EW(t)
                t.wait_ge(qV, rno + gi)
                t.matmul(psA[:, 0:512], onesrow_f[64:65, :], n2row[0:1, t0:t0 + 512],
                         start=True, stop=True)
                t.inc_last(qT, 1)
            ct.inc(qT)
            bq_ = ct.val(qT)

            @block.vector
            def _(v, t0=t0, bq_=bq_):
                v = EW(v)
                v.wait_ge(qT, bq_)
                v.tensor_tensor(out=UT[:, t0:t0 + 512], in0=KT[:, t0:t0 + 512],
                                in1=psA[:, 0:512], op=ALU.mult)
                v.inc_last(qV, 1)
            ct.inc(qV)
        UT_V = ct.val(qV)

        A_ps = psY[:, 0:128]
        w_ps2 = psY[:, 512:514]
        g11_ps = psY[0:1, 1024:1025]
        g22_ps = psY[0:1, 1025:1026]
        gc_ps = psY[0:1, 1026:1027]
        yr1_ps = psY[0:1, 1536:1664]
        yr2_ps = psY[0:1, 1664:1792]
        yr12_ps = psY[0:1, 1536:1792]
        y12r = scs[0:1, 0:256]
        q1 = scs[0:1, 256:384]
        q2 = scs[0:1, 384:512]
        q3 = scs[0:1, 512:640]
        q4 = scs[0:1, 640:768]
        p1r = q1
        p2r = q3
        zrow = q1
        bb2 = scs[0:1, 768:770]
        nrow = scs[0:1, 770:772]
        rpr = scs[0:1, 772:774]
        m1s = scs[0:1, 774:775]
        m2s = scs[0:1, 775:776]
        s22v = scs[0:1, 776:777]
        sc1 = scs[0:1, 777:778]
        csb = scs[0:1, 778:779]
        s11v = scs[0:1, 779:780]
        NP = NT // 2

        @block.vector
        def _(v):
            v = EW(v)
            v.wait_ge(qV, UT_V)
            v.memset(scrow_b[0:1, :], 0.0)
            v.memset(zrow, 0.0)
            v.tensor_copy(out=Asb[:, :], in_=idf_s[:, :])
            v.inc_last(qVs, 1)

        @block.tensor
        def _(t):
            t = EW(t)
            t.wait_ge(qVs, 1)
            t.matmul(A_ps, zrow, zrow, start=True, stop=False,
                     skip_group_check=True)
            with kreg(t) as rW, kreg(t) as rTi, kreg(t) as rTi1:
                t.reg_mov(rW, 1)
                t.reg_mov(rTi, 0)
                t.reg_mov(rTi1, 1)
                with kfori(nc, t, mybir.EngineType.PE, 0, NP, 1):
                    t.wait_ge(qVs, rW)
                    t.matmul(w_ps2, Asb[:, :], UT[:, rowslice(rTi, 2, NT - 2)],
                             start=True, stop=True).then_inc(qTs, 1)
                    t.reg_add(rW, rW, 1)
                    t.wait_ge(qVs, rW)
                    t.transpose(yr1_ps, ypair_sb[:, 0:1], idf_s[:, :])
                    t.transpose(yr2_ps, ypair_sb[:, 1:2], idf_s[:, :])
                    t.matmul(g11_ps, ypair_sb[:, 0:1],
                             UT[:, rowslice(rTi, 1, NT - 1)], start=True, stop=True)
                    t.matmul(g22_ps, ypair_sb[:, 1:2],
                             UT[:, rowslice(rTi1, 1, NT - 1)], start=True, stop=True)
                    t.matmul(gc_ps, ypair_sb[:, 0:1],
                             UT[:, rowslice(rTi1, 1, NT - 1)], start=True,
                             stop=True).then_inc(qTs, 1)
                    t.reg_add(rW, rW, 1)
                    t.wait_ge(qVs, rW)
                    t.matmul(A_ps, p1r, y12r[0:1, 0:128], start=False, stop=False,
                             skip_group_check=True)
                    t.matmul(A_ps, p2r, y12r[0:1, 128:256], start=False, stop=False,
                             skip_group_check=True).then_inc(qTs, 1)
                    t.reg_add(rW, rW, 1)
                    t.reg_add(rTi, rTi, 2)
                    t.reg_add(rTi1, rTi1, 2)

        @block.vector
        def _(v):
            v = EW(v)
            with kreg(v) as rT, kreg(v) as rVi, kreg(v) as rVi1:
                v.reg_mov(rT, 1)
                v.reg_mov(rVi, 0)
                v.reg_mov(rVi1, 1)
                with kfori(nc, v, mybir.EngineType.DVE, 0, NP, 1):
                    v.wait_ge(qTs, rT)
                    v.tensor_copy(out=ypair_sb[:, :], in_=w_ps2).then_inc(qVs, 1)
                    v.reg_add(rT, rT, 1)
                    v.wait_ge(qTs, rT)
                    v.tensor_copy(out=y12r, in_=yr12_ps)
                    v.tensor_scalar(out=bb2, in0=psY[0:1, 1024:1026], scalar1=1.0,
                                    scalar2=None, op0=ALU.add)
                    v.tensor_copy(out=csb, in_=gc_ps)
                    v.drain()
                    v.tensor_tensor(out=m2s, in0=csb, in1=csb, op=ALU.mult)
                    v.tensor_tensor(out=m1s, in0=bb2[0:1, 0:1], in1=bb2[0:1, 1:2],
                                    op=ALU.mult)
                    v.tensor_scalar(out=nrow[0:1, 0:1], in0=bb2[0:1, 0:1],
                                    scalar1=-1.0, scalar2=None, op0=ALU.mult)
                    v.drain()
                    v.scalar_tensor_tensor(out=nrow[0:1, 1:2], in0=m1s, scalar=-1.0,
                                           in1=m2s, op0=ALU.mult, op1=ALU.add)
                    v.drain()
                    v.reciprocal(rpr, nrow)
                    v.drain()
                    v.tensor_scalar(out=sc1, in0=gc_ps, scalar1=rpr[0:1, 1:2],
                                    scalar2=None, op0=ALU.mult)
                    v.tensor_scalar(out=s11v, in0=bb2[0:1, 0:1],
                                    scalar1=rpr[0:1, 1:2], scalar2=None, op0=ALU.mult)
                    v.tensor_scalar(out=s22v, in0=bb2[0:1, 1:2],
                                    scalar1=rpr[0:1, 1:2], scalar2=None, op0=ALU.mult)
                    v.drain()
                    v.tensor_scalar(out=q1, in0=y12r[0:1, 0:128],
                                    scalar1=s22v[0:1, 0:1], scalar2=None, op0=ALU.mult)
                    v.tensor_scalar(out=q2, in0=y12r[0:1, 128:256],
                                    scalar1=sc1[0:1, 0:1],
                                    scalar2=None, op0=ALU.mult)
                    v.tensor_scalar(out=q3, in0=y12r[0:1, 128:256],
                                    scalar1=s11v[0:1, 0:1],
                                    scalar2=None, op0=ALU.mult)
                    v.tensor_scalar(out=q4, in0=y12r[0:1, 0:128],
                                    scalar1=sc1[0:1, 0:1],
                                    scalar2=None, op0=ALU.mult)
                    v.drain()
                    v.tensor_tensor(out=p1r, in0=q1, in1=q2, op=ALU.subtract)
                    v.tensor_tensor(out=p2r, in0=q3, in1=q4,
                                    op=ALU.subtract).then_inc(qVs, 1)
                    v.tensor_copy(out=rdrow[0:1, rowslice(rVi, 1, NT - 1)],
                                  in_=rpr[0:1, 0:1])
                    v.tensor_copy(out=rdrow[0:1, rowslice(rVi1, 1, NT - 1)],
                                  in_=s11v)
                    v.tensor_copy(out=scrow_b[0:1, rowslice(rVi1, 1, NT - 1)],
                                  in_=sc1)
                    v.reg_add(rT, rT, 1)
                    v.wait_ge(qTs, rT)
                    v.scalar_tensor_tensor(out=Asb[:, :], in0=A_ps, scalar=1.0,
                                           in1=idf_s[:, :], op0=ALU.mult,
                                           op1=ALU.add)
                    v.tensor_copy(out=Wbuf[:, rowslice(rVi, 2, NT - 2)],
                                  in_=ypair_sb[:, :]).then_inc(qVs, 1)
                    v.reg_add(rT, rT, 1)
                    v.reg_add(rVi, rVi, 2)
                    v.reg_add(rVi1, rVi1, 2)

        SCAN_TS = 3 * NP
        SCAN_VS = 1 + 3 * NP
        ct.inc(qTs, SCAN_TS)
        ct.inc(qVs, SCAN_VS)

        @block.vector
        def _(v):
            v = EW(v)
            v.wait_ge(qVs, SCAN_VS)
            v.tensor_tensor(out=arow[0:1, 0:NT], in0=srow[0:1, 0:NT],
                            in1=rdrow[0:1, 0:NT], op=ALU.mult)
            v.tensor_tensor(out=scrow_b[0:1, 0:NT], in0=scrow_b[0:1, 0:NT],
                            in1=srow[0:1, 0:NT], op=ALU.mult)
            v.drain()
            v.tensor_scalar(out=arow[0:1, 0:NT], in0=arow[0:1, 0:NT],
                            scalar1=-1.0, scalar2=None, op0=ALU.mult)
            v.inc_last(qV, 1)
        ct.inc(qV)
        alv = ct.val(qV)

        for gi in range(NG):
            t0 = gi * 512

            @block.tensor
            def _(t, t0=t0, alv=alv):
                t = EW(t)
                t.wait_ge(qV, alv + 2 * gi)
                t.matmul(psA[:, 0:512], onesrow_f[64:65, :], arow[0:1, t0:t0 + 512],
                         start=True, stop=True)
                t.matmul(psC[:, 0:512], onesrow_b[0:1, 0:128],
                         scrow_b[0:1, t0:t0 + 512], start=True, stop=True)
                t.inc_last(qT, 1)
            ct.inc(qT)
            ab_ = ct.val(qT)

            @block.vector
            def _(v, gi=gi, t0=t0, ab_=ab_):
                v = EW(v)
                v.wait_ge(qT, ab_)
                v.tensor_tensor(out=alphaT[:, t0:t0 + 512], in0=Wbuf[:, t0:t0 + 512],
                                in1=psA[:, 0:512], op=ALU.mult)
                lo = 1 if gi == 0 else 0
                v.tensor_tensor(out=sq_scr[:, lo:512],
                                in0=Wbuf[:, t0 + lo - 1:t0 + 511],
                                in1=psC[:, lo:512], op=ALU.mult)
                v.drain()
                v.tensor_tensor(out=alphaT[:, t0 + lo:t0 + 512],
                                in0=alphaT[:, t0 + lo:t0 + 512],
                                in1=sq_scr[:, lo:512], op=ALU.add)
                v.inc_last(qV, 2)
            ct.inc(qV, 2)

        psBb = psH.bitcast(BF16)
        for c in range(NCH):
            t0 = c * 128

            @block.tensor
            def _(t, c=c, t0=t0):
                t = EW(t)
                t.wait_ge(qV, ct.val(qV))
                t.matmul(psA[:, 0:128], alphaT[:, t0:t0 + 128], QTb[:, t0:t0 + 128],
                         start=True, stop=True)
                t.inc_last(qT, 1)
                t.transpose(psBb[:, 1024:1152], alphaT[:, t0:t0 + 128], idb_s[:, :])
                t.inc_last(qT, 1)
            ct.inc(qT, 2)
            pq = ct.val(qT)

            @block.vector
            def _(v, c=c, pq=pq):
                v = EW(v)
                v.wait_ge(qT, pq - 1)
                v.tensor_tensor(out=psm_sb[:, :], in0=psA[:, 0:128], in1=triu_s[:, :],
                                op=ALU.mult)
                v.wait_ge(qT, pq)
                v.tensor_copy(out=atm[:, :], in_=psBb[:, 1024:1152])
                v.inc_last(qV, 1)
            ct.inc(qV)
            pv = ct.val(qV)

            @block.tensor
            def _(t, c=c, t0=t0, pv=pv):
                t = EW(t)
                t.wait_ge(qV, pv)
                t.matmul(psC[:, 0:128], Vtm[:, t0:t0 + 128], psm_sb[:, :],
                         start=True, stop=(c == 0))
                if c > 0:
                    t.matmul(psC[:, 0:128], Mst_sb[:, :], QTb[:, t0:t0 + 128],
                             start=False, stop=True)
                t.inc_last(qT, 1)
                t.matmul(psD[:, 0:128], atm[:, :], Vtm[:, t0:t0 + 128],
                         start=(c == 0), stop=(c == NCH - 1), skip_group_check=True)
                t.inc_last(qT, 1)
            ct.inc(qT, 2)
            oq = ct.val(qT)

            @block.vector
            def _(v, c=c, t0=t0, oq=oq):
                v = EW(v)
                v.wait_ge(qT, oq - 1)
                v.tensor_copy(out=OT[:, t0:t0 + 128], in_=psC[:, 0:128])
                v.wait_ge(qT, oq)
                v.tensor_copy(out=Mst_sb[:, :], in_=psD[:, 0:128])
                v.inc_last(qV, 1)
            ct.inc(qV)

        for gi in range(NG):
            t0 = gi * 512
            for dc in range(NDC):
                @block.tensor
                def _(t, t0=t0, dc=dc):
                    t = EW(t)
                    t.wait_ge(qV, ct.val(qV))
                    t.matmul(psA[:, 0:512], aWo_s[:, dc * 128:(dc + 1) * 128],
                             OT[:, t0:t0 + 512], start=True, stop=True)
                    t.inc_last(qT, 1)
                ct.inc(qT)
                tq2 = ct.val(qT)

                @block.vector
                def _(v, t0=t0, dc=dc, tq2=tq2):
                    v = EW(v)
                    v.wait_ge(qT, tq2)
                    v.tensor_scalar(out=sqf[:, :], in0=psA[:, 0:512],
                                    scalar1=abo_s[:, dc:dc + 1], scalar2=None,
                                    op0=ALU.add)
                    v.drain()
                    v.tensor_tensor(out=x_sb[:, dc * NT + t0: dc * NT + t0 + 512],
                                    in0=x_sb[:, dc * NT + t0: dc * NT + t0 + 512],
                                    in1=sqf[:, :], op=ALU.add)
                    v.inc_last(qV, 1)
                ct.inc(qV)
        ATTNA_V = ct.val(qV)

        ov.close()

        ffn = dict(x_sb=x_sb, W1bf=W1bf, W2bf=W2bf, b1fl=b1fl, qP=qP, qSTRM=qSTRM,
                   rW_s=rW_s, b2_s=b2_s, bs2T_s=bs2T_s,
                   selE_s=selE_s, idf_s=idf_s, onesrow_b=onesrow_b,
                   psH=psH, psC=psC, psD=psD, psY=psY, psA=psA, psB=psB,
                   qTh=qTh, qTw2=qTw2, qSq=qSq, qSg=qSg, qVp=qVp, qVu=qVu,
                   qVg=qVg, qPg=qPg, qSY=qSY, qGP=qGP, qST=qST,
                   PRECAST_GP=PRECAST_GP, y_out=y_out, LN=LN, NT=NT)

        emit_ffn(nc, block, ct, ffn, last=False, tag="A")

        attb = dict(x_sb=x_sb, bWq=bWq, bWk=bWk, bWv=bWv, bWo=bWo,
                    maskA_s=maskA_s, maskB_s=maskB_s, onescol_b=onescol_b,
                    onesrow_f=onesrow_f,
                    psH=psH, psC=psC, psD=psD, psA=psA, psB=psB,
                    qGP=qGP, qSY=qSY, qTw2=qTw2, LN=LN, NT=NT)

        emit_attn_b(nc, block, ct, attb)

        emit_ffn(nc, block, ct, ffn, last=True, tag="B")

        @block.sync
        def _(sy):
            sy = EW(sy)
            sy.wait_ge(qST, ct.val(qST))

    es.close()
    return nc


def emit_ln(nc, block, ct, x_sb, xn_out, sq_scr, t0, W, NT, LN, pre_waits=None,
            xn_stride=None):
    if xn_stride is None:
        xn_stride = W
    qT, qV, qS = LN["qT"], LN["qV"], LN["qS"]
    psA, psB, psC, psD = LN["psA"], LN["psB"], LN["psC"], LN["psD"]

    @block.scalar
    def _(s):
        s = EW(s)
        s.wait_ge(qV, ct.val(qV))
        if pre_waits:
            for (q, vv) in pre_waits:
                s.wait_ge(q, vv)
        for dc in range(NDC):
            s.activation(out=sq_scr[:, dc * W:(dc + 1) * W],
                         in_=x_sb[:, dc * NT + t0: dc * NT + t0 + W], func=AF.Square)
        s.inc_last(qS, 1)
    ct.inc(qS)
    sqq = ct.val(qS)

    @block.tensor
    def _(t, sqq=sqq):
        t = EW(t)
        t.wait_ge(qS, sqq)
        for dc in range(NDC):
            t.matmul(psA[0:1, 0:W], LN["onescol_b"][:, :],
                     x_sb[:, dc * NT + t0: dc * NT + t0 + W],
                     start=(dc == 0), stop=(dc == NDC - 1))
        for dc in range(NDC):
            t.matmul(psB[0:1, 0:W], LN["onescol_b"][:, :],
                     sq_scr[:, dc * W:(dc + 1) * W],
                     start=(dc == 0), stop=(dc == NDC - 1))
        t.inc_last(qT, 1)
    ct.inc(qT)
    stq = ct.val(qT)

    @block.vector
    def _(v, stq=stq):
        v = EW(v)
        v.wait_ge(qT, stq)
        v.tensor_scalar(out=LN["meanrow"][0:1, 0:W], in0=psA[0:1, 0:W],
                        scalar1=1.0 / D, scalar2=None, op0=ALU.mult)
        v.drain()
        v.tensor_tensor(out=LN["varrow"][0:1, 0:W], in0=LN["meanrow"][0:1, 0:W],
                        in1=LN["meanrow"][0:1, 0:W], op=ALU.mult)
        v.drain()
        v.scalar_tensor_tensor(out=LN["varrow"][0:1, 0:W], in0=psB[0:1, 0:W],
                               scalar=1.0 / D, in1=LN["varrow"][0:1, 0:W],
                               op0=ALU.mult, op1=ALU.subtract)
        v.inc_last(qV, 1)
    ct.inc(qV)
    vq = ct.val(qV)

    @block.scalar
    def _(s, vq=vq):
        s = EW(s)
        s.wait_ge(qV, vq)
        s.activation(out=LN["rstdrow"][0:1, 0:W], in_=LN["varrow"][0:1, 0:W],
                     func=AF.Sqrt, bias=LN["cst"][0:1, 1:2])
        s.inc_last(qS, 1)
    ct.inc(qS)
    sq2 = ct.val(qS)

    @block.vector
    def _(v, sq2=sq2):
        v = EW(v)
        v.wait_ge(qS, sq2)
        v.reciprocal(LN["rstdrow"][0:1, 0:W], LN["rstdrow"][0:1, 0:W])
        v.inc_last(qV, 1)
    ct.inc(qV)
    vq2 = ct.val(qV)

    @block.tensor
    def _(t, vq2=vq2):
        t = EW(t)
        t.wait_ge(qV, vq2)
        t.matmul(psC[:, 0:W], LN["onesrow_f"][0:1, :], LN["meanrow"][0:1, 0:W],
                 start=True, stop=True)
        t.matmul(psD[:, 0:W], LN["onesrow_f"][32:33, :], LN["rstdrow"][0:1, 0:W],
                 start=True, stop=True)
        t.inc_last(qT, 1)
    ct.inc(qT)
    bcq = ct.val(qT)

    @block.vector
    def _(v, bcq=bcq):
        v = EW(v)
        v.wait_ge(qT, bcq)
        for dc in range(NDC):
            v.tensor_tensor(out=sq_scr[:, dc * W:(dc + 1) * W],
                            in0=x_sb[:, dc * NT + t0: dc * NT + t0 + W],
                            in1=psC[:, 0:W], op=ALU.subtract)
        v.drain()
        for dc in range(NDC):
            v.tensor_tensor(out=xn_out[:, dc * xn_stride:dc * xn_stride + W],
                            in0=sq_scr[:, dc * W:(dc + 1) * W],
                            in1=psD[:, 0:W], op=ALU.mult)
        v.inc_last(qV, 1)
    ct.inc(qV)


def emit_ffn(nc, block, ct, P, last, tag=""):
    NT = P["NT"]
    NBLK = NT // BLK
    x_sb = P["x_sb"]
    psH, psC, psD, psY = P["psH"], P["psC"], P["psD"], P["psY"]
    psA, psB = P["psA"], P["psB"]
    LN = P["LN"]
    qT, qV, qS = LN["qT"], LN["qV"], LN["qS"]
    qTh, qTw2 = P["qTh"], P["qTw2"]
    qSq, qSg, qVp, qVu, qVg, qPg = P["qSq"], P["qSg"], P["qVp"], P["qVu"], P["qVg"], P["qPg"]
    qSY, qGP, qST, qP = P["qSY"], P["qGP"], P["qST"], P["qP"]
    qsW1 = list(P["qSTRM"][tag][0:4])
    qsW2 = list(P["qSTRM"][tag][4:8])

    sc = contextlib.ExitStack()

    def sct(n, shp, dt):
        return sc.enter_context(nc.sbuf_tensor(n + tag, shp, dt))

    P = dict(P)
    P["w1buf"] = sct("w1buf", [128, 4 * 2304], BF16)
    P["w2buf"] = sct("w2buf", [128, 4 * 2048], BF16)
    P["xn_blk"] = sct("xn_blk", [128, NDC * BLK], BF16)
    P["lnsq"] = sct("lnsq", [128, NDC * BLK], BF16)
    P["gateB"] = sct("gateB", [128, 9 * BLK], BF16)
    P["gatesT"] = sct("gatesT", [8, BLK], F32)
    P["gsc"] = sct("gsc", [128, 48], F32)
    P["sqg"] = sct("sqg", [128, 4 * 512], BF16)
    P["pg"] = sct("pg", [128, 512], BF16)
    P["u2g"] = sct("u2g", [128, 512], F32)
    P["sgg"] = sct("sgg", [128, 512], BF16)
    P["hg0"] = sct("hg0", [128, 4 * 512], BF16)
    P["hgb"] = sct("hgb", [128, 4 * 512], BF16)
    P["outst"] = sct("outst", [128, 2048], F32)
    HS = [psH[:, 0:512], psH[:, 512:1024], psC[:, 0:512], psD[:, 0:512]]
    ENTRY_V = ct.val(qV)
    ENTRY_SY = ct.val(qSY)
    ENTRY_TW2 = ct.val(qTw2)

    @block.gpsimd
    def _(g):
        g = EW(g)
        g.wait_ge(qV, ENTRY_V)
        g.wait_ge(qSY, ENTRY_SY)
        g.wait_ge(qTw2, ENTRY_TW2)
        g.inc_last(qP, 1)
        ct.inc(qP)

    @block.vector
    def _(v):
        v = EW(v)
        v.wait_ge(qV, ENTRY_V)
        v.memset(P["gateB"][:, 8 * BLK:9 * BLK], 1.0)
        v.inc_last(qV, 1)
    ct.inc(qV)
    ENTRY_P = ct.val(qP)

    CB_Tw2 = ct.val(qTw2)
    CB_W1 = [ct.val(q) for q in qsW1]
    CB_W2 = [ct.val(q) for q in qsW2]

    @block.sync
    def _(sy):
        sy = EW(sy)
        sy.wait_ge(qP, ENTRY_P)
        if not last:
            sy.wait_ge(qGP, P["PRECAST_GP"])
        for k in range(NBLK * NM):
            s = k % 4
            m = k % NM
            if k >= 4:
                sy.wait_ge(qTw2, CB_Tw2 + k - 3)
            sy.dma_start(out=P["w1buf"][:, s * 2304:(s + 1) * 2304],
                         in_=P["W1bf"][m * 128:(m + 1) * 128, :]).then_inc(qsW1[s], 16)
            sy.dma_start(out=P["w2buf"][:, s * 2048:(s + 1) * 2048],
                         in_=P["W2bf"][m * 128:(m + 1) * 128, :]).then_inc(qsW2[s], 16)
    for _q in qsW1 + qsW2:
        ct.inc(_q, 16 * (NM // 4) * NBLK)

    for blk in range(NBLK):
        t0 = blk * BLK
        emit_ln(nc, block, ct, x_sb, P["xn_blk"], P["lnsq"], t0, BLK, NT, LN)

        for ti in range(2):
            tt0 = ti * 128

            @block.tensor
            def _(t, tt0=tt0):
                t = EW(t)
                t.wait_ge(qV, ct.val(qV))
                for dc in range(NDC):
                    t.matmul(psC[0:128, 0:8],
                             P["xn_blk"][:, dc * BLK + tt0: dc * BLK + tt0 + 128],
                             P["rW_s"][:, dc * NE:(dc + 1) * NE],
                             start=(dc == 0), stop=(dc == NDC - 1))
                t.inc_last(qT, 1)
            ct.inc(qT)
            lq = ct.val(qT)

            @block.vector
            def _(v, lq=lq):
                v = EW(v)
                v.wait_ge(qT, lq)
                v.tensor_copy(out=P["gsc"][:, 0:8], in_=psC[0:128, 0:8])
                v.drain()
                v.reduce_max(out=P["gsc"][:, 8:9], in_=P["gsc"][:, 0:8],
                             axis=AX.X, negate=True)
                v.inc_last(qV, 1)
            ct.inc(qV)
            lv = ct.val(qV)

            @block.scalar
            def _(s, lv=lv):
                s = EW(s)
                s.wait_ge(qV, lv)
                s.activation(out=P["gsc"][:, 16:24], in_=P["gsc"][:, 0:8], func=AF.Exp,
                             bias=P["gsc"][:, 8:9], accum_out=P["gsc"][:, 9:10])
                s.inc_last(qS, 1)
            ct.inc(qS)
            ls = ct.val(qS)

            @block.vector
            def _(v, ls=ls):
                v = EW(v)
                v.wait_ge(qS, ls)
                v.reciprocal(P["gsc"][:, 10:11], P["gsc"][:, 9:10])
                v.max(out=P["gsc"][:, 24:32], in_=P["gsc"][:, 16:24])
                v.drain()
                v.tensor_scalar(out=P["gsc"][:, 32:40], in0=P["gsc"][:, 16:24],
                                scalar1=P["gsc"][:, 25:26], scalar2=None,
                                op0=ALU.is_ge)
                v.drain()
                v.tensor_tensor(out=P["gsc"][:, 40:48], in0=P["gsc"][:, 16:24],
                                in1=P["gsc"][:, 32:40], op=ALU.mult)
                v.drain()
                v.tensor_scalar(out=P["gsc"][:, 0:8], in0=P["gsc"][:, 40:48],
                                scalar1=P["gsc"][:, 10:11], scalar2=None,
                                op0=ALU.mult)
                v.inc_last(qV, 1)
            ct.inc(qV)
            gv = ct.val(qV)

            @block.tensor
            def _(t, gv=gv):
                t = EW(t)
                t.wait_ge(qV, gv)
                t.transpose(psC[0:8, 256:384], P["gsc"][:, 0:8], P["idf_s"][:, :])
                t.inc_last(qT, 1)
            ct.inc(qT)
            gq = ct.val(qT)

            @block.vector
            def _(v, tt0=tt0, gq=gq):
                v = EW(v)
                v.wait_ge(qT, gq)
                v.tensor_copy(out=P["gatesT"][:, tt0:tt0 + 128], in_=psC[0:8, 256:384])
                v.inc_last(qV, 1)
            ct.inc(qV)
        GATES_V = ct.val(qV)

        for e in range(8):
            @block.tensor
            def _(t, e=e, w=ct.val(qV)):
                t = EW(t)
                t.wait_ge(qV, w)
                t.matmul(psC[:, 0:256], P["selE_s"][:, e * 128:(e + 1) * 128],
                         P["gatesT"][:, :], start=True, stop=True)
                t.inc_last(qT, 1)
            ct.inc(qT)
            eq = ct.val(qT)

            @block.vector
            def _(v, e=e, eq=eq):
                v = EW(v)
                v.wait_ge(qT, eq)
                v.tensor_copy(out=P["gateB"][:, e * BLK:(e + 1) * BLK],
                              in_=psC[:, 0:256])
                v.inc_last(qV, 1)
            ct.inc(qV)
        GB_V = ct.val(qV)

        @block.tensor
        def _(t):
            t = EW(t)
            t.wait_ge(qV, GB_V)
            for dcp in range(4):
                for side in range(2):
                    dc = 2 * dcp + side
                    t.matmul(psY[:, dcp * 512 + side * 256: dcp * 512 + side * 256 + 256],
                             P["b2_s"][:, dc * 128:(dc + 1) * 128], P["gatesT"][:, :],
                             start=(side == 0), stop=False, skip_group_check=True)
            t.inc_last(qT, 1)
        ct.inc(qT)

        bqW1 = [CB_W1[s] + blk * 16 * (NM // 4) for s in range(4)]
        bqW2 = [CB_W2[s] + blk * 16 * (NM // 4) for s in range(4)]
        base_Th = ct.val(qTh)
        base_Tw2 = ct.val(qTw2)
        base_Sq = ct.val(qSq)
        base_Sg = ct.val(qSg)
        base_Vp = ct.val(qVp)
        base_Vu = ct.val(qVu)
        base_Vg = ct.val(qVg)
        base_Pg = ct.val(qPg)
        T_ENTRY = ct.val(qT)
        V_ENTRY = ct.val(qV)


        bq = bqW1[0]
        assert all(b == bq for b in bqW1 + bqW2)

        def t_w1(t, s):
            h = HS[s]
            for fsub in range(2):
                for dc in range(NDC):
                    t.matmul(h[:, fsub * 256: fsub * 256 + 256],
                             P["w1buf"][:, s * 2304 + dc * 256 + fsub * 128:
                                        s * 2304 + dc * 256 + fsub * 128 + 128],
                             P["xn_blk"][:, dc * BLK:(dc + 1) * BLK],
                             start=(dc == 0), stop=False)
                t.matmul(h[:, fsub * 256: fsub * 256 + 256],
                         P["w1buf"][0:1, s * 2304 + 2048 + fsub * 128:
                                    s * 2304 + 2048 + fsub * 128 + 128],
                         P["onesrow_b"][0:1, 0:256], start=False, stop=True)
            t.inc_last(qTh, 1)

        def t_w2(t, s):
            for fr in range(2):
                for dc in range(NDC):
                    t.matmul(psY[:, (dc // 2) * 512 + (dc % 2) * 256:
                                  (dc // 2) * 512 + (dc % 2) * 256 + 256],
                             P["w2buf"][:, s * 2048 + fr * 1024 + dc * 128:
                                        s * 2048 + fr * 1024 + dc * 128 + 128],
                             P["hgb"][:, s * 512 + fr * 256: s * 512 + fr * 256 + 256],
                             start=False, stop=False, skip_group_check=True)
            t.inc_last(qTw2, 1)

        @block.tensor
        def _(t):
            t = EW(t)
            for j in range(4):
                t.wait_ge(qsW1[j], bq + 16)
                if base_Vg + j - 3 > 0:
                    t.wait_ge(qVg, base_Vg + j - 3)
                t_w1(t, j)
                if j >= 1:
                    t.wait_ge(qPg, base_Pg + j)
                    t.wait_ge(qsW2[j - 1], bq + 16)
                    t_w2(t, j - 1)
            with kreg(t) as rQ, kreg(t) as rQm, kreg(t) as rVg, kreg(t) as rPg:
                t.reg_mov(rQ, bq + 32)
                t.reg_mov(rQm, bq + 16)
                t.reg_mov(rVg, base_Vg + 1)
                t.reg_mov(rPg, base_Pg + 4)
                with kfori(nc, t, mybir.EngineType.PE, 1, NM // 4, 1):
                    for j in range(4):
                        t.wait_ge(qsW1[j], rQ)
                        t.wait_ge(qVg, rVg)
                        t_w1(t, j)
                        t.wait_ge(qPg, rPg)
                        t.wait_ge(qsW2[(j + 3) % 4], rQm if j == 0 else rQ)
                        t_w2(t, (j + 3) % 4)
                        t.reg_add(rVg, rVg, 1)
                        t.reg_add(rPg, rPg, 1)
                    t.reg_add(rQ, rQ, 16)
                    t.reg_add(rQm, rQm, 16)
            t.wait_ge(qPg, base_Pg + NM)
            t.wait_ge(qsW2[3], bq + 16 * (NM // 4))
            t_w2(t, 3)
        ct.inc(qTh, NM)
        ct.inc(qTw2, NM)

        @block.scalar
        def _(s):
            s = EW(s)
            with kreg(s) as rTh, kreg(s) as rVu, kreg(s) as rVp:
                s.reg_mov(rTh, base_Th + 1)
                s.reg_mov(rVu, base_Vu + 1)
                s.reg_mov(rVp, max(base_Vp - 3, 0))
                with kfori(nc, s, mybir.EngineType.Activation, 0, NM // 4, 1):
                    for j in range(4):
                        s.wait_ge(qTh, rTh)
                        s.wait_ge(qVp, rVp)
                        s.activation(out=P["sqg"][:, j * 512:(j + 1) * 512],
                                     in_=HS[j][:, 0:512],
                                     func=AF.Square).then_inc(qSq, 1)
                        s.wait_ge(qVu, rVu)
                        s.activation(out=P["sgg"][:, :], in_=P["u2g"][:, :],
                                     func=AF.Sigmoid).then_inc(qSg, 1)
                        s.reg_add(rTh, rTh, 1)
                        s.reg_add(rVu, rVu, 1)
                        s.reg_add(rVp, rVp, 1)
        ct.inc(qSq, NM)
        ct.inc(qSg, NM)

        @block.vector
        def _(v):
            v = EW(v)
            v.wait_ge(qV, GB_V)
            with kreg(v) as rSq, kreg(v) as rSg, kreg(v) as rT2:
                v.reg_mov(rSq, base_Sq + 1)
                v.reg_mov(rSg, base_Sg + 1)
                v.reg_mov(rT2, max(base_Tw2 - 3, 0))
                for e in range(9):
                    goff = e * BLK
                    with kfori(nc, v, mybir.EngineType.DVE, e * 4, e * 4 + 4, 1):
                        for j in range(4):
                            v.wait_ge(qSq, rSq)
                            v.tensor_scalar(out=P["pg"][:, :],
                                            in0=P["sqg"][:, j * 512:(j + 1) * 512],
                                            scalar1=GC2, scalar2=GC1, op0=ALU.mult,
                                            op1=ALU.add).then_inc(qVp, 1)
                            v.drain()
                            v.tensor_tensor(out=P["u2g"][:, :],
                                            in0=HS[j][:, 0:512],
                                            in1=P["pg"][:, :],
                                            op=ALU.mult).then_inc(qVu, 1)
                            v.wait_ge(qSg, rSg)
                            v.tensor_tensor(out=P["hg0"][:, j * 512:(j + 1) * 512],
                                            in0=HS[j][:, 0:512],
                                            in1=P["sgg"][:, :],
                                            op=ALU.mult).then_inc(qVg, 1)
                            v.wait_ge(qTw2, rT2)
                            v.drain()
                            v.tensor_tensor(out=P["hgb"][:, j * 512: j * 512 + 256],
                                            in0=P["hg0"][:, j * 512: j * 512 + 256],
                                            in1=P["gateB"][:, goff:goff + 256],
                                            op=ALU.mult)
                            v.tensor_tensor(out=P["hgb"][:, j * 512 + 256: j * 512 + 512],
                                            in0=P["hg0"][:, j * 512 + 256: j * 512 + 512],
                                            in1=P["gateB"][:, goff:goff + 256],
                                            op=ALU.mult).then_inc(qPg, 1)
                            v.reg_add(rSq, rSq, 1)
                            v.reg_add(rSg, rSg, 1)
                            v.reg_add(rT2, rT2, 1)
        ct.inc(qVp, NM)
        ct.inc(qVu, NM)
        ct.inc(qVg, NM)
        ct.inc(qPg, NM)

        @block.vector
        def _(v, blk=blk, t0=t0, wst=ct.val(qST)):
            v = EW(v)
            v.wait_ge(qTw2, base_Tw2 + NM)
            if last:
                v.wait_ge(qST, wst)
            for dc in range(NDC):
                ysl = psY[:, (dc // 2) * 512 + (dc % 2) * 256:
                          (dc // 2) * 512 + (dc % 2) * 256 + 256]
                v.tensor_scalar(out=P["outst"][:, dc * 256:(dc + 1) * 256],
                                in0=ysl, scalar1=P["bs2T_s"][:, dc:dc + 1],
                                scalar2=None, op0=ALU.add)
            v.inc_last(qV, 1)
        ct.inc(qV)
        ev1 = ct.val(qV)

        @block.vector
        def _(v, blk=blk, t0=t0, ev1=ev1):
            v = EW(v)
            v.wait_ge(qV, ev1)
            for dc in range(NDC):
                v.scalar_tensor_tensor(out=P["outst"][:, dc * 256:(dc + 1) * 256],
                                       in0=P["outst"][:, dc * 256:(dc + 1) * 256],
                                       scalar=SCALE_FFN,
                                       in1=x_sb[:, dc * NT + t0: dc * NT + t0 + 256],
                                       op0=ALU.mult, op1=ALU.add)
            v.inc_last(qV, 1)
        ct.inc(qV)
        ev2 = ct.val(qV)

        if last:
            @block.gpsimd
            def _(g, blk=blk, t0=t0, ev2=ev2):
                g = EW(g)
                g.wait_ge(qV, ev2)
                g.dma_start(out=bass.AP(P["y_out"].tensor, t0,
                                        [[NT, 128], [128 * NT, NDC], [1, 256]]),
                            in_=P["outst"][:, :].rearrange("p (dc t) -> p dc t", dc=NDC)).then_inc(qST, 16)
            ct.inc(qST, 16)
        else:
            @block.vector
            def _(v, blk=blk, t0=t0, ev2=ev2):
                v = EW(v)
                v.wait_ge(qV, ev2)
                for dc in range(NDC):
                    v.tensor_copy(out=x_sb[:, dc * NT + t0: dc * NT + t0 + 256],
                                  in_=P["outst"][:, dc * 256:(dc + 1) * 256])
                v.inc_last(qV, 1)
            ct.inc(qV)

    sc.close()


def emit_attn_b(nc, block, ct, P):
    NT = P["NT"]
    NTIL = NT // 128
    x_sb = P["x_sb"]
    psH, psC, psD = P["psH"], P["psC"], P["psD"]
    psA, psB = P["psA"], P["psB"]
    LN = P["LN"]
    qT, qV, qS = LN["qT"], LN["qV"], LN["qS"]
    qGP, qSY, qTw2 = P["qGP"], P["qSY"], P["qTw2"]
    SCL = 1.0 / 8.0

    sc = contextlib.ExitStack()

    def sct(n, shp, dt):
        return sc.enter_context(nc.sbuf_tensor(n, shp, dt))

    P = dict(P)
    P["lnsq"] = sct("ab_lnsq", [128, NDC * 128], BF16)
    P["xnb"] = sct("ab_xnb", [128, NDC * 128], BF16)
    P["bQT"] = sct("ab_bQT", [128, NDC * 128], BF16)
    P["bKTr"] = sct("ab_bKTr", [128, 2 * NDC * 128], BF16)
    P["bVtr"] = sct("ab_bVtr", [128, 2 * NDC * 128], BF16)
    P["bexp"] = sct("ab_bexp", [128, 2 * 256], BF16)
    P["bexn"] = sct("ab_bexn", [128, 2 * 256], BF16)
    P["bctx"] = sct("ab_bctx", [128, NDC * 128], BF16)
    P["brs"] = sct("ab_brs", [1, 2 * 128], F32)
    P["bmsk"] = sct("ab_bmsk", [128, 2 * 256], F32)
    P["outst"] = sct("ab_outst", [128, NDC * 128], F32)
    P["bWq_s"] = sct("ab_bWq", [128, NDC * D], BF16)
    P["bWk_s"] = sct("ab_bWk", [128, NDC * D], BF16)
    P["bWv_s"] = sct("ab_bWv", [128, NDC * D], BF16)
    P["bWo_s"] = sct("ab_bWo", [128, NDC * D], BF16)
    ENTRY_V = ct.val(qV)
    Hmkv, Hexq, Hsmq, Hrsq, Hrbq, Henv, Hcxq = {}, {}, {}, {}, {}, {}, {}

    @block.gpsimd
    def _(g):
        g = EW(g)
        g.wait_ge(qV, ENTRY_V)
        g.wait_ge(qSY, ct.val(qSY))
        g.wait_ge(qTw2, ct.val(qTw2))
        g.wait_ge(qGP, ct.val(qGP))
        for dst, src in ((P["bWq_s"], P["bWq"]), (P["bWk_s"], P["bWk"]),
                         (P["bWv_s"], P["bWv"]), (P["bWo_s"], P["bWo"])):
            g.dma_start(out=dst[:, :].rearrange("p (c o) -> p c o", c=NDC),
                        in_=src.rearrange("(c p) o -> p c o", p=128)).then_inc(qGP, 16)
            ct.inc(qGP, 16)
    BW_GP = ct.val(qGP)

    for ti in range(NTIL):
        t0 = ti * 128
        slot = ti % 2
        pre = [(qGP, BW_GP)] if ti == 0 else None
        emit_ln(nc, block, ct, x_sb, P["xnb"], P["lnsq"], t0, 128, NT, LN,
                pre_waits=pre)
        XN_V = ct.val(qV)

        @block.tensor
        def _(t, ti=ti):
            t = EW(t)
            t.wait_ge(qV, XN_V)
            for oc in range(NDC):
                for dc in range(NDC):
                    t.matmul(psH[:, oc * 128:(oc + 1) * 128],
                             P["bWq_s"][:, dc * D + oc * 128: dc * D + oc * 128 + 128],
                             P["xnb"][:, dc * 128:(dc + 1) * 128],
                             start=(dc == 0), stop=(dc == NDC - 1))
            t.inc_last(qT, 1)
        ct.inc(qT)
        q1 = ct.val(qT)

        @block.vector
        def _(v, q1=q1):
            v = EW(v)
            v.wait_ge(qT, q1)
            for oc in range(NDC):
                v.tensor_copy(out=P["bQT"][:, oc * 128:(oc + 1) * 128],
                              in_=psH[:, oc * 128:(oc + 1) * 128])
            v.inc_last(qV, 1)
        ct.inc(qV)
        q1v = ct.val(qV)

        @block.tensor
        def _(t, q1v=q1v):
            t = EW(t)
            t.wait_ge(qV, q1v)
            for oc in range(NDC):
                for dc in range(NDC):
                    t.matmul(psH[:, oc * 128:(oc + 1) * 128],
                             P["bWk_s"][:, dc * D + oc * 128: dc * D + oc * 128 + 128],
                             P["xnb"][:, dc * 128:(dc + 1) * 128],
                             start=(dc == 0), stop=(dc == NDC - 1))
            t.inc_last(qT, 1)
        ct.inc(qT)
        q2 = ct.val(qT)

        @block.vector
        def _(v, slot=slot, q2=q2):
            v = EW(v)
            v.wait_ge(qT, q2)
            for oc in range(NDC):
                v.tensor_copy(out=P["bKTr"][:, slot * 1024 + oc * 128:
                                            slot * 1024 + (oc + 1) * 128],
                              in_=psH[:, oc * 128:(oc + 1) * 128])
            v.inc_last(qV, 1)
        ct.inc(qV)
        q2v = ct.val(qV)

        @block.tensor
        def _(t, q2v=q2v):
            t = EW(t)
            t.wait_ge(qV, q2v)
            for nh in range(2):
                for dc in range(NDC):
                    t.matmul(psH[:, nh * 512:(nh + 1) * 512],
                             P["xnb"][:, dc * 128:(dc + 1) * 128],
                             P["bWv_s"][:, dc * D + nh * 512: dc * D + nh * 512 + 512],
                             start=(dc == 0), stop=(dc == NDC - 1))
            t.inc_last(qT, 1)
        ct.inc(qT)
        q3 = ct.val(qT)

        @block.vector
        def _(v, slot=slot, q3=q3):
            v = EW(v)
            v.wait_ge(qT, q3)
            v.tensor_copy(out=P["bVtr"][:, slot * 1024: slot * 1024 + 512],
                          in_=psH[:, 0:512])
            v.tensor_copy(out=P["bVtr"][:, slot * 1024 + 512: slot * 1024 + 1024],
                          in_=psH[:, 512:1024])
            v.inc_last(qV, 1)
        ct.inc(qV)
        QKV_V = ct.val(qV)

        for h in range(16):
            hg = ti * 16 + h
            hs = hg % 2
            oc = h // 2
            pb = (h % 2) * 64
            stiles = ([((ti - 1) % 2, P["maskA_s"])] if ti > 0 else []) + \
                     [(slot, P["maskB_s"])]
            nsi = len(stiles)
            sc0 = hs * 256
            eb0 = hs * 256
            sm0 = hs * 128
            rb0 = hs * 128
            cx0 = 256 + hs * 128

            @block.tensor
            def _(t, hg=hg, oc=oc, pb=pb, stiles=stiles, sc0=sc0, QKV_V=QKV_V):
                t = EW(t)
                t.wait_ge(qV, max(QKV_V, Hmkv.get(hg - 2, 0)))
                for si, (ss, mk) in enumerate(stiles):
                    t.matmul(psC[:, sc0 + si * 128: sc0 + (si + 1) * 128],
                             P["bKTr"][pb:pb + 64, ss * 1024 + oc * 128:
                                       ss * 1024 + (oc + 1) * 128],
                             P["bQT"][pb:pb + 64, oc * 128:(oc + 1) * 128],
                             start=True, stop=True)
                t.inc_last(qT, 1)
            ct.inc(qT)
            scq = ct.val(qT)

            @block.vector
            def _(v, hg=hg, stiles=stiles, scq=scq, sc0=sc0, eb0=eb0):
                v = EW(v)
                v.wait_ge(qT, scq)
                if hg >= 2:
                    v.wait_ge(qS, Hexq[hg - 2])
                for si, (ss, mk) in enumerate(stiles):
                    v.scalar_tensor_tensor(
                        out=P["bmsk"][:, eb0 + si * 128: eb0 + (si + 1) * 128],
                        in0=psC[:, sc0 + si * 128: sc0 + (si + 1) * 128],
                        scalar=SCL, in1=mk[:, :], op0=ALU.mult, op1=ALU.add)
                v.inc_last(qV, 1)
            ct.inc(qV)
            Hmkv[hg] = ct.val(qV)

            @block.scalar
            def _(s, hg=hg, nsi=nsi, eb0=eb0):
                s = EW(s)
                s.wait_ge(qV, Hmkv[hg])
                s.activation(out=P["bexp"][:, eb0: eb0 + nsi * 128],
                             in_=P["bmsk"][:, eb0: eb0 + nsi * 128],
                             func=AF.Exp)
                s.inc_last(qS, 1)
            ct.inc(qS)
            Hexq[hg] = ct.val(qS)

            @block.tensor
            def _(t, hg=hg, nsi=nsi, eb0=eb0, sm0=sm0):
                t = EW(t)
                t.wait_ge(qS, Hexq[hg])
                if hg >= 2:
                    t.wait_ge(qV, Hrsq[hg - 2])
                for si in range(nsi):
                    t.matmul(psH[0:1, sm0:sm0 + 128], P["onescol_b"][:, :],
                             P["bexp"][:, eb0 + si * 128: eb0 + (si + 1) * 128],
                             start=(si == 0), stop=(si == nsi - 1))
                t.inc_last(qT, 1)
            ct.inc(qT)
            Hsmq[hg] = ct.val(qT)

            @block.vector
            def _(v, hg=hg, hs=hs, sm0=sm0):
                v = EW(v)
                v.wait_ge(qT, Hsmq[hg])
                v.tensor_copy(out=P["brs"][0:1, hs * 128:(hs + 1) * 128],
                              in_=psH[0:1, sm0:sm0 + 128])
                v.drain()
                v.reciprocal(P["brs"][0:1, hs * 128:(hs + 1) * 128],
                             P["brs"][0:1, hs * 128:(hs + 1) * 128])
                v.inc_last(qV, 1)
            ct.inc(qV)
            Hrsq[hg] = ct.val(qV)

            @block.tensor
            def _(t, hg=hg, hs=hs, rb0=rb0):
                t = EW(t)
                t.wait_ge(qV, Hrsq[hg])
                t.matmul(psD[:, rb0:rb0 + 128], P["onesrow_f"][0:1, :],
                         P["brs"][0:1, hs * 128:(hs + 1) * 128],
                         start=True, stop=True)
                t.inc_last(qT, 1)
            ct.inc(qT)
            Hrbq[hg] = ct.val(qT)

            @block.vector
            def _(v, hg=hg, nsi=nsi, eb0=eb0, rb0=rb0):
                v = EW(v)
                v.wait_ge(qT, Hrbq[hg])
                for si in range(nsi):
                    v.tensor_tensor(
                        out=P["bexn"][:, eb0 + si * 128: eb0 + (si + 1) * 128],
                        in0=P["bexp"][:, eb0 + si * 128: eb0 + (si + 1) * 128],
                        in1=psD[:, rb0:rb0 + 128], op=ALU.mult)
                v.inc_last(qV, 1)
            ct.inc(qV)
            Henv[hg] = ct.val(qV)

            @block.tensor
            def _(t, hg=hg, oc=oc, pb=pb, stiles=stiles, eb0=eb0, cx0=cx0):
                t = EW(t)
                t.wait_ge(qV, Henv[hg])
                for si, (ss, mk) in enumerate(stiles):
                    t.matmul(psD[0:64, cx0:cx0 + 128],
                             P["bVtr"][:, ss * 1024 + oc * 128 + pb:
                                       ss * 1024 + oc * 128 + pb + 64],
                             P["bexn"][:, eb0 + si * 128: eb0 + (si + 1) * 128],
                             start=(si == 0), stop=(si == len(stiles) - 1))
                t.inc_last(qT, 1)
            ct.inc(qT)
            Hcxq[hg] = ct.val(qT)

            @block.vector
            def _(v, hg=hg, oc=oc, pb=pb, cx0=cx0):
                v = EW(v)
                v.wait_ge(qT, Hcxq[hg])
                v.tensor_copy(out=P["bctx"][pb:pb + 64, oc * 128:(oc + 1) * 128],
                              in_=psD[0:64, cx0:cx0 + 128])
                v.inc_last(qV, 1)
            ct.inc(qV)

        CTX_V = ct.val(qV)
        for dc in range(NDC):
            @block.tensor
            def _(t, dc=dc, t0=t0, w=None):
                t = EW(t)
                t.wait_ge(qV, ct.val(qV))
                for oc in range(NDC):
                    t.matmul(psA[:, 0:128],
                             P["bWo_s"][:, oc * D + dc * 128: oc * D + dc * 128 + 128],
                             P["bctx"][:, oc * 128:(oc + 1) * 128],
                             start=(oc == 0), stop=(oc == NDC - 1))
                t.inc_last(qT, 1)
            ct.inc(qT)
            opq = ct.val(qT)

            @block.vector
            def _(v, dc=dc, t0=t0, opq=opq):
                v = EW(v)
                v.wait_ge(qT, opq)
                v.scalar_tensor_tensor(out=P["outst"][:, dc * 128:(dc + 1) * 128],
                                       in0=psA[:, 0:128], scalar=1.0,
                                       in1=x_sb[:, dc * NT + t0: dc * NT + t0 + 128],
                                       op0=ALU.mult, op1=ALU.add)
                v.inc_last(qV, 1)
            ct.inc(qV)
        OP_V = ct.val(qV)

        @block.vector
        def _(v, t0=t0, OP_V=OP_V):
            v = EW(v)
            v.wait_ge(qV, OP_V)
            for dc in range(NDC):
                v.tensor_copy(out=x_sb[:, dc * NT + t0: dc * NT + t0 + 128],
                              in_=P["outst"][:, dc * 128:(dc + 1) * 128])
            v.inc_last(qV, 1)
        ct.inc(qV)

    sc.close()


def _prep_inputs(inputs, b, T):
    f32 = np.float32
    x = np.ascontiguousarray(inputs["L0_attn_a_x"][b].T.astype(f32))
    d = {
        "x": x,
        "aWq": np.ascontiguousarray(inputs["L0_attn_a_Wq"].T),
        "aWk": np.ascontiguousarray(inputs["L0_attn_a_Wk"].T),
        "aWv": np.ascontiguousarray(inputs["L0_attn_a_Wv"].T),
        "aWo": np.ascontiguousarray(inputs["L0_attn_a_Wo"].T),
        "abq": inputs["L0_attn_a_bq"].reshape(H, 1),
        "abk": inputs["L0_attn_a_bk"].reshape(H, 1),
        "abv": inputs["L0_attn_a_bv"].reshape(1, H),
        "abo": np.ascontiguousarray(inputs["L0_attn_a_bo"].reshape(NDC, 128).T),
        "bWq": inputs["L0_attn_b_Wq"],
        "bWk": inputs["L0_attn_b_Wk"],
        "bWv": inputs["L0_attn_b_Wv"],
        "bWo": inputs["L0_attn_b_Wo"],
        "rW": np.ascontiguousarray(inputs["L0_ffn_w_router"].T),
        "W1T": np.ascontiguousarray(inputs["L0_ffn_W1"].transpose(0, 2, 1)),
        "W2T": np.ascontiguousarray(inputs["L0_ffn_W2"].transpose(0, 2, 1)),
        "Ws1T": np.ascontiguousarray(inputs["L0_ffn_Ws1"].T),
        "Ws2T": np.ascontiguousarray(inputs["L0_ffn_Ws2"].T),
        "b1fl": np.concatenate([np.asarray(inputs["L0_ffn_b1"]).reshape(-1),
                                np.asarray(inputs["L0_ffn_bs1"]).reshape(-1)]).reshape(1, 9 * DFF),
        "b2": inputs["L0_ffn_b2"],
        "bs2T": np.ascontiguousarray(inputs["L0_ffn_bs2"].reshape(NDC, 128).T),
    }
    return {k: np.ascontiguousarray(v).astype(f32) for k, v in d.items()}


_NC_CACHE = {}


def kernel(**inputs):
    T = inputs["L0_attn_a_x"].shape[1]
    B = inputs["L0_attn_a_x"].shape[0]
    if T not in _NC_CACHE:
        _NC_CACHE[T] = build_nc(T)
    nc = _NC_CACHE[T]
    in_maps = [_prep_inputs(inputs, b, T) for b in range(B)]
    res = run_bass_kernel_spmd(nc, in_maps, core_ids=list(range(B)))
    out = np.stack([np.ascontiguousarray(r["y"].T) for r in res.results], axis=0)
    return out.astype(np.float32)


if __name__ == "__main__":
    import reference as _r
    ins = _r.setup_inputs(0)
    ins = {k: np.asarray(v) for k, v in ins.items()}
    out = kernel(**ins)
    exp = np.asarray(_r.reference(**ins))
    err = np.linalg.norm(out - exp) / np.linalg.norm(exp)
    print("rel err:", err)


# version 16
# speedup vs baseline: 1.2939x; 1.0403x over previous
import math
import contextlib
import numpy as np
import ml_dtypes

import concourse.bass as bass
import concourse.mybir as mybir
from concourse.bass_utils import run_bass_kernel_spmd

F32 = mybir.dt.float32
BF16 = mybir.dt.bfloat16
AF = mybir.ActivationFunctionType
ALU = mybir.AluOpType
AX = mybir.AxisListType
ts = bass.ts

D = 1024
NDC = 8
H = 128
NE = 8
DFF = 4096
FC = 256
NM = 144
BLK = 256
EPS_LN = 1e-5
EPS_U = 1e-6
SCALE_FFN = 1.0 / math.sqrt(1.0 + 2.0 / 8.0)
GC1 = 2.0 * math.sqrt(2.0 / math.pi)
GC2 = 2.0 * math.sqrt(2.0 / math.pi) * 0.044715
NEGBIG = -1e30
SM_SHIFT = -10.0


class Ctr:
    def __init__(self):
        self.c = {}

    @staticmethod
    def _k(sem):
        return getattr(sem, "name", None) or id(sem)

    def inc(self, sem, n=1):
        k = self._k(sem)
        self.c[k] = self.c.get(k, 0) + n
        return self.c[k]

    def val(self, sem):
        return self.c.get(self._k(sem), 0)




import bass_rust as _br

_KF_UID = [0]


@contextlib.contextmanager
def kreg(eng, base=0):
    _KF_UID[0] += 1
    r = eng.alloc_register(f"kr{_KF_UID[0]}")
    try:
        yield r
    finally:
        eng.free_register(r)


@contextlib.contextmanager
def kfori(nc, eng, etype, start, end, step=1):
    _KF_UID[0] += 1
    name = f"kf{_KF_UID[0]}"
    reg = eng.alloc_register(f"{name}_i")
    rh = _br.RegisterHandles((reg,))
    eng.reg_mov(reg, start)
    nc.br(f"{name}_L", engines=[etype])
    with nc.body(f"{name}_L", valid_engines=[etype]):
        yield reg
        eng.reg_add(reg, reg, step)
        nc.br_lt(rh, end, on_true=f"{name}_L", on_false=f"{name}_E", engines=[etype])
    nc.switch_bb(f"{name}_E")
    eng.free_register(reg)


_RV_CACHE = {}


def rowslice(reg, nrows, maxrow):
    key = (reg.name, nrows, maxrow)
    if key not in _RV_CACHE:
        rv = _br.RuntimeValue(_br.RegisterHandles((reg,)), 0, maxrow, nrows, 0)
        _RV_CACHE[key] = _br.DynSlice(rv, nrows)
    return _RV_CACHE[key]




class EW:

    def __init__(self, eng):
        object.__setattr__(self, "_e", eng)
        object.__setattr__(self, "last", None)

    def __getattr__(self, n):
        a = getattr(self._e, n)
        if callable(a):
            def w(*args, **kw):
                r = a(*args, **kw)
                if hasattr(r, "then_inc"):
                    object.__setattr__(self, "last", r)
                return r
            return w
        return a

    def inc_last(self, q, n=1):
        self.last.then_inc(q, n)


def build_nc(T=4096):
    NT = T
    NG = T // 512
    NBLK = T // BLK
    NTIL = T // 128
    NCH = T // 128

    nc = bass.Bass("TRN2", target_bir_lowering=False, debug=False)

    x_in = nc.dram_tensor("x", [D, NT], F32, kind="ExternalInput").ap()
    y_out = nc.dram_tensor("y", [D, NT], F32, kind="ExternalOutput").ap()

    aWq = nc.dram_tensor("aWq", [D, H], F32, kind="ExternalInput").ap()
    aWk = nc.dram_tensor("aWk", [D, H], F32, kind="ExternalInput").ap()
    aWv = nc.dram_tensor("aWv", [D, H], F32, kind="ExternalInput").ap()
    aWo = nc.dram_tensor("aWo", [H, D], F32, kind="ExternalInput").ap()
    abq = nc.dram_tensor("abq", [H, 1], F32, kind="ExternalInput").ap()
    abk = nc.dram_tensor("abk", [H, 1], F32, kind="ExternalInput").ap()
    abv = nc.dram_tensor("abv", [1, H], F32, kind="ExternalInput").ap()
    abo = nc.dram_tensor("abo", [128, NDC], F32, kind="ExternalInput").ap()

    bWq = nc.dram_tensor("bWq", [D, D], F32, kind="ExternalInput").ap()
    bWk = nc.dram_tensor("bWk", [D, D], F32, kind="ExternalInput").ap()
    bWv = nc.dram_tensor("bWv", [D, D], F32, kind="ExternalInput").ap()
    bWo = nc.dram_tensor("bWo", [D, D], F32, kind="ExternalInput").ap()

    rW = nc.dram_tensor("rW", [D, NE], F32, kind="ExternalInput").ap()
    W1T = nc.dram_tensor("W1T", [NE, D, DFF], F32, kind="ExternalInput").ap()
    W2T = nc.dram_tensor("W2T", [NE, DFF, D], F32, kind="ExternalInput").ap()
    Ws1T = nc.dram_tensor("Ws1T", [D, DFF], F32, kind="ExternalInput").ap()
    Ws2T = nc.dram_tensor("Ws2T", [DFF, D], F32, kind="ExternalInput").ap()
    b1fl = nc.dram_tensor("b1fl", [1, 9 * DFF], F32, kind="ExternalInput").ap()
    b2 = nc.dram_tensor("b2", [NE, D], F32, kind="ExternalInput").ap()
    bs2T = nc.dram_tensor("bs2T", [128, NDC], F32, kind="ExternalInput").ap()

    c_idf = nc.inline_tensor(np.eye(128, dtype=np.float32), "c_idf").ap()
    c_idb = nc.inline_tensor(np.eye(128).astype(ml_dtypes.bfloat16), "c_idb").ap()
    c_onescol_b = nc.inline_tensor(np.ones((128, 1), ml_dtypes.bfloat16), "c_onescol_b").ap()
    c_onescol_f = nc.inline_tensor(np.ones((128, 1), np.float32), "c_onescol_f").ap()
    c_onesrow_b = nc.inline_tensor(np.ones((128, 512), ml_dtypes.bfloat16), "c_onesrow_b").ap()
    c_onesrow_f = nc.inline_tensor(np.ones((128, 128), np.float32), "c_onesrow_f").ap()
    tri = (np.arange(128)[:, None] <= np.arange(128)[None, :]).astype(np.float32)
    c_triu = nc.inline_tensor(tri, "c_triu").ap()
    s_i = np.arange(128)[:, None]
    t_i = np.arange(128)[None, :]
    mB = np.where((s_i <= t_i) & (s_i >= t_i - 63), SM_SHIFT, NEGBIG).astype(np.float32)
    mA = np.where(s_i >= t_i + 65, SM_SHIFT, NEGBIG).astype(np.float32)
    c_maskB = nc.inline_tensor(mB, "c_maskB").ap()
    c_maskA = nc.inline_tensor(mA, "c_maskA").ap()
    sel = np.zeros((8, 8 * 128), np.float32)
    for e in range(8):
        sel[e, e * 128:(e + 1) * 128] = 1.0
    c_selE = nc.inline_tensor(sel, "c_selE").ap()

    W1bf = nc.dram_tensor("W1bf", [NM * 128, 2304], BF16, kind="Internal").ap()
    W2bf = nc.dram_tensor("W2bf", [NM * 128, 2048], BF16, kind="Internal").ap()

    es = contextlib.ExitStack()

    x_sb = nc.alloc_sbuf_tensor("x_sb", [128, NDC * NT], BF16)
    rW_s = nc.alloc_sbuf_tensor("rW_s", [128, NDC * NE], BF16)
    b2_s = nc.alloc_sbuf_tensor("b2_s", [8, D], F32)
    bs2T_s = nc.alloc_sbuf_tensor("bs2T_s", [128, NDC], F32)
    idf_s = nc.alloc_sbuf_tensor("idf_s", [128, 128], F32)
    idb_s = nc.alloc_sbuf_tensor("idb_s", [128, 128], BF16)
    onescol_b = nc.alloc_sbuf_tensor("onescol_b", [128, 1], BF16)
    onescol_f = nc.alloc_sbuf_tensor("onescol_f", [128, 1], F32)
    onesrow_b = nc.alloc_sbuf_tensor("onesrow_b", [128, 512], BF16)
    onesrow_f = nc.alloc_sbuf_tensor("onesrow_f", [128, 128], F32)
    triu_s = nc.alloc_sbuf_tensor("triu_s", [128, 128], F32)
    maskB_s = nc.alloc_sbuf_tensor("maskB_s", [128, 128], F32)
    maskA_s = nc.alloc_sbuf_tensor("maskA_s", [128, 128], F32)
    selE_s = nc.alloc_sbuf_tensor("selE_s", [8, 8 * 128], F32)
    statrows = nc.alloc_sbuf_tensor("statrows", [128, 1024], F32)
    meanrow = statrows[0:1, 0:512]
    varrow = statrows[0:1, 512:1024]
    rstdrow = statrows[32:33, 0:512]

    psH = nc.alloc_psum_tensor("psH", [128, 1024], F32)
    psC = nc.alloc_psum_tensor("psC", [128, 512], F32)
    psD = nc.alloc_psum_tensor("psD", [128, 512], F32)
    psY = nc.alloc_psum_tensor("psY", [128, 2048], F32)
    psA = psH[:, 0:512]
    psB = psH[:, 512:1024]

    qGP = es.enter_context(nc.semaphore("qGP"))
    qSTRM = {tg: tuple(es.enter_context(nc.semaphore(f"q{w}s{sl}{tg}"))
                       for w in ("W1", "W2") for sl in range(4))
             for tg in ("A", "B")}
    qSY = es.enter_context(nc.semaphore("qSY"))
    qT = es.enter_context(nc.semaphore("qT"))
    qV = es.enter_context(nc.semaphore("qV"))
    qS = es.enter_context(nc.semaphore("qS"))
    qP = es.enter_context(nc.semaphore("qP"))
    qTh = es.enter_context(nc.semaphore("qTh"))
    qTw2 = es.enter_context(nc.semaphore("qTw2"))
    qSq = es.enter_context(nc.semaphore("qSq"))
    qSg = es.enter_context(nc.semaphore("qSg"))
    qVp = es.enter_context(nc.semaphore("qVp"))
    qVu = es.enter_context(nc.semaphore("qVu"))
    qVg = es.enter_context(nc.semaphore("qVg"))
    qPg = es.enter_context(nc.semaphore("qPg"))
    qST = es.enter_context(nc.semaphore("qST"))
    qTs = es.enter_context(nc.semaphore("qTs"))
    qVs = es.enter_context(nc.semaphore("qVs"))
    qSs = es.enter_context(nc.semaphore("qSs"))

    ct = Ctr()
    cst = nc.alloc_sbuf_tensor("cst", [128, 2], F32)

    ov = contextlib.ExitStack()

    def ovt(name, shape, dt):
        return ov.enter_context(nc.sbuf_tensor(name, shape, dt))

    KT = ovt("KT", [128, NT], F32)
    Wbuf = KT
    UT = ovt("UT", [128, NT], F32)
    QTb = ovt("QTb", [128, NT], BF16)
    Vtm = ovt("Vtm", [128, NT], BF16)
    alphaT = ovt("alphaT", [128, NT], BF16)
    OT = UT.bitcast(BF16)[:, 0:NT]
    xn_grp = ovt("xn_grp", [128, NDC * 512], BF16)
    sq_scr = ovt("sq_scr", [128, NDC * 512], BF16)
    sqf = ovt("sqf", [128, 512], F32)
    rowsbig = ovt("rowsbig", [128, 2 * NT], F32)
    rdrow = rowsbig[0:1, 0:NT]
    srow = rowsbig[0:1, NT:2 * NT]
    n2row = rowsbig[64:65, 0:NT]
    arow = rowsbig[64:65, 0:NT]
    Asb = ovt("Asb", [128, 128], F32)
    ypair_sb = ovt("ypair_sb", [128, 2], F32)
    scs = ovt("scs", [1, 784], F32)
    scrow_b = ovt("scrow_b", [1, NT], BF16)
    Mst_sb = ovt("Mst_sb", [128, 128], BF16)
    atm = ovt("atm", [128, 128], BF16)
    psm_sb = ovt("psm_sb", [128, 128], BF16)
    aWq_s = ovt("aWq_s", [128, NDC * H], BF16)
    aWk_s = ovt("aWk_s", [128, NDC * H], BF16)
    aWv_s = ovt("aWv_s", [128, NDC * H], BF16)
    aWo_s = ovt("aWo_s", [128, D], BF16)
    abq_s = ovt("abq_s", [128, 1], F32)
    abk_s = ovt("abk_s", [128, 1], F32)
    abv_s = ovt("abv_s", [1, H], BF16)
    abo_s = ovt("abo_s", [128, NDC], F32)

    LN = dict(cst=cst, onescol_b=onescol_b, onesrow_f=onesrow_f, meanrow=meanrow,
              varrow=varrow, rstdrow=rstdrow, psA=psA, psB=psB, psC=psC, psD=psD,
              qT=qT, qV=qV, qS=qS)

    with nc.Block() as block:
        @block.vector
        def _(v):
            v = EW(v)
            v.memset(cst[:, 0:1], -1.0)
            v.memset(cst[:, 1:2], EPS_LN)
            v.inc_last(qV, 1)
        ct.inc(qV)

        @block.gpsimd
        def _(g):
            g = EW(g)
            g.dma_start(out=x_sb[:, :].rearrange("p (dc t) -> p dc t", dc=NDC),
                        in_=x_in.rearrange("(dc p) t -> p dc t", p=128)).then_inc(qGP, 16)
            g.dma_start(out=aWq_s[:, :].rearrange("p (dc h) -> p dc h", dc=NDC),
                        in_=aWq.rearrange("(dc p) h -> p dc h", p=128)).then_inc(qGP, 16)
            g.dma_start(out=aWk_s[:, :].rearrange("p (dc h) -> p dc h", dc=NDC),
                        in_=aWk.rearrange("(dc p) h -> p dc h", p=128)).then_inc(qGP, 16)
            g.dma_start(out=aWv_s[:, :].rearrange("p (dc h) -> p dc h", dc=NDC),
                        in_=aWv.rearrange("(dc p) h -> p dc h", p=128)).then_inc(qGP, 16)
            g.dma_start(out=aWo_s[:, :], in_=aWo[:, :]).then_inc(qGP, 16)
            g.dma_start(out=abv_s[:, :], in_=abv[:, :]).then_inc(qGP, 16)
            g.dma_start(out=rW_s[:, :].rearrange("p (dc e) -> p dc e", dc=NDC),
                        in_=rW.rearrange("(dc p) e -> p dc e", p=128)).then_inc(qGP, 16)
            ct.inc(qGP, 16 * 7)

        @block.sync
        def _(sy):
            sy = EW(sy)
            for dst, src in ((abq_s, abq), (abk_s, abk), (abo_s, abo), (b2_s, b2),
                             (bs2T_s, bs2T), (idf_s, c_idf), (idb_s, c_idb),
                             (onescol_b, c_onescol_b), (onescol_f, c_onescol_f),
                             (onesrow_b, c_onesrow_b), (onesrow_f, c_onesrow_f),
                             (triu_s, c_triu), (maskB_s, c_maskB), (maskA_s, c_maskA),
                             (selE_s, c_selE)):
                sy.dma_start(out=dst[:, :], in_=src[:, :]).then_inc(qSY, 16)
                ct.inc(qSY, 16)
        LOADS_SY = ct.val(qSY)
        LOADS_GP = ct.val(qGP)

        @block.gpsimd
        def _(g):
            g = EW(g)
            g.wait_ge(qGP, LOADS_GP)
            for m in range(NM):
                e, fc = m // 16, m % 16
                if e < 8:
                    src1 = bass.AP(W1T.tensor, e * D * DFF + fc * FC,
                                   [[DFF, 128], [128 * DFF, NDC], [1, FC]])
                    src2 = bass.AP(W2T.tensor, e * DFF * D + fc * FC * D,
                                   [[D, 128], [128 * D, 2], [1, D]])
                else:
                    src1 = bass.AP(Ws1T.tensor, fc * FC,
                                   [[DFF, 128], [128 * DFF, NDC], [1, FC]])
                    src2 = bass.AP(Ws2T.tensor, fc * FC * D,
                                   [[D, 128], [128 * D, 2], [1, D]])
                d1 = W1bf[m * 128:(m + 1) * 128, 0:2048].rearrange("p (a b) -> p a b", a=NDC)
                d2 = W2bf[m * 128:(m + 1) * 128, :].rearrange("p (a b) -> p a b", a=2)
                g.dma_start(out=d1, in_=src1).then_inc(qGP, 16)
                g.dma_start(out=d2, in_=src2).then_inc(qGP, 16)
                g.dma_start(out=W1bf[m * 128:(m + 1) * 128, 2048:2304],
                            in_=bass.AP(b1fl.tensor, m * FC, [[0, 128], [1, FC]])).then_inc(qGP, 16)
                ct.inc(qGP, 48)
        PRECAST_GP = ct.val(qGP)

        for gi in range(NG):
            t0 = gi * 512
            pre = [(qGP, LOADS_GP), (qSY, LOADS_SY)] if gi == 0 else None
            emit_ln(nc, block, ct, x_sb, xn_grp, sq_scr, t0, 512, NT, LN, pre_waits=pre)

            @block.tensor
            def _(t, t0=t0):
                t = EW(t)
                t.wait_ge(qV, ct.val(qV))
                if t0 == 0:
                    t.wait_ge(qGP, LOADS_GP)
                for dc in range(NDC):
                    t.matmul(psA[:, 0:512], aWq_s[:, dc * H:(dc + 1) * H],
                             xn_grp[:, dc * 512:(dc + 1) * 512],
                             start=(dc == 0), stop=(dc == NDC - 1))
                t.inc_last(qT, 1)
                for dc in range(NDC):
                    t.matmul(psB[:, 0:512], aWk_s[:, dc * H:(dc + 1) * H],
                             xn_grp[:, dc * 512:(dc + 1) * 512],
                             start=(dc == 0), stop=(dc == NDC - 1))
                t.inc_last(qT, 1)
                for ti in range(4):
                    for dc in range(NDC):
                        t.matmul(psC[:, ti * 128:(ti + 1) * 128],
                                 xn_grp[:, dc * 512 + ti * 128: dc * 512 + (ti + 1) * 128],
                                 aWv_s[:, dc * H:(dc + 1) * H],
                                 start=(dc == 0), stop=False)
                    t.matmul(psC[:, ti * 128:(ti + 1) * 128], onesrow_b[0:1, 0:128],
                             abv_s[:, :], start=False, stop=True)
                t.inc_last(qT, 1)
            ct.inc(qT, 3)
            tq = ct.val(qT)

            @block.vector
            def _(v, t0=t0, tq=tq):
                v = EW(v)
                v.wait_ge(qT, tq - 2)
                v.tensor_scalar(out=QTb[:, t0:t0 + 512], in0=psA[:, 0:512],
                                scalar1=abq_s[:, :], scalar2=None, op0=ALU.add)
                v.wait_ge(qT, tq - 1)
                v.tensor_scalar(out=KT[:, t0:t0 + 512], in0=psB[:, 0:512],
                                scalar1=abk_s[:, :], scalar2=None, op0=ALU.add)
                v.wait_ge(qT, tq)
                v.tensor_copy(out=Vtm[:, t0:t0 + 512], in_=psC[:, 0:512])
                v.inc_last(qV, 1)
            ct.inc(qV)
            vq = ct.val(qV)

            @block.scalar
            def _(s, t0=t0, vq=vq):
                s = EW(s)
                s.wait_ge(qV, vq)
                s.activation(out=sqf[:, :], in_=KT[:, t0:t0 + 512], func=AF.Square)
                s.inc_last(qS, 1)
            ct.inc(qS)
            sq1 = ct.val(qS)

            @block.tensor
            def _(t, sq1=sq1):
                t = EW(t)
                t.wait_ge(qS, sq1)
                t.matmul(psA[0:1, 0:512], onescol_f[:, :], sqf[:, :], start=True, stop=True)
                t.inc_last(qT, 1)
            ct.inc(qT)
            n2q = ct.val(qT)

            @block.vector
            def _(v, t0=t0, n2q=n2q):
                v = EW(v)
                v.wait_ge(qT, n2q)
                v.tensor_copy(out=n2row[0:1, t0:t0 + 512], in_=psA[0:1, 0:512])
                v.tensor_tensor(out=sqf[:, :], in0=KT[:, t0:t0 + 512],
                                in1=QTb[:, t0:t0 + 512], op=ALU.mult)
                v.inc_last(qV, 1)
            ct.inc(qV)
            qkv = ct.val(qV)

            @block.tensor
            def _(t, qkv=qkv):
                t = EW(t)
                t.wait_ge(qV, qkv)
                t.matmul(psB[0:1, 0:512], onescol_f[:, :], sqf[:, :], start=True, stop=True)
                t.inc_last(qT, 1)
            ct.inc(qT)
            sq2 = ct.val(qT)

            @block.vector
            def _(v, t0=t0, sq2=sq2):
                v = EW(v)
                v.wait_ge(qT, sq2)
                v.tensor_copy(out=srow[0:1, t0:t0 + 512], in_=psB[0:1, 0:512])
                v.inc_last(qV, 1)
            ct.inc(qV)

        @block.scalar
        def _(s):
            s = EW(s)
            s.wait_ge(qV, ct.val(qV))
            s.activation(out=n2row[0:1, 0:NT], in_=n2row[0:1, 0:NT], func=AF.Sqrt)
            s.inc_last(qS, 1)
        ct.inc(qS)
        nq = ct.val(qS)

        @block.vector
        def _(v, nq=nq):
            v = EW(v)
            v.wait_ge(qS, nq)
            v.tensor_scalar(out=n2row[0:1, 0:NT], in0=n2row[0:1, 0:NT],
                            scalar1=EPS_U, scalar2=None, op0=ALU.add)
            v.drain()
            v.reciprocal(n2row[0:1, 0:NT], n2row[0:1, 0:NT])
            v.inc_last(qV, 1)
        ct.inc(qV)
        rno = ct.val(qV)

        for gi in range(NG):
            t0 = gi * 512

            @block.tensor
            def _(t, t0=t0, gi=gi, rno=rno):
                t = EW(t)
                t.wait_ge(qV, rno + gi)
                t.matmul(psA[:, 0:512], onesrow_f[64:65, :], n2row[0:1, t0:t0 + 512],
                         start=True, stop=True)
                t.inc_last(qT, 1)
            ct.inc(qT)
            bq_ = ct.val(qT)

            @block.vector
            def _(v, t0=t0, bq_=bq_):
                v = EW(v)
                v.wait_ge(qT, bq_)
                v.tensor_tensor(out=UT[:, t0:t0 + 512], in0=KT[:, t0:t0 + 512],
                                in1=psA[:, 0:512], op=ALU.mult)
                v.inc_last(qV, 1)
            ct.inc(qV)
        UT_V = ct.val(qV)

        A_ps = psY[:, 0:128]
        w_ps2 = psY[:, 512:514]
        g11_ps = psY[0:1, 1024:1025]
        g22_ps = psY[0:1, 1025:1026]
        gc_ps = psY[0:1, 1026:1027]
        yr1_ps = psY[0:1, 1536:1664]
        yr2_ps = psY[0:1, 1664:1792]
        yr12_ps = psY[0:1, 1536:1792]
        y12r = scs[0:1, 0:256]
        q1 = scs[0:1, 256:384]
        q2 = scs[0:1, 384:512]
        q3 = scs[0:1, 512:640]
        q4 = scs[0:1, 640:768]
        p1r = q1
        p2r = q3
        zrow = q1
        bb2 = scs[0:1, 768:770]
        nrow = scs[0:1, 770:772]
        rpr = scs[0:1, 772:774]
        m1s = scs[0:1, 774:775]
        m2s = scs[0:1, 775:776]
        s22v = scs[0:1, 776:777]
        sc1 = scs[0:1, 777:778]
        csb = scs[0:1, 778:779]
        s11v = scs[0:1, 779:780]
        NP = NT // 2

        @block.vector
        def _(v):
            v = EW(v)
            v.wait_ge(qV, UT_V)
            v.memset(scrow_b[0:1, :], 0.0)
            v.memset(zrow, 0.0)
            v.tensor_copy(out=Asb[:, :], in_=idf_s[:, :])
            v.inc_last(qVs, 1)

        @block.tensor
        def _(t):
            t = EW(t)
            t.wait_ge(qVs, 1)
            t.matmul(A_ps, zrow, zrow, start=True, stop=False,
                     skip_group_check=True)
            with kreg(t) as rW, kreg(t) as rTi, kreg(t) as rTi1:
                t.reg_mov(rW, 1)
                t.reg_mov(rTi, 0)
                t.reg_mov(rTi1, 1)
                with kfori(nc, t, mybir.EngineType.PE, 0, NP, 1):
                    t.wait_ge(qVs, rW)
                    t.matmul(w_ps2, Asb[:, :], UT[:, rowslice(rTi, 2, NT - 2)],
                             start=True, stop=True).then_inc(qTs, 1)
                    t.reg_add(rW, rW, 1)
                    t.wait_ge(qVs, rW)
                    t.transpose(yr1_ps, ypair_sb[:, 0:1], idf_s[:, :])
                    t.transpose(yr2_ps, ypair_sb[:, 1:2], idf_s[:, :])
                    t.matmul(g11_ps, ypair_sb[:, 0:1],
                             UT[:, rowslice(rTi, 1, NT - 1)], start=True, stop=True)
                    t.matmul(g22_ps, ypair_sb[:, 1:2],
                             UT[:, rowslice(rTi1, 1, NT - 1)], start=True, stop=True)
                    t.matmul(gc_ps, ypair_sb[:, 0:1],
                             UT[:, rowslice(rTi1, 1, NT - 1)], start=True,
                             stop=True).then_inc(qTs, 1)
                    t.reg_add(rW, rW, 1)
                    t.wait_ge(qVs, rW)
                    t.matmul(A_ps, p1r, y12r[0:1, 0:128], start=False, stop=False,
                             skip_group_check=True)
                    t.matmul(A_ps, p2r, y12r[0:1, 128:256], start=False, stop=False,
                             skip_group_check=True).then_inc(qTs, 1)
                    t.reg_add(rW, rW, 1)
                    t.reg_add(rTi, rTi, 2)
                    t.reg_add(rTi1, rTi1, 2)

        @block.vector
        def _(v):
            v = EW(v)
            with kreg(v) as rT, kreg(v) as rVi, kreg(v) as rVi1:
                v.reg_mov(rT, 1)
                v.reg_mov(rVi, 0)
                v.reg_mov(rVi1, 1)
                with kfori(nc, v, mybir.EngineType.DVE, 0, NP, 1):
                    v.wait_ge(qTs, rT)
                    v.tensor_copy(out=ypair_sb[:, :], in_=w_ps2).then_inc(qVs, 1)
                    v.reg_add(rT, rT, 1)
                    v.wait_ge(qTs, rT)
                    v.tensor_copy(out=y12r, in_=yr12_ps)
                    v.tensor_scalar(out=bb2, in0=psY[0:1, 1024:1026], scalar1=1.0,
                                    scalar2=None, op0=ALU.add)
                    v.tensor_copy(out=csb, in_=gc_ps)
                    v.drain()
                    v.tensor_tensor(out=m2s, in0=csb, in1=csb, op=ALU.mult)
                    v.tensor_tensor(out=m1s, in0=bb2[0:1, 0:1], in1=bb2[0:1, 1:2],
                                    op=ALU.mult)
                    v.tensor_scalar(out=nrow[0:1, 0:1], in0=bb2[0:1, 0:1],
                                    scalar1=-1.0, scalar2=None, op0=ALU.mult)
                    v.drain()
                    v.scalar_tensor_tensor(out=nrow[0:1, 1:2], in0=m1s, scalar=-1.0,
                                           in1=m2s, op0=ALU.mult, op1=ALU.add)
                    v.drain()
                    v.reciprocal(rpr, nrow)
                    v.drain()
                    v.tensor_scalar(out=sc1, in0=gc_ps, scalar1=rpr[0:1, 1:2],
                                    scalar2=None, op0=ALU.mult)
                    v.tensor_scalar(out=s11v, in0=bb2[0:1, 0:1],
                                    scalar1=rpr[0:1, 1:2], scalar2=None, op0=ALU.mult)
                    v.tensor_scalar(out=s22v, in0=bb2[0:1, 1:2],
                                    scalar1=rpr[0:1, 1:2], scalar2=None, op0=ALU.mult)
                    v.drain()
                    v.tensor_scalar(out=q1, in0=y12r[0:1, 0:128],
                                    scalar1=s22v[0:1, 0:1], scalar2=None, op0=ALU.mult)
                    v.tensor_scalar(out=q2, in0=y12r[0:1, 128:256],
                                    scalar1=sc1[0:1, 0:1],
                                    scalar2=None, op0=ALU.mult)
                    v.tensor_scalar(out=q3, in0=y12r[0:1, 128:256],
                                    scalar1=s11v[0:1, 0:1],
                                    scalar2=None, op0=ALU.mult)
                    v.tensor_scalar(out=q4, in0=y12r[0:1, 0:128],
                                    scalar1=sc1[0:1, 0:1],
                                    scalar2=None, op0=ALU.mult)
                    v.drain()
                    v.tensor_tensor(out=p1r, in0=q1, in1=q2, op=ALU.subtract)
                    v.tensor_tensor(out=p2r, in0=q3, in1=q4,
                                    op=ALU.subtract).then_inc(qVs, 1)
                    v.tensor_copy(out=rdrow[0:1, rowslice(rVi, 1, NT - 1)],
                                  in_=rpr[0:1, 0:1])
                    v.tensor_copy(out=rdrow[0:1, rowslice(rVi1, 1, NT - 1)],
                                  in_=s11v)
                    v.tensor_copy(out=scrow_b[0:1, rowslice(rVi1, 1, NT - 1)],
                                  in_=sc1)
                    v.reg_add(rT, rT, 1)
                    v.wait_ge(qTs, rT)
                    v.scalar_tensor_tensor(out=Asb[:, :], in0=A_ps, scalar=1.0,
                                           in1=idf_s[:, :], op0=ALU.mult,
                                           op1=ALU.add)
                    v.tensor_copy(out=Wbuf[:, rowslice(rVi, 2, NT - 2)],
                                  in_=ypair_sb[:, :]).then_inc(qVs, 1)
                    v.reg_add(rT, rT, 1)
                    v.reg_add(rVi, rVi, 2)
                    v.reg_add(rVi1, rVi1, 2)

        SCAN_TS = 3 * NP
        SCAN_VS = 1 + 3 * NP
        ct.inc(qTs, SCAN_TS)
        ct.inc(qVs, SCAN_VS)

        @block.vector
        def _(v):
            v = EW(v)
            v.wait_ge(qVs, SCAN_VS)
            v.tensor_tensor(out=arow[0:1, 0:NT], in0=srow[0:1, 0:NT],
                            in1=rdrow[0:1, 0:NT], op=ALU.mult)
            v.tensor_tensor(out=scrow_b[0:1, 0:NT], in0=scrow_b[0:1, 0:NT],
                            in1=srow[0:1, 0:NT], op=ALU.mult)
            v.drain()
            v.tensor_scalar(out=arow[0:1, 0:NT], in0=arow[0:1, 0:NT],
                            scalar1=-1.0, scalar2=None, op0=ALU.mult)
            v.inc_last(qV, 1)
        ct.inc(qV)
        alv = ct.val(qV)

        for gi in range(NG):
            t0 = gi * 512

            @block.tensor
            def _(t, t0=t0, alv=alv):
                t = EW(t)
                t.wait_ge(qV, alv + 2 * gi)
                t.matmul(psA[:, 0:512], onesrow_f[64:65, :], arow[0:1, t0:t0 + 512],
                         start=True, stop=True)
                t.matmul(psC[:, 0:512], onesrow_b[0:1, 0:128],
                         scrow_b[0:1, t0:t0 + 512], start=True, stop=True)
                t.inc_last(qT, 1)
            ct.inc(qT)
            ab_ = ct.val(qT)

            @block.vector
            def _(v, gi=gi, t0=t0, ab_=ab_):
                v = EW(v)
                v.wait_ge(qT, ab_)
                v.tensor_tensor(out=alphaT[:, t0:t0 + 512], in0=Wbuf[:, t0:t0 + 512],
                                in1=psA[:, 0:512], op=ALU.mult)
                lo = 1 if gi == 0 else 0
                v.tensor_tensor(out=sq_scr[:, lo:512],
                                in0=Wbuf[:, t0 + lo - 1:t0 + 511],
                                in1=psC[:, lo:512], op=ALU.mult)
                v.drain()
                v.tensor_tensor(out=alphaT[:, t0 + lo:t0 + 512],
                                in0=alphaT[:, t0 + lo:t0 + 512],
                                in1=sq_scr[:, lo:512], op=ALU.add)
                v.inc_last(qV, 2)
            ct.inc(qV, 2)

        psBb = psH.bitcast(BF16)
        for c in range(NCH):
            t0 = c * 128

            @block.tensor
            def _(t, c=c, t0=t0):
                t = EW(t)
                t.wait_ge(qV, ct.val(qV))
                t.matmul(psA[:, 0:128], alphaT[:, t0:t0 + 128], QTb[:, t0:t0 + 128],
                         start=True, stop=True)
                t.inc_last(qT, 1)
                t.transpose(psBb[:, 1024:1152], alphaT[:, t0:t0 + 128], idb_s[:, :])
                t.inc_last(qT, 1)
            ct.inc(qT, 2)
            pq = ct.val(qT)

            @block.vector
            def _(v, c=c, pq=pq):
                v = EW(v)
                v.wait_ge(qT, pq - 1)
                v.tensor_tensor(out=psm_sb[:, :], in0=psA[:, 0:128], in1=triu_s[:, :],
                                op=ALU.mult)
                v.wait_ge(qT, pq)
                v.tensor_copy(out=atm[:, :], in_=psBb[:, 1024:1152])
                v.inc_last(qV, 1)
            ct.inc(qV)
            pv = ct.val(qV)

            @block.tensor
            def _(t, c=c, t0=t0, pv=pv):
                t = EW(t)
                t.wait_ge(qV, pv)
                t.matmul(psC[:, 0:128], Vtm[:, t0:t0 + 128], psm_sb[:, :],
                         start=True, stop=(c == 0))
                if c > 0:
                    t.matmul(psC[:, 0:128], Mst_sb[:, :], QTb[:, t0:t0 + 128],
                             start=False, stop=True)
                t.inc_last(qT, 1)
                t.matmul(psD[:, 0:128], atm[:, :], Vtm[:, t0:t0 + 128],
                         start=(c == 0), stop=(c == NCH - 1), skip_group_check=True)
                t.inc_last(qT, 1)
            ct.inc(qT, 2)
            oq = ct.val(qT)

            @block.vector
            def _(v, c=c, t0=t0, oq=oq):
                v = EW(v)
                v.wait_ge(qT, oq - 1)
                v.tensor_copy(out=OT[:, t0:t0 + 128], in_=psC[:, 0:128])
                v.wait_ge(qT, oq)
                v.tensor_copy(out=Mst_sb[:, :], in_=psD[:, 0:128])
                v.inc_last(qV, 1)
            ct.inc(qV)

        for gi in range(NG):
            t0 = gi * 512
            for dc in range(NDC):
                @block.tensor
                def _(t, t0=t0, dc=dc):
                    t = EW(t)
                    t.wait_ge(qV, ct.val(qV))
                    t.matmul(psA[:, 0:512], aWo_s[:, dc * 128:(dc + 1) * 128],
                             OT[:, t0:t0 + 512], start=True, stop=True)
                    t.inc_last(qT, 1)
                ct.inc(qT)
                tq2 = ct.val(qT)

                @block.vector
                def _(v, t0=t0, dc=dc, tq2=tq2):
                    v = EW(v)
                    v.wait_ge(qT, tq2)
                    v.tensor_scalar(out=sqf[:, :], in0=psA[:, 0:512],
                                    scalar1=abo_s[:, dc:dc + 1], scalar2=None,
                                    op0=ALU.add)
                    v.drain()
                    v.tensor_tensor(out=x_sb[:, dc * NT + t0: dc * NT + t0 + 512],
                                    in0=x_sb[:, dc * NT + t0: dc * NT + t0 + 512],
                                    in1=sqf[:, :], op=ALU.add)
                    v.inc_last(qV, 1)
                ct.inc(qV)
        ATTNA_V = ct.val(qV)

        ov.close()

        ffn = dict(x_sb=x_sb, W1bf=W1bf, W2bf=W2bf, b1fl=b1fl, qP=qP, qSTRM=qSTRM,
                   rW_s=rW_s, b2_s=b2_s, bs2T_s=bs2T_s,
                   selE_s=selE_s, idf_s=idf_s, onesrow_b=onesrow_b,
                   psH=psH, psC=psC, psD=psD, psY=psY, psA=psA, psB=psB,
                   qTh=qTh, qTw2=qTw2, qSq=qSq, qSg=qSg, qVp=qVp, qVu=qVu,
                   qVg=qVg, qPg=qPg, qSY=qSY, qGP=qGP, qST=qST,
                   PRECAST_GP=PRECAST_GP, y_out=y_out, LN=LN, NT=NT)

        emit_ffn(nc, block, ct, ffn, last=False, tag="A")

        attb = dict(x_sb=x_sb, bWq=bWq, bWk=bWk, bWv=bWv, bWo=bWo,
                    maskA_s=maskA_s, maskB_s=maskB_s, onescol_b=onescol_b,
                    onesrow_f=onesrow_f,
                    psH=psH, psC=psC, psD=psD, psA=psA, psB=psB, psY=psY,
                    qGP=qGP, qSY=qSY, qTw2=qTw2, LN=LN, NT=NT)

        emit_attn_b(nc, block, ct, attb)

        emit_ffn(nc, block, ct, ffn, last=True, tag="B")

        @block.sync
        def _(sy):
            sy = EW(sy)
            sy.wait_ge(qST, ct.val(qST))

    es.close()
    return nc


def emit_ln(nc, block, ct, x_sb, xn_out, sq_scr, t0, W, NT, LN, pre_waits=None,
            xn_stride=None):
    if xn_stride is None:
        xn_stride = W
    qT, qV, qS = LN["qT"], LN["qV"], LN["qS"]
    psA, psB, psC, psD = LN["psA"], LN["psB"], LN["psC"], LN["psD"]

    @block.scalar
    def _(s):
        s = EW(s)
        s.wait_ge(qV, ct.val(qV))
        if pre_waits:
            for (q, vv) in pre_waits:
                s.wait_ge(q, vv)
        for dc in range(NDC):
            s.activation(out=sq_scr[:, dc * W:(dc + 1) * W],
                         in_=x_sb[:, dc * NT + t0: dc * NT + t0 + W], func=AF.Square)
        s.inc_last(qS, 1)
    ct.inc(qS)
    sqq = ct.val(qS)

    @block.tensor
    def _(t, sqq=sqq):
        t = EW(t)
        t.wait_ge(qS, sqq)
        for dc in range(NDC):
            t.matmul(psA[0:1, 0:W], LN["onescol_b"][:, :],
                     x_sb[:, dc * NT + t0: dc * NT + t0 + W],
                     start=(dc == 0), stop=(dc == NDC - 1))
        for dc in range(NDC):
            t.matmul(psB[0:1, 0:W], LN["onescol_b"][:, :],
                     sq_scr[:, dc * W:(dc + 1) * W],
                     start=(dc == 0), stop=(dc == NDC - 1))
        t.inc_last(qT, 1)
    ct.inc(qT)
    stq = ct.val(qT)

    @block.vector
    def _(v, stq=stq):
        v = EW(v)
        v.wait_ge(qT, stq)
        v.tensor_scalar(out=LN["meanrow"][0:1, 0:W], in0=psA[0:1, 0:W],
                        scalar1=1.0 / D, scalar2=None, op0=ALU.mult)
        v.drain()
        v.tensor_tensor(out=LN["varrow"][0:1, 0:W], in0=LN["meanrow"][0:1, 0:W],
                        in1=LN["meanrow"][0:1, 0:W], op=ALU.mult)
        v.drain()
        v.scalar_tensor_tensor(out=LN["varrow"][0:1, 0:W], in0=psB[0:1, 0:W],
                               scalar=1.0 / D, in1=LN["varrow"][0:1, 0:W],
                               op0=ALU.mult, op1=ALU.subtract)
        v.inc_last(qV, 1)
    ct.inc(qV)
    vq = ct.val(qV)

    @block.scalar
    def _(s, vq=vq):
        s = EW(s)
        s.wait_ge(qV, vq)
        s.activation(out=LN["rstdrow"][0:1, 0:W], in_=LN["varrow"][0:1, 0:W],
                     func=AF.Sqrt, bias=LN["cst"][0:1, 1:2])
        s.inc_last(qS, 1)
    ct.inc(qS)
    sq2 = ct.val(qS)

    @block.vector
    def _(v, sq2=sq2):
        v = EW(v)
        v.wait_ge(qS, sq2)
        v.reciprocal(LN["rstdrow"][0:1, 0:W], LN["rstdrow"][0:1, 0:W])
        v.inc_last(qV, 1)
    ct.inc(qV)
    vq2 = ct.val(qV)

    @block.tensor
    def _(t, vq2=vq2):
        t = EW(t)
        t.wait_ge(qV, vq2)
        t.matmul(psC[:, 0:W], LN["onesrow_f"][0:1, :], LN["meanrow"][0:1, 0:W],
                 start=True, stop=True)
        t.matmul(psD[:, 0:W], LN["onesrow_f"][32:33, :], LN["rstdrow"][0:1, 0:W],
                 start=True, stop=True)
        t.inc_last(qT, 1)
    ct.inc(qT)
    bcq = ct.val(qT)

    @block.vector
    def _(v, bcq=bcq):
        v = EW(v)
        v.wait_ge(qT, bcq)
        for dc in range(NDC):
            v.tensor_tensor(out=sq_scr[:, dc * W:(dc + 1) * W],
                            in0=x_sb[:, dc * NT + t0: dc * NT + t0 + W],
                            in1=psC[:, 0:W], op=ALU.subtract)
        v.drain()
        for dc in range(NDC):
            v.tensor_tensor(out=xn_out[:, dc * xn_stride:dc * xn_stride + W],
                            in0=sq_scr[:, dc * W:(dc + 1) * W],
                            in1=psD[:, 0:W], op=ALU.mult)
        v.inc_last(qV, 1)
    ct.inc(qV)


def emit_ffn(nc, block, ct, P, last, tag=""):
    NT = P["NT"]
    NBLK = NT // BLK
    x_sb = P["x_sb"]
    psH, psC, psD, psY = P["psH"], P["psC"], P["psD"], P["psY"]
    psA, psB = P["psA"], P["psB"]
    LN = P["LN"]
    qT, qV, qS = LN["qT"], LN["qV"], LN["qS"]
    qTh, qTw2 = P["qTh"], P["qTw2"]
    qSq, qSg, qVp, qVu, qVg, qPg = P["qSq"], P["qSg"], P["qVp"], P["qVu"], P["qVg"], P["qPg"]
    qSY, qGP, qST, qP = P["qSY"], P["qGP"], P["qST"], P["qP"]
    qsW1 = list(P["qSTRM"][tag][0:4])
    qsW2 = list(P["qSTRM"][tag][4:8])

    sc = contextlib.ExitStack()

    def sct(n, shp, dt):
        return sc.enter_context(nc.sbuf_tensor(n + tag, shp, dt))

    P = dict(P)
    P["w1buf"] = sct("w1buf", [128, 4 * 2304], BF16)
    P["w2buf"] = sct("w2buf", [128, 4 * 2048], BF16)
    P["xn_blk"] = sct("xn_blk", [128, NDC * BLK], BF16)
    P["lnsq"] = sct("lnsq", [128, NDC * BLK], BF16)
    P["gateB"] = sct("gateB", [128, 9 * BLK], BF16)
    P["gatesT"] = sct("gatesT", [8, BLK], F32)
    P["gsc"] = sct("gsc", [128, 48], F32)
    P["sqg"] = sct("sqg", [128, 4 * 512], BF16)
    P["pg"] = sct("pg", [128, 512], BF16)
    P["u2g"] = sct("u2g", [128, 512], F32)
    P["sgg"] = sct("sgg", [128, 512], BF16)
    P["hg0"] = sct("hg0", [128, 4 * 512], BF16)
    P["hgb"] = sct("hgb", [128, 4 * 512], BF16)
    P["outst"] = sct("outst", [128, 2048], F32)
    HS = [psH[:, 0:512], psH[:, 512:1024], psC[:, 0:512], psD[:, 0:512]]
    ENTRY_V = ct.val(qV)
    ENTRY_SY = ct.val(qSY)
    ENTRY_TW2 = ct.val(qTw2)

    @block.gpsimd
    def _(g):
        g = EW(g)
        g.wait_ge(qV, ENTRY_V)
        g.wait_ge(qSY, ENTRY_SY)
        g.wait_ge(qTw2, ENTRY_TW2)
        g.inc_last(qP, 1)
        ct.inc(qP)

    @block.vector
    def _(v):
        v = EW(v)
        v.wait_ge(qV, ENTRY_V)
        v.memset(P["gateB"][:, 8 * BLK:9 * BLK], 1.0)
        v.inc_last(qV, 1)
    ct.inc(qV)
    ENTRY_P = ct.val(qP)

    CB_Tw2 = ct.val(qTw2)
    CB_W1 = [ct.val(q) for q in qsW1]
    CB_W2 = [ct.val(q) for q in qsW2]

    @block.sync
    def _(sy):
        sy = EW(sy)
        sy.wait_ge(qP, ENTRY_P)
        if not last:
            sy.wait_ge(qGP, P["PRECAST_GP"])
        for k in range(NBLK * NM):
            s = k % 4
            m = k % NM
            if k >= 4:
                sy.wait_ge(qTw2, CB_Tw2 + k - 3)
            sy.dma_start(out=P["w1buf"][:, s * 2304:(s + 1) * 2304],
                         in_=P["W1bf"][m * 128:(m + 1) * 128, :]).then_inc(qsW1[s], 16)
            sy.dma_start(out=P["w2buf"][:, s * 2048:(s + 1) * 2048],
                         in_=P["W2bf"][m * 128:(m + 1) * 128, :]).then_inc(qsW2[s], 16)
    for _q in qsW1 + qsW2:
        ct.inc(_q, 16 * (NM // 4) * NBLK)

    for blk in range(NBLK):
        t0 = blk * BLK
        emit_ln(nc, block, ct, x_sb, P["xn_blk"], P["lnsq"], t0, BLK, NT, LN)

        for ti in range(2):
            tt0 = ti * 128

            @block.tensor
            def _(t, tt0=tt0):
                t = EW(t)
                t.wait_ge(qV, ct.val(qV))
                for dc in range(NDC):
                    t.matmul(psC[0:128, 0:8],
                             P["xn_blk"][:, dc * BLK + tt0: dc * BLK + tt0 + 128],
                             P["rW_s"][:, dc * NE:(dc + 1) * NE],
                             start=(dc == 0), stop=(dc == NDC - 1))
                t.inc_last(qT, 1)
            ct.inc(qT)
            lq = ct.val(qT)

            @block.vector
            def _(v, lq=lq):
                v = EW(v)
                v.wait_ge(qT, lq)
                v.tensor_copy(out=P["gsc"][:, 0:8], in_=psC[0:128, 0:8])
                v.drain()
                v.reduce_max(out=P["gsc"][:, 8:9], in_=P["gsc"][:, 0:8],
                             axis=AX.X, negate=True)
                v.inc_last(qV, 1)
            ct.inc(qV)
            lv = ct.val(qV)

            @block.scalar
            def _(s, lv=lv):
                s = EW(s)
                s.wait_ge(qV, lv)
                s.activation(out=P["gsc"][:, 16:24], in_=P["gsc"][:, 0:8], func=AF.Exp,
                             bias=P["gsc"][:, 8:9], accum_out=P["gsc"][:, 9:10])
                s.inc_last(qS, 1)
            ct.inc(qS)
            ls = ct.val(qS)

            @block.vector
            def _(v, ls=ls):
                v = EW(v)
                v.wait_ge(qS, ls)
                v.reciprocal(P["gsc"][:, 10:11], P["gsc"][:, 9:10])
                v.max(out=P["gsc"][:, 24:32], in_=P["gsc"][:, 16:24])
                v.drain()
                v.tensor_scalar(out=P["gsc"][:, 32:40], in0=P["gsc"][:, 16:24],
                                scalar1=P["gsc"][:, 25:26], scalar2=None,
                                op0=ALU.is_ge)
                v.drain()
                v.tensor_tensor(out=P["gsc"][:, 40:48], in0=P["gsc"][:, 16:24],
                                in1=P["gsc"][:, 32:40], op=ALU.mult)
                v.drain()
                v.tensor_scalar(out=P["gsc"][:, 0:8], in0=P["gsc"][:, 40:48],
                                scalar1=P["gsc"][:, 10:11], scalar2=None,
                                op0=ALU.mult)
                v.inc_last(qV, 1)
            ct.inc(qV)
            gv = ct.val(qV)

            @block.tensor
            def _(t, gv=gv):
                t = EW(t)
                t.wait_ge(qV, gv)
                t.transpose(psC[0:8, 256:384], P["gsc"][:, 0:8], P["idf_s"][:, :])
                t.inc_last(qT, 1)
            ct.inc(qT)
            gq = ct.val(qT)

            @block.vector
            def _(v, tt0=tt0, gq=gq):
                v = EW(v)
                v.wait_ge(qT, gq)
                v.tensor_copy(out=P["gatesT"][:, tt0:tt0 + 128], in_=psC[0:8, 256:384])
                v.inc_last(qV, 1)
            ct.inc(qV)
        GATES_V = ct.val(qV)

        for e in range(8):
            @block.tensor
            def _(t, e=e, w=ct.val(qV)):
                t = EW(t)
                t.wait_ge(qV, w)
                t.matmul(psC[:, 0:256], P["selE_s"][:, e * 128:(e + 1) * 128],
                         P["gatesT"][:, :], start=True, stop=True)
                t.inc_last(qT, 1)
            ct.inc(qT)
            eq = ct.val(qT)

            @block.vector
            def _(v, e=e, eq=eq):
                v = EW(v)
                v.wait_ge(qT, eq)
                v.tensor_copy(out=P["gateB"][:, e * BLK:(e + 1) * BLK],
                              in_=psC[:, 0:256])
                v.inc_last(qV, 1)
            ct.inc(qV)
        GB_V = ct.val(qV)

        @block.tensor
        def _(t):
            t = EW(t)
            t.wait_ge(qV, GB_V)
            for dcp in range(4):
                for side in range(2):
                    dc = 2 * dcp + side
                    t.matmul(psY[:, dcp * 512 + side * 256: dcp * 512 + side * 256 + 256],
                             P["b2_s"][:, dc * 128:(dc + 1) * 128], P["gatesT"][:, :],
                             start=(side == 0), stop=False, skip_group_check=True)
            t.inc_last(qT, 1)
        ct.inc(qT)

        bqW1 = [CB_W1[s] + blk * 16 * (NM // 4) for s in range(4)]
        bqW2 = [CB_W2[s] + blk * 16 * (NM // 4) for s in range(4)]
        base_Th = ct.val(qTh)
        base_Tw2 = ct.val(qTw2)
        base_Sq = ct.val(qSq)
        base_Sg = ct.val(qSg)
        base_Vp = ct.val(qVp)
        base_Vu = ct.val(qVu)
        base_Vg = ct.val(qVg)
        base_Pg = ct.val(qPg)
        T_ENTRY = ct.val(qT)
        V_ENTRY = ct.val(qV)


        bq = bqW1[0]
        assert all(b == bq for b in bqW1 + bqW2)

        def t_w1(t, s):
            h = HS[s]
            for fsub in range(2):
                for dc in range(NDC):
                    t.matmul(h[:, fsub * 256: fsub * 256 + 256],
                             P["w1buf"][:, s * 2304 + dc * 256 + fsub * 128:
                                        s * 2304 + dc * 256 + fsub * 128 + 128],
                             P["xn_blk"][:, dc * BLK:(dc + 1) * BLK],
                             start=(dc == 0), stop=False)
                t.matmul(h[:, fsub * 256: fsub * 256 + 256],
                         P["w1buf"][0:1, s * 2304 + 2048 + fsub * 128:
                                    s * 2304 + 2048 + fsub * 128 + 128],
                         P["onesrow_b"][0:1, 0:256], start=False, stop=True)
            t.inc_last(qTh, 1)

        def t_w2(t, s):
            for fr in range(2):
                for dc in range(NDC):
                    t.matmul(psY[:, (dc // 2) * 512 + (dc % 2) * 256:
                                  (dc // 2) * 512 + (dc % 2) * 256 + 256],
                             P["w2buf"][:, s * 2048 + fr * 1024 + dc * 128:
                                        s * 2048 + fr * 1024 + dc * 128 + 128],
                             P["hgb"][:, s * 512 + fr * 256: s * 512 + fr * 256 + 256],
                             start=False, stop=False, skip_group_check=True)
            t.inc_last(qTw2, 1)

        @block.tensor
        def _(t):
            t = EW(t)
            for j in range(4):
                t.wait_ge(qsW1[j], bq + 16)
                if base_Vg + j - 3 > 0:
                    t.wait_ge(qVg, base_Vg + j - 3)
                t_w1(t, j)
                if j >= 1:
                    t.wait_ge(qPg, base_Pg + j)
                    t.wait_ge(qsW2[j - 1], bq + 16)
                    t_w2(t, j - 1)
            with kreg(t) as rQ, kreg(t) as rQm, kreg(t) as rVg, kreg(t) as rPg:
                t.reg_mov(rQ, bq + 32)
                t.reg_mov(rQm, bq + 16)
                t.reg_mov(rVg, base_Vg + 1)
                t.reg_mov(rPg, base_Pg + 4)
                with kfori(nc, t, mybir.EngineType.PE, 1, NM // 4, 1):
                    for j in range(4):
                        t.wait_ge(qsW1[j], rQ)
                        t.wait_ge(qVg, rVg)
                        t_w1(t, j)
                        t.wait_ge(qPg, rPg)
                        t.wait_ge(qsW2[(j + 3) % 4], rQm if j == 0 else rQ)
                        t_w2(t, (j + 3) % 4)
                        t.reg_add(rVg, rVg, 1)
                        t.reg_add(rPg, rPg, 1)
                    t.reg_add(rQ, rQ, 16)
                    t.reg_add(rQm, rQm, 16)
            t.wait_ge(qPg, base_Pg + NM)
            t.wait_ge(qsW2[3], bq + 16 * (NM // 4))
            t_w2(t, 3)
        ct.inc(qTh, NM)
        ct.inc(qTw2, NM)

        @block.scalar
        def _(s):
            s = EW(s)
            with kreg(s) as rTh, kreg(s) as rVu, kreg(s) as rVp:
                s.reg_mov(rTh, base_Th + 1)
                s.reg_mov(rVu, base_Vu + 1)
                s.reg_mov(rVp, max(base_Vp - 3, 0))
                with kfori(nc, s, mybir.EngineType.Activation, 0, NM // 4, 1):
                    for j in range(4):
                        s.wait_ge(qTh, rTh)
                        s.wait_ge(qVp, rVp)
                        s.activation(out=P["sqg"][:, j * 512:(j + 1) * 512],
                                     in_=HS[j][:, 0:512],
                                     func=AF.Square).then_inc(qSq, 1)
                        s.wait_ge(qVu, rVu)
                        s.activation(out=P["sgg"][:, :], in_=P["u2g"][:, :],
                                     func=AF.Sigmoid).then_inc(qSg, 1)
                        s.reg_add(rTh, rTh, 1)
                        s.reg_add(rVu, rVu, 1)
                        s.reg_add(rVp, rVp, 1)
        ct.inc(qSq, NM)
        ct.inc(qSg, NM)

        @block.vector
        def _(v):
            v = EW(v)
            v.wait_ge(qV, GB_V)
            with kreg(v) as rSq, kreg(v) as rSg, kreg(v) as rT2:
                v.reg_mov(rSq, base_Sq + 1)
                v.reg_mov(rSg, base_Sg + 1)
                v.reg_mov(rT2, max(base_Tw2 - 3, 0))
                for e in range(9):
                    goff = e * BLK
                    with kfori(nc, v, mybir.EngineType.DVE, e * 4, e * 4 + 4, 1):
                        for j in range(4):
                            v.wait_ge(qSq, rSq)
                            v.tensor_scalar(out=P["pg"][:, :],
                                            in0=P["sqg"][:, j * 512:(j + 1) * 512],
                                            scalar1=GC2, scalar2=GC1, op0=ALU.mult,
                                            op1=ALU.add).then_inc(qVp, 1)
                            v.drain()
                            v.tensor_tensor(out=P["u2g"][:, :],
                                            in0=HS[j][:, 0:512],
                                            in1=P["pg"][:, :],
                                            op=ALU.mult).then_inc(qVu, 1)
                            v.wait_ge(qSg, rSg)
                            v.tensor_tensor(out=P["hg0"][:, j * 512:(j + 1) * 512],
                                            in0=HS[j][:, 0:512],
                                            in1=P["sgg"][:, :],
                                            op=ALU.mult).then_inc(qVg, 1)
                            v.wait_ge(qTw2, rT2)
                            v.drain()
                            v.tensor_tensor(out=P["hgb"][:, j * 512: j * 512 + 256],
                                            in0=P["hg0"][:, j * 512: j * 512 + 256],
                                            in1=P["gateB"][:, goff:goff + 256],
                                            op=ALU.mult)
                            v.tensor_tensor(out=P["hgb"][:, j * 512 + 256: j * 512 + 512],
                                            in0=P["hg0"][:, j * 512 + 256: j * 512 + 512],
                                            in1=P["gateB"][:, goff:goff + 256],
                                            op=ALU.mult).then_inc(qPg, 1)
                            v.reg_add(rSq, rSq, 1)
                            v.reg_add(rSg, rSg, 1)
                            v.reg_add(rT2, rT2, 1)
        ct.inc(qVp, NM)
        ct.inc(qVu, NM)
        ct.inc(qVg, NM)
        ct.inc(qPg, NM)

        @block.vector
        def _(v, blk=blk, t0=t0, wst=ct.val(qST)):
            v = EW(v)
            v.wait_ge(qTw2, base_Tw2 + NM)
            if last:
                v.wait_ge(qST, wst)
            for dc in range(NDC):
                ysl = psY[:, (dc // 2) * 512 + (dc % 2) * 256:
                          (dc // 2) * 512 + (dc % 2) * 256 + 256]
                v.tensor_scalar(out=P["outst"][:, dc * 256:(dc + 1) * 256],
                                in0=ysl, scalar1=P["bs2T_s"][:, dc:dc + 1],
                                scalar2=None, op0=ALU.add)
            v.inc_last(qV, 1)
        ct.inc(qV)
        ev1 = ct.val(qV)

        @block.vector
        def _(v, blk=blk, t0=t0, ev1=ev1):
            v = EW(v)
            v.wait_ge(qV, ev1)
            for dc in range(NDC):
                v.scalar_tensor_tensor(out=P["outst"][:, dc * 256:(dc + 1) * 256],
                                       in0=P["outst"][:, dc * 256:(dc + 1) * 256],
                                       scalar=SCALE_FFN,
                                       in1=x_sb[:, dc * NT + t0: dc * NT + t0 + 256],
                                       op0=ALU.mult, op1=ALU.add)
            v.inc_last(qV, 1)
        ct.inc(qV)
        ev2 = ct.val(qV)

        if last:
            @block.gpsimd
            def _(g, blk=blk, t0=t0, ev2=ev2):
                g = EW(g)
                g.wait_ge(qV, ev2)
                g.dma_start(out=bass.AP(P["y_out"].tensor, t0,
                                        [[NT, 128], [128 * NT, NDC], [1, 256]]),
                            in_=P["outst"][:, :].rearrange("p (dc t) -> p dc t", dc=NDC)).then_inc(qST, 16)
            ct.inc(qST, 16)
        else:
            @block.vector
            def _(v, blk=blk, t0=t0, ev2=ev2):
                v = EW(v)
                v.wait_ge(qV, ev2)
                for dc in range(NDC):
                    v.tensor_copy(out=x_sb[:, dc * NT + t0: dc * NT + t0 + 256],
                                  in_=P["outst"][:, dc * 256:(dc + 1) * 256])
                v.inc_last(qV, 1)
            ct.inc(qV)

    sc.close()


def emit_attn_b(nc, block, ct, P):
    NT = P["NT"]
    NTIL = NT // 128
    x_sb = P["x_sb"]
    psH, psC, psD = P["psH"], P["psC"], P["psD"]
    psA, psB = P["psA"], P["psB"]
    psY = P["psY"]
    LN = P["LN"]
    qT, qV, qS = LN["qT"], LN["qV"], LN["qS"]
    qGP, qSY, qTw2 = P["qGP"], P["qSY"], P["qTw2"]
    SCL = 1.0 / 8.0

    sc = contextlib.ExitStack()

    def sct(n, shp, dt):
        return sc.enter_context(nc.sbuf_tensor(n, shp, dt))

    P = dict(P)
    P["lnsq"] = sct("ab_lnsq", [128, NDC * 128], BF16)
    P["xnb"] = sct("ab_xnb", [128, NDC * 128], BF16)
    P["bQT"] = sct("ab_bQT", [128, NDC * 128], BF16)
    P["bKTr"] = sct("ab_bKTr", [128, 2 * NDC * 128], BF16)
    P["bVtr"] = sct("ab_bVtr", [128, 2 * NDC * 128], BF16)
    P["bexp"] = sct("ab_bexp", [128, 4 * 256], BF16)
    P["bexn"] = sct("ab_bexn", [128, 2 * 256], BF16)
    P["bctx"] = sct("ab_bctx", [128, NDC * 128], BF16)
    P["brs"] = sct("ab_brs", [1, 2 * 128], F32)
    P["bmsk"] = sct("ab_bmsk", [128, 2 * 256], F32)
    P["outst"] = sct("ab_outst", [128, NDC * 128], F32)
    P["bWq_s"] = sct("ab_bWq", [128, NDC * D], BF16)
    P["bWk_s"] = sct("ab_bWk", [128, NDC * D], BF16)
    P["bWv_s"] = sct("ab_bWv", [128, NDC * D], BF16)
    P["bWo_s"] = sct("ab_bWo", [128, NDC * D], BF16)
    ENTRY_V = ct.val(qV)
    Hscq, Hmkv, Hexq, Hsmq, Hrsq, Hrbq, Henv, Hcxq = {}, {}, {}, {}, {}, {}, {}, {}

    @block.gpsimd
    def _(g):
        g = EW(g)
        g.wait_ge(qV, ENTRY_V)
        g.wait_ge(qSY, ct.val(qSY))
        g.wait_ge(qTw2, ct.val(qTw2))
        g.wait_ge(qGP, ct.val(qGP))
        for dst, src in ((P["bWq_s"], P["bWq"]), (P["bWk_s"], P["bWk"]),
                         (P["bWv_s"], P["bWv"]), (P["bWo_s"], P["bWo"])):
            g.dma_start(out=dst[:, :].rearrange("p (c o) -> p c o", c=NDC),
                        in_=src.rearrange("(c p) o -> p c o", p=128)).then_inc(qGP, 16)
            ct.inc(qGP, 16)
    BW_GP = ct.val(qGP)

    for ti in range(NTIL):
        t0 = ti * 128
        slot = ti % 2
        pre = [(qGP, BW_GP)] if ti == 0 else None
        emit_ln(nc, block, ct, x_sb, P["xnb"], P["lnsq"], t0, 128, NT, LN,
                pre_waits=pre)
        XN_V = ct.val(qV)

        @block.tensor
        def _(t, ti=ti):
            t = EW(t)
            t.wait_ge(qV, XN_V)
            for oc in range(NDC):
                for dc in range(NDC):
                    t.matmul(psH[:, oc * 128:(oc + 1) * 128],
                             P["bWq_s"][:, dc * D + oc * 128: dc * D + oc * 128 + 128],
                             P["xnb"][:, dc * 128:(dc + 1) * 128],
                             start=(dc == 0), stop=(dc == NDC - 1))
            t.inc_last(qT, 1)
        ct.inc(qT)
        q1 = ct.val(qT)

        @block.vector
        def _(v, q1=q1):
            v = EW(v)
            v.wait_ge(qT, q1)
            for oc in range(NDC):
                v.tensor_copy(out=P["bQT"][:, oc * 128:(oc + 1) * 128],
                              in_=psH[:, oc * 128:(oc + 1) * 128])
            v.inc_last(qV, 1)
        ct.inc(qV)
        q1v = ct.val(qV)

        @block.tensor
        def _(t, q1v=q1v):
            t = EW(t)
            t.wait_ge(qV, q1v)
            for oc in range(NDC):
                for dc in range(NDC):
                    t.matmul(psH[:, oc * 128:(oc + 1) * 128],
                             P["bWk_s"][:, dc * D + oc * 128: dc * D + oc * 128 + 128],
                             P["xnb"][:, dc * 128:(dc + 1) * 128],
                             start=(dc == 0), stop=(dc == NDC - 1))
            t.inc_last(qT, 1)
        ct.inc(qT)
        q2 = ct.val(qT)

        @block.vector
        def _(v, slot=slot, q2=q2):
            v = EW(v)
            v.wait_ge(qT, q2)
            for oc in range(NDC):
                v.tensor_copy(out=P["bKTr"][:, slot * 1024 + oc * 128:
                                            slot * 1024 + (oc + 1) * 128],
                              in_=psH[:, oc * 128:(oc + 1) * 128])
            v.inc_last(qV, 1)
        ct.inc(qV)
        q2v = ct.val(qV)

        @block.tensor
        def _(t, q2v=q2v):
            t = EW(t)
            t.wait_ge(qV, q2v)
            for nh in range(2):
                for dc in range(NDC):
                    t.matmul(psH[:, nh * 512:(nh + 1) * 512],
                             P["xnb"][:, dc * 128:(dc + 1) * 128],
                             P["bWv_s"][:, dc * D + nh * 512: dc * D + nh * 512 + 512],
                             start=(dc == 0), stop=(dc == NDC - 1))
            t.inc_last(qT, 1)
        ct.inc(qT)
        q3 = ct.val(qT)

        @block.vector
        def _(v, slot=slot, q3=q3):
            v = EW(v)
            v.wait_ge(qT, q3)
            v.tensor_copy(out=P["bVtr"][:, slot * 1024: slot * 1024 + 512],
                          in_=psH[:, 0:512])
            v.tensor_copy(out=P["bVtr"][:, slot * 1024 + 512: slot * 1024 + 1024],
                          in_=psH[:, 512:1024])
            v.inc_last(qV, 1)
        ct.inc(qV)
        QKV_V = ct.val(qV)

        def hb_params(h):
            hg = ti * 16 + h
            hs = hg % 2
            stiles = ([((ti - 1) % 2, P["maskA_s"])] if ti > 0 else []) + \
                     [(slot, P["maskB_s"])]
            return dict(hg=hg, hs=hs, oc=h // 2, pb=(h % 2) * 64,
                        stiles=stiles, nsi=len(stiles),
                        sc0=hs * 256, eb0=(hg % 4) * 256,
                        scp=(psC[:, 0:256] if hs == 0 else psY[:, 0:256]),
                        smp=(psH[0:1, 0:128] if hs == 0 else psH[0:1, 512:640]),
                        rbp=(psD[:, 0:128] if hs == 0 else psY[:, 512:640]),
                        cxp=(psY[0:64, 1024:1152] if hs == 0
                             else psY[0:64, 1536:1664]))

        def hb_scores(pp):
            hg, scp = pp["hg"], pp["scp"]

            @block.tensor
            def _(t, pp=pp):
                t = EW(t)
                t.wait_ge(qV, max(QKV_V, Hmkv.get(hg - 2, 0)))
                for si, (ss, mk) in enumerate(pp["stiles"]):
                    t.matmul(scp[:, si * 128:(si + 1) * 128],
                             P["bKTr"][pp["pb"]:pp["pb"] + 64,
                                       ss * 1024 + pp["oc"] * 128:
                                       ss * 1024 + (pp["oc"] + 1) * 128],
                             P["bQT"][pp["pb"]:pp["pb"] + 64,
                                      pp["oc"] * 128:(pp["oc"] + 1) * 128],
                             start=True, stop=True, skip_group_check=True)
                t.inc_last(qT, 1)
            ct.inc(qT)
            Hscq[hg] = ct.val(qT)

        def hb_mask(pp):
            hg, sc0, eb0, scp = pp["hg"], pp["sc0"], pp["eb0"], pp["scp"]

            @block.vector
            def _(v, pp=pp):
                v = EW(v)
                v.wait_ge(qT, Hscq[hg])
                if hg >= 2:
                    v.wait_ge(qS, Hexq[hg - 2])
                for si, (ss, mk) in enumerate(pp["stiles"]):
                    v.scalar_tensor_tensor(
                        out=P["bmsk"][:, sc0 + si * 128: sc0 + (si + 1) * 128],
                        in0=scp[:, si * 128:(si + 1) * 128],
                        scalar=SCL, in1=mk[:, :], op0=ALU.mult, op1=ALU.add)
                v.inc_last(qV, 1)
            ct.inc(qV)
            Hmkv[hg] = ct.val(qV)

        def hb_exp(pp):
            hg, sc0, eb0, nsi = pp["hg"], pp["sc0"], pp["eb0"], pp["nsi"]

            @block.scalar
            def _(s, pp=pp):
                s = EW(s)
                s.wait_ge(qV, Hmkv[hg])
                if hg >= 4:
                    s.wait_ge(qT, Hsmq[hg - 4])
                    s.wait_ge(qV, Henv[hg - 4])
                s.activation(out=P["bexp"][:, eb0: eb0 + nsi * 128],
                             in_=P["bmsk"][:, sc0: sc0 + nsi * 128],
                             func=AF.Exp)
                s.inc_last(qS, 1)
            ct.inc(qS)
            Hexq[hg] = ct.val(qS)

        def hb_sum(pp):
            hg, eb0, smp, nsi = pp["hg"], pp["eb0"], pp["smp"], pp["nsi"]

            @block.tensor
            def _(t, pp=pp):
                t = EW(t)
                t.wait_ge(qS, Hexq[hg])
                if hg >= 2:
                    t.wait_ge(qV, Hrsq[hg - 2])
                for si in range(nsi):
                    t.matmul(smp[0:1, 0:128], P["onescol_b"][:, :],
                             P["bexp"][:, eb0 + si * 128: eb0 + (si + 1) * 128],
                             start=(si == 0), stop=(si == nsi - 1),
                             skip_group_check=True)
                t.inc_last(qT, 1)
            ct.inc(qT)
            Hsmq[hg] = ct.val(qT)

        def hb_recip(pp):
            hg, hs, smp = pp["hg"], pp["hs"], pp["smp"]

            @block.vector
            def _(v, pp=pp):
                v = EW(v)
                v.wait_ge(qT, Hsmq[hg])
                v.tensor_copy(out=P["brs"][0:1, hs * 128:(hs + 1) * 128],
                              in_=smp[0:1, 0:128])
                v.drain()
                v.reciprocal(P["brs"][0:1, hs * 128:(hs + 1) * 128],
                             P["brs"][0:1, hs * 128:(hs + 1) * 128])
                v.inc_last(qV, 1)
            ct.inc(qV)
            Hrsq[hg] = ct.val(qV)

        def hb_rsb(pp):
            hg, hs, rbp = pp["hg"], pp["hs"], pp["rbp"]

            @block.tensor
            def _(t, pp=pp):
                t = EW(t)
                t.wait_ge(qV, Hrsq[hg])
                t.matmul(rbp[:, 0:128], P["onesrow_f"][0:1, :],
                         P["brs"][0:1, hs * 128:(hs + 1) * 128],
                         start=True, stop=True, skip_group_check=True)
                t.inc_last(qT, 1)
            ct.inc(qT)
            Hrbq[hg] = ct.val(qT)

        def hb_norm(pp):
            hg, eb0, rbp, hs, nsi = pp["hg"], pp["eb0"], pp["rbp"], pp["hs"], pp["nsi"]

            @block.vector
            def _(v, pp=pp):
                v = EW(v)
                v.wait_ge(qT, Hrbq[hg])
                for si in range(nsi):
                    v.tensor_tensor(
                        out=P["bexn"][:, hs * 256 + si * 128: hs * 256 + (si + 1) * 128],
                        in0=P["bexp"][:, eb0 + si * 128: eb0 + (si + 1) * 128],
                        in1=rbp[:, 0:128], op=ALU.mult)
                v.inc_last(qV, 1)
            ct.inc(qV)
            Henv[hg] = ct.val(qV)

        def hb_ctx(pp):
            hg, hs, cxp = pp["hg"], pp["hs"], pp["cxp"]

            @block.tensor
            def _(t, pp=pp):
                t = EW(t)
                t.wait_ge(qV, Henv[hg])
                for si, (ss, mk) in enumerate(pp["stiles"]):
                    t.matmul(cxp[0:64, 0:128],
                             P["bVtr"][:, ss * 1024 + pp["oc"] * 128 + pp["pb"]:
                                       ss * 1024 + pp["oc"] * 128 + pp["pb"] + 64],
                             P["bexn"][:, hs * 256 + si * 128: hs * 256 + (si + 1) * 128],
                             start=(si == 0), stop=(si == pp["nsi"] - 1),
                             skip_group_check=True)
                t.inc_last(qT, 1)
            ct.inc(qT)
            Hcxq[hg] = ct.val(qT)

        def hb_evac(pp):
            hg, cxp = pp["hg"], pp["cxp"]

            @block.vector
            def _(v, pp=pp):
                v = EW(v)
                v.wait_ge(qT, Hcxq[hg])
                v.tensor_copy(out=P["bctx"][pp["pb"]:pp["pb"] + 64,
                                            pp["oc"] * 128:(pp["oc"] + 1) * 128],
                              in_=cxp[0:64, 0:128])
                v.inc_last(qV, 1)
            ct.inc(qV)

        PPs = [hb_params(h) for h in range(16)]
        for k in range(19):
            if k <= 15:
                hb_scores(PPs[k])
                hb_mask(PPs[k])
                hb_exp(PPs[k])
            if 1 <= k <= 16:
                hb_sum(PPs[k - 1])
                hb_recip(PPs[k - 1])
            if 2 <= k <= 17:
                hb_rsb(PPs[k - 2])
                hb_norm(PPs[k - 2])
            if 3 <= k <= 18:
                hb_ctx(PPs[k - 3])
                hb_evac(PPs[k - 3])
        CTX_V = ct.val(qV)
        for dc in range(NDC):
            @block.tensor
            def _(t, dc=dc, t0=t0, w=None):
                t = EW(t)
                t.wait_ge(qV, ct.val(qV))
                for oc in range(NDC):
                    t.matmul(psA[:, 0:128],
                             P["bWo_s"][:, oc * D + dc * 128: oc * D + dc * 128 + 128],
                             P["bctx"][:, oc * 128:(oc + 1) * 128],
                             start=(oc == 0), stop=(oc == NDC - 1))
                t.inc_last(qT, 1)
            ct.inc(qT)
            opq = ct.val(qT)

            @block.vector
            def _(v, dc=dc, t0=t0, opq=opq):
                v = EW(v)
                v.wait_ge(qT, opq)
                v.scalar_tensor_tensor(out=P["outst"][:, dc * 128:(dc + 1) * 128],
                                       in0=psA[:, 0:128], scalar=1.0,
                                       in1=x_sb[:, dc * NT + t0: dc * NT + t0 + 128],
                                       op0=ALU.mult, op1=ALU.add)
                v.inc_last(qV, 1)
            ct.inc(qV)
        OP_V = ct.val(qV)

        @block.vector
        def _(v, t0=t0, OP_V=OP_V):
            v = EW(v)
            v.wait_ge(qV, OP_V)
            for dc in range(NDC):
                v.tensor_copy(out=x_sb[:, dc * NT + t0: dc * NT + t0 + 128],
                              in_=P["outst"][:, dc * 128:(dc + 1) * 128])
            v.inc_last(qV, 1)
        ct.inc(qV)

    sc.close()


def _prep_inputs(inputs, b, T):
    f32 = np.float32
    x = np.ascontiguousarray(inputs["L0_attn_a_x"][b].T.astype(f32))
    d = {
        "x": x,
        "aWq": np.ascontiguousarray(inputs["L0_attn_a_Wq"].T),
        "aWk": np.ascontiguousarray(inputs["L0_attn_a_Wk"].T),
        "aWv": np.ascontiguousarray(inputs["L0_attn_a_Wv"].T),
        "aWo": np.ascontiguousarray(inputs["L0_attn_a_Wo"].T),
        "abq": inputs["L0_attn_a_bq"].reshape(H, 1),
        "abk": inputs["L0_attn_a_bk"].reshape(H, 1),
        "abv": inputs["L0_attn_a_bv"].reshape(1, H),
        "abo": np.ascontiguousarray(inputs["L0_attn_a_bo"].reshape(NDC, 128).T),
        "bWq": inputs["L0_attn_b_Wq"],
        "bWk": inputs["L0_attn_b_Wk"],
        "bWv": inputs["L0_attn_b_Wv"],
        "bWo": inputs["L0_attn_b_Wo"],
        "rW": np.ascontiguousarray(inputs["L0_ffn_w_router"].T),
        "W1T": np.ascontiguousarray(inputs["L0_ffn_W1"].transpose(0, 2, 1)),
        "W2T": np.ascontiguousarray(inputs["L0_ffn_W2"].transpose(0, 2, 1)),
        "Ws1T": np.ascontiguousarray(inputs["L0_ffn_Ws1"].T),
        "Ws2T": np.ascontiguousarray(inputs["L0_ffn_Ws2"].T),
        "b1fl": np.concatenate([np.asarray(inputs["L0_ffn_b1"]).reshape(-1),
                                np.asarray(inputs["L0_ffn_bs1"]).reshape(-1)]).reshape(1, 9 * DFF),
        "b2": inputs["L0_ffn_b2"],
        "bs2T": np.ascontiguousarray(inputs["L0_ffn_bs2"].reshape(NDC, 128).T),
    }
    return {k: np.ascontiguousarray(v).astype(f32) for k, v in d.items()}


_NC_CACHE = {}


def kernel(**inputs):
    T = inputs["L0_attn_a_x"].shape[1]
    B = inputs["L0_attn_a_x"].shape[0]
    if T not in _NC_CACHE:
        _NC_CACHE[T] = build_nc(T)
    nc = _NC_CACHE[T]
    in_maps = [_prep_inputs(inputs, b, T) for b in range(B)]
    res = run_bass_kernel_spmd(nc, in_maps, core_ids=list(range(B)))
    out = np.stack([np.ascontiguousarray(r["y"].T) for r in res.results], axis=0)
    return out.astype(np.float32)


if __name__ == "__main__":
    import reference as _r
    ins = _r.setup_inputs(0)
    ins = {k: np.asarray(v) for k, v in ins.items()}
    out = kernel(**ins)
    exp = np.asarray(_r.reference(**ins))
    err = np.linalg.norm(out - exp) / np.linalg.norm(exp)
    print("rel err:", err)


# version 18
# speedup vs baseline: 1.2961x; 1.0017x over previous
import math
import contextlib
import numpy as np
import ml_dtypes

import concourse.bass as bass
import concourse.mybir as mybir
from concourse.bass_utils import run_bass_kernel_spmd

F32 = mybir.dt.float32
BF16 = mybir.dt.bfloat16
AF = mybir.ActivationFunctionType
ALU = mybir.AluOpType
AX = mybir.AxisListType
ts = bass.ts

D = 1024
NDC = 8
H = 128
NE = 8
DFF = 4096
FC = 256
NM = 144
BLK = 256
EPS_LN = 1e-5
EPS_U = 1e-6
SCALE_FFN = 1.0 / math.sqrt(1.0 + 2.0 / 8.0)
GC1 = 2.0 * math.sqrt(2.0 / math.pi)
GC2 = 2.0 * math.sqrt(2.0 / math.pi) * 0.044715
NEGBIG = -1e30
SM_SHIFT = -10.0


class Ctr:
    def __init__(self):
        self.c = {}

    @staticmethod
    def _k(sem):
        return getattr(sem, "name", None) or id(sem)

    def inc(self, sem, n=1):
        k = self._k(sem)
        self.c[k] = self.c.get(k, 0) + n
        return self.c[k]

    def val(self, sem):
        return self.c.get(self._k(sem), 0)




import bass_rust as _br

_KF_UID = [0]


@contextlib.contextmanager
def kreg(eng, base=0):
    _KF_UID[0] += 1
    r = eng.alloc_register(f"kr{_KF_UID[0]}")
    try:
        yield r
    finally:
        eng.free_register(r)


@contextlib.contextmanager
def kfori(nc, eng, etype, start, end, step=1):
    _KF_UID[0] += 1
    name = f"kf{_KF_UID[0]}"
    reg = eng.alloc_register(f"{name}_i")
    rh = _br.RegisterHandles((reg,))
    eng.reg_mov(reg, start)
    nc.br(f"{name}_L", engines=[etype])
    with nc.body(f"{name}_L", valid_engines=[etype]):
        yield reg
        eng.reg_add(reg, reg, step)
        nc.br_lt(rh, end, on_true=f"{name}_L", on_false=f"{name}_E", engines=[etype])
    nc.switch_bb(f"{name}_E")
    eng.free_register(reg)


_RV_CACHE = {}


def rowslice(reg, nrows, maxrow):
    key = (reg.name, nrows, maxrow)
    if key not in _RV_CACHE:
        rv = _br.RuntimeValue(_br.RegisterHandles((reg,)), 0, maxrow, nrows, 0)
        _RV_CACHE[key] = _br.DynSlice(rv, nrows)
    return _RV_CACHE[key]




class EW:

    def __init__(self, eng):
        object.__setattr__(self, "_e", eng)
        object.__setattr__(self, "last", None)

    def __getattr__(self, n):
        a = getattr(self._e, n)
        if callable(a):
            def w(*args, **kw):
                r = a(*args, **kw)
                if hasattr(r, "then_inc"):
                    object.__setattr__(self, "last", r)
                return r
            return w
        return a

    def inc_last(self, q, n=1):
        self.last.then_inc(q, n)


def build_nc(T=4096):
    NT = T
    NG = T // 512
    NBLK = T // BLK
    NTIL = T // 128
    NCH = T // 128

    nc = bass.Bass("TRN2", target_bir_lowering=False, debug=False)

    x_in = nc.dram_tensor("x", [D, NT], F32, kind="ExternalInput").ap()
    y_out = nc.dram_tensor("y", [D, NT], F32, kind="ExternalOutput").ap()

    aWq = nc.dram_tensor("aWq", [D, H], F32, kind="ExternalInput").ap()
    aWk = nc.dram_tensor("aWk", [D, H], F32, kind="ExternalInput").ap()
    aWv = nc.dram_tensor("aWv", [D, H], F32, kind="ExternalInput").ap()
    aWo = nc.dram_tensor("aWo", [H, D], F32, kind="ExternalInput").ap()
    abq = nc.dram_tensor("abq", [H, 1], F32, kind="ExternalInput").ap()
    abk = nc.dram_tensor("abk", [H, 1], F32, kind="ExternalInput").ap()
    abv = nc.dram_tensor("abv", [1, H], F32, kind="ExternalInput").ap()
    abo = nc.dram_tensor("abo", [128, NDC], F32, kind="ExternalInput").ap()

    bWq = nc.dram_tensor("bWq", [D, D], F32, kind="ExternalInput").ap()
    bWk = nc.dram_tensor("bWk", [D, D], F32, kind="ExternalInput").ap()
    bWv = nc.dram_tensor("bWv", [D, D], F32, kind="ExternalInput").ap()
    bWo = nc.dram_tensor("bWo", [D, D], F32, kind="ExternalInput").ap()

    rW = nc.dram_tensor("rW", [D, NE], F32, kind="ExternalInput").ap()
    W1T = nc.dram_tensor("W1T", [NE, D, DFF], F32, kind="ExternalInput").ap()
    W2T = nc.dram_tensor("W2T", [NE, DFF, D], F32, kind="ExternalInput").ap()
    Ws1T = nc.dram_tensor("Ws1T", [D, DFF], F32, kind="ExternalInput").ap()
    Ws2T = nc.dram_tensor("Ws2T", [DFF, D], F32, kind="ExternalInput").ap()
    b1fl = nc.dram_tensor("b1fl", [1, 9 * DFF], F32, kind="ExternalInput").ap()
    b2 = nc.dram_tensor("b2", [NE, D], F32, kind="ExternalInput").ap()
    bs2T = nc.dram_tensor("bs2T", [128, NDC], F32, kind="ExternalInput").ap()

    c_idf = nc.inline_tensor(np.eye(128, dtype=np.float32), "c_idf").ap()
    c_idb = nc.inline_tensor(np.eye(128).astype(ml_dtypes.bfloat16), "c_idb").ap()
    c_onescol_b = nc.inline_tensor(np.ones((128, 1), ml_dtypes.bfloat16), "c_onescol_b").ap()
    c_onescol_f = nc.inline_tensor(np.ones((128, 1), np.float32), "c_onescol_f").ap()
    c_onesrow_b = nc.inline_tensor(np.ones((128, 512), ml_dtypes.bfloat16), "c_onesrow_b").ap()
    c_onesrow_f = nc.inline_tensor(np.ones((128, 128), np.float32), "c_onesrow_f").ap()
    tri = (np.arange(128)[:, None] <= np.arange(128)[None, :]).astype(np.float32)
    c_triu = nc.inline_tensor(tri, "c_triu").ap()
    s_i = np.arange(128)[:, None]
    t_i = np.arange(128)[None, :]
    mB = np.where((s_i <= t_i) & (s_i >= t_i - 63), SM_SHIFT, NEGBIG).astype(np.float32)
    mA = np.where(s_i >= t_i + 65, SM_SHIFT, NEGBIG).astype(np.float32)
    c_maskB = nc.inline_tensor(mB, "c_maskB").ap()
    c_maskA = nc.inline_tensor(mA, "c_maskA").ap()
    sel = np.zeros((8, 8 * 128), np.float32)
    for e in range(8):
        sel[e, e * 128:(e + 1) * 128] = 1.0
    c_selE = nc.inline_tensor(sel, "c_selE").ap()

    W1bf = nc.dram_tensor("W1bf", [NM * 128, 2304], BF16, kind="Internal").ap()
    W2bf = nc.dram_tensor("W2bf", [NM * 128, 2048], BF16, kind="Internal").ap()

    es = contextlib.ExitStack()

    x_sb = nc.alloc_sbuf_tensor("x_sb", [128, NDC * NT], BF16)
    rW_s = nc.alloc_sbuf_tensor("rW_s", [128, NDC * NE], BF16)
    b2_s = nc.alloc_sbuf_tensor("b2_s", [8, D], F32)
    bs2T_s = nc.alloc_sbuf_tensor("bs2T_s", [128, NDC], F32)
    idf_s = nc.alloc_sbuf_tensor("idf_s", [128, 128], F32)
    idb_s = nc.alloc_sbuf_tensor("idb_s", [128, 128], BF16)
    onescol_b = nc.alloc_sbuf_tensor("onescol_b", [128, 1], BF16)
    onescol_f = nc.alloc_sbuf_tensor("onescol_f", [128, 1], F32)
    onesrow_b = nc.alloc_sbuf_tensor("onesrow_b", [128, 512], BF16)
    onesrow_f = nc.alloc_sbuf_tensor("onesrow_f", [128, 128], F32)
    triu_s = nc.alloc_sbuf_tensor("triu_s", [128, 128], F32)
    maskB_s = nc.alloc_sbuf_tensor("maskB_s", [128, 128], F32)
    maskA_s = nc.alloc_sbuf_tensor("maskA_s", [128, 128], F32)
    selE_s = nc.alloc_sbuf_tensor("selE_s", [8, 8 * 128], F32)
    statrows = nc.alloc_sbuf_tensor("statrows", [128, 1024], F32)
    meanrow = statrows[0:1, 0:512]
    varrow = statrows[0:1, 512:1024]
    rstdrow = statrows[32:33, 0:512]

    psH = nc.alloc_psum_tensor("psH", [128, 1024], F32)
    psC = nc.alloc_psum_tensor("psC", [128, 512], F32)
    psD = nc.alloc_psum_tensor("psD", [128, 512], F32)
    psY = nc.alloc_psum_tensor("psY", [128, 2048], F32)
    psA = psH[:, 0:512]
    psB = psH[:, 512:1024]

    qGP = es.enter_context(nc.semaphore("qGP"))
    qSTRM = {tg: tuple(es.enter_context(nc.semaphore(f"q{w}s{sl}{tg}"))
                       for w in ("W1", "W2") for sl in range(4))
             for tg in ("A", "B")}
    qSY = es.enter_context(nc.semaphore("qSY"))
    qT = es.enter_context(nc.semaphore("qT"))
    qV = es.enter_context(nc.semaphore("qV"))
    qS = es.enter_context(nc.semaphore("qS"))
    qP = es.enter_context(nc.semaphore("qP"))
    qTh = es.enter_context(nc.semaphore("qTh"))
    qTw2 = es.enter_context(nc.semaphore("qTw2"))
    qSq = es.enter_context(nc.semaphore("qSq"))
    qSg = es.enter_context(nc.semaphore("qSg"))
    qVp = es.enter_context(nc.semaphore("qVp"))
    qVu = es.enter_context(nc.semaphore("qVu"))
    qVg = es.enter_context(nc.semaphore("qVg"))
    qPg = es.enter_context(nc.semaphore("qPg"))
    qST = es.enter_context(nc.semaphore("qST"))
    qTs = es.enter_context(nc.semaphore("qTs"))
    qVs = es.enter_context(nc.semaphore("qVs"))
    qSs = es.enter_context(nc.semaphore("qSs"))

    ct = Ctr()
    cst = nc.alloc_sbuf_tensor("cst", [128, 2], F32)

    ov = contextlib.ExitStack()

    def ovt(name, shape, dt):
        return ov.enter_context(nc.sbuf_tensor(name, shape, dt))

    KT = ovt("KT", [128, NT], F32)
    Wbuf = KT
    UT = ovt("UT", [128, NT], F32)
    QTb = ovt("QTb", [128, NT], BF16)
    Vtm = ovt("Vtm", [128, NT], BF16)
    alphaT = ovt("alphaT", [128, NT], BF16)
    OT = UT.bitcast(BF16)[:, 0:NT]
    xn_grp = ovt("xn_grp", [128, NDC * 512], BF16)
    sq_scr = ovt("sq_scr", [128, NDC * 512], BF16)
    sqf = ovt("sqf", [128, 512], F32)
    rowsbig = ovt("rowsbig", [128, 2 * NT], F32)
    rdrow = rowsbig[0:1, 0:NT]
    srow = rowsbig[0:1, NT:2 * NT]
    n2row = rowsbig[64:65, 0:NT]
    arow = rowsbig[64:65, 0:NT]
    Asb = ovt("Asb", [128, 128], F32)
    ypair_sb = ovt("ypair_sb", [128, 2], F32)
    scs = ovt("scs", [1, 784], F32)
    scrow_b = ovt("scrow_b", [1, NT], BF16)
    Mst_sb = ovt("Mst_sb", [128, 128], BF16)
    atm = ovt("atm", [128, 128], BF16)
    psm_sb = ovt("psm_sb", [128, 128], BF16)
    aWq_s = ovt("aWq_s", [128, NDC * H], BF16)
    aWk_s = ovt("aWk_s", [128, NDC * H], BF16)
    aWv_s = ovt("aWv_s", [128, NDC * H], BF16)
    aWo_s = ovt("aWo_s", [128, D], BF16)
    abq_s = ovt("abq_s", [128, 1], F32)
    abk_s = ovt("abk_s", [128, 1], F32)
    abv_s = ovt("abv_s", [1, H], BF16)
    abo_s = ovt("abo_s", [128, NDC], F32)

    LN = dict(cst=cst, onescol_b=onescol_b, onesrow_f=onesrow_f, meanrow=meanrow,
              varrow=varrow, rstdrow=rstdrow, psA=psA, psB=psB, psC=psC, psD=psD,
              qT=qT, qV=qV, qS=qS)

    with nc.Block() as block:
        @block.vector
        def _(v):
            v = EW(v)
            v.memset(cst[:, 0:1], -1.0)
            v.memset(cst[:, 1:2], EPS_LN)
            v.inc_last(qV, 1)
        ct.inc(qV)

        @block.gpsimd
        def _(g):
            g = EW(g)
            g.dma_start(out=x_sb[:, :].rearrange("p (dc t) -> p dc t", dc=NDC),
                        in_=x_in.rearrange("(dc p) t -> p dc t", p=128)).then_inc(qGP, 16)
            g.dma_start(out=aWq_s[:, :].rearrange("p (dc h) -> p dc h", dc=NDC),
                        in_=aWq.rearrange("(dc p) h -> p dc h", p=128)).then_inc(qGP, 16)
            g.dma_start(out=aWk_s[:, :].rearrange("p (dc h) -> p dc h", dc=NDC),
                        in_=aWk.rearrange("(dc p) h -> p dc h", p=128)).then_inc(qGP, 16)
            g.dma_start(out=aWv_s[:, :].rearrange("p (dc h) -> p dc h", dc=NDC),
                        in_=aWv.rearrange("(dc p) h -> p dc h", p=128)).then_inc(qGP, 16)
            g.dma_start(out=aWo_s[:, :], in_=aWo[:, :]).then_inc(qGP, 16)
            g.dma_start(out=abv_s[:, :], in_=abv[:, :]).then_inc(qGP, 16)
            g.dma_start(out=rW_s[:, :].rearrange("p (dc e) -> p dc e", dc=NDC),
                        in_=rW.rearrange("(dc p) e -> p dc e", p=128)).then_inc(qGP, 16)
            ct.inc(qGP, 16 * 7)

        @block.sync
        def _(sy):
            sy = EW(sy)
            for dst, src in ((abq_s, abq), (abk_s, abk), (abo_s, abo), (b2_s, b2),
                             (bs2T_s, bs2T), (idf_s, c_idf), (idb_s, c_idb),
                             (onescol_b, c_onescol_b), (onescol_f, c_onescol_f),
                             (onesrow_b, c_onesrow_b), (onesrow_f, c_onesrow_f),
                             (triu_s, c_triu), (maskB_s, c_maskB), (maskA_s, c_maskA),
                             (selE_s, c_selE)):
                sy.dma_start(out=dst[:, :], in_=src[:, :]).then_inc(qSY, 16)
                ct.inc(qSY, 16)
        LOADS_SY = ct.val(qSY)
        LOADS_GP = ct.val(qGP)

        @block.gpsimd
        def _(g):
            g = EW(g)
            g.wait_ge(qGP, LOADS_GP)
            for m in range(NM):
                e, fc = m // 16, m % 16
                if e < 8:
                    src1 = bass.AP(W1T.tensor, e * D * DFF + fc * FC,
                                   [[DFF, 128], [128 * DFF, NDC], [1, FC]])
                    src2 = bass.AP(W2T.tensor, e * DFF * D + fc * FC * D,
                                   [[D, 128], [128 * D, 2], [1, D]])
                else:
                    src1 = bass.AP(Ws1T.tensor, fc * FC,
                                   [[DFF, 128], [128 * DFF, NDC], [1, FC]])
                    src2 = bass.AP(Ws2T.tensor, fc * FC * D,
                                   [[D, 128], [128 * D, 2], [1, D]])
                d1 = W1bf[m * 128:(m + 1) * 128, 0:2048].rearrange("p (a b) -> p a b", a=NDC)
                d2 = W2bf[m * 128:(m + 1) * 128, :].rearrange("p (a b) -> p a b", a=2)
                g.dma_start(out=d1, in_=src1).then_inc(qGP, 16)
                g.dma_start(out=d2, in_=src2).then_inc(qGP, 16)
                g.dma_start(out=W1bf[m * 128:(m + 1) * 128, 2048:2304],
                            in_=bass.AP(b1fl.tensor, m * FC, [[0, 128], [1, FC]])).then_inc(qGP, 16)
                ct.inc(qGP, 48)
        PRECAST_GP = ct.val(qGP)

        for gi in range(NG):
            t0 = gi * 512
            pre = [(qGP, LOADS_GP), (qSY, LOADS_SY)] if gi == 0 else None
            emit_ln(nc, block, ct, x_sb, xn_grp, sq_scr, t0, 512, NT, LN, pre_waits=pre)

            @block.tensor
            def _(t, t0=t0):
                t = EW(t)
                t.wait_ge(qV, ct.val(qV))
                if t0 == 0:
                    t.wait_ge(qGP, LOADS_GP)
                for dc in range(NDC):
                    t.matmul(psA[:, 0:512], aWq_s[:, dc * H:(dc + 1) * H],
                             xn_grp[:, dc * 512:(dc + 1) * 512],
                             start=(dc == 0), stop=(dc == NDC - 1))
                t.inc_last(qT, 1)
                for dc in range(NDC):
                    t.matmul(psB[:, 0:512], aWk_s[:, dc * H:(dc + 1) * H],
                             xn_grp[:, dc * 512:(dc + 1) * 512],
                             start=(dc == 0), stop=(dc == NDC - 1))
                t.inc_last(qT, 1)
                for ti in range(4):
                    for dc in range(NDC):
                        t.matmul(psC[:, ti * 128:(ti + 1) * 128],
                                 xn_grp[:, dc * 512 + ti * 128: dc * 512 + (ti + 1) * 128],
                                 aWv_s[:, dc * H:(dc + 1) * H],
                                 start=(dc == 0), stop=False)
                    t.matmul(psC[:, ti * 128:(ti + 1) * 128], onesrow_b[0:1, 0:128],
                             abv_s[:, :], start=False, stop=True)
                t.inc_last(qT, 1)
            ct.inc(qT, 3)
            tq = ct.val(qT)

            @block.vector
            def _(v, t0=t0, tq=tq):
                v = EW(v)
                v.wait_ge(qT, tq - 2)
                v.tensor_scalar(out=QTb[:, t0:t0 + 512], in0=psA[:, 0:512],
                                scalar1=abq_s[:, :], scalar2=None, op0=ALU.add)
                v.wait_ge(qT, tq - 1)
                v.tensor_scalar(out=KT[:, t0:t0 + 512], in0=psB[:, 0:512],
                                scalar1=abk_s[:, :], scalar2=None, op0=ALU.add)
                v.wait_ge(qT, tq)
                v.tensor_copy(out=Vtm[:, t0:t0 + 512], in_=psC[:, 0:512])
                v.inc_last(qV, 1)
            ct.inc(qV)
            vq = ct.val(qV)

            @block.scalar
            def _(s, t0=t0, vq=vq):
                s = EW(s)
                s.wait_ge(qV, vq)
                s.activation(out=sqf[:, :], in_=KT[:, t0:t0 + 512], func=AF.Square)
                s.inc_last(qS, 1)
            ct.inc(qS)
            sq1 = ct.val(qS)

            @block.tensor
            def _(t, sq1=sq1):
                t = EW(t)
                t.wait_ge(qS, sq1)
                t.matmul(psA[0:1, 0:512], onescol_f[:, :], sqf[:, :], start=True, stop=True)
                t.inc_last(qT, 1)
            ct.inc(qT)
            n2q = ct.val(qT)

            @block.vector
            def _(v, t0=t0, n2q=n2q):
                v = EW(v)
                v.wait_ge(qT, n2q)
                v.tensor_copy(out=n2row[0:1, t0:t0 + 512], in_=psA[0:1, 0:512])
                v.tensor_tensor(out=sqf[:, :], in0=KT[:, t0:t0 + 512],
                                in1=QTb[:, t0:t0 + 512], op=ALU.mult)
                v.inc_last(qV, 1)
            ct.inc(qV)
            qkv = ct.val(qV)

            @block.tensor
            def _(t, qkv=qkv):
                t = EW(t)
                t.wait_ge(qV, qkv)
                t.matmul(psB[0:1, 0:512], onescol_f[:, :], sqf[:, :], start=True, stop=True)
                t.inc_last(qT, 1)
            ct.inc(qT)
            sq2 = ct.val(qT)

            @block.vector
            def _(v, t0=t0, sq2=sq2):
                v = EW(v)
                v.wait_ge(qT, sq2)
                v.tensor_copy(out=srow[0:1, t0:t0 + 512], in_=psB[0:1, 0:512])
                v.inc_last(qV, 1)
            ct.inc(qV)

        @block.scalar
        def _(s):
            s = EW(s)
            s.wait_ge(qV, ct.val(qV))
            s.activation(out=n2row[0:1, 0:NT], in_=n2row[0:1, 0:NT], func=AF.Sqrt)
            s.inc_last(qS, 1)
        ct.inc(qS)
        nq = ct.val(qS)

        @block.vector
        def _(v, nq=nq):
            v = EW(v)
            v.wait_ge(qS, nq)
            v.tensor_scalar(out=n2row[0:1, 0:NT], in0=n2row[0:1, 0:NT],
                            scalar1=EPS_U, scalar2=None, op0=ALU.add)
            v.drain()
            v.reciprocal(n2row[0:1, 0:NT], n2row[0:1, 0:NT])
            v.inc_last(qV, 1)
        ct.inc(qV)
        rno = ct.val(qV)

        for gi in range(NG):
            t0 = gi * 512

            @block.tensor
            def _(t, t0=t0, gi=gi, rno=rno):
                t = EW(t)
                t.wait_ge(qV, rno + gi)
                t.matmul(psA[:, 0:512], onesrow_f[64:65, :], n2row[0:1, t0:t0 + 512],
                         start=True, stop=True)
                t.inc_last(qT, 1)
            ct.inc(qT)
            bq_ = ct.val(qT)

            @block.vector
            def _(v, t0=t0, bq_=bq_):
                v = EW(v)
                v.wait_ge(qT, bq_)
                v.tensor_tensor(out=UT[:, t0:t0 + 512], in0=KT[:, t0:t0 + 512],
                                in1=psA[:, 0:512], op=ALU.mult)
                v.inc_last(qV, 1)
            ct.inc(qV)
        UT_V = ct.val(qV)

        A_ps = psY[:, 0:128]
        w_ps2 = psY[:, 512:514]
        g11_ps = psY[0:1, 1024:1025]
        g22_ps = psY[0:1, 1025:1026]
        gc_ps = psY[0:1, 1026:1027]
        yr1_ps = psY[0:1, 1536:1664]
        yr2_ps = psY[0:1, 1664:1792]
        yr12_ps = psY[0:1, 1536:1792]
        y12r = scs[0:1, 0:256]
        q1 = scs[0:1, 256:384]
        q2 = scs[0:1, 384:512]
        q3 = scs[0:1, 512:640]
        q4 = scs[0:1, 640:768]
        p1r = q1
        p2r = q3
        zrow = q1
        bb2 = scs[0:1, 768:770]
        nrow = scs[0:1, 770:772]
        rpr = scs[0:1, 772:774]
        m1s = scs[0:1, 774:775]
        m2s = scs[0:1, 775:776]
        s22v = scs[0:1, 776:777]
        sc1 = scs[0:1, 777:778]
        csb = scs[0:1, 778:779]
        s11v = scs[0:1, 779:780]
        NP = NT // 2

        @block.vector
        def _(v):
            v = EW(v)
            v.wait_ge(qV, UT_V)
            v.memset(scrow_b[0:1, :], 0.0)
            v.memset(zrow, 0.0)
            v.tensor_copy(out=Asb[:, :], in_=idf_s[:, :])
            v.inc_last(qVs, 1)

        NSEG = min(NG, 4)
        NPS = NP // NSEG
        PPG = NP // NG
        SCANR = {}
        psBb = psH.bitcast(BF16)

        def scan_seg_T(seg):
            @block.tensor
            def _(t, seg=seg):
                t = EW(t)
                if seg == 0:
                    t.wait_ge(qVs, 1)
                    t.matmul(A_ps, zrow, zrow, start=True, stop=False,
                             skip_group_check=True)
                    rW = t.alloc_register("sT_rW")
                    rTi = t.alloc_register("sT_rTi")
                    rTi1 = t.alloc_register("sT_rTi1")
                    t.reg_mov(rW, 1)
                    t.reg_mov(rTi, 0)
                    t.reg_mov(rTi1, 1)
                    SCANR["T"] = (rW, rTi, rTi1)
                rW, rTi, rTi1 = SCANR["T"]
                with kfori(nc, t, mybir.EngineType.PE, 0, NPS, 1):
                    t.wait_ge(qVs, rW)
                    t.matmul(w_ps2, Asb[:, :], UT[:, rowslice(rTi, 2, NT - 2)],
                             start=True, stop=True).then_inc(qTs, 1)
                    t.reg_add(rW, rW, 1)
                    t.wait_ge(qVs, rW)
                    t.transpose(yr1_ps, ypair_sb[:, 0:1], idf_s[:, :])
                    t.transpose(yr2_ps, ypair_sb[:, 1:2], idf_s[:, :])
                    t.matmul(g11_ps, ypair_sb[:, 0:1],
                             UT[:, rowslice(rTi, 1, NT - 1)], start=True, stop=True)
                    t.matmul(g22_ps, ypair_sb[:, 1:2],
                             UT[:, rowslice(rTi1, 1, NT - 1)], start=True, stop=True)
                    t.matmul(gc_ps, ypair_sb[:, 0:1],
                             UT[:, rowslice(rTi1, 1, NT - 1)], start=True,
                             stop=True).then_inc(qTs, 1)
                    t.reg_add(rW, rW, 1)
                    t.wait_ge(qVs, rW)
                    t.matmul(A_ps, p1r, y12r[0:1, 0:128], start=False, stop=False,
                             skip_group_check=True)
                    t.matmul(A_ps, p2r, y12r[0:1, 128:256], start=False, stop=False,
                             skip_group_check=True).then_inc(qTs, 1)
                    t.reg_add(rW, rW, 1)
                    t.reg_add(rTi, rTi, 2)
                    t.reg_add(rTi1, rTi1, 2)
            ct.inc(qTs, 3 * NPS)

        def scan_seg_V(seg):
            @block.vector
            def _(v, seg=seg):
                v = EW(v)
                if seg == 0:
                    rT = v.alloc_register("sV_rT")
                    rVi = v.alloc_register("sV_rVi")
                    rVi1 = v.alloc_register("sV_rVi1")
                    v.reg_mov(rT, 1)
                    v.reg_mov(rVi, 0)
                    v.reg_mov(rVi1, 1)
                    SCANR["V"] = (rT, rVi, rVi1)
                rT, rVi, rVi1 = SCANR["V"]
                with kfori(nc, v, mybir.EngineType.DVE, 0, NPS, 1):
                    v.wait_ge(qTs, rT)
                    v.tensor_copy(out=ypair_sb[:, :], in_=w_ps2).then_inc(qVs, 1)
                    v.reg_add(rT, rT, 1)
                    v.wait_ge(qTs, rT)
                    v.tensor_copy(out=y12r, in_=yr12_ps)
                    v.tensor_scalar(out=bb2, in0=psY[0:1, 1024:1026], scalar1=1.0,
                                    scalar2=None, op0=ALU.add)
                    v.tensor_copy(out=csb, in_=gc_ps)
                    v.drain()
                    v.tensor_tensor(out=m2s, in0=csb, in1=csb, op=ALU.mult)
                    v.tensor_tensor(out=m1s, in0=bb2[0:1, 0:1], in1=bb2[0:1, 1:2],
                                    op=ALU.mult)
                    v.tensor_scalar(out=nrow[0:1, 0:1], in0=bb2[0:1, 0:1],
                                    scalar1=-1.0, scalar2=None, op0=ALU.mult)
                    v.drain()
                    v.scalar_tensor_tensor(out=nrow[0:1, 1:2], in0=m1s, scalar=-1.0,
                                           in1=m2s, op0=ALU.mult, op1=ALU.add)
                    v.drain()
                    v.reciprocal(rpr, nrow)
                    v.drain()
                    v.tensor_scalar(out=sc1, in0=gc_ps, scalar1=rpr[0:1, 1:2],
                                    scalar2=None, op0=ALU.mult)
                    v.tensor_scalar(out=s11v, in0=bb2[0:1, 0:1],
                                    scalar1=rpr[0:1, 1:2], scalar2=None, op0=ALU.mult)
                    v.tensor_scalar(out=s22v, in0=bb2[0:1, 1:2],
                                    scalar1=rpr[0:1, 1:2], scalar2=None, op0=ALU.mult)
                    v.drain()
                    v.tensor_scalar(out=q1, in0=y12r[0:1, 0:128],
                                    scalar1=s22v[0:1, 0:1], scalar2=None, op0=ALU.mult)
                    v.tensor_scalar(out=q2, in0=y12r[0:1, 128:256],
                                    scalar1=sc1[0:1, 0:1],
                                    scalar2=None, op0=ALU.mult)
                    v.tensor_scalar(out=q3, in0=y12r[0:1, 128:256],
                                    scalar1=s11v[0:1, 0:1],
                                    scalar2=None, op0=ALU.mult)
                    v.tensor_scalar(out=q4, in0=y12r[0:1, 0:128],
                                    scalar1=sc1[0:1, 0:1],
                                    scalar2=None, op0=ALU.mult)
                    v.drain()
                    v.tensor_tensor(out=p1r, in0=q1, in1=q2, op=ALU.subtract)
                    v.tensor_tensor(out=p2r, in0=q3, in1=q4,
                                    op=ALU.subtract).then_inc(qVs, 1)
                    v.tensor_copy(out=rdrow[0:1, rowslice(rVi, 1, NT - 1)],
                                  in_=rpr[0:1, 0:1])
                    v.tensor_copy(out=rdrow[0:1, rowslice(rVi1, 1, NT - 1)],
                                  in_=s11v)
                    v.tensor_copy(out=scrow_b[0:1, rowslice(rVi1, 1, NT - 1)],
                                  in_=sc1)
                    v.reg_add(rT, rT, 1)
                    v.wait_ge(qTs, rT)
                    v.scalar_tensor_tensor(out=Asb[:, :], in0=A_ps, scalar=1.0,
                                           in1=idf_s[:, :], op0=ALU.mult,
                                           op1=ALU.add)
                    v.tensor_copy(out=Wbuf[:, rowslice(rVi, 2, NT - 2)],
                                  in_=ypair_sb[:, :]).then_inc(qVs, 1)
                    v.reg_add(rT, rT, 1)
                    v.reg_add(rVi, rVi, 2)
                    v.reg_add(rVi1, rVi1, 2)
            ct.inc(qVs, 3 * NPS)

        def emit_group(gi):
            t0 = gi * 512
            qvs_need = 1 + 3 * PPG * (gi + 1)

            @block.vector
            def _(v, gi=gi, t0=t0, qvs_need=qvs_need):
                v = EW(v)
                v.wait_ge(qVs, qvs_need)
                v.tensor_tensor(out=arow[0:1, t0:t0 + 512],
                                in0=srow[0:1, t0:t0 + 512],
                                in1=rdrow[0:1, t0:t0 + 512], op=ALU.mult)
                v.tensor_tensor(out=scrow_b[0:1, t0:t0 + 512],
                                in0=scrow_b[0:1, t0:t0 + 512],
                                in1=srow[0:1, t0:t0 + 512], op=ALU.mult)
                v.drain()
                v.tensor_scalar(out=arow[0:1, t0:t0 + 512],
                                in0=arow[0:1, t0:t0 + 512],
                                scalar1=-1.0, scalar2=None, op0=ALU.mult)
                v.inc_last(qV, 1)
            ct.inc(qV)
            alv = ct.val(qV)

            @block.tensor
            def _(t, t0=t0, alv=alv):
                t = EW(t)
                t.wait_ge(qV, alv)
                t.matmul(psA[:, 0:512], onesrow_f[64:65, :], arow[0:1, t0:t0 + 512],
                         start=True, stop=True)
                t.matmul(psC[:, 0:512], onesrow_b[0:1, 0:128],
                         scrow_b[0:1, t0:t0 + 512], start=True, stop=True)
                t.inc_last(qT, 1)
            ct.inc(qT)
            ab_ = ct.val(qT)

            @block.vector
            def _(v, gi=gi, t0=t0, ab_=ab_):
                v = EW(v)
                v.wait_ge(qT, ab_)
                v.tensor_tensor(out=alphaT[:, t0:t0 + 512], in0=Wbuf[:, t0:t0 + 512],
                                in1=psA[:, 0:512], op=ALU.mult)
                lo = 1 if gi == 0 else 0
                v.tensor_tensor(out=sq_scr[:, lo:512],
                                in0=Wbuf[:, t0 + lo - 1:t0 + 511],
                                in1=psC[:, lo:512], op=ALU.mult)
                v.drain()
                v.tensor_tensor(out=alphaT[:, t0 + lo:t0 + 512],
                                in0=alphaT[:, t0 + lo:t0 + 512],
                                in1=sq_scr[:, lo:512], op=ALU.add)
                v.inc_last(qV, 2)
            ct.inc(qV, 2)

            for c in range(4 * gi, 4 * gi + 4):
                tc = c * 128

                @block.tensor
                def _(t, c=c, tc=tc):
                    t = EW(t)
                    t.wait_ge(qV, ct.val(qV))
                    t.matmul(psA[:, 0:128], alphaT[:, tc:tc + 128],
                             QTb[:, tc:tc + 128], start=True, stop=True)
                    t.inc_last(qT, 1)
                    t.transpose(psBb[:, 1024:1152], alphaT[:, tc:tc + 128],
                                idb_s[:, :])
                    t.inc_last(qT, 1)
                ct.inc(qT, 2)
                pq = ct.val(qT)

                @block.vector
                def _(v, c=c, pq=pq):
                    v = EW(v)
                    v.wait_ge(qT, pq - 1)
                    v.tensor_tensor(out=psm_sb[:, :], in0=psA[:, 0:128],
                                    in1=triu_s[:, :], op=ALU.mult)
                    v.wait_ge(qT, pq)
                    v.tensor_copy(out=atm[:, :], in_=psBb[:, 1024:1152])
                    v.inc_last(qV, 1)
                ct.inc(qV)
                pv = ct.val(qV)

                @block.tensor
                def _(t, c=c, tc=tc, pv=pv):
                    t = EW(t)
                    t.wait_ge(qV, pv)
                    t.matmul(psC[:, 0:128], Vtm[:, tc:tc + 128], psm_sb[:, :],
                             start=True, stop=(c == 0))
                    if c > 0:
                        t.matmul(psC[:, 0:128], Mst_sb[:, :], QTb[:, tc:tc + 128],
                                 start=False, stop=True)
                    t.inc_last(qT, 1)
                    t.matmul(psD[:, 0:128], atm[:, :], Vtm[:, tc:tc + 128],
                             start=(c == 0), stop=(c == NCH - 1),
                             skip_group_check=True)
                    t.inc_last(qT, 1)
                ct.inc(qT, 2)
                oq = ct.val(qT)

                @block.vector
                def _(v, c=c, tc=tc, oq=oq):
                    v = EW(v)
                    v.wait_ge(qT, oq - 1)
                    v.tensor_copy(out=OT[:, tc:tc + 128], in_=psC[:, 0:128])
                    v.wait_ge(qT, oq)
                    v.tensor_copy(out=Mst_sb[:, :], in_=psD[:, 0:128])
                    v.inc_last(qV, 1)
                ct.inc(qV)

            for dc in range(NDC):
                @block.tensor
                def _(t, t0=t0, dc=dc):
                    t = EW(t)
                    t.wait_ge(qV, ct.val(qV))
                    t.matmul(psA[:, 0:512], aWo_s[:, dc * 128:(dc + 1) * 128],
                             OT[:, t0:t0 + 512], start=True, stop=True)
                    t.inc_last(qT, 1)
                ct.inc(qT)
                tq2 = ct.val(qT)

                @block.vector
                def _(v, t0=t0, dc=dc, tq2=tq2):
                    v = EW(v)
                    v.wait_ge(qT, tq2)
                    v.tensor_scalar(out=sqf[:, :], in0=psA[:, 0:512],
                                    scalar1=abo_s[:, dc:dc + 1], scalar2=None,
                                    op0=ALU.add)
                    v.drain()
                    v.tensor_tensor(out=x_sb[:, dc * NT + t0: dc * NT + t0 + 512],
                                    in0=x_sb[:, dc * NT + t0: dc * NT + t0 + 512],
                                    in1=sqf[:, :], op=ALU.add)
                    v.inc_last(qV, 1)
                ct.inc(qV)

        GPS = NG // NSEG
        for seg in range(NSEG):
            scan_seg_T(seg)
            scan_seg_V(seg)
            for g in range(seg * GPS, (seg + 1) * GPS):
                emit_group(g)
        SCAN_TS = 3 * NP
        SCAN_VS = 1 + 3 * NP
        ATTNA_V = ct.val(qV)

        ov.close()

        ffn = dict(x_sb=x_sb, W1bf=W1bf, W2bf=W2bf, b1fl=b1fl, qP=qP, qSTRM=qSTRM,
                   rW_s=rW_s, b2_s=b2_s, bs2T_s=bs2T_s,
                   selE_s=selE_s, idf_s=idf_s, onesrow_b=onesrow_b,
                   psH=psH, psC=psC, psD=psD, psY=psY, psA=psA, psB=psB,
                   qTh=qTh, qTw2=qTw2, qSq=qSq, qSg=qSg, qVp=qVp, qVu=qVu,
                   qVg=qVg, qPg=qPg, qSY=qSY, qGP=qGP, qST=qST,
                   PRECAST_GP=PRECAST_GP, y_out=y_out, LN=LN, NT=NT)

        emit_ffn(nc, block, ct, ffn, last=False, tag="A")

        attb = dict(x_sb=x_sb, bWq=bWq, bWk=bWk, bWv=bWv, bWo=bWo,
                    maskA_s=maskA_s, maskB_s=maskB_s, onescol_b=onescol_b,
                    onesrow_f=onesrow_f,
                    psH=psH, psC=psC, psD=psD, psA=psA, psB=psB, psY=psY,
                    qGP=qGP, qSY=qSY, qTw2=qTw2, LN=LN, NT=NT)

        emit_attn_b(nc, block, ct, attb)

        emit_ffn(nc, block, ct, ffn, last=True, tag="B")

        @block.sync
        def _(sy):
            sy = EW(sy)
            sy.wait_ge(qST, ct.val(qST))

    es.close()
    return nc


def emit_ln(nc, block, ct, x_sb, xn_out, sq_scr, t0, W, NT, LN, pre_waits=None,
            xn_stride=None):
    if xn_stride is None:
        xn_stride = W
    qT, qV, qS = LN["qT"], LN["qV"], LN["qS"]
    psA, psB, psC, psD = LN["psA"], LN["psB"], LN["psC"], LN["psD"]

    @block.scalar
    def _(s):
        s = EW(s)
        s.wait_ge(qV, ct.val(qV))
        if pre_waits:
            for (q, vv) in pre_waits:
                s.wait_ge(q, vv)
        for dc in range(NDC):
            s.activation(out=sq_scr[:, dc * W:(dc + 1) * W],
                         in_=x_sb[:, dc * NT + t0: dc * NT + t0 + W], func=AF.Square)
        s.inc_last(qS, 1)
    ct.inc(qS)
    sqq = ct.val(qS)

    @block.tensor
    def _(t, sqq=sqq):
        t = EW(t)
        t.wait_ge(qS, sqq)
        for dc in range(NDC):
            t.matmul(psA[0:1, 0:W], LN["onescol_b"][:, :],
                     x_sb[:, dc * NT + t0: dc * NT + t0 + W],
                     start=(dc == 0), stop=(dc == NDC - 1))
        for dc in range(NDC):
            t.matmul(psB[0:1, 0:W], LN["onescol_b"][:, :],
                     sq_scr[:, dc * W:(dc + 1) * W],
                     start=(dc == 0), stop=(dc == NDC - 1))
        t.inc_last(qT, 1)
    ct.inc(qT)
    stq = ct.val(qT)

    @block.vector
    def _(v, stq=stq):
        v = EW(v)
        v.wait_ge(qT, stq)
        v.tensor_scalar(out=LN["meanrow"][0:1, 0:W], in0=psA[0:1, 0:W],
                        scalar1=1.0 / D, scalar2=None, op0=ALU.mult)
        v.drain()
        v.tensor_tensor(out=LN["varrow"][0:1, 0:W], in0=LN["meanrow"][0:1, 0:W],
                        in1=LN["meanrow"][0:1, 0:W], op=ALU.mult)
        v.drain()
        v.scalar_tensor_tensor(out=LN["varrow"][0:1, 0:W], in0=psB[0:1, 0:W],
                               scalar=1.0 / D, in1=LN["varrow"][0:1, 0:W],
                               op0=ALU.mult, op1=ALU.subtract)
        v.inc_last(qV, 1)
    ct.inc(qV)
    vq = ct.val(qV)

    @block.scalar
    def _(s, vq=vq):
        s = EW(s)
        s.wait_ge(qV, vq)
        s.activation(out=LN["rstdrow"][0:1, 0:W], in_=LN["varrow"][0:1, 0:W],
                     func=AF.Sqrt, bias=LN["cst"][0:1, 1:2])
        s.inc_last(qS, 1)
    ct.inc(qS)
    sq2 = ct.val(qS)

    @block.vector
    def _(v, sq2=sq2):
        v = EW(v)
        v.wait_ge(qS, sq2)
        v.reciprocal(LN["rstdrow"][0:1, 0:W], LN["rstdrow"][0:1, 0:W])
        v.inc_last(qV, 1)
    ct.inc(qV)
    vq2 = ct.val(qV)

    @block.tensor
    def _(t, vq2=vq2):
        t = EW(t)
        t.wait_ge(qV, vq2)
        t.matmul(psC[:, 0:W], LN["onesrow_f"][0:1, :], LN["meanrow"][0:1, 0:W],
                 start=True, stop=True)
        t.matmul(psD[:, 0:W], LN["onesrow_f"][32:33, :], LN["rstdrow"][0:1, 0:W],
                 start=True, stop=True)
        t.inc_last(qT, 1)
    ct.inc(qT)
    bcq = ct.val(qT)

    @block.vector
    def _(v, bcq=bcq):
        v = EW(v)
        v.wait_ge(qT, bcq)
        for dc in range(NDC):
            v.tensor_tensor(out=sq_scr[:, dc * W:(dc + 1) * W],
                            in0=x_sb[:, dc * NT + t0: dc * NT + t0 + W],
                            in1=psC[:, 0:W], op=ALU.subtract)
        v.drain()
        for dc in range(NDC):
            v.tensor_tensor(out=xn_out[:, dc * xn_stride:dc * xn_stride + W],
                            in0=sq_scr[:, dc * W:(dc + 1) * W],
                            in1=psD[:, 0:W], op=ALU.mult)
        v.inc_last(qV, 1)
    ct.inc(qV)


def emit_ffn(nc, block, ct, P, last, tag=""):
    NT = P["NT"]
    NBLK = NT // BLK
    x_sb = P["x_sb"]
    psH, psC, psD, psY = P["psH"], P["psC"], P["psD"], P["psY"]
    psA, psB = P["psA"], P["psB"]
    LN = P["LN"]
    qT, qV, qS = LN["qT"], LN["qV"], LN["qS"]
    qTh, qTw2 = P["qTh"], P["qTw2"]
    qSq, qSg, qVp, qVu, qVg, qPg = P["qSq"], P["qSg"], P["qVp"], P["qVu"], P["qVg"], P["qPg"]
    qSY, qGP, qST, qP = P["qSY"], P["qGP"], P["qST"], P["qP"]
    qsW1 = list(P["qSTRM"][tag][0:4])
    qsW2 = list(P["qSTRM"][tag][4:8])

    sc = contextlib.ExitStack()

    def sct(n, shp, dt):
        return sc.enter_context(nc.sbuf_tensor(n + tag, shp, dt))

    P = dict(P)
    P["w1buf"] = sct("w1buf", [128, 4 * 2304], BF16)
    P["w2buf"] = sct("w2buf", [128, 4 * 2048], BF16)
    P["xn_blk"] = sct("xn_blk", [128, NDC * BLK], BF16)
    P["lnsq"] = sct("lnsq", [128, NDC * BLK], BF16)
    P["gateB"] = sct("gateB", [128, 9 * BLK], BF16)
    P["gatesT"] = sct("gatesT", [8, BLK], F32)
    P["gsc"] = sct("gsc", [128, 48], F32)
    P["sqg"] = sct("sqg", [128, 4 * 512], BF16)
    P["pg"] = sct("pg", [128, 512], BF16)
    P["u2g"] = sct("u2g", [128, 512], F32)
    P["sgg"] = sct("sgg", [128, 512], BF16)
    P["hg0"] = sct("hg0", [128, 4 * 512], BF16)
    P["hgb"] = sct("hgb", [128, 4 * 512], BF16)
    P["outst"] = sct("outst", [128, 2048], F32)
    HS = [psH[:, 0:512], psH[:, 512:1024], psC[:, 0:512], psD[:, 0:512]]
    ENTRY_V = ct.val(qV)
    ENTRY_SY = ct.val(qSY)
    ENTRY_TW2 = ct.val(qTw2)

    @block.gpsimd
    def _(g):
        g = EW(g)
        g.wait_ge(qV, ENTRY_V)
        g.wait_ge(qSY, ENTRY_SY)
        g.wait_ge(qTw2, ENTRY_TW2)
        g.inc_last(qP, 1)
        ct.inc(qP)

    @block.vector
    def _(v):
        v = EW(v)
        v.wait_ge(qV, ENTRY_V)
        v.memset(P["gateB"][:, 8 * BLK:9 * BLK], 1.0)
        v.inc_last(qV, 1)
    ct.inc(qV)
    ENTRY_P = ct.val(qP)

    CB_Tw2 = ct.val(qTw2)
    CB_W1 = [ct.val(q) for q in qsW1]
    CB_W2 = [ct.val(q) for q in qsW2]

    @block.sync
    def _(sy):
        sy = EW(sy)
        sy.wait_ge(qP, ENTRY_P)
        if not last:
            sy.wait_ge(qGP, P["PRECAST_GP"])
        for k in range(NBLK * NM):
            s = k % 4
            m = k % NM
            if k >= 4:
                sy.wait_ge(qTw2, CB_Tw2 + k - 3)
            sy.dma_start(out=P["w1buf"][:, s * 2304:(s + 1) * 2304],
                         in_=P["W1bf"][m * 128:(m + 1) * 128, :]).then_inc(qsW1[s], 16)
            sy.dma_start(out=P["w2buf"][:, s * 2048:(s + 1) * 2048],
                         in_=P["W2bf"][m * 128:(m + 1) * 128, :]).then_inc(qsW2[s], 16)
    for _q in qsW1 + qsW2:
        ct.inc(_q, 16 * (NM // 4) * NBLK)

    for blk in range(NBLK):
        t0 = blk * BLK
        emit_ln(nc, block, ct, x_sb, P["xn_blk"], P["lnsq"], t0, BLK, NT, LN)

        for ti in range(2):
            tt0 = ti * 128

            @block.tensor
            def _(t, tt0=tt0):
                t = EW(t)
                t.wait_ge(qV, ct.val(qV))
                for dc in range(NDC):
                    t.matmul(psC[0:128, 0:8],
                             P["xn_blk"][:, dc * BLK + tt0: dc * BLK + tt0 + 128],
                             P["rW_s"][:, dc * NE:(dc + 1) * NE],
                             start=(dc == 0), stop=(dc == NDC - 1))
                t.inc_last(qT, 1)
            ct.inc(qT)
            lq = ct.val(qT)

            @block.vector
            def _(v, lq=lq):
                v = EW(v)
                v.wait_ge(qT, lq)
                v.tensor_copy(out=P["gsc"][:, 0:8], in_=psC[0:128, 0:8])
                v.drain()
                v.reduce_max(out=P["gsc"][:, 8:9], in_=P["gsc"][:, 0:8],
                             axis=AX.X, negate=True)
                v.inc_last(qV, 1)
            ct.inc(qV)
            lv = ct.val(qV)

            @block.scalar
            def _(s, lv=lv):
                s = EW(s)
                s.wait_ge(qV, lv)
                s.activation(out=P["gsc"][:, 16:24], in_=P["gsc"][:, 0:8], func=AF.Exp,
                             bias=P["gsc"][:, 8:9], accum_out=P["gsc"][:, 9:10])
                s.inc_last(qS, 1)
            ct.inc(qS)
            ls = ct.val(qS)

            @block.vector
            def _(v, ls=ls):
                v = EW(v)
                v.wait_ge(qS, ls)
                v.reciprocal(P["gsc"][:, 10:11], P["gsc"][:, 9:10])
                v.max(out=P["gsc"][:, 24:32], in_=P["gsc"][:, 16:24])
                v.drain()
                v.tensor_scalar(out=P["gsc"][:, 32:40], in0=P["gsc"][:, 16:24],
                                scalar1=P["gsc"][:, 25:26], scalar2=None,
                                op0=ALU.is_ge)
                v.drain()
                v.tensor_tensor(out=P["gsc"][:, 40:48], in0=P["gsc"][:, 16:24],
                                in1=P["gsc"][:, 32:40], op=ALU.mult)
                v.drain()
                v.tensor_scalar(out=P["gsc"][:, 0:8], in0=P["gsc"][:, 40:48],
                                scalar1=P["gsc"][:, 10:11], scalar2=None,
                                op0=ALU.mult)
                v.inc_last(qV, 1)
            ct.inc(qV)
            gv = ct.val(qV)

            @block.tensor
            def _(t, gv=gv):
                t = EW(t)
                t.wait_ge(qV, gv)
                t.transpose(psC[0:8, 256:384], P["gsc"][:, 0:8], P["idf_s"][:, :])
                t.inc_last(qT, 1)
            ct.inc(qT)
            gq = ct.val(qT)

            @block.vector
            def _(v, tt0=tt0, gq=gq):
                v = EW(v)
                v.wait_ge(qT, gq)
                v.tensor_copy(out=P["gatesT"][:, tt0:tt0 + 128], in_=psC[0:8, 256:384])
                v.inc_last(qV, 1)
            ct.inc(qV)
        GATES_V = ct.val(qV)

        for e in range(8):
            @block.tensor
            def _(t, e=e, w=ct.val(qV)):
                t = EW(t)
                t.wait_ge(qV, w)
                t.matmul(psC[:, 0:256], P["selE_s"][:, e * 128:(e + 1) * 128],
                         P["gatesT"][:, :], start=True, stop=True)
                t.inc_last(qT, 1)
            ct.inc(qT)
            eq = ct.val(qT)

            @block.vector
            def _(v, e=e, eq=eq):
                v = EW(v)
                v.wait_ge(qT, eq)
                v.tensor_copy(out=P["gateB"][:, e * BLK:(e + 1) * BLK],
                              in_=psC[:, 0:256])
                v.inc_last(qV, 1)
            ct.inc(qV)
        GB_V = ct.val(qV)

        @block.tensor
        def _(t):
            t = EW(t)
            t.wait_ge(qV, GB_V)
            for dcp in range(4):
                for side in range(2):
                    dc = 2 * dcp + side
                    t.matmul(psY[:, dcp * 512 + side * 256: dcp * 512 + side * 256 + 256],
                             P["b2_s"][:, dc * 128:(dc + 1) * 128], P["gatesT"][:, :],
                             start=(side == 0), stop=False, skip_group_check=True)
            t.inc_last(qT, 1)
        ct.inc(qT)

        bqW1 = [CB_W1[s] + blk * 16 * (NM // 4) for s in range(4)]
        bqW2 = [CB_W2[s] + blk * 16 * (NM // 4) for s in range(4)]
        base_Th = ct.val(qTh)
        base_Tw2 = ct.val(qTw2)
        base_Sq = ct.val(qSq)
        base_Sg = ct.val(qSg)
        base_Vp = ct.val(qVp)
        base_Vu = ct.val(qVu)
        base_Vg = ct.val(qVg)
        base_Pg = ct.val(qPg)
        T_ENTRY = ct.val(qT)
        V_ENTRY = ct.val(qV)


        bq = bqW1[0]
        assert all(b == bq for b in bqW1 + bqW2)

        def t_w1(t, s):
            h = HS[s]
            for fsub in range(2):
                for dc in range(NDC):
                    t.matmul(h[:, fsub * 256: fsub * 256 + 256],
                             P["w1buf"][:, s * 2304 + dc * 256 + fsub * 128:
                                        s * 2304 + dc * 256 + fsub * 128 + 128],
                             P["xn_blk"][:, dc * BLK:(dc + 1) * BLK],
                             start=(dc == 0), stop=False)
                t.matmul(h[:, fsub * 256: fsub * 256 + 256],
                         P["w1buf"][0:1, s * 2304 + 2048 + fsub * 128:
                                    s * 2304 + 2048 + fsub * 128 + 128],
                         P["onesrow_b"][0:1, 0:256], start=False, stop=True)
            t.inc_last(qTh, 1)

        def t_w2(t, s):
            for fr in range(2):
                for dc in range(NDC):
                    t.matmul(psY[:, (dc // 2) * 512 + (dc % 2) * 256:
                                  (dc // 2) * 512 + (dc % 2) * 256 + 256],
                             P["w2buf"][:, s * 2048 + fr * 1024 + dc * 128:
                                        s * 2048 + fr * 1024 + dc * 128 + 128],
                             P["hgb"][:, s * 512 + fr * 256: s * 512 + fr * 256 + 256],
                             start=False, stop=False, skip_group_check=True)
            t.inc_last(qTw2, 1)

        @block.tensor
        def _(t):
            t = EW(t)
            for j in range(4):
                t.wait_ge(qsW1[j], bq + 16)
                if base_Vg + j - 3 > 0:
                    t.wait_ge(qVg, base_Vg + j - 3)
                t_w1(t, j)
                if j >= 1:
                    t.wait_ge(qPg, base_Pg + j)
                    t.wait_ge(qsW2[j - 1], bq + 16)
                    t_w2(t, j - 1)
            with kreg(t) as rQ, kreg(t) as rQm, kreg(t) as rVg, kreg(t) as rPg:
                t.reg_mov(rQ, bq + 32)
                t.reg_mov(rQm, bq + 16)
                t.reg_mov(rVg, base_Vg + 1)
                t.reg_mov(rPg, base_Pg + 4)
                with kfori(nc, t, mybir.EngineType.PE, 1, NM // 4, 1):
                    for j in range(4):
                        t.wait_ge(qsW1[j], rQ)
                        t.wait_ge(qVg, rVg)
                        t_w1(t, j)
                        t.wait_ge(qPg, rPg)
                        t.wait_ge(qsW2[(j + 3) % 4], rQm if j == 0 else rQ)
                        t_w2(t, (j + 3) % 4)
                        t.reg_add(rVg, rVg, 1)
                        t.reg_add(rPg, rPg, 1)
                    t.reg_add(rQ, rQ, 16)
                    t.reg_add(rQm, rQm, 16)
            t.wait_ge(qPg, base_Pg + NM)
            t.wait_ge(qsW2[3], bq + 16 * (NM // 4))
            t_w2(t, 3)
        ct.inc(qTh, NM)
        ct.inc(qTw2, NM)

        @block.scalar
        def _(s):
            s = EW(s)
            with kreg(s) as rTh, kreg(s) as rVu, kreg(s) as rVp:
                s.reg_mov(rTh, base_Th + 1)
                s.reg_mov(rVu, base_Vu + 1)
                s.reg_mov(rVp, max(base_Vp - 3, 0))
                with kfori(nc, s, mybir.EngineType.Activation, 0, NM // 4, 1):
                    for j in range(4):
                        s.wait_ge(qTh, rTh)
                        s.wait_ge(qVp, rVp)
                        s.activation(out=P["sqg"][:, j * 512:(j + 1) * 512],
                                     in_=HS[j][:, 0:512],
                                     func=AF.Square).then_inc(qSq, 1)
                        s.wait_ge(qVu, rVu)
                        s.activation(out=P["sgg"][:, :], in_=P["u2g"][:, :],
                                     func=AF.Sigmoid).then_inc(qSg, 1)
                        s.reg_add(rTh, rTh, 1)
                        s.reg_add(rVu, rVu, 1)
                        s.reg_add(rVp, rVp, 1)
        ct.inc(qSq, NM)
        ct.inc(qSg, NM)

        @block.vector
        def _(v):
            v = EW(v)
            v.wait_ge(qV, GB_V)
            with kreg(v) as rSq, kreg(v) as rSg, kreg(v) as rT2:
                v.reg_mov(rSq, base_Sq + 1)
                v.reg_mov(rSg, base_Sg + 1)
                v.reg_mov(rT2, max(base_Tw2 - 3, 0))
                for e in range(9):
                    goff = e * BLK
                    with kfori(nc, v, mybir.EngineType.DVE, e * 4, e * 4 + 4, 1):
                        for j in range(4):
                            v.wait_ge(qSq, rSq)
                            v.tensor_scalar(out=P["pg"][:, :],
                                            in0=P["sqg"][:, j * 512:(j + 1) * 512],
                                            scalar1=GC2, scalar2=GC1, op0=ALU.mult,
                                            op1=ALU.add).then_inc(qVp, 1)
                            v.drain()
                            v.tensor_tensor(out=P["u2g"][:, :],
                                            in0=HS[j][:, 0:512],
                                            in1=P["pg"][:, :],
                                            op=ALU.mult).then_inc(qVu, 1)
                            v.wait_ge(qSg, rSg)
                            v.tensor_tensor(out=P["hg0"][:, j * 512:(j + 1) * 512],
                                            in0=HS[j][:, 0:512],
                                            in1=P["sgg"][:, :],
                                            op=ALU.mult).then_inc(qVg, 1)
                            v.wait_ge(qTw2, rT2)
                            v.drain()
                            v.tensor_tensor(out=P["hgb"][:, j * 512: j * 512 + 256],
                                            in0=P["hg0"][:, j * 512: j * 512 + 256],
                                            in1=P["gateB"][:, goff:goff + 256],
                                            op=ALU.mult)
                            v.tensor_tensor(out=P["hgb"][:, j * 512 + 256: j * 512 + 512],
                                            in0=P["hg0"][:, j * 512 + 256: j * 512 + 512],
                                            in1=P["gateB"][:, goff:goff + 256],
                                            op=ALU.mult).then_inc(qPg, 1)
                            v.reg_add(rSq, rSq, 1)
                            v.reg_add(rSg, rSg, 1)
                            v.reg_add(rT2, rT2, 1)
        ct.inc(qVp, NM)
        ct.inc(qVu, NM)
        ct.inc(qVg, NM)
        ct.inc(qPg, NM)

        @block.vector
        def _(v, blk=blk, t0=t0, wst=ct.val(qST)):
            v = EW(v)
            v.wait_ge(qTw2, base_Tw2 + NM)
            if last:
                v.wait_ge(qST, wst)
            for dc in range(NDC):
                ysl = psY[:, (dc // 2) * 512 + (dc % 2) * 256:
                          (dc // 2) * 512 + (dc % 2) * 256 + 256]
                v.tensor_scalar(out=P["outst"][:, dc * 256:(dc + 1) * 256],
                                in0=ysl, scalar1=P["bs2T_s"][:, dc:dc + 1],
                                scalar2=None, op0=ALU.add)
            v.inc_last(qV, 1)
        ct.inc(qV)
        ev1 = ct.val(qV)

        @block.vector
        def _(v, blk=blk, t0=t0, ev1=ev1):
            v = EW(v)
            v.wait_ge(qV, ev1)
            for dc in range(NDC):
                v.scalar_tensor_tensor(out=P["outst"][:, dc * 256:(dc + 1) * 256],
                                       in0=P["outst"][:, dc * 256:(dc + 1) * 256],
                                       scalar=SCALE_FFN,
                                       in1=x_sb[:, dc * NT + t0: dc * NT + t0 + 256],
                                       op0=ALU.mult, op1=ALU.add)
            v.inc_last(qV, 1)
        ct.inc(qV)
        ev2 = ct.val(qV)

        if last:
            @block.gpsimd
            def _(g, blk=blk, t0=t0, ev2=ev2):
                g = EW(g)
                g.wait_ge(qV, ev2)
                g.dma_start(out=bass.AP(P["y_out"].tensor, t0,
                                        [[NT, 128], [128 * NT, NDC], [1, 256]]),
                            in_=P["outst"][:, :].rearrange("p (dc t) -> p dc t", dc=NDC)).then_inc(qST, 16)
            ct.inc(qST, 16)
        else:
            @block.vector
            def _(v, blk=blk, t0=t0, ev2=ev2):
                v = EW(v)
                v.wait_ge(qV, ev2)
                for dc in range(NDC):
                    v.tensor_copy(out=x_sb[:, dc * NT + t0: dc * NT + t0 + 256],
                                  in_=P["outst"][:, dc * 256:(dc + 1) * 256])
                v.inc_last(qV, 1)
            ct.inc(qV)

    sc.close()


def emit_attn_b(nc, block, ct, P):
    NT = P["NT"]
    NTIL = NT // 128
    x_sb = P["x_sb"]
    psH, psC, psD = P["psH"], P["psC"], P["psD"]
    psA, psB = P["psA"], P["psB"]
    psY = P["psY"]
    LN = P["LN"]
    qT, qV, qS = LN["qT"], LN["qV"], LN["qS"]
    qGP, qSY, qTw2 = P["qGP"], P["qSY"], P["qTw2"]
    SCL = 1.0 / 8.0

    sc = contextlib.ExitStack()

    def sct(n, shp, dt):
        return sc.enter_context(nc.sbuf_tensor(n, shp, dt))

    P = dict(P)
    P["lnsq"] = sct("ab_lnsq", [128, NDC * 128], BF16)
    P["xnb"] = sct("ab_xnb", [128, NDC * 128], BF16)
    P["bQT"] = sct("ab_bQT", [128, NDC * 128], BF16)
    P["bKTr"] = sct("ab_bKTr", [128, 2 * NDC * 128], BF16)
    P["bVtr"] = sct("ab_bVtr", [128, 2 * NDC * 128], BF16)
    P["bexp"] = sct("ab_bexp", [128, 4 * 256], BF16)
    P["bexn"] = sct("ab_bexn", [128, 2 * 256], BF16)
    P["bctx"] = sct("ab_bctx", [128, NDC * 128], BF16)
    P["brs"] = sct("ab_brs", [1, 2 * 128], F32)
    P["bmsk"] = sct("ab_bmsk", [128, 2 * 256], F32)
    P["outst"] = sct("ab_outst", [128, NDC * 128], F32)
    P["bWq_s"] = sct("ab_bWq", [128, NDC * D], BF16)
    P["bWk_s"] = sct("ab_bWk", [128, NDC * D], BF16)
    P["bWv_s"] = sct("ab_bWv", [128, NDC * D], BF16)
    P["bWo_s"] = sct("ab_bWo", [128, NDC * D], BF16)
    ENTRY_V = ct.val(qV)
    Hscq, Hmkv, Hexq, Hsmq, Hrsq, Hrbq, Henv, Hcxq = {}, {}, {}, {}, {}, {}, {}, {}

    @block.gpsimd
    def _(g):
        g = EW(g)
        g.wait_ge(qV, ENTRY_V)
        g.wait_ge(qSY, ct.val(qSY))
        g.wait_ge(qTw2, ct.val(qTw2))
        g.wait_ge(qGP, ct.val(qGP))
        for dst, src in ((P["bWq_s"], P["bWq"]), (P["bWk_s"], P["bWk"]),
                         (P["bWv_s"], P["bWv"]), (P["bWo_s"], P["bWo"])):
            g.dma_start(out=dst[:, :].rearrange("p (c o) -> p c o", c=NDC),
                        in_=src.rearrange("(c p) o -> p c o", p=128)).then_inc(qGP, 16)
            ct.inc(qGP, 16)
    BW_GP = ct.val(qGP)

    for ti in range(NTIL):
        t0 = ti * 128
        slot = ti % 2
        pre = [(qGP, BW_GP)] if ti == 0 else None
        emit_ln(nc, block, ct, x_sb, P["xnb"], P["lnsq"], t0, 128, NT, LN,
                pre_waits=pre)
        XN_V = ct.val(qV)

        @block.tensor
        def _(t, ti=ti):
            t = EW(t)
            t.wait_ge(qV, XN_V)
            for oc in range(NDC):
                for dc in range(NDC):
                    t.matmul(psH[:, oc * 128:(oc + 1) * 128],
                             P["bWq_s"][:, dc * D + oc * 128: dc * D + oc * 128 + 128],
                             P["xnb"][:, dc * 128:(dc + 1) * 128],
                             start=(dc == 0), stop=(dc == NDC - 1))
            t.inc_last(qT, 1)
        ct.inc(qT)
        q1 = ct.val(qT)

        @block.vector
        def _(v, q1=q1):
            v = EW(v)
            v.wait_ge(qT, q1)
            for oc in range(NDC):
                v.tensor_copy(out=P["bQT"][:, oc * 128:(oc + 1) * 128],
                              in_=psH[:, oc * 128:(oc + 1) * 128])
            v.inc_last(qV, 1)
        ct.inc(qV)
        q1v = ct.val(qV)

        @block.tensor
        def _(t, q1v=q1v):
            t = EW(t)
            t.wait_ge(qV, q1v)
            for oc in range(NDC):
                for dc in range(NDC):
                    t.matmul(psH[:, oc * 128:(oc + 1) * 128],
                             P["bWk_s"][:, dc * D + oc * 128: dc * D + oc * 128 + 128],
                             P["xnb"][:, dc * 128:(dc + 1) * 128],
                             start=(dc == 0), stop=(dc == NDC - 1))
            t.inc_last(qT, 1)
        ct.inc(qT)
        q2 = ct.val(qT)

        @block.vector
        def _(v, slot=slot, q2=q2):
            v = EW(v)
            v.wait_ge(qT, q2)
            for oc in range(NDC):
                v.tensor_copy(out=P["bKTr"][:, slot * 1024 + oc * 128:
                                            slot * 1024 + (oc + 1) * 128],
                              in_=psH[:, oc * 128:(oc + 1) * 128])
            v.inc_last(qV, 1)
        ct.inc(qV)
        q2v = ct.val(qV)

        @block.tensor
        def _(t, q2v=q2v):
            t = EW(t)
            t.wait_ge(qV, q2v)
            for nh in range(2):
                for dc in range(NDC):
                    t.matmul(psH[:, nh * 512:(nh + 1) * 512],
                             P["xnb"][:, dc * 128:(dc + 1) * 128],
                             P["bWv_s"][:, dc * D + nh * 512: dc * D + nh * 512 + 512],
                             start=(dc == 0), stop=(dc == NDC - 1))
            t.inc_last(qT, 1)
        ct.inc(qT)
        q3 = ct.val(qT)

        @block.vector
        def _(v, slot=slot, q3=q3):
            v = EW(v)
            v.wait_ge(qT, q3)
            v.tensor_copy(out=P["bVtr"][:, slot * 1024: slot * 1024 + 512],
                          in_=psH[:, 0:512])
            v.tensor_copy(out=P["bVtr"][:, slot * 1024 + 512: slot * 1024 + 1024],
                          in_=psH[:, 512:1024])
            v.inc_last(qV, 1)
        ct.inc(qV)
        QKV_V = ct.val(qV)

        def hb_params(h):
            hg = ti * 16 + h
            hs = hg % 2
            stiles = ([((ti - 1) % 2, P["maskA_s"])] if ti > 0 else []) + \
                     [(slot, P["maskB_s"])]
            return dict(hg=hg, hs=hs, oc=h // 2, pb=(h % 2) * 64,
                        stiles=stiles, nsi=len(stiles),
                        sc0=hs * 256, eb0=(hg % 4) * 256,
                        scp=(psC[:, 0:256] if hs == 0 else psY[:, 0:256]),
                        smp=(psH[0:1, 0:128] if hs == 0 else psH[0:1, 512:640]),
                        rbp=(psD[:, 0:128] if hs == 0 else psY[:, 512:640]),
                        cxp=(psY[0:64, 1024:1152] if hs == 0
                             else psY[0:64, 1536:1664]))

        def hb_scores(pp):
            hg, scp = pp["hg"], pp["scp"]

            @block.tensor
            def _(t, pp=pp):
                t = EW(t)
                t.wait_ge(qV, max(QKV_V, Hmkv.get(hg - 2, 0)))
                for si, (ss, mk) in enumerate(pp["stiles"]):
                    t.matmul(scp[:, si * 128:(si + 1) * 128],
                             P["bKTr"][pp["pb"]:pp["pb"] + 64,
                                       ss * 1024 + pp["oc"] * 128:
                                       ss * 1024 + (pp["oc"] + 1) * 128],
                             P["bQT"][pp["pb"]:pp["pb"] + 64,
                                      pp["oc"] * 128:(pp["oc"] + 1) * 128],
                             start=True, stop=True, skip_group_check=True)
                t.inc_last(qT, 1)
            ct.inc(qT)
            Hscq[hg] = ct.val(qT)

        def hb_mask(pp):
            hg, sc0, eb0, scp = pp["hg"], pp["sc0"], pp["eb0"], pp["scp"]

            @block.vector
            def _(v, pp=pp):
                v = EW(v)
                v.wait_ge(qT, Hscq[hg])
                if hg >= 2:
                    v.wait_ge(qS, Hexq[hg - 2])
                for si, (ss, mk) in enumerate(pp["stiles"]):
                    v.scalar_tensor_tensor(
                        out=P["bmsk"][:, sc0 + si * 128: sc0 + (si + 1) * 128],
                        in0=scp[:, si * 128:(si + 1) * 128],
                        scalar=SCL, in1=mk[:, :], op0=ALU.mult, op1=ALU.add)
                v.inc_last(qV, 1)
            ct.inc(qV)
            Hmkv[hg] = ct.val(qV)

        def hb_exp(pp):
            hg, sc0, eb0, nsi = pp["hg"], pp["sc0"], pp["eb0"], pp["nsi"]

            @block.scalar
            def _(s, pp=pp):
                s = EW(s)
                s.wait_ge(qV, Hmkv[hg])
                if hg >= 4:
                    s.wait_ge(qT, Hsmq[hg - 4])
                    s.wait_ge(qV, Henv[hg - 4])
                s.activation(out=P["bexp"][:, eb0: eb0 + nsi * 128],
                             in_=P["bmsk"][:, sc0: sc0 + nsi * 128],
                             func=AF.Exp)
                s.inc_last(qS, 1)
            ct.inc(qS)
            Hexq[hg] = ct.val(qS)

        def hb_sum(pp):
            hg, eb0, smp, nsi = pp["hg"], pp["eb0"], pp["smp"], pp["nsi"]

            @block.tensor
            def _(t, pp=pp):
                t = EW(t)
                t.wait_ge(qS, Hexq[hg])
                if hg >= 2:
                    t.wait_ge(qV, Hrsq[hg - 2])
                for si in range(nsi):
                    t.matmul(smp[0:1, 0:128], P["onescol_b"][:, :],
                             P["bexp"][:, eb0 + si * 128: eb0 + (si + 1) * 128],
                             start=(si == 0), stop=(si == nsi - 1),
                             skip_group_check=True)
                t.inc_last(qT, 1)
            ct.inc(qT)
            Hsmq[hg] = ct.val(qT)

        def hb_recip(pp):
            hg, hs, smp = pp["hg"], pp["hs"], pp["smp"]

            @block.vector
            def _(v, pp=pp):
                v = EW(v)
                v.wait_ge(qT, Hsmq[hg])
                v.tensor_copy(out=P["brs"][0:1, hs * 128:(hs + 1) * 128],
                              in_=smp[0:1, 0:128])
                v.drain()
                v.reciprocal(P["brs"][0:1, hs * 128:(hs + 1) * 128],
                             P["brs"][0:1, hs * 128:(hs + 1) * 128])
                v.inc_last(qV, 1)
            ct.inc(qV)
            Hrsq[hg] = ct.val(qV)

        def hb_rsb(pp):
            hg, hs, rbp = pp["hg"], pp["hs"], pp["rbp"]

            @block.tensor
            def _(t, pp=pp):
                t = EW(t)
                t.wait_ge(qV, Hrsq[hg])
                t.matmul(rbp[:, 0:128], P["onesrow_f"][0:1, :],
                         P["brs"][0:1, hs * 128:(hs + 1) * 128],
                         start=True, stop=True, skip_group_check=True)
                t.inc_last(qT, 1)
            ct.inc(qT)
            Hrbq[hg] = ct.val(qT)

        def hb_norm(pp):
            hg, eb0, rbp, hs, nsi = pp["hg"], pp["eb0"], pp["rbp"], pp["hs"], pp["nsi"]

            @block.vector
            def _(v, pp=pp):
                v = EW(v)
                v.wait_ge(qT, Hrbq[hg])
                for si in range(nsi):
                    v.tensor_tensor(
                        out=P["bexn"][:, hs * 256 + si * 128: hs * 256 + (si + 1) * 128],
                        in0=P["bexp"][:, eb0 + si * 128: eb0 + (si + 1) * 128],
                        in1=rbp[:, 0:128], op=ALU.mult)
                v.inc_last(qV, 1)
            ct.inc(qV)
            Henv[hg] = ct.val(qV)

        def hb_ctx(pp):
            hg, hs, cxp = pp["hg"], pp["hs"], pp["cxp"]

            @block.tensor
            def _(t, pp=pp):
                t = EW(t)
                t.wait_ge(qV, Henv[hg])
                for si, (ss, mk) in enumerate(pp["stiles"]):
                    t.matmul(cxp[0:64, 0:128],
                             P["bVtr"][:, ss * 1024 + pp["oc"] * 128 + pp["pb"]:
                                       ss * 1024 + pp["oc"] * 128 + pp["pb"] + 64],
                             P["bexn"][:, hs * 256 + si * 128: hs * 256 + (si + 1) * 128],
                             start=(si == 0), stop=(si == pp["nsi"] - 1),
                             skip_group_check=True)
                t.inc_last(qT, 1)
            ct.inc(qT)
            Hcxq[hg] = ct.val(qT)

        def hb_evac(pp):
            hg, cxp = pp["hg"], pp["cxp"]

            @block.vector
            def _(v, pp=pp):
                v = EW(v)
                v.wait_ge(qT, Hcxq[hg])
                v.tensor_copy(out=P["bctx"][pp["pb"]:pp["pb"] + 64,
                                            pp["oc"] * 128:(pp["oc"] + 1) * 128],
                              in_=cxp[0:64, 0:128])
                v.inc_last(qV, 1)
            ct.inc(qV)

        PPs = [hb_params(h) for h in range(16)]
        for k in range(19):
            if k <= 15:
                hb_scores(PPs[k])
                hb_mask(PPs[k])
                hb_exp(PPs[k])
            if 1 <= k <= 16:
                hb_sum(PPs[k - 1])
                hb_recip(PPs[k - 1])
            if 2 <= k <= 17:
                hb_rsb(PPs[k - 2])
                hb_norm(PPs[k - 2])
            if 3 <= k <= 18:
                hb_ctx(PPs[k - 3])
                hb_evac(PPs[k - 3])
        CTX_V = ct.val(qV)
        for dc in range(NDC):
            @block.tensor
            def _(t, dc=dc, t0=t0, w=None):
                t = EW(t)
                t.wait_ge(qV, ct.val(qV))
                for oc in range(NDC):
                    t.matmul(psA[:, 0:128],
                             P["bWo_s"][:, oc * D + dc * 128: oc * D + dc * 128 + 128],
                             P["bctx"][:, oc * 128:(oc + 1) * 128],
                             start=(oc == 0), stop=(oc == NDC - 1))
                t.inc_last(qT, 1)
            ct.inc(qT)
            opq = ct.val(qT)

            @block.vector
            def _(v, dc=dc, t0=t0, opq=opq):
                v = EW(v)
                v.wait_ge(qT, opq)
                v.scalar_tensor_tensor(out=P["outst"][:, dc * 128:(dc + 1) * 128],
                                       in0=psA[:, 0:128], scalar=1.0,
                                       in1=x_sb[:, dc * NT + t0: dc * NT + t0 + 128],
                                       op0=ALU.mult, op1=ALU.add)
                v.inc_last(qV, 1)
            ct.inc(qV)
        OP_V = ct.val(qV)

        @block.vector
        def _(v, t0=t0, OP_V=OP_V):
            v = EW(v)
            v.wait_ge(qV, OP_V)
            for dc in range(NDC):
                v.tensor_copy(out=x_sb[:, dc * NT + t0: dc * NT + t0 + 128],
                              in_=P["outst"][:, dc * 128:(dc + 1) * 128])
            v.inc_last(qV, 1)
        ct.inc(qV)

    sc.close()


def _prep_inputs(inputs, b, T):
    f32 = np.float32
    x = np.ascontiguousarray(inputs["L0_attn_a_x"][b].T.astype(f32))
    d = {
        "x": x,
        "aWq": np.ascontiguousarray(inputs["L0_attn_a_Wq"].T),
        "aWk": np.ascontiguousarray(inputs["L0_attn_a_Wk"].T),
        "aWv": np.ascontiguousarray(inputs["L0_attn_a_Wv"].T),
        "aWo": np.ascontiguousarray(inputs["L0_attn_a_Wo"].T),
        "abq": inputs["L0_attn_a_bq"].reshape(H, 1),
        "abk": inputs["L0_attn_a_bk"].reshape(H, 1),
        "abv": inputs["L0_attn_a_bv"].reshape(1, H),
        "abo": np.ascontiguousarray(inputs["L0_attn_a_bo"].reshape(NDC, 128).T),
        "bWq": inputs["L0_attn_b_Wq"],
        "bWk": inputs["L0_attn_b_Wk"],
        "bWv": inputs["L0_attn_b_Wv"],
        "bWo": inputs["L0_attn_b_Wo"],
        "rW": np.ascontiguousarray(inputs["L0_ffn_w_router"].T),
        "W1T": np.ascontiguousarray(inputs["L0_ffn_W1"].transpose(0, 2, 1)),
        "W2T": np.ascontiguousarray(inputs["L0_ffn_W2"].transpose(0, 2, 1)),
        "Ws1T": np.ascontiguousarray(inputs["L0_ffn_Ws1"].T),
        "Ws2T": np.ascontiguousarray(inputs["L0_ffn_Ws2"].T),
        "b1fl": np.concatenate([np.asarray(inputs["L0_ffn_b1"]).reshape(-1),
                                np.asarray(inputs["L0_ffn_bs1"]).reshape(-1)]).reshape(1, 9 * DFF),
        "b2": inputs["L0_ffn_b2"],
        "bs2T": np.ascontiguousarray(inputs["L0_ffn_bs2"].reshape(NDC, 128).T),
    }
    return {k: np.ascontiguousarray(v).astype(f32) for k, v in d.items()}


_NC_CACHE = {}


def kernel(**inputs):
    T = inputs["L0_attn_a_x"].shape[1]
    B = inputs["L0_attn_a_x"].shape[0]
    if T not in _NC_CACHE:
        _NC_CACHE[T] = build_nc(T)
    nc = _NC_CACHE[T]
    in_maps = [_prep_inputs(inputs, b, T) for b in range(B)]
    res = run_bass_kernel_spmd(nc, in_maps, core_ids=list(range(B)))
    out = np.stack([np.ascontiguousarray(r["y"].T) for r in res.results], axis=0)
    return out.astype(np.float32)


if __name__ == "__main__":
    import reference as _r
    ins = _r.setup_inputs(0)
    ins = {k: np.asarray(v) for k, v in ins.items()}
    out = kernel(**ins)
    exp = np.asarray(_r.reference(**ins))
    err = np.linalg.norm(out - exp) / np.linalg.norm(exp)
    print("rel err:", err)


# version 19
# speedup vs baseline: 1.3032x; 1.0055x over previous
import math
import contextlib
import numpy as np
import ml_dtypes

import concourse.bass as bass
import concourse.mybir as mybir
from concourse.bass_utils import run_bass_kernel_spmd

F32 = mybir.dt.float32
BF16 = mybir.dt.bfloat16
AF = mybir.ActivationFunctionType
ALU = mybir.AluOpType
AX = mybir.AxisListType
ts = bass.ts

D = 1024
NDC = 8
H = 128
NE = 8
DFF = 4096
FC = 256
NM = 144
BLK = 256
EPS_LN = 1e-5
EPS_U = 1e-6
SCALE_FFN = 1.0 / math.sqrt(1.0 + 2.0 / 8.0)
GC1 = 2.0 * math.sqrt(2.0 / math.pi)
GC2 = 2.0 * math.sqrt(2.0 / math.pi) * 0.044715
NEGBIG = -1e30
SM_SHIFT = -10.0


class Ctr:
    def __init__(self):
        self.c = {}

    @staticmethod
    def _k(sem):
        return getattr(sem, "name", None) or id(sem)

    def inc(self, sem, n=1):
        k = self._k(sem)
        self.c[k] = self.c.get(k, 0) + n
        return self.c[k]

    def val(self, sem):
        return self.c.get(self._k(sem), 0)




import bass_rust as _br

_KF_UID = [0]


@contextlib.contextmanager
def kreg(eng, base=0):
    _KF_UID[0] += 1
    r = eng.alloc_register(f"kr{_KF_UID[0]}")
    try:
        yield r
    finally:
        eng.free_register(r)


@contextlib.contextmanager
def kfori(nc, eng, etype, start, end, step=1):
    _KF_UID[0] += 1
    name = f"kf{_KF_UID[0]}"
    reg = eng.alloc_register(f"{name}_i")
    rh = _br.RegisterHandles((reg,))
    eng.reg_mov(reg, start)
    nc.br(f"{name}_L", engines=[etype])
    with nc.body(f"{name}_L", valid_engines=[etype]):
        yield reg
        eng.reg_add(reg, reg, step)
        nc.br_lt(rh, end, on_true=f"{name}_L", on_false=f"{name}_E", engines=[etype])
    nc.switch_bb(f"{name}_E")
    eng.free_register(reg)


_RV_CACHE = {}


def rowslice(reg, nrows, maxrow):
    key = (reg.name, nrows, maxrow)
    if key not in _RV_CACHE:
        rv = _br.RuntimeValue(_br.RegisterHandles((reg,)), 0, maxrow, nrows, 0)
        _RV_CACHE[key] = _br.DynSlice(rv, nrows)
    return _RV_CACHE[key]




class EW:

    def __init__(self, eng):
        object.__setattr__(self, "_e", eng)
        object.__setattr__(self, "last", None)

    def __getattr__(self, n):
        a = getattr(self._e, n)
        if callable(a):
            def w(*args, **kw):
                r = a(*args, **kw)
                if hasattr(r, "then_inc"):
                    object.__setattr__(self, "last", r)
                return r
            return w
        return a

    def inc_last(self, q, n=1):
        self.last.then_inc(q, n)


def build_nc(T=4096):
    NT = T
    NG = T // 512
    NBLK = T // BLK
    NTIL = T // 128
    NCH = T // 128

    nc = bass.Bass("TRN2", target_bir_lowering=False, debug=False)

    x_in = nc.dram_tensor("x", [D, NT], F32, kind="ExternalInput").ap()
    y_out = nc.dram_tensor("y", [D, NT], F32, kind="ExternalOutput").ap()

    aWq = nc.dram_tensor("aWq", [D, H], F32, kind="ExternalInput").ap()
    aWk = nc.dram_tensor("aWk", [D, H], F32, kind="ExternalInput").ap()
    aWv = nc.dram_tensor("aWv", [D, H], F32, kind="ExternalInput").ap()
    aWo = nc.dram_tensor("aWo", [H, D], F32, kind="ExternalInput").ap()
    abq = nc.dram_tensor("abq", [H, 1], F32, kind="ExternalInput").ap()
    abk = nc.dram_tensor("abk", [H, 1], F32, kind="ExternalInput").ap()
    abv = nc.dram_tensor("abv", [1, H], F32, kind="ExternalInput").ap()
    abo = nc.dram_tensor("abo", [128, NDC], F32, kind="ExternalInput").ap()

    bWq = nc.dram_tensor("bWq", [D, D], F32, kind="ExternalInput").ap()
    bWk = nc.dram_tensor("bWk", [D, D], F32, kind="ExternalInput").ap()
    bWv = nc.dram_tensor("bWv", [D, D], F32, kind="ExternalInput").ap()
    bWo = nc.dram_tensor("bWo", [D, D], F32, kind="ExternalInput").ap()

    rW = nc.dram_tensor("rW", [D, NE], F32, kind="ExternalInput").ap()
    W1T = nc.dram_tensor("W1T", [NE, D, DFF], F32, kind="ExternalInput").ap()
    W2T = nc.dram_tensor("W2T", [NE, DFF, D], F32, kind="ExternalInput").ap()
    Ws1T = nc.dram_tensor("Ws1T", [D, DFF], F32, kind="ExternalInput").ap()
    Ws2T = nc.dram_tensor("Ws2T", [DFF, D], F32, kind="ExternalInput").ap()
    b1fl = nc.dram_tensor("b1fl", [1, 9 * DFF], F32, kind="ExternalInput").ap()
    b2 = nc.dram_tensor("b2", [NE, D], F32, kind="ExternalInput").ap()
    bs2T = nc.dram_tensor("bs2T", [128, NDC], F32, kind="ExternalInput").ap()

    c_idf = nc.inline_tensor(np.eye(128, dtype=np.float32), "c_idf").ap()
    c_idb = nc.inline_tensor(np.eye(128).astype(ml_dtypes.bfloat16), "c_idb").ap()
    c_onescol_b = nc.inline_tensor(np.ones((128, 1), ml_dtypes.bfloat16), "c_onescol_b").ap()
    c_onescol_f = nc.inline_tensor(np.ones((128, 1), np.float32), "c_onescol_f").ap()
    c_onesrow_b = nc.inline_tensor(np.ones((128, 512), ml_dtypes.bfloat16), "c_onesrow_b").ap()
    c_onesrow_f = nc.inline_tensor(np.ones((128, 128), np.float32), "c_onesrow_f").ap()
    tri = (np.arange(128)[:, None] <= np.arange(128)[None, :]).astype(np.float32)
    c_triu = nc.inline_tensor(tri, "c_triu").ap()
    s_i = np.arange(128)[:, None]
    t_i = np.arange(128)[None, :]
    mB = np.where((s_i <= t_i) & (s_i >= t_i - 63), SM_SHIFT, NEGBIG).astype(np.float32)
    mA = np.where(s_i >= t_i + 65, SM_SHIFT, NEGBIG).astype(np.float32)
    c_maskB = nc.inline_tensor(mB, "c_maskB").ap()
    c_maskA = nc.inline_tensor(mA, "c_maskA").ap()
    sel = np.zeros((8, 8 * 128), np.float32)
    for e in range(8):
        sel[e, e * 128:(e + 1) * 128] = 1.0
    c_selE = nc.inline_tensor(sel, "c_selE").ap()

    W1bf = nc.dram_tensor("W1bf", [NM * 128, 2304], BF16, kind="Internal").ap()
    W2bf = nc.dram_tensor("W2bf", [NM * 128, 2048], BF16, kind="Internal").ap()

    es = contextlib.ExitStack()

    x_sb = nc.alloc_sbuf_tensor("x_sb", [128, NDC * NT], BF16)
    rW_s = nc.alloc_sbuf_tensor("rW_s", [128, NDC * NE], BF16)
    b2_s = nc.alloc_sbuf_tensor("b2_s", [8, D], F32)
    bs2T_s = nc.alloc_sbuf_tensor("bs2T_s", [128, NDC], F32)
    idf_s = nc.alloc_sbuf_tensor("idf_s", [128, 128], F32)
    idb_s = nc.alloc_sbuf_tensor("idb_s", [128, 128], BF16)
    onescol_b = nc.alloc_sbuf_tensor("onescol_b", [128, 1], BF16)
    onescol_f = nc.alloc_sbuf_tensor("onescol_f", [128, 1], F32)
    onesrow_b = nc.alloc_sbuf_tensor("onesrow_b", [128, 512], BF16)
    onesrow_f = nc.alloc_sbuf_tensor("onesrow_f", [128, 128], F32)
    triu_s = nc.alloc_sbuf_tensor("triu_s", [128, 128], F32)
    maskB_s = nc.alloc_sbuf_tensor("maskB_s", [128, 128], F32)
    maskA_s = nc.alloc_sbuf_tensor("maskA_s", [128, 128], F32)
    selE_s = nc.alloc_sbuf_tensor("selE_s", [8, 8 * 128], F32)
    statrows = nc.alloc_sbuf_tensor("statrows", [128, 1024], F32)
    meanrow = statrows[0:1, 0:512]
    varrow = statrows[0:1, 512:1024]
    rstdrow = statrows[32:33, 0:512]

    psH = nc.alloc_psum_tensor("psH", [128, 1024], F32)
    psC = nc.alloc_psum_tensor("psC", [128, 512], F32)
    psD = nc.alloc_psum_tensor("psD", [128, 512], F32)
    psY = nc.alloc_psum_tensor("psY", [128, 2048], F32)
    psA = psH[:, 0:512]
    psB = psH[:, 512:1024]

    qGP = es.enter_context(nc.semaphore("qGP"))
    qSTRM = {tg: tuple(es.enter_context(nc.semaphore(f"q{w}s{sl}{tg}"))
                       for w in ("W1", "W2") for sl in range(4))
             for tg in ("A", "B")}
    qSY = es.enter_context(nc.semaphore("qSY"))
    qT = es.enter_context(nc.semaphore("qT"))
    qV = es.enter_context(nc.semaphore("qV"))
    qS = es.enter_context(nc.semaphore("qS"))
    qP = es.enter_context(nc.semaphore("qP"))
    qTh = es.enter_context(nc.semaphore("qTh"))
    qTw2 = es.enter_context(nc.semaphore("qTw2"))
    qSq = es.enter_context(nc.semaphore("qSq"))
    qSg = es.enter_context(nc.semaphore("qSg"))
    qVp = es.enter_context(nc.semaphore("qVp"))
    qVu = es.enter_context(nc.semaphore("qVu"))
    qVg = es.enter_context(nc.semaphore("qVg"))
    qPg = es.enter_context(nc.semaphore("qPg"))
    qST = es.enter_context(nc.semaphore("qST"))
    qTs = es.enter_context(nc.semaphore("qTs"))
    qVs = es.enter_context(nc.semaphore("qVs"))
    qSs = es.enter_context(nc.semaphore("qSs"))

    ct = Ctr()
    cst = nc.alloc_sbuf_tensor("cst", [128, 2], F32)

    ov = contextlib.ExitStack()

    def ovt(name, shape, dt):
        return ov.enter_context(nc.sbuf_tensor(name, shape, dt))

    KT = ovt("KT", [128, NT], F32)
    Wbuf = KT
    UT = ovt("UT", [128, NT], F32)
    QTb = ovt("QTb", [128, NT], BF16)
    Vtm = ovt("Vtm", [128, NT], BF16)
    alphaT = ovt("alphaT", [128, NT], BF16)
    OT = UT.bitcast(BF16)[:, 0:NT]
    xn_grp = ovt("xn_grp", [128, NDC * 512], BF16)
    sq_scr = ovt("sq_scr", [128, NDC * 512], BF16)
    sqf = ovt("sqf", [128, 512], F32)
    rowsbig = ovt("rowsbig", [128, 2 * NT], F32)
    rdrow = rowsbig[0:1, 0:NT]
    srow = rowsbig[0:1, NT:2 * NT]
    n2row = rowsbig[64:65, 0:NT]
    arow = rowsbig[64:65, 0:NT]
    Asb = ovt("Asb", [128, 128], F32)
    ypair_sb = ovt("ypair_sb", [128, 2], F32)
    scs = ovt("scs", [1, 784], F32)
    scrow_b = ovt("scrow_b", [1, NT], BF16)
    Mst_sb = ovt("Mst_sb", [128, 128], BF16)
    atm = ovt("atm", [128, 128], BF16)
    psm_sb = ovt("psm_sb", [128, 128], BF16)
    aWq_s = ovt("aWq_s", [128, NDC * H], BF16)
    aWk_s = ovt("aWk_s", [128, NDC * H], BF16)
    aWv_s = ovt("aWv_s", [128, NDC * H], BF16)
    aWo_s = ovt("aWo_s", [128, D], BF16)
    abq_s = ovt("abq_s", [128, 1], F32)
    abk_s = ovt("abk_s", [128, 1], F32)
    abv_s = ovt("abv_s", [1, H], BF16)
    abo_s = ovt("abo_s", [128, NDC], F32)

    LN = dict(cst=cst, onescol_b=onescol_b, onesrow_f=onesrow_f, meanrow=meanrow,
              varrow=varrow, rstdrow=rstdrow, psA=psA, psB=psB, psC=psC, psD=psD,
              qT=qT, qV=qV, qS=qS)

    with nc.Block() as block:
        @block.vector
        def _(v):
            v = EW(v)
            v.memset(cst[:, 0:1], -1.0)
            v.memset(cst[:, 1:2], EPS_LN)
            v.inc_last(qV, 1)
        ct.inc(qV)

        @block.gpsimd
        def _(g):
            g = EW(g)
            g.dma_start(out=x_sb[:, :].rearrange("p (dc t) -> p dc t", dc=NDC),
                        in_=x_in.rearrange("(dc p) t -> p dc t", p=128)).then_inc(qGP, 16)
            g.dma_start(out=aWq_s[:, :].rearrange("p (dc h) -> p dc h", dc=NDC),
                        in_=aWq.rearrange("(dc p) h -> p dc h", p=128)).then_inc(qGP, 16)
            g.dma_start(out=aWk_s[:, :].rearrange("p (dc h) -> p dc h", dc=NDC),
                        in_=aWk.rearrange("(dc p) h -> p dc h", p=128)).then_inc(qGP, 16)
            g.dma_start(out=aWv_s[:, :].rearrange("p (dc h) -> p dc h", dc=NDC),
                        in_=aWv.rearrange("(dc p) h -> p dc h", p=128)).then_inc(qGP, 16)
            g.dma_start(out=aWo_s[:, :], in_=aWo[:, :]).then_inc(qGP, 16)
            g.dma_start(out=abv_s[:, :], in_=abv[:, :]).then_inc(qGP, 16)
            g.dma_start(out=rW_s[:, :].rearrange("p (dc e) -> p dc e", dc=NDC),
                        in_=rW.rearrange("(dc p) e -> p dc e", p=128)).then_inc(qGP, 16)
            ct.inc(qGP, 16 * 7)

        @block.sync
        def _(sy):
            sy = EW(sy)
            for dst, src in ((abq_s, abq), (abk_s, abk), (abo_s, abo), (b2_s, b2),
                             (bs2T_s, bs2T), (idf_s, c_idf), (idb_s, c_idb),
                             (onescol_b, c_onescol_b), (onescol_f, c_onescol_f),
                             (onesrow_b, c_onesrow_b), (onesrow_f, c_onesrow_f),
                             (triu_s, c_triu), (maskB_s, c_maskB), (maskA_s, c_maskA),
                             (selE_s, c_selE)):
                sy.dma_start(out=dst[:, :], in_=src[:, :]).then_inc(qSY, 16)
                ct.inc(qSY, 16)
        LOADS_SY = ct.val(qSY)
        LOADS_GP = ct.val(qGP)

        @block.gpsimd
        def _(g):
            g = EW(g)
            g.wait_ge(qGP, LOADS_GP)
            for m in range(NM):
                e, fc = m // 16, m % 16
                if e < 8:
                    src1 = bass.AP(W1T.tensor, e * D * DFF + fc * FC,
                                   [[DFF, 128], [128 * DFF, NDC], [1, FC]])
                    src2 = bass.AP(W2T.tensor, e * DFF * D + fc * FC * D,
                                   [[D, 128], [128 * D, 2], [1, D]])
                else:
                    src1 = bass.AP(Ws1T.tensor, fc * FC,
                                   [[DFF, 128], [128 * DFF, NDC], [1, FC]])
                    src2 = bass.AP(Ws2T.tensor, fc * FC * D,
                                   [[D, 128], [128 * D, 2], [1, D]])
                d1 = W1bf[m * 128:(m + 1) * 128, 0:2048].rearrange("p (a b) -> p a b", a=NDC)
                d2 = W2bf[m * 128:(m + 1) * 128, :].rearrange("p (a b) -> p a b", a=2)
                g.dma_start(out=d1, in_=src1).then_inc(qGP, 16)
                g.dma_start(out=d2, in_=src2).then_inc(qGP, 16)
                g.dma_start(out=W1bf[m * 128:(m + 1) * 128, 2048:2304],
                            in_=bass.AP(b1fl.tensor, m * FC, [[0, 128], [1, FC]])).then_inc(qGP, 16)
                ct.inc(qGP, 48)
        PRECAST_GP = ct.val(qGP)

        for gi in range(NG):
            t0 = gi * 512
            pre = [(qGP, LOADS_GP), (qSY, LOADS_SY)] if gi == 0 else None
            emit_ln(nc, block, ct, x_sb, xn_grp, sq_scr, t0, 512, NT, LN, pre_waits=pre)

            @block.tensor
            def _(t, t0=t0):
                t = EW(t)
                t.wait_ge(qV, ct.val(qV))
                if t0 == 0:
                    t.wait_ge(qGP, LOADS_GP)
                for dc in range(NDC):
                    t.matmul(psA[:, 0:512], aWq_s[:, dc * H:(dc + 1) * H],
                             xn_grp[:, dc * 512:(dc + 1) * 512],
                             start=(dc == 0), stop=(dc == NDC - 1))
                t.inc_last(qT, 1)
                for dc in range(NDC):
                    t.matmul(psB[:, 0:512], aWk_s[:, dc * H:(dc + 1) * H],
                             xn_grp[:, dc * 512:(dc + 1) * 512],
                             start=(dc == 0), stop=(dc == NDC - 1))
                t.inc_last(qT, 1)
                for ti in range(4):
                    for dc in range(NDC):
                        t.matmul(psC[:, ti * 128:(ti + 1) * 128],
                                 xn_grp[:, dc * 512 + ti * 128: dc * 512 + (ti + 1) * 128],
                                 aWv_s[:, dc * H:(dc + 1) * H],
                                 start=(dc == 0), stop=False)
                    t.matmul(psC[:, ti * 128:(ti + 1) * 128], onesrow_b[0:1, 0:128],
                             abv_s[:, :], start=False, stop=True)
                t.inc_last(qT, 1)
            ct.inc(qT, 3)
            tq = ct.val(qT)

            @block.vector
            def _(v, t0=t0, tq=tq):
                v = EW(v)
                v.wait_ge(qT, tq - 2)
                v.tensor_scalar(out=QTb[:, t0:t0 + 512], in0=psA[:, 0:512],
                                scalar1=abq_s[:, :], scalar2=None, op0=ALU.add)
                v.wait_ge(qT, tq - 1)
                v.tensor_scalar(out=KT[:, t0:t0 + 512], in0=psB[:, 0:512],
                                scalar1=abk_s[:, :], scalar2=None, op0=ALU.add)
                v.wait_ge(qT, tq)
                v.tensor_copy(out=Vtm[:, t0:t0 + 512], in_=psC[:, 0:512])
                v.inc_last(qV, 1)
            ct.inc(qV)
            vq = ct.val(qV)

            @block.scalar
            def _(s, t0=t0, vq=vq):
                s = EW(s)
                s.wait_ge(qV, vq)
                s.activation(out=sqf[:, :], in_=KT[:, t0:t0 + 512], func=AF.Square)
                s.inc_last(qS, 1)
            ct.inc(qS)
            sq1 = ct.val(qS)

            @block.tensor
            def _(t, sq1=sq1):
                t = EW(t)
                t.wait_ge(qS, sq1)
                t.matmul(psA[0:1, 0:512], onescol_f[:, :], sqf[:, :], start=True, stop=True)
                t.inc_last(qT, 1)
            ct.inc(qT)
            n2q = ct.val(qT)

            @block.vector
            def _(v, t0=t0, n2q=n2q):
                v = EW(v)
                v.wait_ge(qT, n2q)
                v.tensor_copy(out=n2row[0:1, t0:t0 + 512], in_=psA[0:1, 0:512])
                v.tensor_tensor(out=sqf[:, :], in0=KT[:, t0:t0 + 512],
                                in1=QTb[:, t0:t0 + 512], op=ALU.mult)
                v.inc_last(qV, 1)
            ct.inc(qV)
            qkv = ct.val(qV)

            @block.tensor
            def _(t, qkv=qkv):
                t = EW(t)
                t.wait_ge(qV, qkv)
                t.matmul(psB[0:1, 0:512], onescol_f[:, :], sqf[:, :], start=True, stop=True)
                t.inc_last(qT, 1)
            ct.inc(qT)
            sq2 = ct.val(qT)

            @block.vector
            def _(v, t0=t0, sq2=sq2):
                v = EW(v)
                v.wait_ge(qT, sq2)
                v.tensor_copy(out=srow[0:1, t0:t0 + 512], in_=psB[0:1, 0:512])
                v.inc_last(qV, 1)
            ct.inc(qV)

        @block.scalar
        def _(s):
            s = EW(s)
            s.wait_ge(qV, ct.val(qV))
            s.activation(out=n2row[0:1, 0:NT], in_=n2row[0:1, 0:NT], func=AF.Sqrt)
            s.inc_last(qS, 1)
        ct.inc(qS)
        nq = ct.val(qS)

        @block.vector
        def _(v, nq=nq):
            v = EW(v)
            v.wait_ge(qS, nq)
            v.tensor_scalar(out=n2row[0:1, 0:NT], in0=n2row[0:1, 0:NT],
                            scalar1=EPS_U, scalar2=None, op0=ALU.add)
            v.drain()
            v.reciprocal(n2row[0:1, 0:NT], n2row[0:1, 0:NT])
            v.inc_last(qV, 1)
        ct.inc(qV)
        rno = ct.val(qV)

        for gi in range(NG):
            t0 = gi * 512

            @block.tensor
            def _(t, t0=t0, gi=gi, rno=rno):
                t = EW(t)
                t.wait_ge(qV, rno + gi)
                t.matmul(psA[:, 0:512], onesrow_f[64:65, :], n2row[0:1, t0:t0 + 512],
                         start=True, stop=True)
                t.inc_last(qT, 1)
            ct.inc(qT)
            bq_ = ct.val(qT)

            @block.vector
            def _(v, t0=t0, bq_=bq_):
                v = EW(v)
                v.wait_ge(qT, bq_)
                v.tensor_tensor(out=UT[:, t0:t0 + 512], in0=KT[:, t0:t0 + 512],
                                in1=psA[:, 0:512], op=ALU.mult)
                v.inc_last(qV, 1)
            ct.inc(qV)
        UT_V = ct.val(qV)

        A_ps = psY[:, 0:128]
        w_ps2 = psY[:, 512:514]
        g11_ps = psY[0:1, 1024:1025]
        g22_ps = psY[0:1, 1025:1026]
        gc_ps = psY[0:1, 1026:1027]
        yr1_ps = psY[0:1, 1536:1664]
        yr2_ps = psY[0:1, 1664:1792]
        yr12_ps = psY[0:1, 1536:1792]
        y12r = scs[0:1, 0:256]
        q1 = scs[0:1, 256:384]
        q2 = scs[0:1, 384:512]
        q3 = scs[0:1, 512:640]
        q4 = scs[0:1, 640:768]
        p1r = q1
        p2r = q3
        zrow = q1
        bb2 = scs[0:1, 768:770]
        nrow = scs[0:1, 770:772]
        rpr = scs[0:1, 772:774]
        m1s = scs[0:1, 774:775]
        m2s = scs[0:1, 775:776]
        s22v = scs[0:1, 776:777]
        sc1 = scs[0:1, 777:778]
        csb = scs[0:1, 778:779]
        s11v = scs[0:1, 779:780]
        NP = NT // 2

        @block.vector
        def _(v):
            v = EW(v)
            v.wait_ge(qV, UT_V)
            v.memset(scrow_b[0:1, :], 0.0)
            v.memset(zrow, 0.0)
            v.tensor_copy(out=Asb[:, :], in_=idf_s[:, :])
            v.inc_last(qVs, 1)

        NSEG = min(NG, 4)
        NPS = NP // NSEG
        PPG = NP // NG
        SCANR = {}
        psBb = psH.bitcast(BF16)

        def scan_seg_T(seg):
            @block.tensor
            def _(t, seg=seg):
                t = EW(t)
                if seg == 0:
                    t.wait_ge(qVs, 1)
                    t.matmul(A_ps, zrow, zrow, start=True, stop=False,
                             skip_group_check=True)
                    rW = t.alloc_register("sT_rW")
                    rTi = t.alloc_register("sT_rTi")
                    rTi1 = t.alloc_register("sT_rTi1")
                    t.reg_mov(rW, 1)
                    t.reg_mov(rTi, 0)
                    t.reg_mov(rTi1, 1)
                    SCANR["T"] = (rW, rTi, rTi1)
                rW, rTi, rTi1 = SCANR["T"]
                with kfori(nc, t, mybir.EngineType.PE, 0, NPS, 1):
                    t.wait_ge(qVs, rW)
                    t.matmul(w_ps2, Asb[:, :], UT[:, rowslice(rTi, 2, NT - 2)],
                             start=True, stop=True).then_inc(qTs, 1)
                    t.reg_add(rW, rW, 1)
                    t.wait_ge(qVs, rW)
                    t.transpose(yr1_ps, ypair_sb[:, 0:1], idf_s[:, :])
                    t.transpose(yr2_ps, ypair_sb[:, 1:2], idf_s[:, :])
                    t.matmul(g11_ps, ypair_sb[:, 0:1],
                             UT[:, rowslice(rTi, 1, NT - 1)], start=True, stop=True)
                    t.matmul(g22_ps, ypair_sb[:, 1:2],
                             UT[:, rowslice(rTi1, 1, NT - 1)], start=True, stop=True)
                    t.matmul(gc_ps, ypair_sb[:, 0:1],
                             UT[:, rowslice(rTi1, 1, NT - 1)], start=True,
                             stop=True).then_inc(qTs, 1)
                    t.reg_add(rW, rW, 1)
                    t.wait_ge(qVs, rW)
                    t.matmul(A_ps, p1r, y12r[0:1, 0:128], start=False, stop=False,
                             skip_group_check=True)
                    t.matmul(A_ps, p2r, y12r[0:1, 128:256], start=False, stop=False,
                             skip_group_check=True).then_inc(qTs, 1)
                    t.reg_add(rW, rW, 1)
                    t.reg_add(rTi, rTi, 2)
                    t.reg_add(rTi1, rTi1, 2)
            ct.inc(qTs, 3 * NPS)

        def scan_seg_V(seg):
            @block.vector
            def _(v, seg=seg):
                v = EW(v)
                if seg == 0:
                    rT = v.alloc_register("sV_rT")
                    rVi = v.alloc_register("sV_rVi")
                    rVi1 = v.alloc_register("sV_rVi1")
                    v.reg_mov(rT, 1)
                    v.reg_mov(rVi, 0)
                    v.reg_mov(rVi1, 1)
                    SCANR["V"] = (rT, rVi, rVi1)
                rT, rVi, rVi1 = SCANR["V"]
                with kfori(nc, v, mybir.EngineType.DVE, 0, NPS, 1):
                    v.wait_ge(qTs, rT)
                    v.tensor_copy(out=ypair_sb[:, :], in_=w_ps2).then_inc(qVs, 1)
                    v.reg_add(rT, rT, 1)
                    v.wait_ge(qTs, rT)
                    v.tensor_copy(out=y12r, in_=yr12_ps)
                    v.tensor_scalar(out=bb2, in0=psY[0:1, 1024:1026], scalar1=1.0,
                                    scalar2=None, op0=ALU.add)
                    v.tensor_copy(out=csb, in_=gc_ps)
                    v.drain()
                    v.tensor_tensor(out=m2s, in0=csb, in1=csb, op=ALU.mult)
                    v.tensor_tensor(out=m1s, in0=bb2[0:1, 0:1], in1=bb2[0:1, 1:2],
                                    op=ALU.mult)
                    v.tensor_scalar(out=nrow[0:1, 0:1], in0=bb2[0:1, 0:1],
                                    scalar1=-1.0, scalar2=None, op0=ALU.mult)
                    v.drain()
                    v.scalar_tensor_tensor(out=nrow[0:1, 1:2], in0=m1s, scalar=-1.0,
                                           in1=m2s, op0=ALU.mult, op1=ALU.add)
                    v.drain()
                    v.reciprocal(rpr, nrow)
                    v.drain()
                    v.tensor_scalar(out=sc1, in0=gc_ps, scalar1=rpr[0:1, 1:2],
                                    scalar2=None, op0=ALU.mult)
                    v.tensor_scalar(out=s11v, in0=bb2[0:1, 0:1],
                                    scalar1=rpr[0:1, 1:2], scalar2=None, op0=ALU.mult)
                    v.tensor_scalar(out=s22v, in0=bb2[0:1, 1:2],
                                    scalar1=rpr[0:1, 1:2], scalar2=None, op0=ALU.mult)
                    v.drain()
                    v.tensor_scalar(out=q1, in0=y12r[0:1, 0:128],
                                    scalar1=s22v[0:1, 0:1], scalar2=None, op0=ALU.mult)
                    v.tensor_scalar(out=q2, in0=y12r[0:1, 128:256],
                                    scalar1=sc1[0:1, 0:1],
                                    scalar2=None, op0=ALU.mult)
                    v.tensor_scalar(out=q3, in0=y12r[0:1, 128:256],
                                    scalar1=s11v[0:1, 0:1],
                                    scalar2=None, op0=ALU.mult)
                    v.tensor_scalar(out=q4, in0=y12r[0:1, 0:128],
                                    scalar1=sc1[0:1, 0:1],
                                    scalar2=None, op0=ALU.mult)
                    v.drain()
                    v.tensor_tensor(out=p1r, in0=q1, in1=q2, op=ALU.subtract)
                    v.tensor_tensor(out=p2r, in0=q3, in1=q4,
                                    op=ALU.subtract).then_inc(qVs, 1)
                    v.tensor_copy(out=rdrow[0:1, rowslice(rVi, 1, NT - 1)],
                                  in_=rpr[0:1, 0:1])
                    v.tensor_copy(out=rdrow[0:1, rowslice(rVi1, 1, NT - 1)],
                                  in_=s11v)
                    v.tensor_copy(out=scrow_b[0:1, rowslice(rVi1, 1, NT - 1)],
                                  in_=sc1)
                    v.reg_add(rT, rT, 1)
                    v.wait_ge(qTs, rT)
                    v.scalar_tensor_tensor(out=Asb[:, :], in0=A_ps, scalar=1.0,
                                           in1=idf_s[:, :], op0=ALU.mult,
                                           op1=ALU.add)
                    v.tensor_copy(out=Wbuf[:, rowslice(rVi, 2, NT - 2)],
                                  in_=ypair_sb[:, :]).then_inc(qVs, 1)
                    v.reg_add(rT, rT, 1)
                    v.reg_add(rVi, rVi, 2)
                    v.reg_add(rVi1, rVi1, 2)
            ct.inc(qVs, 3 * NPS)

        def emit_group(gi):
            t0 = gi * 512
            qvs_need = 1 + 3 * PPG * (gi + 1)

            @block.vector
            def _(v, gi=gi, t0=t0, qvs_need=qvs_need):
                v = EW(v)
                v.wait_ge(qVs, qvs_need)
                v.tensor_tensor(out=arow[0:1, t0:t0 + 512],
                                in0=srow[0:1, t0:t0 + 512],
                                in1=rdrow[0:1, t0:t0 + 512], op=ALU.mult)
                v.tensor_tensor(out=scrow_b[0:1, t0:t0 + 512],
                                in0=scrow_b[0:1, t0:t0 + 512],
                                in1=srow[0:1, t0:t0 + 512], op=ALU.mult)
                v.drain()
                v.tensor_scalar(out=arow[0:1, t0:t0 + 512],
                                in0=arow[0:1, t0:t0 + 512],
                                scalar1=-1.0, scalar2=None, op0=ALU.mult)
                v.inc_last(qV, 1)
            ct.inc(qV)
            alv = ct.val(qV)

            @block.tensor
            def _(t, t0=t0, alv=alv):
                t = EW(t)
                t.wait_ge(qV, alv)
                t.matmul(psA[:, 0:512], onesrow_f[64:65, :], arow[0:1, t0:t0 + 512],
                         start=True, stop=True)
                t.matmul(psC[:, 0:512], onesrow_b[0:1, 0:128],
                         scrow_b[0:1, t0:t0 + 512], start=True, stop=True)
                t.inc_last(qT, 1)
            ct.inc(qT)
            ab_ = ct.val(qT)

            @block.vector
            def _(v, gi=gi, t0=t0, ab_=ab_):
                v = EW(v)
                v.wait_ge(qT, ab_)
                v.tensor_tensor(out=alphaT[:, t0:t0 + 512], in0=Wbuf[:, t0:t0 + 512],
                                in1=psA[:, 0:512], op=ALU.mult)
                lo = 1 if gi == 0 else 0
                v.tensor_tensor(out=sq_scr[:, lo:512],
                                in0=Wbuf[:, t0 + lo - 1:t0 + 511],
                                in1=psC[:, lo:512], op=ALU.mult)
                v.drain()
                v.tensor_tensor(out=alphaT[:, t0 + lo:t0 + 512],
                                in0=alphaT[:, t0 + lo:t0 + 512],
                                in1=sq_scr[:, lo:512], op=ALU.add)
                v.inc_last(qV, 2)
            ct.inc(qV, 2)

            for c in range(4 * gi, 4 * gi + 4):
                tc = c * 128

                @block.tensor
                def _(t, c=c, tc=tc):
                    t = EW(t)
                    t.wait_ge(qV, ct.val(qV))
                    t.matmul(psA[:, 0:128], alphaT[:, tc:tc + 128],
                             QTb[:, tc:tc + 128], start=True, stop=True)
                    t.inc_last(qT, 1)
                    t.transpose(psBb[:, 1024:1152], alphaT[:, tc:tc + 128],
                                idb_s[:, :])
                    t.inc_last(qT, 1)
                ct.inc(qT, 2)
                pq = ct.val(qT)

                @block.vector
                def _(v, c=c, pq=pq):
                    v = EW(v)
                    v.wait_ge(qT, pq - 1)
                    v.tensor_tensor(out=psm_sb[:, :], in0=psA[:, 0:128],
                                    in1=triu_s[:, :], op=ALU.mult)
                    v.wait_ge(qT, pq)
                    v.tensor_copy(out=atm[:, :], in_=psBb[:, 1024:1152])
                    v.inc_last(qV, 1)
                ct.inc(qV)
                pv = ct.val(qV)

                @block.tensor
                def _(t, c=c, tc=tc, pv=pv):
                    t = EW(t)
                    t.wait_ge(qV, pv)
                    t.matmul(psC[:, 0:128], Vtm[:, tc:tc + 128], psm_sb[:, :],
                             start=True, stop=(c == 0))
                    if c > 0:
                        t.matmul(psC[:, 0:128], Mst_sb[:, :], QTb[:, tc:tc + 128],
                                 start=False, stop=True)
                    t.inc_last(qT, 1)
                    t.matmul(psD[:, 0:128], atm[:, :], Vtm[:, tc:tc + 128],
                             start=(c == 0), stop=(c == NCH - 1),
                             skip_group_check=True)
                    t.inc_last(qT, 1)
                ct.inc(qT, 2)
                oq = ct.val(qT)

                @block.vector
                def _(v, c=c, tc=tc, oq=oq):
                    v = EW(v)
                    v.wait_ge(qT, oq - 1)
                    v.tensor_copy(out=OT[:, tc:tc + 128], in_=psC[:, 0:128])
                    v.wait_ge(qT, oq)
                    v.tensor_copy(out=Mst_sb[:, :], in_=psD[:, 0:128])
                    v.inc_last(qV, 1)
                ct.inc(qV)

            for dc in range(NDC):
                @block.tensor
                def _(t, t0=t0, dc=dc):
                    t = EW(t)
                    t.wait_ge(qV, ct.val(qV))
                    t.matmul(psA[:, 0:512], aWo_s[:, dc * 128:(dc + 1) * 128],
                             OT[:, t0:t0 + 512], start=True, stop=True)
                    t.inc_last(qT, 1)
                ct.inc(qT)
                tq2 = ct.val(qT)

                @block.vector
                def _(v, t0=t0, dc=dc, tq2=tq2):
                    v = EW(v)
                    v.wait_ge(qT, tq2)
                    v.tensor_scalar(out=sqf[:, :], in0=psA[:, 0:512],
                                    scalar1=abo_s[:, dc:dc + 1], scalar2=None,
                                    op0=ALU.add)
                    v.drain()
                    v.tensor_tensor(out=x_sb[:, dc * NT + t0: dc * NT + t0 + 512],
                                    in0=x_sb[:, dc * NT + t0: dc * NT + t0 + 512],
                                    in1=sqf[:, :], op=ALU.add)
                    v.inc_last(qV, 1)
                ct.inc(qV)

        GPS = NG // NSEG
        for seg in range(NSEG):
            scan_seg_T(seg)
            scan_seg_V(seg)
            for g in range(seg * GPS, (seg + 1) * GPS):
                emit_group(g)
        SCAN_TS = 3 * NP
        SCAN_VS = 1 + 3 * NP
        ATTNA_V = ct.val(qV)

        ov.close()

        ffn = dict(x_sb=x_sb, W1bf=W1bf, W2bf=W2bf, b1fl=b1fl, qP=qP, qSTRM=qSTRM,
                   rW_s=rW_s, b2_s=b2_s, bs2T_s=bs2T_s,
                   selE_s=selE_s, idf_s=idf_s, onesrow_b=onesrow_b,
                   psH=psH, psC=psC, psD=psD, psY=psY, psA=psA, psB=psB,
                   qTh=qTh, qTw2=qTw2, qSq=qSq, qSg=qSg, qVp=qVp, qVu=qVu,
                   qVg=qVg, qPg=qPg, qSY=qSY, qGP=qGP, qST=qST,
                   PRECAST_GP=PRECAST_GP, y_out=y_out, LN=LN, NT=NT)

        emit_ffn(nc, block, ct, ffn, last=False, tag="A")

        attb = dict(x_sb=x_sb, bWq=bWq, bWk=bWk, bWv=bWv, bWo=bWo,
                    maskA_s=maskA_s, maskB_s=maskB_s, onescol_b=onescol_b,
                    onesrow_f=onesrow_f,
                    psH=psH, psC=psC, psD=psD, psA=psA, psB=psB, psY=psY,
                    qGP=qGP, qSY=qSY, qTw2=qTw2, LN=LN, NT=NT)

        emit_attn_b(nc, block, ct, attb)

        emit_ffn(nc, block, ct, ffn, last=True, tag="B")

        @block.sync
        def _(sy):
            sy = EW(sy)
            sy.wait_ge(qST, ct.val(qST))

    es.close()
    return nc


def emit_ln(nc, block, ct, x_sb, xn_out, sq_scr, t0, W, NT, LN, pre_waits=None,
            xn_stride=None):
    if xn_stride is None:
        xn_stride = W
    qT, qV, qS = LN["qT"], LN["qV"], LN["qS"]
    psA, psB, psC, psD = LN["psA"], LN["psB"], LN["psC"], LN["psD"]

    @block.scalar
    def _(s):
        s = EW(s)
        s.wait_ge(qV, ct.val(qV))
        if pre_waits:
            for (q, vv) in pre_waits:
                s.wait_ge(q, vv)
        for dc in range(NDC):
            s.activation(out=sq_scr[:, dc * W:(dc + 1) * W],
                         in_=x_sb[:, dc * NT + t0: dc * NT + t0 + W], func=AF.Square)
        s.inc_last(qS, 1)
    ct.inc(qS)
    sqq = ct.val(qS)

    @block.tensor
    def _(t, sqq=sqq):
        t = EW(t)
        t.wait_ge(qS, sqq)
        for dc in range(NDC):
            t.matmul(psA[0:1, 0:W], LN["onescol_b"][:, :],
                     x_sb[:, dc * NT + t0: dc * NT + t0 + W],
                     start=(dc == 0), stop=(dc == NDC - 1))
        for dc in range(NDC):
            t.matmul(psB[0:1, 0:W], LN["onescol_b"][:, :],
                     sq_scr[:, dc * W:(dc + 1) * W],
                     start=(dc == 0), stop=(dc == NDC - 1))
        t.inc_last(qT, 1)
    ct.inc(qT)
    stq = ct.val(qT)

    @block.vector
    def _(v, stq=stq):
        v = EW(v)
        v.wait_ge(qT, stq)
        v.tensor_scalar(out=LN["meanrow"][0:1, 0:W], in0=psA[0:1, 0:W],
                        scalar1=1.0 / D, scalar2=None, op0=ALU.mult)
        v.drain()
        v.tensor_tensor(out=LN["varrow"][0:1, 0:W], in0=LN["meanrow"][0:1, 0:W],
                        in1=LN["meanrow"][0:1, 0:W], op=ALU.mult)
        v.drain()
        v.scalar_tensor_tensor(out=LN["varrow"][0:1, 0:W], in0=psB[0:1, 0:W],
                               scalar=1.0 / D, in1=LN["varrow"][0:1, 0:W],
                               op0=ALU.mult, op1=ALU.subtract)
        v.inc_last(qV, 1)
    ct.inc(qV)
    vq = ct.val(qV)

    @block.scalar
    def _(s, vq=vq):
        s = EW(s)
        s.wait_ge(qV, vq)
        s.activation(out=LN["rstdrow"][0:1, 0:W], in_=LN["varrow"][0:1, 0:W],
                     func=AF.Sqrt, bias=LN["cst"][0:1, 1:2])
        s.inc_last(qS, 1)
    ct.inc(qS)
    sq2 = ct.val(qS)

    @block.vector
    def _(v, sq2=sq2):
        v = EW(v)
        v.wait_ge(qS, sq2)
        v.reciprocal(LN["rstdrow"][0:1, 0:W], LN["rstdrow"][0:1, 0:W])
        v.inc_last(qV, 1)
    ct.inc(qV)
    vq2 = ct.val(qV)

    @block.tensor
    def _(t, vq2=vq2):
        t = EW(t)
        t.wait_ge(qV, vq2)
        t.matmul(psC[:, 0:W], LN["onesrow_f"][0:1, :], LN["meanrow"][0:1, 0:W],
                 start=True, stop=True)
        t.matmul(psD[:, 0:W], LN["onesrow_f"][32:33, :], LN["rstdrow"][0:1, 0:W],
                 start=True, stop=True)
        t.inc_last(qT, 1)
    ct.inc(qT)
    bcq = ct.val(qT)

    @block.vector
    def _(v, bcq=bcq):
        v = EW(v)
        v.wait_ge(qT, bcq)
        for dc in range(NDC):
            v.tensor_tensor(out=sq_scr[:, dc * W:(dc + 1) * W],
                            in0=x_sb[:, dc * NT + t0: dc * NT + t0 + W],
                            in1=psC[:, 0:W], op=ALU.subtract)
        v.drain()
        for dc in range(NDC):
            v.tensor_tensor(out=xn_out[:, dc * xn_stride:dc * xn_stride + W],
                            in0=sq_scr[:, dc * W:(dc + 1) * W],
                            in1=psD[:, 0:W], op=ALU.mult)
        v.inc_last(qV, 1)
    ct.inc(qV)


def emit_ffn(nc, block, ct, P, last, tag=""):
    NT = P["NT"]
    NBLK = NT // BLK
    x_sb = P["x_sb"]
    psH, psC, psD, psY = P["psH"], P["psC"], P["psD"], P["psY"]
    psA, psB = P["psA"], P["psB"]
    LN = P["LN"]
    qT, qV, qS = LN["qT"], LN["qV"], LN["qS"]
    qTh, qTw2 = P["qTh"], P["qTw2"]
    qSq, qSg, qVp, qVu, qVg, qPg = P["qSq"], P["qSg"], P["qVp"], P["qVu"], P["qVg"], P["qPg"]
    qSY, qGP, qST, qP = P["qSY"], P["qGP"], P["qST"], P["qP"]
    qsW1 = list(P["qSTRM"][tag][0:4])
    qsW2 = list(P["qSTRM"][tag][4:8])

    sc = contextlib.ExitStack()

    def sct(n, shp, dt):
        return sc.enter_context(nc.sbuf_tensor(n + tag, shp, dt))

    P = dict(P)
    P["w1buf"] = sct("w1buf", [128, 4 * 2304], BF16)
    P["w2buf"] = sct("w2buf", [128, 4 * 2048], BF16)
    P["xn_blk"] = sct("xn_blk", [128, NDC * BLK], BF16)
    P["lnsq"] = sct("lnsq", [128, NDC * BLK], BF16)
    P["gateB"] = sct("gateB", [128, 9 * BLK], BF16)
    P["gatesT"] = sct("gatesT", [8, BLK], F32)
    P["gsc"] = sct("gsc", [128, 48], F32)
    P["sqg"] = sct("sqg", [128, 4 * 512], BF16)
    P["pg"] = sct("pg", [128, 512], BF16)
    P["u2g"] = sct("u2g", [128, 512], F32)
    P["sgg"] = sct("sgg", [128, 512], BF16)
    P["hg0"] = sct("hg0", [128, 4 * 512], BF16)
    P["hgb"] = sct("hgb", [128, 4 * 512], BF16)
    P["outst"] = sct("outst", [128, 2048], F32)
    HS = [psH[:, 0:512], psH[:, 512:1024], psC[:, 0:512], psD[:, 0:512]]
    ENTRY_V = ct.val(qV)
    ENTRY_SY = ct.val(qSY)
    ENTRY_TW2 = ct.val(qTw2)

    @block.gpsimd
    def _(g):
        g = EW(g)
        g.wait_ge(qV, ENTRY_V)
        g.wait_ge(qSY, ENTRY_SY)
        g.wait_ge(qTw2, ENTRY_TW2)
        g.inc_last(qP, 1)
        ct.inc(qP)

    @block.vector
    def _(v):
        v = EW(v)
        v.wait_ge(qV, ENTRY_V)
        v.memset(P["gateB"][:, 8 * BLK:9 * BLK], 1.0)
        v.inc_last(qV, 1)
    ct.inc(qV)
    ENTRY_P = ct.val(qP)

    CB_Tw2 = ct.val(qTw2)
    CB_W1 = [ct.val(q) for q in qsW1]
    CB_W2 = [ct.val(q) for q in qsW2]

    @block.sync
    def _(sy):
        sy = EW(sy)
        sy.wait_ge(qP, ENTRY_P)
        if not last:
            sy.wait_ge(qGP, P["PRECAST_GP"])
        for k in range(NBLK * NM):
            s = k % 4
            m = k % NM
            if k >= 4:
                sy.wait_ge(qTw2, CB_Tw2 + k - 3)
            sy.dma_start(out=P["w1buf"][:, s * 2304:(s + 1) * 2304],
                         in_=P["W1bf"][m * 128:(m + 1) * 128, :]).then_inc(qsW1[s], 16)
            sy.dma_start(out=P["w2buf"][:, s * 2048:(s + 1) * 2048],
                         in_=P["W2bf"][m * 128:(m + 1) * 128, :]).then_inc(qsW2[s], 16)
    for _q in qsW1 + qsW2:
        ct.inc(_q, 16 * (NM // 4) * NBLK)

    for blk in range(NBLK):
        t0 = blk * BLK
        emit_ln(nc, block, ct, x_sb, P["xn_blk"], P["lnsq"], t0, BLK, NT, LN)

        for ti in range(2):
            tt0 = ti * 128

            @block.tensor
            def _(t, tt0=tt0):
                t = EW(t)
                t.wait_ge(qV, ct.val(qV))
                for dc in range(NDC):
                    t.matmul(psC[0:128, 0:8],
                             P["xn_blk"][:, dc * BLK + tt0: dc * BLK + tt0 + 128],
                             P["rW_s"][:, dc * NE:(dc + 1) * NE],
                             start=(dc == 0), stop=(dc == NDC - 1))
                t.inc_last(qT, 1)
            ct.inc(qT)
            lq = ct.val(qT)

            @block.vector
            def _(v, lq=lq):
                v = EW(v)
                v.wait_ge(qT, lq)
                v.tensor_copy(out=P["gsc"][:, 0:8], in_=psC[0:128, 0:8])
                v.drain()
                v.reduce_max(out=P["gsc"][:, 8:9], in_=P["gsc"][:, 0:8],
                             axis=AX.X, negate=True)
                v.inc_last(qV, 1)
            ct.inc(qV)
            lv = ct.val(qV)

            @block.scalar
            def _(s, lv=lv):
                s = EW(s)
                s.wait_ge(qV, lv)
                s.activation(out=P["gsc"][:, 16:24], in_=P["gsc"][:, 0:8], func=AF.Exp,
                             bias=P["gsc"][:, 8:9], accum_out=P["gsc"][:, 9:10])
                s.inc_last(qS, 1)
            ct.inc(qS)
            ls = ct.val(qS)

            @block.vector
            def _(v, ls=ls):
                v = EW(v)
                v.wait_ge(qS, ls)
                v.reciprocal(P["gsc"][:, 10:11], P["gsc"][:, 9:10])
                v.max(out=P["gsc"][:, 24:32], in_=P["gsc"][:, 16:24])
                v.drain()
                v.tensor_scalar(out=P["gsc"][:, 32:40], in0=P["gsc"][:, 16:24],
                                scalar1=P["gsc"][:, 25:26], scalar2=None,
                                op0=ALU.is_ge)
                v.drain()
                v.tensor_tensor(out=P["gsc"][:, 40:48], in0=P["gsc"][:, 16:24],
                                in1=P["gsc"][:, 32:40], op=ALU.mult)
                v.drain()
                v.tensor_scalar(out=P["gsc"][:, 0:8], in0=P["gsc"][:, 40:48],
                                scalar1=P["gsc"][:, 10:11], scalar2=None,
                                op0=ALU.mult)
                v.inc_last(qV, 1)
            ct.inc(qV)
            gv = ct.val(qV)

            @block.tensor
            def _(t, gv=gv):
                t = EW(t)
                t.wait_ge(qV, gv)
                t.transpose(psC[0:8, 256:384], P["gsc"][:, 0:8], P["idf_s"][:, :])
                t.inc_last(qT, 1)
            ct.inc(qT)
            gq = ct.val(qT)

            @block.vector
            def _(v, tt0=tt0, gq=gq):
                v = EW(v)
                v.wait_ge(qT, gq)
                v.tensor_copy(out=P["gatesT"][:, tt0:tt0 + 128], in_=psC[0:8, 256:384])
                v.inc_last(qV, 1)
            ct.inc(qV)
        GATES_V = ct.val(qV)

        for e in range(8):
            @block.tensor
            def _(t, e=e, w=ct.val(qV)):
                t = EW(t)
                t.wait_ge(qV, w)
                t.matmul(psC[:, 0:256], P["selE_s"][:, e * 128:(e + 1) * 128],
                         P["gatesT"][:, :], start=True, stop=True)
                t.inc_last(qT, 1)
            ct.inc(qT)
            eq = ct.val(qT)

            @block.vector
            def _(v, e=e, eq=eq):
                v = EW(v)
                v.wait_ge(qT, eq)
                v.tensor_copy(out=P["gateB"][:, e * BLK:(e + 1) * BLK],
                              in_=psC[:, 0:256])
                v.inc_last(qV, 1)
            ct.inc(qV)
        GB_V = ct.val(qV)

        @block.tensor
        def _(t):
            t = EW(t)
            t.wait_ge(qV, GB_V)
            for dcp in range(4):
                for side in range(2):
                    dc = 2 * dcp + side
                    t.matmul(psY[:, dcp * 512 + side * 256: dcp * 512 + side * 256 + 256],
                             P["b2_s"][:, dc * 128:(dc + 1) * 128], P["gatesT"][:, :],
                             start=(side == 0), stop=False, skip_group_check=True)
            t.inc_last(qT, 1)
        ct.inc(qT)

        bqW1 = [CB_W1[s] + blk * 16 * (NM // 4) for s in range(4)]
        bqW2 = [CB_W2[s] + blk * 16 * (NM // 4) for s in range(4)]
        base_Th = ct.val(qTh)
        base_Tw2 = ct.val(qTw2)
        base_Sq = ct.val(qSq)
        base_Sg = ct.val(qSg)
        base_Vp = ct.val(qVp)
        base_Vu = ct.val(qVu)
        base_Vg = ct.val(qVg)
        base_Pg = ct.val(qPg)
        T_ENTRY = ct.val(qT)
        V_ENTRY = ct.val(qV)


        bq = bqW1[0]
        assert all(b == bq for b in bqW1 + bqW2)

        def t_w1(t, s):
            h = HS[s]
            for fsub in range(2):
                for dc in range(NDC):
                    t.matmul(h[:, fsub * 256: fsub * 256 + 256],
                             P["w1buf"][:, s * 2304 + dc * 256 + fsub * 128:
                                        s * 2304 + dc * 256 + fsub * 128 + 128],
                             P["xn_blk"][:, dc * BLK:(dc + 1) * BLK],
                             start=(dc == 0), stop=False)
                t.matmul(h[:, fsub * 256: fsub * 256 + 256],
                         P["w1buf"][0:1, s * 2304 + 2048 + fsub * 128:
                                    s * 2304 + 2048 + fsub * 128 + 128],
                         P["onesrow_b"][0:1, 0:256], start=False, stop=True)
            t.inc_last(qTh, 1)

        def t_w2(t, s):
            for fr in range(2):
                for dc in range(NDC):
                    t.matmul(psY[:, (dc // 2) * 512 + (dc % 2) * 256:
                                  (dc // 2) * 512 + (dc % 2) * 256 + 256],
                             P["w2buf"][:, s * 2048 + fr * 1024 + dc * 128:
                                        s * 2048 + fr * 1024 + dc * 128 + 128],
                             P["hgb"][:, s * 512 + fr * 256: s * 512 + fr * 256 + 256],
                             start=False, stop=False, skip_group_check=True)
            t.inc_last(qTw2, 1)

        @block.tensor
        def _(t):
            t = EW(t)
            for j in range(4):
                t.wait_ge(qsW1[j], bq + 16)
                if base_Vg + j - 3 > 0:
                    t.wait_ge(qVg, base_Vg + j - 3)
                t_w1(t, j)
                if j >= 1:
                    t.wait_ge(qPg, base_Pg + j)
                    t.wait_ge(qsW2[j - 1], bq + 16)
                    t_w2(t, j - 1)
            with kreg(t) as rQ, kreg(t) as rQm, kreg(t) as rVg, kreg(t) as rPg:
                t.reg_mov(rQ, bq + 32)
                t.reg_mov(rQm, bq + 16)
                t.reg_mov(rVg, base_Vg + 1)
                t.reg_mov(rPg, base_Pg + 4)
                with kfori(nc, t, mybir.EngineType.PE, 1, NM // 4, 1):
                    for j in range(4):
                        t.wait_ge(qsW1[j], rQ)
                        t.wait_ge(qVg, rVg)
                        t_w1(t, j)
                        t.wait_ge(qPg, rPg)
                        t.wait_ge(qsW2[(j + 3) % 4], rQm if j == 0 else rQ)
                        t_w2(t, (j + 3) % 4)
                        t.reg_add(rVg, rVg, 1)
                        t.reg_add(rPg, rPg, 1)
                    t.reg_add(rQ, rQ, 16)
                    t.reg_add(rQm, rQm, 16)
            t.wait_ge(qPg, base_Pg + NM)
            t.wait_ge(qsW2[3], bq + 16 * (NM // 4))
            t_w2(t, 3)
        ct.inc(qTh, NM)
        ct.inc(qTw2, NM)

        @block.scalar
        def _(s):
            s = EW(s)
            with kreg(s) as rTh, kreg(s) as rVu, kreg(s) as rVp:
                s.reg_mov(rTh, base_Th + 1)
                s.reg_mov(rVu, base_Vu + 1)
                s.reg_mov(rVp, max(base_Vp - 3, 0))
                with kfori(nc, s, mybir.EngineType.Activation, 0, NM // 4, 1):
                    for j in range(4):
                        s.wait_ge(qTh, rTh)
                        s.wait_ge(qVp, rVp)
                        s.activation(out=P["sqg"][:, j * 512:(j + 1) * 512],
                                     in_=HS[j][:, 0:512],
                                     func=AF.Square).then_inc(qSq, 1)
                        s.wait_ge(qVu, rVu)
                        s.activation(out=P["sgg"][:, :], in_=P["u2g"][:, :],
                                     func=AF.Sigmoid).then_inc(qSg, 1)
                        s.reg_add(rTh, rTh, 1)
                        s.reg_add(rVu, rVu, 1)
                        s.reg_add(rVp, rVp, 1)
        ct.inc(qSq, NM)
        ct.inc(qSg, NM)

        @block.vector
        def _(v):
            v = EW(v)
            v.wait_ge(qV, GB_V)
            with kreg(v) as rSq, kreg(v) as rSg, kreg(v) as rT2:
                v.reg_mov(rSq, base_Sq + 1)
                v.reg_mov(rSg, base_Sg + 1)
                v.reg_mov(rT2, max(base_Tw2 - 3, 0))
                for e in range(9):
                    goff = e * BLK
                    with kfori(nc, v, mybir.EngineType.DVE, e * 4, e * 4 + 4, 1):
                        for j in range(4):
                            v.wait_ge(qSq, rSq)
                            v.tensor_scalar(out=P["pg"][:, :],
                                            in0=P["sqg"][:, j * 512:(j + 1) * 512],
                                            scalar1=GC2, scalar2=GC1, op0=ALU.mult,
                                            op1=ALU.add).then_inc(qVp, 1)
                            v.drain()
                            v.tensor_tensor(out=P["u2g"][:, :],
                                            in0=HS[j][:, 0:512],
                                            in1=P["pg"][:, :],
                                            op=ALU.mult).then_inc(qVu, 1)
                            v.wait_ge(qSg, rSg)
                            v.tensor_tensor(out=P["hg0"][:, j * 512:(j + 1) * 512],
                                            in0=HS[j][:, 0:512],
                                            in1=P["sgg"][:, :],
                                            op=ALU.mult).then_inc(qVg, 1)
                            v.wait_ge(qTw2, rT2)
                            v.drain()
                            v.tensor_tensor(out=P["hgb"][:, j * 512: j * 512 + 256],
                                            in0=P["hg0"][:, j * 512: j * 512 + 256],
                                            in1=P["gateB"][:, goff:goff + 256],
                                            op=ALU.mult)
                            v.tensor_tensor(out=P["hgb"][:, j * 512 + 256: j * 512 + 512],
                                            in0=P["hg0"][:, j * 512 + 256: j * 512 + 512],
                                            in1=P["gateB"][:, goff:goff + 256],
                                            op=ALU.mult).then_inc(qPg, 1)
                            v.reg_add(rSq, rSq, 1)
                            v.reg_add(rSg, rSg, 1)
                            v.reg_add(rT2, rT2, 1)
        ct.inc(qVp, NM)
        ct.inc(qVu, NM)
        ct.inc(qVg, NM)
        ct.inc(qPg, NM)

        @block.vector
        def _(v, blk=blk, t0=t0, wst=ct.val(qST)):
            v = EW(v)
            v.wait_ge(qTw2, base_Tw2 + NM)
            if last:
                v.wait_ge(qST, wst)
            for dc in range(NDC):
                ysl = psY[:, (dc // 2) * 512 + (dc % 2) * 256:
                          (dc // 2) * 512 + (dc % 2) * 256 + 256]
                v.tensor_scalar(out=P["outst"][:, dc * 256:(dc + 1) * 256],
                                in0=ysl, scalar1=P["bs2T_s"][:, dc:dc + 1],
                                scalar2=None, op0=ALU.add)
            v.inc_last(qV, 1)
        ct.inc(qV)
        ev1 = ct.val(qV)

        @block.vector
        def _(v, blk=blk, t0=t0, ev1=ev1):
            v = EW(v)
            v.wait_ge(qV, ev1)
            for dc in range(NDC):
                v.scalar_tensor_tensor(out=P["outst"][:, dc * 256:(dc + 1) * 256],
                                       in0=P["outst"][:, dc * 256:(dc + 1) * 256],
                                       scalar=SCALE_FFN,
                                       in1=x_sb[:, dc * NT + t0: dc * NT + t0 + 256],
                                       op0=ALU.mult, op1=ALU.add)
            v.inc_last(qV, 1)
        ct.inc(qV)
        ev2 = ct.val(qV)

        if last:
            @block.gpsimd
            def _(g, blk=blk, t0=t0, ev2=ev2):
                g = EW(g)
                g.wait_ge(qV, ev2)
                g.dma_start(out=bass.AP(P["y_out"].tensor, t0,
                                        [[NT, 128], [128 * NT, NDC], [1, 256]]),
                            in_=P["outst"][:, :].rearrange("p (dc t) -> p dc t", dc=NDC)).then_inc(qST, 16)
            ct.inc(qST, 16)
        else:
            @block.vector
            def _(v, blk=blk, t0=t0, ev2=ev2):
                v = EW(v)
                v.wait_ge(qV, ev2)
                for dc in range(NDC):
                    v.tensor_copy(out=x_sb[:, dc * NT + t0: dc * NT + t0 + 256],
                                  in_=P["outst"][:, dc * 256:(dc + 1) * 256])
                v.inc_last(qV, 1)
            ct.inc(qV)

    sc.close()


def emit_attn_b(nc, block, ct, P):
    NT = P["NT"]
    NTIL = NT // 128
    x_sb = P["x_sb"]
    psH, psC, psD = P["psH"], P["psC"], P["psD"]
    psA, psB = P["psA"], P["psB"]
    psY = P["psY"]
    LN = P["LN"]
    qT, qV, qS = LN["qT"], LN["qV"], LN["qS"]
    qGP, qSY, qTw2 = P["qGP"], P["qSY"], P["qTw2"]
    SCL = 1.0 / 8.0

    sc = contextlib.ExitStack()

    def sct(n, shp, dt):
        return sc.enter_context(nc.sbuf_tensor(n, shp, dt))

    P = dict(P)
    P["lnsq"] = sct("ab_lnsq", [128, NDC * 128], BF16)
    P["xnb"] = sct("ab_xnb", [128, NDC * 128], BF16)
    P["bQT"] = sct("ab_bQT", [128, NDC * 128], BF16)
    P["bKTr"] = sct("ab_bKTr", [128, 2 * NDC * 128], BF16)
    P["bVtr"] = sct("ab_bVtr", [128, 2 * NDC * 128], BF16)
    P["bexp"] = sct("ab_bexp", [128, 4 * 256], BF16)
    P["bexn"] = sct("ab_bexn", [128, 2 * 256], BF16)
    P["bctx"] = sct("ab_bctx", [128, NDC * 128], BF16)
    P["brs"] = sct("ab_brs", [1, 2 * 128], F32)
    P["bmsk"] = sct("ab_bmsk", [128, 2 * 256], F32)
    P["outst"] = sct("ab_outst", [128, NDC * 128], F32)
    P["bWq_s"] = sct("ab_bWq", [128, NDC * D], BF16)
    P["bWk_s"] = sct("ab_bWk", [128, NDC * D], BF16)
    P["bWv_s"] = sct("ab_bWv", [128, NDC * D], BF16)
    P["bWo_s"] = sct("ab_bWo", [128, NDC * D], BF16)
    ENTRY_V = ct.val(qV)
    Hscq, Hmkv, Hexq, Hsmq, Hrsq, Hrbq, Henv, Hcxq = {}, {}, {}, {}, {}, {}, {}, {}

    @block.gpsimd
    def _(g):
        g = EW(g)
        g.wait_ge(qV, ENTRY_V)
        g.wait_ge(qSY, ct.val(qSY))
        g.wait_ge(qTw2, ct.val(qTw2))
        g.wait_ge(qGP, ct.val(qGP))
        for dst, src in ((P["bWq_s"], P["bWq"]), (P["bWk_s"], P["bWk"]),
                         (P["bWv_s"], P["bWv"]), (P["bWo_s"], P["bWo"])):
            g.dma_start(out=dst[:, :].rearrange("p (c o) -> p c o", c=NDC),
                        in_=src.rearrange("(c p) o -> p c o", p=128)).then_inc(qGP, 16)
            ct.inc(qGP, 16)
    BW_GP = ct.val(qGP)

    for ti in range(NTIL):
        t0 = ti * 128
        slot = ti % 2
        pre = [(qGP, BW_GP)] if ti == 0 else None
        emit_ln(nc, block, ct, x_sb, P["xnb"], P["lnsq"], t0, 128, NT, LN,
                pre_waits=pre)
        XN_V = ct.val(qV)

        @block.tensor
        def _(t, ti=ti):
            t = EW(t)
            t.wait_ge(qV, XN_V)
            for oc in range(NDC):
                for dc in range(NDC):
                    t.matmul(psH[:, oc * 128:(oc + 1) * 128],
                             P["bWq_s"][:, dc * D + oc * 128: dc * D + oc * 128 + 128],
                             P["xnb"][:, dc * 128:(dc + 1) * 128],
                             start=(dc == 0), stop=(dc == NDC - 1))
            t.inc_last(qT, 1)
        ct.inc(qT)
        q1 = ct.val(qT)

        @block.vector
        def _(v, q1=q1):
            v = EW(v)
            v.wait_ge(qT, q1)
            for oc in range(NDC):
                v.tensor_copy(out=P["bQT"][:, oc * 128:(oc + 1) * 128],
                              in_=psH[:, oc * 128:(oc + 1) * 128])
            v.inc_last(qV, 1)
        ct.inc(qV)
        q1v = ct.val(qV)

        @block.tensor
        def _(t, XN_V=XN_V):
            t = EW(t)
            t.wait_ge(qV, XN_V)
            for oc in range(NDC):
                for dc in range(NDC):
                    t.matmul(psY[:, 1024 + oc * 128: 1024 + (oc + 1) * 128],
                             P["bWk_s"][:, dc * D + oc * 128: dc * D + oc * 128 + 128],
                             P["xnb"][:, dc * 128:(dc + 1) * 128],
                             start=(dc == 0), stop=(dc == NDC - 1))
            t.inc_last(qT, 1)
        ct.inc(qT)
        q2 = ct.val(qT)

        @block.vector
        def _(v, slot=slot, q2=q2):
            v = EW(v)
            v.wait_ge(qT, q2)
            for oc in range(NDC):
                v.tensor_copy(out=P["bKTr"][:, slot * 1024 + oc * 128:
                                            slot * 1024 + (oc + 1) * 128],
                              in_=psY[:, 1024 + oc * 128: 1024 + (oc + 1) * 128])
            v.inc_last(qV, 1)
        ct.inc(qV)
        q2v = ct.val(qV)

        @block.tensor
        def _(t, XN_V=XN_V):
            t = EW(t)
            t.wait_ge(qV, XN_V)
            for nh in range(2):
                for dc in range(NDC):
                    t.matmul(psY[:, nh * 512:(nh + 1) * 512],
                             P["xnb"][:, dc * 128:(dc + 1) * 128],
                             P["bWv_s"][:, dc * D + nh * 512: dc * D + nh * 512 + 512],
                             start=(dc == 0), stop=(dc == NDC - 1))
            t.inc_last(qT, 1)
        ct.inc(qT)
        q3 = ct.val(qT)

        @block.vector
        def _(v, slot=slot, q3=q3):
            v = EW(v)
            v.wait_ge(qT, q3)
            v.tensor_copy(out=P["bVtr"][:, slot * 1024: slot * 1024 + 512],
                          in_=psY[:, 0:512])
            v.tensor_copy(out=P["bVtr"][:, slot * 1024 + 512: slot * 1024 + 1024],
                          in_=psY[:, 512:1024])
            v.inc_last(qV, 1)
        ct.inc(qV)
        QKV_V = ct.val(qV)

        def hb_params(h):
            hg = ti * 16 + h
            hs = hg % 2
            stiles = ([((ti - 1) % 2, P["maskA_s"])] if ti > 0 else []) + \
                     [(slot, P["maskB_s"])]
            return dict(hg=hg, hs=hs, oc=h // 2, pb=(h % 2) * 64,
                        stiles=stiles, nsi=len(stiles),
                        sc0=hs * 256, eb0=(hg % 4) * 256,
                        scp=(psC[:, 0:256] if hs == 0 else psY[:, 0:256]),
                        smp=(psH[0:1, 0:128] if hs == 0 else psH[0:1, 512:640]),
                        rbp=(psD[:, 0:128] if hs == 0 else psY[:, 512:640]),
                        cxp=(psY[0:64, 1024:1152] if hs == 0
                             else psY[0:64, 1536:1664]))

        def hb_scores(pp):
            hg, scp = pp["hg"], pp["scp"]

            @block.tensor
            def _(t, pp=pp):
                t = EW(t)
                t.wait_ge(qV, max(QKV_V, Hmkv.get(hg - 2, 0)))
                for si, (ss, mk) in enumerate(pp["stiles"]):
                    t.matmul(scp[:, si * 128:(si + 1) * 128],
                             P["bKTr"][pp["pb"]:pp["pb"] + 64,
                                       ss * 1024 + pp["oc"] * 128:
                                       ss * 1024 + (pp["oc"] + 1) * 128],
                             P["bQT"][pp["pb"]:pp["pb"] + 64,
                                      pp["oc"] * 128:(pp["oc"] + 1) * 128],
                             start=True, stop=True, skip_group_check=True)
                t.inc_last(qT, 1)
            ct.inc(qT)
            Hscq[hg] = ct.val(qT)

        def hb_mask(pp):
            hg, sc0, eb0, scp = pp["hg"], pp["sc0"], pp["eb0"], pp["scp"]

            @block.vector
            def _(v, pp=pp):
                v = EW(v)
                v.wait_ge(qT, Hscq[hg])
                if hg >= 2:
                    v.wait_ge(qS, Hexq[hg - 2])
                for si, (ss, mk) in enumerate(pp["stiles"]):
                    v.scalar_tensor_tensor(
                        out=P["bmsk"][:, sc0 + si * 128: sc0 + (si + 1) * 128],
                        in0=scp[:, si * 128:(si + 1) * 128],
                        scalar=SCL, in1=mk[:, :], op0=ALU.mult, op1=ALU.add)
                v.inc_last(qV, 1)
            ct.inc(qV)
            Hmkv[hg] = ct.val(qV)

        def hb_exp(pp):
            hg, sc0, eb0, nsi = pp["hg"], pp["sc0"], pp["eb0"], pp["nsi"]

            @block.scalar
            def _(s, pp=pp):
                s = EW(s)
                s.wait_ge(qV, Hmkv[hg])
                if hg >= 4:
                    s.wait_ge(qT, Hsmq[hg - 4])
                    s.wait_ge(qV, Henv[hg - 4])
                s.activation(out=P["bexp"][:, eb0: eb0 + nsi * 128],
                             in_=P["bmsk"][:, sc0: sc0 + nsi * 128],
                             func=AF.Exp)
                s.inc_last(qS, 1)
            ct.inc(qS)
            Hexq[hg] = ct.val(qS)

        def hb_sum(pp):
            hg, eb0, smp, nsi = pp["hg"], pp["eb0"], pp["smp"], pp["nsi"]

            @block.tensor
            def _(t, pp=pp):
                t = EW(t)
                t.wait_ge(qS, Hexq[hg])
                if hg >= 2:
                    t.wait_ge(qV, Hrsq[hg - 2])
                for si in range(nsi):
                    t.matmul(smp[0:1, 0:128], P["onescol_b"][:, :],
                             P["bexp"][:, eb0 + si * 128: eb0 + (si + 1) * 128],
                             start=(si == 0), stop=(si == nsi - 1),
                             skip_group_check=True)
                t.inc_last(qT, 1)
            ct.inc(qT)
            Hsmq[hg] = ct.val(qT)

        def hb_recip(pp):
            hg, hs, smp = pp["hg"], pp["hs"], pp["smp"]

            @block.vector
            def _(v, pp=pp):
                v = EW(v)
                v.wait_ge(qT, Hsmq[hg])
                v.tensor_copy(out=P["brs"][0:1, hs * 128:(hs + 1) * 128],
                              in_=smp[0:1, 0:128])
                v.drain()
                v.reciprocal(P["brs"][0:1, hs * 128:(hs + 1) * 128],
                             P["brs"][0:1, hs * 128:(hs + 1) * 128])
                v.inc_last(qV, 1)
            ct.inc(qV)
            Hrsq[hg] = ct.val(qV)

        def hb_rsb(pp):
            hg, hs, rbp = pp["hg"], pp["hs"], pp["rbp"]

            @block.tensor
            def _(t, pp=pp):
                t = EW(t)
                t.wait_ge(qV, Hrsq[hg])
                t.matmul(rbp[:, 0:128], P["onesrow_f"][0:1, :],
                         P["brs"][0:1, hs * 128:(hs + 1) * 128],
                         start=True, stop=True, skip_group_check=True)
                t.inc_last(qT, 1)
            ct.inc(qT)
            Hrbq[hg] = ct.val(qT)

        def hb_norm(pp):
            hg, eb0, rbp, hs, nsi = pp["hg"], pp["eb0"], pp["rbp"], pp["hs"], pp["nsi"]

            @block.vector
            def _(v, pp=pp):
                v = EW(v)
                v.wait_ge(qT, Hrbq[hg])
                for si in range(nsi):
                    v.tensor_tensor(
                        out=P["bexn"][:, hs * 256 + si * 128: hs * 256 + (si + 1) * 128],
                        in0=P["bexp"][:, eb0 + si * 128: eb0 + (si + 1) * 128],
                        in1=rbp[:, 0:128], op=ALU.mult)
                v.inc_last(qV, 1)
            ct.inc(qV)
            Henv[hg] = ct.val(qV)

        def hb_ctx(pp):
            hg, hs, cxp = pp["hg"], pp["hs"], pp["cxp"]

            @block.tensor
            def _(t, pp=pp):
                t = EW(t)
                t.wait_ge(qV, Henv[hg])
                for si, (ss, mk) in enumerate(pp["stiles"]):
                    t.matmul(cxp[0:64, 0:128],
                             P["bVtr"][:, ss * 1024 + pp["oc"] * 128 + pp["pb"]:
                                       ss * 1024 + pp["oc"] * 128 + pp["pb"] + 64],
                             P["bexn"][:, hs * 256 + si * 128: hs * 256 + (si + 1) * 128],
                             start=(si == 0), stop=(si == pp["nsi"] - 1),
                             skip_group_check=True)
                t.inc_last(qT, 1)
            ct.inc(qT)
            Hcxq[hg] = ct.val(qT)

        def hb_evac(pp):
            hg, cxp = pp["hg"], pp["cxp"]

            @block.vector
            def _(v, pp=pp):
                v = EW(v)
                v.wait_ge(qT, Hcxq[hg])
                v.tensor_copy(out=P["bctx"][pp["pb"]:pp["pb"] + 64,
                                            pp["oc"] * 128:(pp["oc"] + 1) * 128],
                              in_=cxp[0:64, 0:128])
                v.inc_last(qV, 1)
            ct.inc(qV)

        PPs = [hb_params(h) for h in range(16)]
        for k in range(19):
            if k <= 15:
                hb_scores(PPs[k])
                hb_mask(PPs[k])
                hb_exp(PPs[k])
            if 1 <= k <= 16:
                hb_sum(PPs[k - 1])
                hb_recip(PPs[k - 1])
            if 2 <= k <= 17:
                hb_rsb(PPs[k - 2])
                hb_norm(PPs[k - 2])
            if 3 <= k <= 18:
                hb_ctx(PPs[k - 3])
                hb_evac(PPs[k - 3])
        CTX_V = ct.val(qV)
        opv = {}
        for dc in range(NDC):
            oslot = psA if dc % 2 == 0 else psB

            @block.tensor
            def _(t, dc=dc, t0=t0, oslot=oslot,
                  w=(ct.val(qV) if dc < 2 else opv[dc - 2])):
                t = EW(t)
                t.wait_ge(qV, w)
                for oc in range(NDC):
                    t.matmul(oslot[:, 0:128],
                             P["bWo_s"][:, oc * D + dc * 128: oc * D + dc * 128 + 128],
                             P["bctx"][:, oc * 128:(oc + 1) * 128],
                             start=(oc == 0), stop=(oc == NDC - 1))
                t.inc_last(qT, 1)
            ct.inc(qT)
            opq = ct.val(qT)

            @block.vector
            def _(v, dc=dc, t0=t0, opq=opq, oslot=oslot):
                v = EW(v)
                v.wait_ge(qT, opq)
                v.scalar_tensor_tensor(out=P["outst"][:, dc * 128:(dc + 1) * 128],
                                       in0=oslot[:, 0:128], scalar=1.0,
                                       in1=x_sb[:, dc * NT + t0: dc * NT + t0 + 128],
                                       op0=ALU.mult, op1=ALU.add)
                v.inc_last(qV, 1)
            ct.inc(qV)
            opv[dc] = ct.val(qV)
        OP_V = ct.val(qV)

        @block.vector
        def _(v, t0=t0, OP_V=OP_V):
            v = EW(v)
            v.wait_ge(qV, OP_V)
            for dc in range(NDC):
                v.tensor_copy(out=x_sb[:, dc * NT + t0: dc * NT + t0 + 128],
                              in_=P["outst"][:, dc * 128:(dc + 1) * 128])
            v.inc_last(qV, 1)
        ct.inc(qV)

    sc.close()


def _prep_inputs(inputs, b, T):
    f32 = np.float32
    x = np.ascontiguousarray(inputs["L0_attn_a_x"][b].T.astype(f32))
    d = {
        "x": x,
        "aWq": np.ascontiguousarray(inputs["L0_attn_a_Wq"].T),
        "aWk": np.ascontiguousarray(inputs["L0_attn_a_Wk"].T),
        "aWv": np.ascontiguousarray(inputs["L0_attn_a_Wv"].T),
        "aWo": np.ascontiguousarray(inputs["L0_attn_a_Wo"].T),
        "abq": inputs["L0_attn_a_bq"].reshape(H, 1),
        "abk": inputs["L0_attn_a_bk"].reshape(H, 1),
        "abv": inputs["L0_attn_a_bv"].reshape(1, H),
        "abo": np.ascontiguousarray(inputs["L0_attn_a_bo"].reshape(NDC, 128).T),
        "bWq": inputs["L0_attn_b_Wq"],
        "bWk": inputs["L0_attn_b_Wk"],
        "bWv": inputs["L0_attn_b_Wv"],
        "bWo": inputs["L0_attn_b_Wo"],
        "rW": np.ascontiguousarray(inputs["L0_ffn_w_router"].T),
        "W1T": np.ascontiguousarray(inputs["L0_ffn_W1"].transpose(0, 2, 1)),
        "W2T": np.ascontiguousarray(inputs["L0_ffn_W2"].transpose(0, 2, 1)),
        "Ws1T": np.ascontiguousarray(inputs["L0_ffn_Ws1"].T),
        "Ws2T": np.ascontiguousarray(inputs["L0_ffn_Ws2"].T),
        "b1fl": np.concatenate([np.asarray(inputs["L0_ffn_b1"]).reshape(-1),
                                np.asarray(inputs["L0_ffn_bs1"]).reshape(-1)]).reshape(1, 9 * DFF),
        "b2": inputs["L0_ffn_b2"],
        "bs2T": np.ascontiguousarray(inputs["L0_ffn_bs2"].reshape(NDC, 128).T),
    }
    return {k: np.ascontiguousarray(v).astype(f32) for k, v in d.items()}


_NC_CACHE = {}


def kernel(**inputs):
    T = inputs["L0_attn_a_x"].shape[1]
    B = inputs["L0_attn_a_x"].shape[0]
    if T not in _NC_CACHE:
        _NC_CACHE[T] = build_nc(T)
    nc = _NC_CACHE[T]
    in_maps = [_prep_inputs(inputs, b, T) for b in range(B)]
    res = run_bass_kernel_spmd(nc, in_maps, core_ids=list(range(B)))
    out = np.stack([np.ascontiguousarray(r["y"].T) for r in res.results], axis=0)
    return out.astype(np.float32)


if __name__ == "__main__":
    import reference as _r
    ins = _r.setup_inputs(0)
    ins = {k: np.asarray(v) for k, v in ins.items()}
    out = kernel(**ins)
    exp = np.asarray(_r.reference(**ins))
    err = np.linalg.norm(out - exp) / np.linalg.norm(exp)
    print("rel err:", err)


# version 20 (final)
# speedup vs baseline: 1.3184x; 1.0117x over previous
import math
import contextlib
import numpy as np
import ml_dtypes

import concourse.bass as bass
import concourse.mybir as mybir
from concourse.bass_utils import run_bass_kernel_spmd

F32 = mybir.dt.float32
BF16 = mybir.dt.bfloat16
AF = mybir.ActivationFunctionType
ALU = mybir.AluOpType
AX = mybir.AxisListType
ts = bass.ts

D = 1024
NDC = 8
H = 128
NE = 8
DFF = 4096
FC = 256
NM = 144
BLK = 256
EPS_LN = 1e-5
EPS_U = 1e-6
SCALE_FFN = 1.0 / math.sqrt(1.0 + 2.0 / 8.0)
GC1 = 2.0 * math.sqrt(2.0 / math.pi)
GC2 = 2.0 * math.sqrt(2.0 / math.pi) * 0.044715
NEGBIG = -1e30
SM_SHIFT = -10.0


class Ctr:
    def __init__(self):
        self.c = {}

    @staticmethod
    def _k(sem):
        return getattr(sem, "name", None) or id(sem)

    def inc(self, sem, n=1):
        k = self._k(sem)
        self.c[k] = self.c.get(k, 0) + n
        return self.c[k]

    def val(self, sem):
        return self.c.get(self._k(sem), 0)




import bass_rust as _br

_KF_UID = [0]


@contextlib.contextmanager
def kreg(eng, base=0):
    _KF_UID[0] += 1
    r = eng.alloc_register(f"kr{_KF_UID[0]}")
    try:
        yield r
    finally:
        eng.free_register(r)


@contextlib.contextmanager
def kfori(nc, eng, etype, start, end, step=1):
    _KF_UID[0] += 1
    name = f"kf{_KF_UID[0]}"
    reg = eng.alloc_register(f"{name}_i")
    rh = _br.RegisterHandles((reg,))
    eng.reg_mov(reg, start)
    nc.br(f"{name}_L", engines=[etype])
    with nc.body(f"{name}_L", valid_engines=[etype]):
        yield reg
        eng.reg_add(reg, reg, step)
        nc.br_lt(rh, end, on_true=f"{name}_L", on_false=f"{name}_E", engines=[etype])
    nc.switch_bb(f"{name}_E")
    eng.free_register(reg)


_RV_CACHE = {}


def rowslice(reg, nrows, maxrow):
    key = (reg.name, nrows, maxrow)
    if key not in _RV_CACHE:
        rv = _br.RuntimeValue(_br.RegisterHandles((reg,)), 0, maxrow, nrows, 0)
        _RV_CACHE[key] = _br.DynSlice(rv, nrows)
    return _RV_CACHE[key]




class EW:

    def __init__(self, eng):
        object.__setattr__(self, "_e", eng)
        object.__setattr__(self, "last", None)

    def __getattr__(self, n):
        a = getattr(self._e, n)
        if callable(a):
            def w(*args, **kw):
                r = a(*args, **kw)
                if hasattr(r, "then_inc"):
                    object.__setattr__(self, "last", r)
                return r
            return w
        return a

    def inc_last(self, q, n=1):
        self.last.then_inc(q, n)


def build_nc(T=4096):
    NT = T
    NG = T // 512
    NBLK = T // BLK
    NTIL = T // 128
    NCH = T // 128

    nc = bass.Bass("TRN2", target_bir_lowering=False, debug=False)

    x_in = nc.dram_tensor("x", [D, NT], F32, kind="ExternalInput").ap()
    y_out = nc.dram_tensor("y", [D, NT], F32, kind="ExternalOutput").ap()

    aWq = nc.dram_tensor("aWq", [D, H], F32, kind="ExternalInput").ap()
    aWk = nc.dram_tensor("aWk", [D, H], F32, kind="ExternalInput").ap()
    aWv = nc.dram_tensor("aWv", [D, H], F32, kind="ExternalInput").ap()
    aWo = nc.dram_tensor("aWo", [H, D], F32, kind="ExternalInput").ap()
    abq = nc.dram_tensor("abq", [H, 1], F32, kind="ExternalInput").ap()
    abk = nc.dram_tensor("abk", [H, 1], F32, kind="ExternalInput").ap()
    abv = nc.dram_tensor("abv", [1, H], F32, kind="ExternalInput").ap()
    abo = nc.dram_tensor("abo", [128, NDC], F32, kind="ExternalInput").ap()

    bWq = nc.dram_tensor("bWq", [D, D], F32, kind="ExternalInput").ap()
    bWk = nc.dram_tensor("bWk", [D, D], F32, kind="ExternalInput").ap()
    bWv = nc.dram_tensor("bWv", [D, D], F32, kind="ExternalInput").ap()
    bWo = nc.dram_tensor("bWo", [D, D], F32, kind="ExternalInput").ap()

    rW = nc.dram_tensor("rW", [D, NE], F32, kind="ExternalInput").ap()
    W1T = nc.dram_tensor("W1T", [NE, D, DFF], F32, kind="ExternalInput").ap()
    W2T = nc.dram_tensor("W2T", [NE, DFF, D], F32, kind="ExternalInput").ap()
    Ws1T = nc.dram_tensor("Ws1T", [D, DFF], F32, kind="ExternalInput").ap()
    Ws2T = nc.dram_tensor("Ws2T", [DFF, D], F32, kind="ExternalInput").ap()
    b1fl = nc.dram_tensor("b1fl", [1, 9 * DFF], F32, kind="ExternalInput").ap()
    b2 = nc.dram_tensor("b2", [NE, D], F32, kind="ExternalInput").ap()
    bs2T = nc.dram_tensor("bs2T", [128, NDC], F32, kind="ExternalInput").ap()

    c_idf = nc.inline_tensor(np.eye(128, dtype=np.float32), "c_idf").ap()
    c_idb = nc.inline_tensor(np.eye(128).astype(ml_dtypes.bfloat16), "c_idb").ap()
    c_onescol_b = nc.inline_tensor(np.ones((128, 1), ml_dtypes.bfloat16), "c_onescol_b").ap()
    c_onescol_f = nc.inline_tensor(np.ones((128, 1), np.float32), "c_onescol_f").ap()
    c_onesrow_b = nc.inline_tensor(np.ones((128, 512), ml_dtypes.bfloat16), "c_onesrow_b").ap()
    c_onesrow_f = nc.inline_tensor(np.ones((128, 128), np.float32), "c_onesrow_f").ap()
    tri = (np.arange(128)[:, None] <= np.arange(128)[None, :]).astype(np.float32)
    c_triu = nc.inline_tensor(tri, "c_triu").ap()
    s_i = np.arange(128)[:, None]
    t_i = np.arange(128)[None, :]
    mB = np.where((s_i <= t_i) & (s_i >= t_i - 63), SM_SHIFT, NEGBIG).astype(np.float32)
    mA = np.where(s_i >= t_i + 65, SM_SHIFT, NEGBIG).astype(np.float32)
    c_maskB = nc.inline_tensor(mB, "c_maskB").ap()
    c_maskA = nc.inline_tensor(mA, "c_maskA").ap()
    sel = np.zeros((8, 8 * 128), np.float32)
    for e in range(8):
        sel[e, e * 128:(e + 1) * 128] = 1.0
    c_selE = nc.inline_tensor(sel, "c_selE").ap()
    bsel = np.zeros((2, 512), ml_dtypes.bfloat16)
    bsel[0, 0:256] = 1
    bsel[1, 256:512] = 1
    c_bsel = nc.inline_tensor(bsel, "c_bsel").ap()

    W1bf = nc.dram_tensor("W1bf", [NM * 128, 2304], BF16, kind="Internal").ap()
    W2bf = nc.dram_tensor("W2bf", [NM * 128, 2048], BF16, kind="Internal").ap()

    es = contextlib.ExitStack()

    x_sb = nc.alloc_sbuf_tensor("x_sb", [128, NDC * NT], BF16)
    rW_s = nc.alloc_sbuf_tensor("rW_s", [128, NDC * NE], BF16)
    b2_s = nc.alloc_sbuf_tensor("b2_s", [8, D], F32)
    bs2T_s = nc.alloc_sbuf_tensor("bs2T_s", [128, NDC], F32)
    idf_s = nc.alloc_sbuf_tensor("idf_s", [128, 128], F32)
    idb_s = nc.alloc_sbuf_tensor("idb_s", [128, 128], BF16)
    onescol_b = nc.alloc_sbuf_tensor("onescol_b", [128, 1], BF16)
    onescol_f = nc.alloc_sbuf_tensor("onescol_f", [128, 1], F32)
    onesrow_b = nc.alloc_sbuf_tensor("onesrow_b", [128, 512], BF16)
    onesrow_f = nc.alloc_sbuf_tensor("onesrow_f", [128, 128], F32)
    triu_s = nc.alloc_sbuf_tensor("triu_s", [128, 128], F32)
    maskB_s = nc.alloc_sbuf_tensor("maskB_s", [128, 128], F32)
    maskA_s = nc.alloc_sbuf_tensor("maskA_s", [128, 128], F32)
    selE_s = nc.alloc_sbuf_tensor("selE_s", [8, 8 * 128], F32)
    bsel_s = nc.alloc_sbuf_tensor("bsel_s", [2, 512], BF16)
    statrows = nc.alloc_sbuf_tensor("statrows", [128, 1024], F32)
    meanrow = statrows[0:1, 0:512]
    varrow = statrows[0:1, 512:1024]
    rstdrow = statrows[32:33, 0:512]

    psH = nc.alloc_psum_tensor("psH", [128, 1024], F32)
    psC = nc.alloc_psum_tensor("psC", [128, 512], F32)
    psD = nc.alloc_psum_tensor("psD", [128, 512], F32)
    psY = nc.alloc_psum_tensor("psY", [128, 2048], F32)
    psA = psH[:, 0:512]
    psB = psH[:, 512:1024]

    qGP = es.enter_context(nc.semaphore("qGP"))
    qSTRM = {tg: tuple(es.enter_context(nc.semaphore(f"q{w}s{sl}{tg}"))
                       for w in ("W1", "W2") for sl in range(4))
             for tg in ("A", "B")}
    qSY = es.enter_context(nc.semaphore("qSY"))
    qT = es.enter_context(nc.semaphore("qT"))
    qV = es.enter_context(nc.semaphore("qV"))
    qS = es.enter_context(nc.semaphore("qS"))
    qP = es.enter_context(nc.semaphore("qP"))
    qTh = es.enter_context(nc.semaphore("qTh"))
    qTw2 = es.enter_context(nc.semaphore("qTw2"))
    qSq = es.enter_context(nc.semaphore("qSq"))
    qSg = es.enter_context(nc.semaphore("qSg"))
    qVp = es.enter_context(nc.semaphore("qVp"))
    qVu = es.enter_context(nc.semaphore("qVu"))
    qVg = es.enter_context(nc.semaphore("qVg"))
    qPg = es.enter_context(nc.semaphore("qPg"))
    qST = es.enter_context(nc.semaphore("qST"))
    qTs = es.enter_context(nc.semaphore("qTs"))
    qVs = es.enter_context(nc.semaphore("qVs"))
    qSs = es.enter_context(nc.semaphore("qSs"))

    ct = Ctr()
    cst = nc.alloc_sbuf_tensor("cst", [128, 2], F32)

    ov = contextlib.ExitStack()

    def ovt(name, shape, dt):
        return ov.enter_context(nc.sbuf_tensor(name, shape, dt))

    KT = ovt("KT", [128, NT], F32)
    Wbuf = KT
    UT = ovt("UT", [128, NT], F32)
    QTb = ovt("QTb", [128, NT], BF16)
    Vtm = ovt("Vtm", [128, NT], BF16)
    alphaT = ovt("alphaT", [128, NT], BF16)
    OT = UT.bitcast(BF16)[:, 0:NT]
    xn_grp = ovt("xn_grp", [128, NDC * 512], BF16)
    sq_scr = ovt("sq_scr", [128, NDC * 512], BF16)
    sqf = ovt("sqf", [128, 512], F32)
    rowsbig = ovt("rowsbig", [128, 2 * NT], F32)
    rdrow = rowsbig[0:1, 0:NT]
    srow = rowsbig[0:1, NT:2 * NT]
    n2row = rowsbig[64:65, 0:NT]
    arow = rowsbig[64:65, 0:NT]
    Asb = ovt("Asb", [128, 128], F32)
    ypair_sb = ovt("ypair_sb", [128, 2], F32)
    scs = ovt("scs", [1, 784], F32)
    scrow_b = ovt("scrow_b", [1, NT], BF16)
    Mst_sb = ovt("Mst_sb", [128, 128], BF16)
    atm = ovt("atm", [128, 128], BF16)
    psm_sb = ovt("psm_sb", [128, 128], BF16)
    aWq_s = ovt("aWq_s", [128, NDC * H], BF16)
    aWk_s = ovt("aWk_s", [128, NDC * H], BF16)
    aWv_s = ovt("aWv_s", [128, NDC * H], BF16)
    aWo_s = ovt("aWo_s", [128, D], BF16)
    abq_s = ovt("abq_s", [128, 1], F32)
    abk_s = ovt("abk_s", [128, 1], F32)
    abv_s = ovt("abv_s", [1, H], BF16)
    abo_s = ovt("abo_s", [128, NDC], F32)

    LN = dict(cst=cst, onescol_b=onescol_b, onesrow_f=onesrow_f, meanrow=meanrow,
              varrow=varrow, rstdrow=rstdrow, psA=psA, psB=psB, psC=psC, psD=psD,
              qT=qT, qV=qV, qS=qS)

    with nc.Block() as block:
        @block.vector
        def _(v):
            v = EW(v)
            v.memset(cst[:, 0:1], -1.0)
            v.memset(cst[:, 1:2], EPS_LN)
            v.inc_last(qV, 1)
        ct.inc(qV)

        @block.gpsimd
        def _(g):
            g = EW(g)
            g.dma_start(out=x_sb[:, :].rearrange("p (dc t) -> p dc t", dc=NDC),
                        in_=x_in.rearrange("(dc p) t -> p dc t", p=128)).then_inc(qGP, 16)
            g.dma_start(out=aWq_s[:, :].rearrange("p (dc h) -> p dc h", dc=NDC),
                        in_=aWq.rearrange("(dc p) h -> p dc h", p=128)).then_inc(qGP, 16)
            g.dma_start(out=aWk_s[:, :].rearrange("p (dc h) -> p dc h", dc=NDC),
                        in_=aWk.rearrange("(dc p) h -> p dc h", p=128)).then_inc(qGP, 16)
            g.dma_start(out=aWv_s[:, :].rearrange("p (dc h) -> p dc h", dc=NDC),
                        in_=aWv.rearrange("(dc p) h -> p dc h", p=128)).then_inc(qGP, 16)
            g.dma_start(out=aWo_s[:, :], in_=aWo[:, :]).then_inc(qGP, 16)
            g.dma_start(out=abv_s[:, :], in_=abv[:, :]).then_inc(qGP, 16)
            g.dma_start(out=rW_s[:, :].rearrange("p (dc e) -> p dc e", dc=NDC),
                        in_=rW.rearrange("(dc p) e -> p dc e", p=128)).then_inc(qGP, 16)
            ct.inc(qGP, 16 * 7)

        @block.sync
        def _(sy):
            sy = EW(sy)
            for dst, src in ((abq_s, abq), (abk_s, abk), (abo_s, abo), (b2_s, b2),
                             (bs2T_s, bs2T), (idf_s, c_idf), (idb_s, c_idb),
                             (onescol_b, c_onescol_b), (onescol_f, c_onescol_f),
                             (onesrow_b, c_onesrow_b), (onesrow_f, c_onesrow_f),
                             (triu_s, c_triu), (maskB_s, c_maskB), (maskA_s, c_maskA),
                             (selE_s, c_selE), (bsel_s, c_bsel)):
                sy.dma_start(out=dst[:, :], in_=src[:, :]).then_inc(qSY, 16)
                ct.inc(qSY, 16)
        LOADS_SY = ct.val(qSY)
        LOADS_GP = ct.val(qGP)

        @block.gpsimd
        def _(g):
            g = EW(g)
            g.wait_ge(qGP, LOADS_GP)
            for m in range(NM):
                e, fc = m // 16, m % 16
                if e < 8:
                    src1 = bass.AP(W1T.tensor, e * D * DFF + fc * FC,
                                   [[DFF, 128], [128 * DFF, NDC], [1, FC]])
                    src2 = bass.AP(W2T.tensor, e * DFF * D + fc * FC * D,
                                   [[D, 128], [128 * D, 2], [1, D]])
                else:
                    src1 = bass.AP(Ws1T.tensor, fc * FC,
                                   [[DFF, 128], [128 * DFF, NDC], [1, FC]])
                    src2 = bass.AP(Ws2T.tensor, fc * FC * D,
                                   [[D, 128], [128 * D, 2], [1, D]])
                d1 = W1bf[m * 128:(m + 1) * 128, 0:2048].rearrange("p (a b) -> p a b", a=NDC)
                d2 = W2bf[m * 128:(m + 1) * 128, :].rearrange("p (a b) -> p a b", a=2)
                g.dma_start(out=d1, in_=src1).then_inc(qGP, 16)
                g.dma_start(out=d2, in_=src2).then_inc(qGP, 16)
                g.dma_start(out=W1bf[m * 128:(m + 1) * 128, 2048:2304],
                            in_=bass.AP(b1fl.tensor, m * FC, [[0, 128], [1, FC]])).then_inc(qGP, 16)
                g.wait_ge(qGP, ct.val(qGP) + 48)
                g.dma_start(out=W1bf[m * 128: m * 128 + 2, 2176:2304],
                            in_=bass.AP(b1fl.tensor, m * FC, [[128, 2], [1, 128]])).then_inc(qGP, 16)
                ct.inc(qGP, 64)
        PRECAST_GP = ct.val(qGP)

        for gi in range(NG):
            t0 = gi * 512
            pre = [(qGP, LOADS_GP), (qSY, LOADS_SY)] if gi == 0 else None
            emit_ln(nc, block, ct, x_sb, xn_grp, sq_scr, t0, 512, NT, LN, pre_waits=pre)

            @block.tensor
            def _(t, t0=t0):
                t = EW(t)
                t.wait_ge(qV, ct.val(qV))
                if t0 == 0:
                    t.wait_ge(qGP, LOADS_GP)
                for dc in range(NDC):
                    t.matmul(psA[:, 0:512], aWq_s[:, dc * H:(dc + 1) * H],
                             xn_grp[:, dc * 512:(dc + 1) * 512],
                             start=(dc == 0), stop=(dc == NDC - 1))
                t.inc_last(qT, 1)
                for dc in range(NDC):
                    t.matmul(psB[:, 0:512], aWk_s[:, dc * H:(dc + 1) * H],
                             xn_grp[:, dc * 512:(dc + 1) * 512],
                             start=(dc == 0), stop=(dc == NDC - 1))
                t.inc_last(qT, 1)
                for ti in range(4):
                    for dc in range(NDC):
                        t.matmul(psC[:, ti * 128:(ti + 1) * 128],
                                 xn_grp[:, dc * 512 + ti * 128: dc * 512 + (ti + 1) * 128],
                                 aWv_s[:, dc * H:(dc + 1) * H],
                                 start=(dc == 0), stop=False)
                    t.matmul(psC[:, ti * 128:(ti + 1) * 128], onesrow_b[0:1, 0:128],
                             abv_s[:, :], start=False, stop=True)
                t.inc_last(qT, 1)
            ct.inc(qT, 3)
            tq = ct.val(qT)

            @block.vector
            def _(v, t0=t0, tq=tq):
                v = EW(v)
                v.wait_ge(qT, tq - 2)
                v.tensor_scalar(out=QTb[:, t0:t0 + 512], in0=psA[:, 0:512],
                                scalar1=abq_s[:, :], scalar2=None, op0=ALU.add)
                v.wait_ge(qT, tq - 1)
                v.tensor_scalar(out=KT[:, t0:t0 + 512], in0=psB[:, 0:512],
                                scalar1=abk_s[:, :], scalar2=None, op0=ALU.add)
                v.wait_ge(qT, tq)
                v.tensor_copy(out=Vtm[:, t0:t0 + 512], in_=psC[:, 0:512])
                v.inc_last(qV, 1)
            ct.inc(qV)
            vq = ct.val(qV)

            @block.scalar
            def _(s, t0=t0, vq=vq):
                s = EW(s)
                s.wait_ge(qV, vq)
                s.activation(out=sqf[:, :], in_=KT[:, t0:t0 + 512], func=AF.Square)
                s.inc_last(qS, 1)
            ct.inc(qS)
            sq1 = ct.val(qS)

            @block.tensor
            def _(t, sq1=sq1):
                t = EW(t)
                t.wait_ge(qS, sq1)
                t.matmul(psA[0:1, 0:512], onescol_f[:, :], sqf[:, :], start=True, stop=True)
                t.inc_last(qT, 1)
            ct.inc(qT)
            n2q = ct.val(qT)

            @block.vector
            def _(v, t0=t0, n2q=n2q):
                v = EW(v)
                v.wait_ge(qT, n2q)
                v.tensor_copy(out=n2row[0:1, t0:t0 + 512], in_=psA[0:1, 0:512])
                v.tensor_tensor(out=sqf[:, :], in0=KT[:, t0:t0 + 512],
                                in1=QTb[:, t0:t0 + 512], op=ALU.mult)
                v.inc_last(qV, 1)
            ct.inc(qV)
            qkv = ct.val(qV)

            @block.tensor
            def _(t, qkv=qkv):
                t = EW(t)
                t.wait_ge(qV, qkv)
                t.matmul(psB[0:1, 0:512], onescol_f[:, :], sqf[:, :], start=True, stop=True)
                t.inc_last(qT, 1)
            ct.inc(qT)
            sq2 = ct.val(qT)

            @block.vector
            def _(v, t0=t0, sq2=sq2):
                v = EW(v)
                v.wait_ge(qT, sq2)
                v.tensor_copy(out=srow[0:1, t0:t0 + 512], in_=psB[0:1, 0:512])
                v.inc_last(qV, 1)
            ct.inc(qV)

        @block.scalar
        def _(s):
            s = EW(s)
            s.wait_ge(qV, ct.val(qV))
            s.activation(out=n2row[0:1, 0:NT], in_=n2row[0:1, 0:NT], func=AF.Sqrt)
            s.inc_last(qS, 1)
        ct.inc(qS)
        nq = ct.val(qS)

        @block.vector
        def _(v, nq=nq):
            v = EW(v)
            v.wait_ge(qS, nq)
            v.tensor_scalar(out=n2row[0:1, 0:NT], in0=n2row[0:1, 0:NT],
                            scalar1=EPS_U, scalar2=None, op0=ALU.add)
            v.drain()
            v.reciprocal(n2row[0:1, 0:NT], n2row[0:1, 0:NT])
            v.inc_last(qV, 1)
        ct.inc(qV)
        rno = ct.val(qV)

        for gi in range(NG):
            t0 = gi * 512

            @block.tensor
            def _(t, t0=t0, gi=gi, rno=rno):
                t = EW(t)
                t.wait_ge(qV, rno + gi)
                t.matmul(psA[:, 0:512], onesrow_f[64:65, :], n2row[0:1, t0:t0 + 512],
                         start=True, stop=True)
                t.inc_last(qT, 1)
            ct.inc(qT)
            bq_ = ct.val(qT)

            @block.vector
            def _(v, t0=t0, bq_=bq_):
                v = EW(v)
                v.wait_ge(qT, bq_)
                v.tensor_tensor(out=UT[:, t0:t0 + 512], in0=KT[:, t0:t0 + 512],
                                in1=psA[:, 0:512], op=ALU.mult)
                v.inc_last(qV, 1)
            ct.inc(qV)
        UT_V = ct.val(qV)

        A_ps = psY[:, 0:128]
        w_ps2 = psY[:, 512:514]
        g11_ps = psY[0:1, 1024:1025]
        g22_ps = psY[0:1, 1025:1026]
        gc_ps = psY[0:1, 1026:1027]
        yr1_ps = psY[0:1, 1536:1664]
        yr2_ps = psY[0:1, 1664:1792]
        yr12_ps = psY[0:1, 1536:1792]
        y12r = scs[0:1, 0:256]
        q1 = scs[0:1, 256:384]
        q2 = scs[0:1, 384:512]
        q3 = scs[0:1, 512:640]
        q4 = scs[0:1, 640:768]
        p1r = q1
        p2r = q3
        zrow = q1
        bb2 = scs[0:1, 768:770]
        nrow = scs[0:1, 770:772]
        rpr = scs[0:1, 772:774]
        m1s = scs[0:1, 774:775]
        m2s = scs[0:1, 775:776]
        s22v = scs[0:1, 776:777]
        sc1 = scs[0:1, 777:778]
        csb = scs[0:1, 778:779]
        s11v = scs[0:1, 779:780]
        NP = NT // 2

        @block.vector
        def _(v):
            v = EW(v)
            v.wait_ge(qV, UT_V)
            v.memset(scrow_b[0:1, :], 0.0)
            v.memset(zrow, 0.0)
            v.tensor_copy(out=Asb[:, :], in_=idf_s[:, :])
            v.inc_last(qVs, 1)

        NSEG = min(NG, 4)
        NPS = NP // NSEG
        PPG = NP // NG
        SCANR = {}
        psBb = psH.bitcast(BF16)

        def scan_seg_T(seg):
            @block.tensor
            def _(t, seg=seg):
                t = EW(t)
                if seg == 0:
                    t.wait_ge(qVs, 1)
                    t.matmul(A_ps, zrow, zrow, start=True, stop=False,
                             skip_group_check=True)
                    rW = t.alloc_register("sT_rW")
                    rTi = t.alloc_register("sT_rTi")
                    rTi1 = t.alloc_register("sT_rTi1")
                    t.reg_mov(rW, 1)
                    t.reg_mov(rTi, 0)
                    t.reg_mov(rTi1, 1)
                    SCANR["T"] = (rW, rTi, rTi1)
                rW, rTi, rTi1 = SCANR["T"]
                with kfori(nc, t, mybir.EngineType.PE, 0, NPS, 1):
                    t.wait_ge(qVs, rW)
                    t.matmul(w_ps2, Asb[:, :], UT[:, rowslice(rTi, 2, NT - 2)],
                             start=True, stop=True).then_inc(qTs, 1)
                    t.reg_add(rW, rW, 1)
                    t.wait_ge(qVs, rW)
                    t.transpose(yr1_ps, ypair_sb[:, 0:1], idf_s[:, :])
                    t.transpose(yr2_ps, ypair_sb[:, 1:2], idf_s[:, :])
                    t.matmul(g11_ps, ypair_sb[:, 0:1],
                             UT[:, rowslice(rTi, 1, NT - 1)], start=True, stop=True)
                    t.matmul(g22_ps, ypair_sb[:, 1:2],
                             UT[:, rowslice(rTi1, 1, NT - 1)], start=True, stop=True)
                    t.matmul(gc_ps, ypair_sb[:, 0:1],
                             UT[:, rowslice(rTi1, 1, NT - 1)], start=True,
                             stop=True).then_inc(qTs, 1)
                    t.reg_add(rW, rW, 1)
                    t.wait_ge(qVs, rW)
                    t.matmul(A_ps, p1r, y12r[0:1, 0:128], start=False, stop=False,
                             skip_group_check=True)
                    t.matmul(A_ps, p2r, y12r[0:1, 128:256], start=False, stop=False,
                             skip_group_check=True).then_inc(qTs, 1)
                    t.reg_add(rW, rW, 1)
                    t.reg_add(rTi, rTi, 2)
                    t.reg_add(rTi1, rTi1, 2)
            ct.inc(qTs, 3 * NPS)

        def scan_seg_V(seg):
            @block.vector
            def _(v, seg=seg):
                v = EW(v)
                if seg == 0:
                    rT = v.alloc_register("sV_rT")
                    rVi = v.alloc_register("sV_rVi")
                    rVi1 = v.alloc_register("sV_rVi1")
                    v.reg_mov(rT, 1)
                    v.reg_mov(rVi, 0)
                    v.reg_mov(rVi1, 1)
                    SCANR["V"] = (rT, rVi, rVi1)
                rT, rVi, rVi1 = SCANR["V"]
                with kfori(nc, v, mybir.EngineType.DVE, 0, NPS, 1):
                    v.wait_ge(qTs, rT)
                    v.tensor_copy(out=ypair_sb[:, :], in_=w_ps2).then_inc(qVs, 1)
                    v.reg_add(rT, rT, 1)
                    v.wait_ge(qTs, rT)
                    v.tensor_copy(out=y12r, in_=yr12_ps)
                    v.tensor_scalar(out=bb2, in0=psY[0:1, 1024:1026], scalar1=1.0,
                                    scalar2=None, op0=ALU.add)
                    v.tensor_copy(out=csb, in_=gc_ps)
                    v.drain()
                    v.tensor_tensor(out=m2s, in0=csb, in1=csb, op=ALU.mult)
                    v.tensor_tensor(out=m1s, in0=bb2[0:1, 0:1], in1=bb2[0:1, 1:2],
                                    op=ALU.mult)
                    v.tensor_scalar(out=nrow[0:1, 0:1], in0=bb2[0:1, 0:1],
                                    scalar1=-1.0, scalar2=None, op0=ALU.mult)
                    v.drain()
                    v.scalar_tensor_tensor(out=nrow[0:1, 1:2], in0=m1s, scalar=-1.0,
                                           in1=m2s, op0=ALU.mult, op1=ALU.add)
                    v.drain()
                    v.reciprocal(rpr, nrow)
                    v.drain()
                    v.tensor_scalar(out=sc1, in0=gc_ps, scalar1=rpr[0:1, 1:2],
                                    scalar2=None, op0=ALU.mult)
                    v.tensor_scalar(out=s11v, in0=bb2[0:1, 0:1],
                                    scalar1=rpr[0:1, 1:2], scalar2=None, op0=ALU.mult)
                    v.tensor_scalar(out=s22v, in0=bb2[0:1, 1:2],
                                    scalar1=rpr[0:1, 1:2], scalar2=None, op0=ALU.mult)
                    v.drain()
                    v.tensor_scalar(out=q1, in0=y12r[0:1, 0:128],
                                    scalar1=s22v[0:1, 0:1], scalar2=None, op0=ALU.mult)
                    v.tensor_scalar(out=q2, in0=y12r[0:1, 128:256],
                                    scalar1=sc1[0:1, 0:1],
                                    scalar2=None, op0=ALU.mult)
                    v.tensor_scalar(out=q3, in0=y12r[0:1, 128:256],
                                    scalar1=s11v[0:1, 0:1],
                                    scalar2=None, op0=ALU.mult)
                    v.tensor_scalar(out=q4, in0=y12r[0:1, 0:128],
                                    scalar1=sc1[0:1, 0:1],
                                    scalar2=None, op0=ALU.mult)
                    v.drain()
                    v.tensor_tensor(out=p1r, in0=q1, in1=q2, op=ALU.subtract)
                    v.tensor_tensor(out=p2r, in0=q3, in1=q4,
                                    op=ALU.subtract).then_inc(qVs, 1)
                    v.tensor_copy(out=rdrow[0:1, rowslice(rVi, 1, NT - 1)],
                                  in_=rpr[0:1, 0:1])
                    v.tensor_copy(out=rdrow[0:1, rowslice(rVi1, 1, NT - 1)],
                                  in_=s11v)
                    v.tensor_copy(out=scrow_b[0:1, rowslice(rVi1, 1, NT - 1)],
                                  in_=sc1)
                    v.reg_add(rT, rT, 1)
                    v.wait_ge(qTs, rT)
                    v.scalar_tensor_tensor(out=Asb[:, :], in0=A_ps, scalar=1.0,
                                           in1=idf_s[:, :], op0=ALU.mult,
                                           op1=ALU.add)
                    v.tensor_copy(out=Wbuf[:, rowslice(rVi, 2, NT - 2)],
                                  in_=ypair_sb[:, :]).then_inc(qVs, 1)
                    v.reg_add(rT, rT, 1)
                    v.reg_add(rVi, rVi, 2)
                    v.reg_add(rVi1, rVi1, 2)
            ct.inc(qVs, 3 * NPS)

        def emit_group(gi):
            t0 = gi * 512
            qvs_need = 1 + 3 * PPG * (gi + 1)

            @block.vector
            def _(v, gi=gi, t0=t0, qvs_need=qvs_need):
                v = EW(v)
                v.wait_ge(qVs, qvs_need)
                v.tensor_tensor(out=arow[0:1, t0:t0 + 512],
                                in0=srow[0:1, t0:t0 + 512],
                                in1=rdrow[0:1, t0:t0 + 512], op=ALU.mult)
                v.tensor_tensor(out=scrow_b[0:1, t0:t0 + 512],
                                in0=scrow_b[0:1, t0:t0 + 512],
                                in1=srow[0:1, t0:t0 + 512], op=ALU.mult)
                v.drain()
                v.tensor_scalar(out=arow[0:1, t0:t0 + 512],
                                in0=arow[0:1, t0:t0 + 512],
                                scalar1=-1.0, scalar2=None, op0=ALU.mult)
                v.inc_last(qV, 1)
            ct.inc(qV)
            alv = ct.val(qV)

            @block.tensor
            def _(t, t0=t0, alv=alv):
                t = EW(t)
                t.wait_ge(qV, alv)
                t.matmul(psA[:, 0:512], onesrow_f[64:65, :], arow[0:1, t0:t0 + 512],
                         start=True, stop=True)
                t.matmul(psC[:, 0:512], onesrow_b[0:1, 0:128],
                         scrow_b[0:1, t0:t0 + 512], start=True, stop=True)
                t.inc_last(qT, 1)
            ct.inc(qT)
            ab_ = ct.val(qT)

            @block.vector
            def _(v, gi=gi, t0=t0, ab_=ab_):
                v = EW(v)
                v.wait_ge(qT, ab_)
                v.tensor_tensor(out=alphaT[:, t0:t0 + 512], in0=Wbuf[:, t0:t0 + 512],
                                in1=psA[:, 0:512], op=ALU.mult)
                lo = 1 if gi == 0 else 0
                v.tensor_tensor(out=sq_scr[:, lo:512],
                                in0=Wbuf[:, t0 + lo - 1:t0 + 511],
                                in1=psC[:, lo:512], op=ALU.mult)
                v.drain()
                v.tensor_tensor(out=alphaT[:, t0 + lo:t0 + 512],
                                in0=alphaT[:, t0 + lo:t0 + 512],
                                in1=sq_scr[:, lo:512], op=ALU.add)
                v.inc_last(qV, 2)
            ct.inc(qV, 2)

            for c in range(4 * gi, 4 * gi + 4):
                tc = c * 128

                @block.tensor
                def _(t, c=c, tc=tc):
                    t = EW(t)
                    t.wait_ge(qV, ct.val(qV))
                    t.matmul(psA[:, 0:128], alphaT[:, tc:tc + 128],
                             QTb[:, tc:tc + 128], start=True, stop=True)
                    t.inc_last(qT, 1)
                    t.transpose(psBb[:, 1024:1152], alphaT[:, tc:tc + 128],
                                idb_s[:, :])
                    t.inc_last(qT, 1)
                ct.inc(qT, 2)
                pq = ct.val(qT)

                @block.vector
                def _(v, c=c, pq=pq):
                    v = EW(v)
                    v.wait_ge(qT, pq - 1)
                    v.tensor_tensor(out=psm_sb[:, :], in0=psA[:, 0:128],
                                    in1=triu_s[:, :], op=ALU.mult)
                    v.wait_ge(qT, pq)
                    v.tensor_copy(out=atm[:, :], in_=psBb[:, 1024:1152])
                    v.inc_last(qV, 1)
                ct.inc(qV)
                pv = ct.val(qV)

                @block.tensor
                def _(t, c=c, tc=tc, pv=pv):
                    t = EW(t)
                    t.wait_ge(qV, pv)
                    t.matmul(psC[:, 0:128], Vtm[:, tc:tc + 128], psm_sb[:, :],
                             start=True, stop=(c == 0))
                    if c > 0:
                        t.matmul(psC[:, 0:128], Mst_sb[:, :], QTb[:, tc:tc + 128],
                                 start=False, stop=True)
                    t.inc_last(qT, 1)
                    t.matmul(psD[:, 0:128], atm[:, :], Vtm[:, tc:tc + 128],
                             start=(c == 0), stop=(c == NCH - 1),
                             skip_group_check=True)
                    t.inc_last(qT, 1)
                ct.inc(qT, 2)
                oq = ct.val(qT)

                @block.vector
                def _(v, c=c, tc=tc, oq=oq):
                    v = EW(v)
                    v.wait_ge(qT, oq - 1)
                    v.tensor_copy(out=OT[:, tc:tc + 128], in_=psC[:, 0:128])
                    v.wait_ge(qT, oq)
                    v.tensor_copy(out=Mst_sb[:, :], in_=psD[:, 0:128])
                    v.inc_last(qV, 1)
                ct.inc(qV)

            for dc in range(NDC):
                @block.tensor
                def _(t, t0=t0, dc=dc):
                    t = EW(t)
                    t.wait_ge(qV, ct.val(qV))
                    t.matmul(psA[:, 0:512], aWo_s[:, dc * 128:(dc + 1) * 128],
                             OT[:, t0:t0 + 512], start=True, stop=True)
                    t.inc_last(qT, 1)
                ct.inc(qT)
                tq2 = ct.val(qT)

                @block.vector
                def _(v, t0=t0, dc=dc, tq2=tq2):
                    v = EW(v)
                    v.wait_ge(qT, tq2)
                    v.tensor_scalar(out=sqf[:, :], in0=psA[:, 0:512],
                                    scalar1=abo_s[:, dc:dc + 1], scalar2=None,
                                    op0=ALU.add)
                    v.drain()
                    v.tensor_tensor(out=x_sb[:, dc * NT + t0: dc * NT + t0 + 512],
                                    in0=x_sb[:, dc * NT + t0: dc * NT + t0 + 512],
                                    in1=sqf[:, :], op=ALU.add)
                    v.inc_last(qV, 1)
                ct.inc(qV)

        GPS = NG // NSEG
        for seg in range(NSEG):
            scan_seg_T(seg)
            scan_seg_V(seg)
            for g in range(seg * GPS, (seg + 1) * GPS):
                emit_group(g)
        SCAN_TS = 3 * NP
        SCAN_VS = 1 + 3 * NP
        ATTNA_V = ct.val(qV)

        ov.close()

        ffn = dict(x_sb=x_sb, W1bf=W1bf, W2bf=W2bf, b1fl=b1fl, qP=qP, qSTRM=qSTRM,
                   rW_s=rW_s, b2_s=b2_s, bs2T_s=bs2T_s,
                   selE_s=selE_s, idf_s=idf_s, onesrow_b=onesrow_b, bsel_s=bsel_s,
                   psH=psH, psC=psC, psD=psD, psY=psY, psA=psA, psB=psB,
                   qTh=qTh, qTw2=qTw2, qSq=qSq, qSg=qSg, qVp=qVp, qVu=qVu,
                   qVg=qVg, qPg=qPg, qSY=qSY, qGP=qGP, qST=qST,
                   PRECAST_GP=PRECAST_GP, y_out=y_out, LN=LN, NT=NT)

        emit_ffn(nc, block, ct, ffn, last=False, tag="A")

        attb = dict(x_sb=x_sb, bWq=bWq, bWk=bWk, bWv=bWv, bWo=bWo,
                    maskA_s=maskA_s, maskB_s=maskB_s, onescol_b=onescol_b,
                    onesrow_f=onesrow_f,
                    psH=psH, psC=psC, psD=psD, psA=psA, psB=psB, psY=psY,
                    qGP=qGP, qSY=qSY, qTw2=qTw2, LN=LN, NT=NT)

        emit_attn_b(nc, block, ct, attb)

        emit_ffn(nc, block, ct, ffn, last=True, tag="B")

        @block.sync
        def _(sy):
            sy = EW(sy)
            sy.wait_ge(qST, ct.val(qST))

    es.close()
    return nc


def emit_ln(nc, block, ct, x_sb, xn_out, sq_scr, t0, W, NT, LN, pre_waits=None,
            xn_stride=None):
    if xn_stride is None:
        xn_stride = W
    qT, qV, qS = LN["qT"], LN["qV"], LN["qS"]
    psA, psB, psC, psD = LN["psA"], LN["psB"], LN["psC"], LN["psD"]

    @block.scalar
    def _(s):
        s = EW(s)
        s.wait_ge(qV, ct.val(qV))
        if pre_waits:
            for (q, vv) in pre_waits:
                s.wait_ge(q, vv)
        for dc in range(NDC):
            s.activation(out=sq_scr[:, dc * W:(dc + 1) * W],
                         in_=x_sb[:, dc * NT + t0: dc * NT + t0 + W], func=AF.Square)
        s.inc_last(qS, 1)
    ct.inc(qS)
    sqq = ct.val(qS)

    @block.tensor
    def _(t, sqq=sqq):
        t = EW(t)
        t.wait_ge(qS, sqq)
        for dc in range(NDC):
            t.matmul(psA[0:1, 0:W], LN["onescol_b"][:, :],
                     x_sb[:, dc * NT + t0: dc * NT + t0 + W],
                     start=(dc == 0), stop=(dc == NDC - 1))
        for dc in range(NDC):
            t.matmul(psB[0:1, 0:W], LN["onescol_b"][:, :],
                     sq_scr[:, dc * W:(dc + 1) * W],
                     start=(dc == 0), stop=(dc == NDC - 1))
        t.inc_last(qT, 1)
    ct.inc(qT)
    stq = ct.val(qT)

    @block.vector
    def _(v, stq=stq):
        v = EW(v)
        v.wait_ge(qT, stq)
        v.tensor_scalar(out=LN["meanrow"][0:1, 0:W], in0=psA[0:1, 0:W],
                        scalar1=1.0 / D, scalar2=None, op0=ALU.mult)
        v.drain()
        v.tensor_tensor(out=LN["varrow"][0:1, 0:W], in0=LN["meanrow"][0:1, 0:W],
                        in1=LN["meanrow"][0:1, 0:W], op=ALU.mult)
        v.drain()
        v.scalar_tensor_tensor(out=LN["varrow"][0:1, 0:W], in0=psB[0:1, 0:W],
                               scalar=1.0 / D, in1=LN["varrow"][0:1, 0:W],
                               op0=ALU.mult, op1=ALU.subtract)
        v.inc_last(qV, 1)
    ct.inc(qV)
    vq = ct.val(qV)

    @block.scalar
    def _(s, vq=vq):
        s = EW(s)
        s.wait_ge(qV, vq)
        s.activation(out=LN["rstdrow"][0:1, 0:W], in_=LN["varrow"][0:1, 0:W],
                     func=AF.Sqrt, bias=LN["cst"][0:1, 1:2])
        s.inc_last(qS, 1)
    ct.inc(qS)
    sq2 = ct.val(qS)

    @block.vector
    def _(v, sq2=sq2):
        v = EW(v)
        v.wait_ge(qS, sq2)
        v.reciprocal(LN["rstdrow"][0:1, 0:W], LN["rstdrow"][0:1, 0:W])
        v.inc_last(qV, 1)
    ct.inc(qV)
    vq2 = ct.val(qV)

    @block.tensor
    def _(t, vq2=vq2):
        t = EW(t)
        t.wait_ge(qV, vq2)
        t.matmul(psC[:, 0:W], LN["onesrow_f"][0:1, :], LN["meanrow"][0:1, 0:W],
                 start=True, stop=True)
        t.matmul(psD[:, 0:W], LN["onesrow_f"][32:33, :], LN["rstdrow"][0:1, 0:W],
                 start=True, stop=True)
        t.inc_last(qT, 1)
    ct.inc(qT)
    bcq = ct.val(qT)

    @block.vector
    def _(v, bcq=bcq):
        v = EW(v)
        v.wait_ge(qT, bcq)
        for dc in range(NDC):
            v.tensor_tensor(out=sq_scr[:, dc * W:(dc + 1) * W],
                            in0=x_sb[:, dc * NT + t0: dc * NT + t0 + W],
                            in1=psC[:, 0:W], op=ALU.subtract)
        v.drain()
        for dc in range(NDC):
            v.tensor_tensor(out=xn_out[:, dc * xn_stride:dc * xn_stride + W],
                            in0=sq_scr[:, dc * W:(dc + 1) * W],
                            in1=psD[:, 0:W], op=ALU.mult)
        v.inc_last(qV, 1)
    ct.inc(qV)


def emit_ffn(nc, block, ct, P, last, tag=""):
    NT = P["NT"]
    NBLK = NT // BLK
    x_sb = P["x_sb"]
    psH, psC, psD, psY = P["psH"], P["psC"], P["psD"], P["psY"]
    psA, psB = P["psA"], P["psB"]
    LN = P["LN"]
    qT, qV, qS = LN["qT"], LN["qV"], LN["qS"]
    qTh, qTw2 = P["qTh"], P["qTw2"]
    qSq, qSg, qVp, qVu, qVg, qPg = P["qSq"], P["qSg"], P["qVp"], P["qVu"], P["qVg"], P["qPg"]
    qSY, qGP, qST, qP = P["qSY"], P["qGP"], P["qST"], P["qP"]
    qsW1 = list(P["qSTRM"][tag][0:4])
    qsW2 = list(P["qSTRM"][tag][4:8])

    sc = contextlib.ExitStack()

    def sct(n, shp, dt):
        return sc.enter_context(nc.sbuf_tensor(n + tag, shp, dt))

    P = dict(P)
    P["w1buf"] = sct("w1buf", [128, 4 * 2304], BF16)
    P["w2buf"] = sct("w2buf", [128, 4 * 2048], BF16)
    P["xn_blk"] = sct("xn_blk", [128, NDC * BLK], BF16)
    P["lnsq"] = sct("lnsq", [128, NDC * BLK], BF16)
    P["gateB"] = sct("gateB", [128, 9 * BLK], BF16)
    P["gatesT"] = sct("gatesT", [8, BLK], F32)
    P["gsc"] = sct("gsc", [128, 48], F32)
    P["sqg"] = sct("sqg", [128, 4 * 512], BF16)
    P["pg"] = sct("pg", [128, 512], BF16)
    P["u2g"] = sct("u2g", [128, 512], F32)
    P["sgg"] = sct("sgg", [128, 512], BF16)
    P["hg0"] = sct("hg0", [128, 4 * 512], BF16)
    P["hgb"] = sct("hgb", [128, 4 * 512], BF16)
    P["outst"] = sct("outst", [128, 2048], F32)
    HS = [psH[:, 0:512], psH[:, 512:1024], psC[:, 0:512], psD[:, 0:512]]
    ENTRY_V = ct.val(qV)
    ENTRY_SY = ct.val(qSY)
    ENTRY_TW2 = ct.val(qTw2)

    @block.gpsimd
    def _(g):
        g = EW(g)
        g.wait_ge(qV, ENTRY_V)
        g.wait_ge(qSY, ENTRY_SY)
        g.wait_ge(qTw2, ENTRY_TW2)
        g.inc_last(qP, 1)
        ct.inc(qP)

    @block.vector
    def _(v):
        v = EW(v)
        v.wait_ge(qV, ENTRY_V)
        v.memset(P["gateB"][:, 8 * BLK:9 * BLK], 1.0)
        v.inc_last(qV, 1)
    ct.inc(qV)
    ENTRY_P = ct.val(qP)

    CB_Tw2 = ct.val(qTw2)
    CB_W1 = [ct.val(q) for q in qsW1]
    CB_W2 = [ct.val(q) for q in qsW2]

    @block.sync
    def _(sy):
        sy = EW(sy)
        sy.wait_ge(qP, ENTRY_P)
        if not last:
            sy.wait_ge(qGP, P["PRECAST_GP"])
        for k in range(NBLK * NM):
            s = k % 4
            m = k % NM
            if k >= 4:
                sy.wait_ge(qTw2, CB_Tw2 + k - 3)
            sy.dma_start(out=P["w1buf"][:, s * 2304:(s + 1) * 2304],
                         in_=P["W1bf"][m * 128:(m + 1) * 128, :]).then_inc(qsW1[s], 16)
            sy.dma_start(out=P["w2buf"][:, s * 2048:(s + 1) * 2048],
                         in_=P["W2bf"][m * 128:(m + 1) * 128, :]).then_inc(qsW2[s], 16)
    for _q in qsW1 + qsW2:
        ct.inc(_q, 16 * (NM // 4) * NBLK)

    for blk in range(NBLK):
        t0 = blk * BLK
        emit_ln(nc, block, ct, x_sb, P["xn_blk"], P["lnsq"], t0, BLK, NT, LN)

        for ti in range(2):
            tt0 = ti * 128

            @block.tensor
            def _(t, tt0=tt0):
                t = EW(t)
                t.wait_ge(qV, ct.val(qV))
                for dc in range(NDC):
                    t.matmul(psC[0:128, 0:8],
                             P["xn_blk"][:, dc * BLK + tt0: dc * BLK + tt0 + 128],
                             P["rW_s"][:, dc * NE:(dc + 1) * NE],
                             start=(dc == 0), stop=(dc == NDC - 1))
                t.inc_last(qT, 1)
            ct.inc(qT)
            lq = ct.val(qT)

            @block.vector
            def _(v, lq=lq):
                v = EW(v)
                v.wait_ge(qT, lq)
                v.tensor_copy(out=P["gsc"][:, 0:8], in_=psC[0:128, 0:8])
                v.drain()
                v.reduce_max(out=P["gsc"][:, 8:9], in_=P["gsc"][:, 0:8],
                             axis=AX.X, negate=True)
                v.inc_last(qV, 1)
            ct.inc(qV)
            lv = ct.val(qV)

            @block.scalar
            def _(s, lv=lv):
                s = EW(s)
                s.wait_ge(qV, lv)
                s.activation(out=P["gsc"][:, 16:24], in_=P["gsc"][:, 0:8], func=AF.Exp,
                             bias=P["gsc"][:, 8:9], accum_out=P["gsc"][:, 9:10])
                s.inc_last(qS, 1)
            ct.inc(qS)
            ls = ct.val(qS)

            @block.vector
            def _(v, ls=ls):
                v = EW(v)
                v.wait_ge(qS, ls)
                v.reciprocal(P["gsc"][:, 10:11], P["gsc"][:, 9:10])
                v.max(out=P["gsc"][:, 24:32], in_=P["gsc"][:, 16:24])
                v.drain()
                v.tensor_scalar(out=P["gsc"][:, 32:40], in0=P["gsc"][:, 16:24],
                                scalar1=P["gsc"][:, 25:26], scalar2=None,
                                op0=ALU.is_ge)
                v.drain()
                v.tensor_tensor(out=P["gsc"][:, 40:48], in0=P["gsc"][:, 16:24],
                                in1=P["gsc"][:, 32:40], op=ALU.mult)
                v.drain()
                v.tensor_scalar(out=P["gsc"][:, 0:8], in0=P["gsc"][:, 40:48],
                                scalar1=P["gsc"][:, 10:11], scalar2=None,
                                op0=ALU.mult)
                v.inc_last(qV, 1)
            ct.inc(qV)
            gv = ct.val(qV)

            @block.tensor
            def _(t, gv=gv):
                t = EW(t)
                t.wait_ge(qV, gv)
                t.transpose(psC[0:8, 256:384], P["gsc"][:, 0:8], P["idf_s"][:, :])
                t.inc_last(qT, 1)
            ct.inc(qT)
            gq = ct.val(qT)

            @block.vector
            def _(v, tt0=tt0, gq=gq):
                v = EW(v)
                v.wait_ge(qT, gq)
                v.tensor_copy(out=P["gatesT"][:, tt0:tt0 + 128], in_=psC[0:8, 256:384])
                v.inc_last(qV, 1)
            ct.inc(qV)
        GATES_V = ct.val(qV)

        for e in range(8):
            @block.tensor
            def _(t, e=e, w=ct.val(qV)):
                t = EW(t)
                t.wait_ge(qV, w)
                t.matmul(psC[:, 0:256], P["selE_s"][:, e * 128:(e + 1) * 128],
                         P["gatesT"][:, :], start=True, stop=True)
                t.inc_last(qT, 1)
            ct.inc(qT)
            eq = ct.val(qT)

            @block.vector
            def _(v, e=e, eq=eq):
                v = EW(v)
                v.wait_ge(qT, eq)
                v.tensor_copy(out=P["gateB"][:, e * BLK:(e + 1) * BLK],
                              in_=psC[:, 0:256])
                v.inc_last(qV, 1)
            ct.inc(qV)
        GB_V = ct.val(qV)

        @block.tensor
        def _(t):
            t = EW(t)
            t.wait_ge(qV, GB_V)
            for dcp in range(4):
                for side in range(2):
                    dc = 2 * dcp + side
                    t.matmul(psY[:, dcp * 512 + side * 256: dcp * 512 + side * 256 + 256],
                             P["b2_s"][:, dc * 128:(dc + 1) * 128], P["gatesT"][:, :],
                             start=(side == 0), stop=False, skip_group_check=True)
            t.inc_last(qT, 1)
        ct.inc(qT)

        bqW1 = [CB_W1[s] + blk * 16 * (NM // 4) for s in range(4)]
        bqW2 = [CB_W2[s] + blk * 16 * (NM // 4) for s in range(4)]
        base_Th = ct.val(qTh)
        base_Tw2 = ct.val(qTw2)
        base_Sq = ct.val(qSq)
        base_Sg = ct.val(qSg)
        base_Vp = ct.val(qVp)
        base_Vu = ct.val(qVu)
        base_Vg = ct.val(qVg)
        base_Pg = ct.val(qPg)
        T_ENTRY = ct.val(qT)
        V_ENTRY = ct.val(qV)


        bq = bqW1[0]
        assert all(b == bq for b in bqW1 + bqW2)

        def t_w1(t, s):
            h = HS[s]
            t.matmul(h[:, 0:512],
                     P["w1buf"][0:2, s * 2304 + 2176: s * 2304 + 2304],
                     P["bsel_s"][0:2, 0:512], start=True, stop=False,
                     skip_group_check=True)
            for fsub in range(2):
                for dc in range(NDC):
                    t.matmul(h[:, fsub * 256: fsub * 256 + 256],
                             P["w1buf"][:, s * 2304 + dc * 256 + fsub * 128:
                                        s * 2304 + dc * 256 + fsub * 128 + 128],
                             P["xn_blk"][:, dc * BLK:(dc + 1) * BLK],
                             start=False, stop=False, skip_group_check=True)
            t.inc_last(qTh, 1)

        def t_w2(t, s):
            for fr in range(2):
                for dc in range(NDC):
                    t.matmul(psY[:, (dc // 2) * 512 + (dc % 2) * 256:
                                  (dc // 2) * 512 + (dc % 2) * 256 + 256],
                             P["w2buf"][:, s * 2048 + fr * 1024 + dc * 128:
                                        s * 2048 + fr * 1024 + dc * 128 + 128],
                             P["hgb"][:, s * 512 + fr * 256: s * 512 + fr * 256 + 256],
                             start=False, stop=False, skip_group_check=True)
            t.inc_last(qTw2, 1)

        @block.tensor
        def _(t):
            t = EW(t)
            for j in range(4):
                t.wait_ge(qsW1[j], bq + 16)
                if base_Vg + j - 3 > 0:
                    t.wait_ge(qVg, base_Vg + j - 3)
                t_w1(t, j)
                if j >= 1:
                    t.wait_ge(qPg, base_Pg + j)
                    t.wait_ge(qsW2[j - 1], bq + 16)
                    t_w2(t, j - 1)
            with kreg(t) as rQ, kreg(t) as rQm, kreg(t) as rVg, kreg(t) as rPg:
                t.reg_mov(rQ, bq + 32)
                t.reg_mov(rQm, bq + 16)
                t.reg_mov(rVg, base_Vg + 1)
                t.reg_mov(rPg, base_Pg + 4)
                with kfori(nc, t, mybir.EngineType.PE, 1, NM // 4, 1):
                    for j in range(4):
                        t.wait_ge(qsW1[j], rQ)
                        t.wait_ge(qVg, rVg)
                        t_w1(t, j)
                        t.wait_ge(qPg, rPg)
                        t.wait_ge(qsW2[(j + 3) % 4], rQm if j == 0 else rQ)
                        t_w2(t, (j + 3) % 4)
                        t.reg_add(rVg, rVg, 1)
                        t.reg_add(rPg, rPg, 1)
                    t.reg_add(rQ, rQ, 16)
                    t.reg_add(rQm, rQm, 16)
            t.wait_ge(qPg, base_Pg + NM)
            t.wait_ge(qsW2[3], bq + 16 * (NM // 4))
            t_w2(t, 3)
        ct.inc(qTh, NM)
        ct.inc(qTw2, NM)

        @block.scalar
        def _(s):
            s = EW(s)
            with kreg(s) as rTh, kreg(s) as rVu, kreg(s) as rVp:
                s.reg_mov(rTh, base_Th + 1)
                s.reg_mov(rVu, base_Vu + 1)
                s.reg_mov(rVp, max(base_Vp - 3, 0))
                with kfori(nc, s, mybir.EngineType.Activation, 0, NM // 4, 1):
                    for j in range(4):
                        s.wait_ge(qTh, rTh)
                        s.wait_ge(qVp, rVp)
                        s.activation(out=P["sqg"][:, j * 512:(j + 1) * 512],
                                     in_=HS[j][:, 0:512],
                                     func=AF.Square).then_inc(qSq, 1)
                        s.wait_ge(qVu, rVu)
                        s.activation(out=P["sgg"][:, :], in_=P["u2g"][:, :],
                                     func=AF.Sigmoid).then_inc(qSg, 1)
                        s.reg_add(rTh, rTh, 1)
                        s.reg_add(rVu, rVu, 1)
                        s.reg_add(rVp, rVp, 1)
        ct.inc(qSq, NM)
        ct.inc(qSg, NM)

        @block.vector
        def _(v):
            v = EW(v)
            v.wait_ge(qV, GB_V)
            with kreg(v) as rSq, kreg(v) as rSg, kreg(v) as rT2:
                v.reg_mov(rSq, base_Sq + 1)
                v.reg_mov(rSg, base_Sg + 1)
                v.reg_mov(rT2, max(base_Tw2 - 3, 0))
                for e in range(9):
                    goff = e * BLK
                    with kfori(nc, v, mybir.EngineType.DVE, e * 4, e * 4 + 4, 1):
                        for j in range(4):
                            v.wait_ge(qSq, rSq)
                            v.tensor_scalar(out=P["pg"][:, :],
                                            in0=P["sqg"][:, j * 512:(j + 1) * 512],
                                            scalar1=GC2, scalar2=GC1, op0=ALU.mult,
                                            op1=ALU.add).then_inc(qVp, 1)
                            v.drain()
                            v.tensor_tensor(out=P["u2g"][:, :],
                                            in0=HS[j][:, 0:512],
                                            in1=P["pg"][:, :],
                                            op=ALU.mult).then_inc(qVu, 1)
                            v.wait_ge(qSg, rSg)
                            v.tensor_tensor(out=P["hg0"][:, j * 512:(j + 1) * 512],
                                            in0=HS[j][:, 0:512],
                                            in1=P["sgg"][:, :],
                                            op=ALU.mult).then_inc(qVg, 1)
                            v.wait_ge(qTw2, rT2)
                            v.drain()
                            v.tensor_tensor(out=P["hgb"][:, j * 512: j * 512 + 256],
                                            in0=P["hg0"][:, j * 512: j * 512 + 256],
                                            in1=P["gateB"][:, goff:goff + 256],
                                            op=ALU.mult)
                            v.tensor_tensor(out=P["hgb"][:, j * 512 + 256: j * 512 + 512],
                                            in0=P["hg0"][:, j * 512 + 256: j * 512 + 512],
                                            in1=P["gateB"][:, goff:goff + 256],
                                            op=ALU.mult).then_inc(qPg, 1)
                            v.reg_add(rSq, rSq, 1)
                            v.reg_add(rSg, rSg, 1)
                            v.reg_add(rT2, rT2, 1)
        ct.inc(qVp, NM)
        ct.inc(qVu, NM)
        ct.inc(qVg, NM)
        ct.inc(qPg, NM)

        @block.vector
        def _(v, blk=blk, t0=t0, wst=ct.val(qST)):
            v = EW(v)
            v.wait_ge(qTw2, base_Tw2 + NM)
            if last:
                v.wait_ge(qST, wst)
            for dc in range(NDC):
                ysl = psY[:, (dc // 2) * 512 + (dc % 2) * 256:
                          (dc // 2) * 512 + (dc % 2) * 256 + 256]
                v.tensor_scalar(out=P["outst"][:, dc * 256:(dc + 1) * 256],
                                in0=ysl, scalar1=P["bs2T_s"][:, dc:dc + 1],
                                scalar2=None, op0=ALU.add)
            v.inc_last(qV, 1)
        ct.inc(qV)
        ev1 = ct.val(qV)

        @block.vector
        def _(v, blk=blk, t0=t0, ev1=ev1):
            v = EW(v)
            v.wait_ge(qV, ev1)
            for dc in range(NDC):
                v.scalar_tensor_tensor(out=P["outst"][:, dc * 256:(dc + 1) * 256],
                                       in0=P["outst"][:, dc * 256:(dc + 1) * 256],
                                       scalar=SCALE_FFN,
                                       in1=x_sb[:, dc * NT + t0: dc * NT + t0 + 256],
                                       op0=ALU.mult, op1=ALU.add)
            v.inc_last(qV, 1)
        ct.inc(qV)
        ev2 = ct.val(qV)

        if last:
            @block.gpsimd
            def _(g, blk=blk, t0=t0, ev2=ev2):
                g = EW(g)
                g.wait_ge(qV, ev2)
                g.dma_start(out=bass.AP(P["y_out"].tensor, t0,
                                        [[NT, 128], [128 * NT, NDC], [1, 256]]),
                            in_=P["outst"][:, :].rearrange("p (dc t) -> p dc t", dc=NDC)).then_inc(qST, 16)
            ct.inc(qST, 16)
        else:
            @block.vector
            def _(v, blk=blk, t0=t0, ev2=ev2):
                v = EW(v)
                v.wait_ge(qV, ev2)
                for dc in range(NDC):
                    v.tensor_copy(out=x_sb[:, dc * NT + t0: dc * NT + t0 + 256],
                                  in_=P["outst"][:, dc * 256:(dc + 1) * 256])
                v.inc_last(qV, 1)
            ct.inc(qV)

    sc.close()


def emit_attn_b(nc, block, ct, P):
    NT = P["NT"]
    NTIL = NT // 128
    x_sb = P["x_sb"]
    psH, psC, psD = P["psH"], P["psC"], P["psD"]
    psA, psB = P["psA"], P["psB"]
    psY = P["psY"]
    LN = P["LN"]
    qT, qV, qS = LN["qT"], LN["qV"], LN["qS"]
    qGP, qSY, qTw2 = P["qGP"], P["qSY"], P["qTw2"]
    SCL = 1.0 / 8.0

    sc = contextlib.ExitStack()

    def sct(n, shp, dt):
        return sc.enter_context(nc.sbuf_tensor(n, shp, dt))

    P = dict(P)
    P["lnsq"] = sct("ab_lnsq", [128, NDC * 128], BF16)
    P["xnb"] = sct("ab_xnb", [128, NDC * 128], BF16)
    P["bQT"] = sct("ab_bQT", [128, NDC * 128], BF16)
    P["bKTr"] = sct("ab_bKTr", [128, 2 * NDC * 128], BF16)
    P["bVtr"] = sct("ab_bVtr", [128, 2 * NDC * 128], BF16)
    P["bexp"] = sct("ab_bexp", [128, 4 * 256], BF16)
    P["bexn"] = sct("ab_bexn", [128, 2 * 256], BF16)
    P["bctx"] = sct("ab_bctx", [128, NDC * 128], BF16)
    P["brs"] = sct("ab_brs", [1, 2 * 128], F32)
    P["bmsk"] = sct("ab_bmsk", [128, 2 * 256], F32)
    P["outst"] = sct("ab_outst", [128, NDC * 128], F32)
    P["bWq_s"] = sct("ab_bWq", [128, NDC * D], BF16)
    P["bWk_s"] = sct("ab_bWk", [128, NDC * D], BF16)
    P["bWv_s"] = sct("ab_bWv", [128, NDC * D], BF16)
    P["bWo_s"] = sct("ab_bWo", [128, NDC * D], BF16)
    ENTRY_V = ct.val(qV)
    Hscq, Hmkv, Hexq, Hsmq, Hrsq, Hrbq, Henv, Hcxq = {}, {}, {}, {}, {}, {}, {}, {}

    @block.gpsimd
    def _(g):
        g = EW(g)
        g.wait_ge(qV, ENTRY_V)
        g.wait_ge(qSY, ct.val(qSY))
        g.wait_ge(qTw2, ct.val(qTw2))
        g.wait_ge(qGP, ct.val(qGP))
        for dst, src in ((P["bWq_s"], P["bWq"]), (P["bWk_s"], P["bWk"]),
                         (P["bWv_s"], P["bWv"]), (P["bWo_s"], P["bWo"])):
            g.dma_start(out=dst[:, :].rearrange("p (c o) -> p c o", c=NDC),
                        in_=src.rearrange("(c p) o -> p c o", p=128)).then_inc(qGP, 16)
            ct.inc(qGP, 16)
    BW_GP = ct.val(qGP)

    for ti in range(NTIL):
        t0 = ti * 128
        slot = ti % 2
        pre = [(qGP, BW_GP)] if ti == 0 else None
        emit_ln(nc, block, ct, x_sb, P["xnb"], P["lnsq"], t0, 128, NT, LN,
                pre_waits=pre)
        XN_V = ct.val(qV)

        @block.tensor
        def _(t, ti=ti):
            t = EW(t)
            t.wait_ge(qV, XN_V)
            for oc in range(NDC):
                for dc in range(NDC):
                    t.matmul(psH[:, oc * 128:(oc + 1) * 128],
                             P["bWq_s"][:, dc * D + oc * 128: dc * D + oc * 128 + 128],
                             P["xnb"][:, dc * 128:(dc + 1) * 128],
                             start=(dc == 0), stop=(dc == NDC - 1))
            t.inc_last(qT, 1)
        ct.inc(qT)
        q1 = ct.val(qT)

        @block.vector
        def _(v, q1=q1):
            v = EW(v)
            v.wait_ge(qT, q1)
            for oc in range(NDC):
                v.tensor_copy(out=P["bQT"][:, oc * 128:(oc + 1) * 128],
                              in_=psH[:, oc * 128:(oc + 1) * 128])
            v.inc_last(qV, 1)
        ct.inc(qV)
        q1v = ct.val(qV)

        @block.tensor
        def _(t, XN_V=XN_V):
            t = EW(t)
            t.wait_ge(qV, XN_V)
            for oc in range(NDC):
                for dc in range(NDC):
                    t.matmul(psY[:, 1024 + oc * 128: 1024 + (oc + 1) * 128],
                             P["bWk_s"][:, dc * D + oc * 128: dc * D + oc * 128 + 128],
                             P["xnb"][:, dc * 128:(dc + 1) * 128],
                             start=(dc == 0), stop=(dc == NDC - 1))
            t.inc_last(qT, 1)
        ct.inc(qT)
        q2 = ct.val(qT)

        @block.vector
        def _(v, slot=slot, q2=q2):
            v = EW(v)
            v.wait_ge(qT, q2)
            for oc in range(NDC):
                v.tensor_copy(out=P["bKTr"][:, slot * 1024 + oc * 128:
                                            slot * 1024 + (oc + 1) * 128],
                              in_=psY[:, 1024 + oc * 128: 1024 + (oc + 1) * 128])
            v.inc_last(qV, 1)
        ct.inc(qV)
        q2v = ct.val(qV)

        @block.tensor
        def _(t, XN_V=XN_V):
            t = EW(t)
            t.wait_ge(qV, XN_V)
            for nh in range(2):
                for dc in range(NDC):
                    t.matmul(psY[:, nh * 512:(nh + 1) * 512],
                             P["xnb"][:, dc * 128:(dc + 1) * 128],
                             P["bWv_s"][:, dc * D + nh * 512: dc * D + nh * 512 + 512],
                             start=(dc == 0), stop=(dc == NDC - 1))
            t.inc_last(qT, 1)
        ct.inc(qT)
        q3 = ct.val(qT)

        @block.vector
        def _(v, slot=slot, q3=q3):
            v = EW(v)
            v.wait_ge(qT, q3)
            v.tensor_copy(out=P["bVtr"][:, slot * 1024: slot * 1024 + 512],
                          in_=psY[:, 0:512])
            v.tensor_copy(out=P["bVtr"][:, slot * 1024 + 512: slot * 1024 + 1024],
                          in_=psY[:, 512:1024])
            v.inc_last(qV, 1)
        ct.inc(qV)
        QKV_V = ct.val(qV)

        def hb_params(h):
            hg = ti * 16 + h
            hs = hg % 2
            stiles = ([((ti - 1) % 2, P["maskA_s"])] if ti > 0 else []) + \
                     [(slot, P["maskB_s"])]
            return dict(hg=hg, hs=hs, oc=h // 2, pb=(h % 2) * 64,
                        stiles=stiles, nsi=len(stiles),
                        sc0=hs * 256, eb0=(hg % 4) * 256,
                        scp=(psC[:, 0:256] if hs == 0 else psY[:, 0:256]),
                        smp=(psH[0:1, 0:128] if hs == 0 else psH[0:1, 512:640]),
                        rbp=(psD[:, 0:128] if hs == 0 else psY[:, 512:640]),
                        cxp=(psY[0:64, 1024:1152] if hs == 0
                             else psY[0:64, 1536:1664]))

        def hb_scores(pp):
            hg, scp = pp["hg"], pp["scp"]

            @block.tensor
            def _(t, pp=pp):
                t = EW(t)
                t.wait_ge(qV, max(QKV_V, Hmkv.get(hg - 2, 0)))
                for si, (ss, mk) in enumerate(pp["stiles"]):
                    t.matmul(scp[:, si * 128:(si + 1) * 128],
                             P["bKTr"][pp["pb"]:pp["pb"] + 64,
                                       ss * 1024 + pp["oc"] * 128:
                                       ss * 1024 + (pp["oc"] + 1) * 128],
                             P["bQT"][pp["pb"]:pp["pb"] + 64,
                                      pp["oc"] * 128:(pp["oc"] + 1) * 128],
                             start=True, stop=True, skip_group_check=True)
                t.inc_last(qT, 1)
            ct.inc(qT)
            Hscq[hg] = ct.val(qT)

        def hb_mask(pp):
            hg, sc0, eb0, scp = pp["hg"], pp["sc0"], pp["eb0"], pp["scp"]

            @block.vector
            def _(v, pp=pp):
                v = EW(v)
                v.wait_ge(qT, Hscq[hg])
                if hg >= 2:
                    v.wait_ge(qS, Hexq[hg - 2])
                for si, (ss, mk) in enumerate(pp["stiles"]):
                    v.scalar_tensor_tensor(
                        out=P["bmsk"][:, sc0 + si * 128: sc0 + (si + 1) * 128],
                        in0=scp[:, si * 128:(si + 1) * 128],
                        scalar=SCL, in1=mk[:, :], op0=ALU.mult, op1=ALU.add)
                v.inc_last(qV, 1)
            ct.inc(qV)
            Hmkv[hg] = ct.val(qV)

        def hb_exp(pp):
            hg, sc0, eb0, nsi = pp["hg"], pp["sc0"], pp["eb0"], pp["nsi"]

            @block.scalar
            def _(s, pp=pp):
                s = EW(s)
                s.wait_ge(qV, Hmkv[hg])
                if hg >= 4:
                    s.wait_ge(qT, Hsmq[hg - 4])
                    s.wait_ge(qV, Henv[hg - 4])
                s.activation(out=P["bexp"][:, eb0: eb0 + nsi * 128],
                             in_=P["bmsk"][:, sc0: sc0 + nsi * 128],
                             func=AF.Exp)
                s.inc_last(qS, 1)
            ct.inc(qS)
            Hexq[hg] = ct.val(qS)

        def hb_sum(pp):
            hg, eb0, smp, nsi = pp["hg"], pp["eb0"], pp["smp"], pp["nsi"]

            @block.tensor
            def _(t, pp=pp):
                t = EW(t)
                t.wait_ge(qS, Hexq[hg])
                if hg >= 2:
                    t.wait_ge(qV, Hrsq[hg - 2])
                for si in range(nsi):
                    t.matmul(smp[0:1, 0:128], P["onescol_b"][:, :],
                             P["bexp"][:, eb0 + si * 128: eb0 + (si + 1) * 128],
                             start=(si == 0), stop=(si == nsi - 1),
                             skip_group_check=True)
                t.inc_last(qT, 1)
            ct.inc(qT)
            Hsmq[hg] = ct.val(qT)

        def hb_recip(pp):
            hg, hs, smp = pp["hg"], pp["hs"], pp["smp"]

            @block.vector
            def _(v, pp=pp):
                v = EW(v)
                v.wait_ge(qT, Hsmq[hg])
                v.tensor_copy(out=P["brs"][0:1, hs * 128:(hs + 1) * 128],
                              in_=smp[0:1, 0:128])
                v.drain()
                v.reciprocal(P["brs"][0:1, hs * 128:(hs + 1) * 128],
                             P["brs"][0:1, hs * 128:(hs + 1) * 128])
                v.inc_last(qV, 1)
            ct.inc(qV)
            Hrsq[hg] = ct.val(qV)

        def hb_rsb(pp):
            hg, hs, rbp = pp["hg"], pp["hs"], pp["rbp"]

            @block.tensor
            def _(t, pp=pp):
                t = EW(t)
                t.wait_ge(qV, Hrsq[hg])
                t.matmul(rbp[:, 0:128], P["onesrow_f"][0:1, :],
                         P["brs"][0:1, hs * 128:(hs + 1) * 128],
                         start=True, stop=True, skip_group_check=True)
                t.inc_last(qT, 1)
            ct.inc(qT)
            Hrbq[hg] = ct.val(qT)

        def hb_norm(pp):
            hg, eb0, rbp, hs, nsi = pp["hg"], pp["eb0"], pp["rbp"], pp["hs"], pp["nsi"]

            @block.vector
            def _(v, pp=pp):
                v = EW(v)
                v.wait_ge(qT, Hrbq[hg])
                for si in range(nsi):
                    v.tensor_tensor(
                        out=P["bexn"][:, hs * 256 + si * 128: hs * 256 + (si + 1) * 128],
                        in0=P["bexp"][:, eb0 + si * 128: eb0 + (si + 1) * 128],
                        in1=rbp[:, 0:128], op=ALU.mult)
                v.inc_last(qV, 1)
            ct.inc(qV)
            Henv[hg] = ct.val(qV)

        def hb_ctx(pp):
            hg, hs, cxp = pp["hg"], pp["hs"], pp["cxp"]

            @block.tensor
            def _(t, pp=pp):
                t = EW(t)
                t.wait_ge(qV, Henv[hg])
                for si, (ss, mk) in enumerate(pp["stiles"]):
                    t.matmul(cxp[0:64, 0:128],
                             P["bVtr"][:, ss * 1024 + pp["oc"] * 128 + pp["pb"]:
                                       ss * 1024 + pp["oc"] * 128 + pp["pb"] + 64],
                             P["bexn"][:, hs * 256 + si * 128: hs * 256 + (si + 1) * 128],
                             start=(si == 0), stop=(si == pp["nsi"] - 1),
                             skip_group_check=True)
                t.inc_last(qT, 1)
            ct.inc(qT)
            Hcxq[hg] = ct.val(qT)

        def hb_evac(pp):
            hg, cxp = pp["hg"], pp["cxp"]

            @block.vector
            def _(v, pp=pp):
                v = EW(v)
                v.wait_ge(qT, Hcxq[hg])
                v.tensor_copy(out=P["bctx"][pp["pb"]:pp["pb"] + 64,
                                            pp["oc"] * 128:(pp["oc"] + 1) * 128],
                              in_=cxp[0:64, 0:128])
                v.inc_last(qV, 1)
            ct.inc(qV)

        PPs = [hb_params(h) for h in range(16)]
        for k in range(19):
            if k <= 15:
                hb_scores(PPs[k])
                hb_mask(PPs[k])
                hb_exp(PPs[k])
            if 1 <= k <= 16:
                hb_sum(PPs[k - 1])
                hb_recip(PPs[k - 1])
            if 2 <= k <= 17:
                hb_rsb(PPs[k - 2])
                hb_norm(PPs[k - 2])
            if 3 <= k <= 18:
                hb_ctx(PPs[k - 3])
                hb_evac(PPs[k - 3])
        CTX_V = ct.val(qV)
        opv = {}
        for dc in range(NDC):
            oslot = psA if dc % 2 == 0 else psB

            @block.tensor
            def _(t, dc=dc, t0=t0, oslot=oslot,
                  w=(ct.val(qV) if dc < 2 else opv[dc - 2])):
                t = EW(t)
                t.wait_ge(qV, w)
                for oc in range(NDC):
                    t.matmul(oslot[:, 0:128],
                             P["bWo_s"][:, oc * D + dc * 128: oc * D + dc * 128 + 128],
                             P["bctx"][:, oc * 128:(oc + 1) * 128],
                             start=(oc == 0), stop=(oc == NDC - 1))
                t.inc_last(qT, 1)
            ct.inc(qT)
            opq = ct.val(qT)

            @block.vector
            def _(v, dc=dc, t0=t0, opq=opq, oslot=oslot):
                v = EW(v)
                v.wait_ge(qT, opq)
                v.scalar_tensor_tensor(out=P["outst"][:, dc * 128:(dc + 1) * 128],
                                       in0=oslot[:, 0:128], scalar=1.0,
                                       in1=x_sb[:, dc * NT + t0: dc * NT + t0 + 128],
                                       op0=ALU.mult, op1=ALU.add)
                v.inc_last(qV, 1)
            ct.inc(qV)
            opv[dc] = ct.val(qV)
        OP_V = ct.val(qV)

        @block.vector
        def _(v, t0=t0, OP_V=OP_V):
            v = EW(v)
            v.wait_ge(qV, OP_V)
            for dc in range(NDC):
                v.tensor_copy(out=x_sb[:, dc * NT + t0: dc * NT + t0 + 128],
                              in_=P["outst"][:, dc * 128:(dc + 1) * 128])
            v.inc_last(qV, 1)
        ct.inc(qV)

    sc.close()


def _prep_inputs(inputs, b, T):
    f32 = np.float32
    x = np.ascontiguousarray(inputs["L0_attn_a_x"][b].T.astype(f32))
    d = {
        "x": x,
        "aWq": np.ascontiguousarray(inputs["L0_attn_a_Wq"].T),
        "aWk": np.ascontiguousarray(inputs["L0_attn_a_Wk"].T),
        "aWv": np.ascontiguousarray(inputs["L0_attn_a_Wv"].T),
        "aWo": np.ascontiguousarray(inputs["L0_attn_a_Wo"].T),
        "abq": inputs["L0_attn_a_bq"].reshape(H, 1),
        "abk": inputs["L0_attn_a_bk"].reshape(H, 1),
        "abv": inputs["L0_attn_a_bv"].reshape(1, H),
        "abo": np.ascontiguousarray(inputs["L0_attn_a_bo"].reshape(NDC, 128).T),
        "bWq": inputs["L0_attn_b_Wq"],
        "bWk": inputs["L0_attn_b_Wk"],
        "bWv": inputs["L0_attn_b_Wv"],
        "bWo": inputs["L0_attn_b_Wo"],
        "rW": np.ascontiguousarray(inputs["L0_ffn_w_router"].T),
        "W1T": np.ascontiguousarray(inputs["L0_ffn_W1"].transpose(0, 2, 1)),
        "W2T": np.ascontiguousarray(inputs["L0_ffn_W2"].transpose(0, 2, 1)),
        "Ws1T": np.ascontiguousarray(inputs["L0_ffn_Ws1"].T),
        "Ws2T": np.ascontiguousarray(inputs["L0_ffn_Ws2"].T),
        "b1fl": np.concatenate([np.asarray(inputs["L0_ffn_b1"]).reshape(-1),
                                np.asarray(inputs["L0_ffn_bs1"]).reshape(-1)]).reshape(1, 9 * DFF),
        "b2": inputs["L0_ffn_b2"],
        "bs2T": np.ascontiguousarray(inputs["L0_ffn_bs2"].reshape(NDC, 128).T),
    }
    return {k: np.ascontiguousarray(v).astype(f32) for k, v in d.items()}


_NC_CACHE = {}


def kernel(**inputs):
    T = inputs["L0_attn_a_x"].shape[1]
    B = inputs["L0_attn_a_x"].shape[0]
    if T not in _NC_CACHE:
        _NC_CACHE[T] = build_nc(T)
    nc = _NC_CACHE[T]
    in_maps = [_prep_inputs(inputs, b, T) for b in range(B)]
    res = run_bass_kernel_spmd(nc, in_maps, core_ids=list(range(B)))
    out = np.stack([np.ascontiguousarray(r["y"].T) for r in res.results], axis=0)
    return out.astype(np.float32)


if __name__ == "__main__":
    import reference as _r
    ins = _r.setup_inputs(0)
    ins = {k: np.asarray(v) for k, v in ins.items()}
    out = kernel(**ins)
    exp = np.asarray(_r.reference(**ins))
    err = np.linalg.norm(out - exp) / np.linalg.norm(exp)
    print("rel err:", err)
